# version 07
# baseline (speedup 1.0000x reference)
import numpy as np
from contextlib import ExitStack
import concourse.bass as bass
import concourse.mybir as mybir
from concourse.bass_utils import run_bass_kernel_spmd

F32 = mybir.dt.float32
BF16 = mybir.dt.bfloat16
F16 = mybir.dt.float16
AF = mybir.ActivationFunctionType
ALU = mybir.AluOpType
AX = mybir.AxisListType

D = 1024
T = 4096
NC = 8
NH = 16
HD = 64
QT = 512
NQT = T // QT
ENGS = ("sync", "tensor", "vector", "scalar", "gpsimd")

DBG = []
EXTRA_OUT = []
E_LAYERS = 6

C_SHIFT_A = 12.0
C_SHIFT_MLA = 16.0
C_SHIFT_E = 16.0


class Sem:
    __slots__ = ("h", "n", "name")
    def __init__(self, h, name):
        self.h = h; self.n = 0; self.name = name
    def inc(self, k=1):
        self.n += k
        return (self, k)


class EvFacade:
    def __init__(self, pp, name="ev"):
        self.pp = pp
        self.sems = {e: pp.sem(f"{name}_{e[0]}{e[1]}") for e in ENGS}
    def inc(self, k=1):
        return ("EVF", self, k)
    def now_list(self):
        return [(s, s.n) for s in self.sems.values() if s.n > 0]


class P:
    def __init__(self, nc, es):
        self.nc = nc
        self.es = es
        self.q = {e: [] for e in ENGS}
        self._sems = {}
        self.psem = {e: self.sem("prog_" + e) for e in ENGS}

    def sem(self, name):
        if name not in self._sems:
            self._sems[name] = Sem(self.es.enter_context(self.nc.semaphore(name)), name)
        return self._sems[name]

    def op(self, eng, meth, *args, inc=None, **kw):
        incs = []
        if inc is not None:
            if isinstance(inc, tuple) and len(inc) == 3 and inc[0] == "EVF":
                incs.append(inc[1].sems[eng].inc(inc[2]))
            else:
                incs.append(inc)
        self.q[eng].append(["op", meth, args, kw, incs])

    def wait(self, eng, sem, val=None):
        if isinstance(sem, EvFacade):
            assert val is None
            for (s, v) in sem.now_list():
                self.q[eng].append(["wait", s, v])
            return
        v = sem.n if val is None else val
        if v > 0:
            self.q[eng].append(["wait", sem, v])

    def extra_inc(self, eng, sem):
        for item in reversed(self.q[eng]):
            if item[0] == "op":
                if item[4]:
                    self.q[eng].append(["op", "drain", (), {}, [sem.inc(1)]])
                else:
                    k = 16 if item[1] in ("dma_start", "dma_start_transpose",
                                          "collective_compute") else 1
                    item[4].append(sem.inc(k))
                return True
        return False

    def block(self, engines=ENGS):
        snap = {e: self.psem[e].n for e in ENGS}
        marks = {e: len(self.q[e]) for e in ENGS}
        for e in engines:
            for o in ENGS:
                if o != e and snap[o] > 0:
                    self.wait(e, self.psem[o], snap[o])
        return marks

    def block_end(self, marks):
        fenced = set()
        for e, sname in (("sync", "st_s"), ("gpsimd", "st_g")):
            s = self._sems.get(sname)
            if s is not None and s.n > 0 and any(
                    it[0] == "op" and it[1] == "dma_start" and any(x[0] is s for x in it[4])
                    for it in self.q[e][marks[e]:]):
                self.wait(e, s, s.n)
                fenced.add(e)
        for e in ENGS:
            emitted = any(it[0] == "op" for it in self.q[e][marks[e]:])
            if emitted:
                if e in fenced:
                    self.q[e].append(["op", "drain", (), {}, [self.psem[e].inc(1)]])
                else:
                    self.extra_inc(e, self.psem[e])

    def replay_into(self):
        with self.nc.Block() as block:
            for name in ENGS:
                getattr(block, name)(lambda eng, nm=name: self._replay(eng, nm))

    def _replay(self, eng, name):
        for item in self.q[name]:
            if item[0] == "wait":
                _, sem, val = item
                eng.wait_ge(sem.h, val)
            else:
                _, meth, args, kw, incs = item
                r = getattr(eng, meth)(*args, **kw)
                for (sem, k) in incs:
                    r.then_inc(sem.h, k)


class Bank:
    __slots__ = ("t", "war", "held")
    def __init__(self, t):
        self.t = t; self.war = None; self.held = False


class BankPool:
    def __init__(self, pp, banks):
        self.pp = pp
        self.banks = [Bank(t) for t in banks]
        self.rr = 0

    def get(self, eng="tensor"):
        for _ in range(len(self.banks)):
            b = self.banks[self.rr % len(self.banks)]
            self.rr += 1
            if not b.held:
                break
        else:
            raise RuntimeError("all banks held")
        b.held = True
        if b.war is not None:
            for (s, v) in b.war:
                self.pp.wait(eng, s, v)
            b.war = None
        return b

    def rel(self, b, sem):
        b.war = sem.now_list() if isinstance(sem, EvFacade) else [(sem, sem.n)]
        b.held = False


class Slots:
    def __init__(self, pp, n):
        self.pp = pp; self.war = [None] * n; self.i = 0
    def get(self, writer_eng):
        s = self.i % len(self.war); self.i += 1
        if self.war[s] is not None:
            for (ss, v) in self.war[s]:
                self.pp.wait(writer_eng, ss, v)
            self.war[s] = None
        return s
    def used(self, s, sem):
        self.war[s] = sem.now_list() if isinstance(sem, EvFacade) else [(sem, sem.n)]


import os
PIPE_DEPTH_OVERRIDE = os.environ.get("PIPE_DEPTH")


class LSlots:
    def __init__(self, pp, n, name):
        self.pp = pp
        self.war = [None] * n
        self.lsem = [pp.sem(f"{name}{i}") for i in range(n)]
        self.i = 0
    def get(self, eng):
        s = self.i % len(self.war); self.i += 1
        if self.war[s] is not None:
            for (ss, v) in self.war[s]:
                self.pp.wait(eng, ss, v)
            self.war[s] = None
        return s
    def linc(self, s, k=16):
        return self.lsem[s].inc(k)
    def wait_loaded(self, eng, s):
        self.pp.wait(eng, self.lsem[s])
    def used(self, s, sem):
        self.war[s] = sem.now_list() if isinstance(sem, EvFacade) else [(sem, sem.n)]


class Pipe:
    def __init__(self, depth=3):
        if PIPE_DEPTH_OVERRIDE is not None:
            depth = int(PIPE_DEPTH_OVERRIDE)
        self.depth = depth
        self.q = []
    def push(self, fn):
        self.q.append(fn)
        if len(self.q) > self.depth:
            self.q.pop(0)()
    def flush(self):
        for fn in self.q:
            fn()
        self.q = []


def mm_group(pp, out_ap, pairs, inc=None):
    n = len(pairs)
    for i, (l, r) in enumerate(pairs):
        pp.op("tensor", "matmul", out_ap, l, r, start=(i == 0), stop=(i == n - 1),
              inc=(inc if i == n - 1 else None))


def _mask_tiles(deltas, qw, lo, hi):
    out = []
    kl = np.arange(128)[:, None]
    ql = np.arange(qw)[None, :]
    for d in deltas:
        diff = ql - (kl + d)
        out.append(((diff >= lo) & (diff <= hi)).astype(np.float32))
    return np.concatenate(out, axis=1)


def host_constants():
    import ml_dtypes
    c = {}
    c["maskA"] = _mask_tiles([-128, 0, 128, 256, 384, 512], QT, -128, 128).astype(ml_dtypes.bfloat16)
    c["maskB"] = _mask_tiles([0, 128, 256, 384], QT, 0, 10 ** 9).astype(ml_dtypes.bfloat16)
    c["maskE"] = _mask_tiles([-256, -128, 0, 128, 256, 384], QT, 0, 256).astype(ml_dtypes.bfloat16)
    half = 32
    pos = np.arange(T, dtype=np.float64)
    inv_freq = 1.0 / 10000.0 ** (np.arange(half, dtype=np.float64) / half)
    ang = pos[None, :] * inv_freq[:, None]
    cos = np.cos(ang); sin = np.sin(ang)
    C1 = np.concatenate([cos, cos], axis=0)
    S1 = np.concatenate([-sin, sin], axis=0)
    c["ropeB_C"] = np.tile(C1, (2, 1)).astype(ml_dtypes.bfloat16)
    c["ropeB_S"] = np.tile(S1, (2, 1)).astype(ml_dtypes.bfloat16)
    P = np.zeros((128, 128), np.float32)
    for base in (0, 64):
        for j in range(32):
            P[base + j + 32, base + j] = 1.0
            P[base + j, base + j + 32] = 1.0
    c["permB"] = P.astype(ml_dtypes.bfloat16)
    bd2 = np.zeros((128, 2), np.float32)
    bd2[0:64, 0] = 1.0; bd2[64:128, 1] = 1.0
    c["bd2"] = bd2.astype(ml_dtypes.bfloat16)
    ex2 = np.zeros((2, 128), np.float32)
    ex2[0, 0:64] = 1.0; ex2[1, 64:128] = 1.0
    c["ex2"] = ex2.astype(ml_dtypes.bfloat16)
    c["ex2f"] = ex2
    DH = 64
    half_idx = np.arange(0, DH, 2, dtype=np.float64)
    inv_freq = 1.0 / 10000.0 ** (half_idx / DH)
    t_ = np.arange(T, dtype=np.float64)
    freqs = t_[:, None] * inv_freq[None, :]
    freqs = np.concatenate([freqs, freqs], -1)
    base_scale = (half_idx + 0.4 * DH) / (1.4 * DH)
    power = (t_ - T // 2) / (256 // 2)
    scale = base_scale[None, :] ** power[:, None]
    scale = np.concatenate([scale, scale], -1)
    cosf = np.cos(freqs); sinf = np.sin(freqs)
    Cq_ = (cosf * scale).T; Sq_ = (sinf * scale).T
    Ck_ = (cosf / scale).T; Sk_ = (sinf / scale).T
    c["E_Cq"] = np.tile(Cq_, (2, 1)).astype(ml_dtypes.bfloat16)
    c["E_Sq"] = np.tile(Sq_, (2, 1)).astype(ml_dtypes.bfloat16)
    c["E_Ck"] = np.tile(Ck_, (2, 1)).astype(ml_dtypes.bfloat16)
    c["E_Sk"] = np.tile(Sk_, (2, 1)).astype(ml_dtypes.bfloat16)
    PE_ = np.zeros((128, 128), np.float32)
    for base in (0, 64):
        for j in range(32):
            PE_[base + j + 32, base + j] = -1.0
            PE_[base + j, base + j + 32] = 1.0
    c["E_permE"] = PE_.astype(ml_dtypes.bfloat16)
    return c


def _cols(vec, nchunks):
    return np.ascontiguousarray(np.asarray(vec).reshape(nchunks, 128).T)


def ln_stats_rows(pp, banks, sq_slots, sq_sb, ones_col, src_chunks, rowap, mm, ev, sqsem, eps_ap=None):
    nch = len(src_chunks)
    bsum = banks.get(); bsq = banks.get()
    for c, xc in enumerate(src_chunks):
        s = sq_slots.get("scalar")
        ssl = sq_sb[:, s * QT:(s + 1) * QT][:, 0:xc.shape[1]]
        pp.op("scalar", "activation", ssl, xc, AF.Square, inc=sqsem.inc())
        pp.op("tensor", "matmul", bsum.t[0:1, 0:xc.shape[1]], ones_col[:], xc,
              start=(c == 0), stop=(c == nch - 1), inc=(mm.inc() if c == nch - 1 else None))
        pp.wait("tensor", sqsem)
        pp.op("tensor", "matmul", bsq.t[0:1, 0:xc.shape[1]], ones_col[:], ssl,
              start=(c == 0), stop=(c == nch - 1), inc=mm.inc())
        sq_slots.used(s, mm)
    W = src_chunks[0].shape[1]
    nfeat = float(nch * 128)
    pp.wait("vector", mm)
    pp.op("vector", "tensor_scalar_mul", rowap(0), bsum.t[0:1, 0:W], 1.0 / nfeat)
    pp.op("vector", "tensor_scalar_mul", rowap(1), bsq.t[0:1, 0:W], 1.0 / nfeat, inc=ev.inc())
    banks.rel(bsum, ev); banks.rel(bsq, ev)
    pp.op("vector", "tensor_mul", rowap(2), rowap(0), rowap(0))
    pp.op("vector", "tensor_sub", rowap(2), rowap(1), rowap(2), inc=ev.inc())
    pp.wait("scalar", ev)
    pp.op("scalar", "activation", rowap(3), rowap(2), AF.Ln, bias=eps_ap)
    pp.op("scalar", "activation", rowap(3), rowap(3), AF.Exp, scale=-0.5, inc=ev.inc())
    pp.wait("vector", ev)
    pp.op("vector", "tensor_mul", rowap(4), rowap(0), rowap(3), inc=ev.inc())


def emit_phaseA(pp, ctx):
    nc = ctx["nc"]; inp = ctx["inp"]; dram = ctx["dram"]
    es = pp.es
    sb = lambda name, shape, dt: es.enter_context(nc.sbuf_tensor(name, shape, dt))

    x_sb = sb("x_sb", [128, 8 * QT], F32)
    sq_sb = sb("sq_sb", [128, 2 * QT], F32)
    tmp_sb = sb("tmp_sb", [128, QT], F32)
    t_ring = sb("t_ring", [128, NC * 8 * 128], BF16)
    q_ring = sb("q_ring", [128, NC * 2 * QT], BF16)
    k_ring = sb("k_ring", [128, NC * 12 * 128], BF16)
    v_ring = sb("v_ring", [128, 12 * (NH * 65)], BF16)
    exp_sb = sb("exp_sb", [128, 2 * QT], F32)
    attnT_sb = sb("attnT_sb", [128, 3 * QT], BF16)
    ctxb_sb = sb("ctxb_sb", [128, NC * QT], BF16)
    dnr_sb = sb("dnr_sb", [1, 2 * QT], F32)
    rows = sb("rows", [1, 5 * QT], F32)
    x1_sb = sb("x1_sb", [128, QT], F32)
    w_in_sb = sb("w_in_sb", [128, NC * 3 * D], BF16)
    w_out_sb = sb("w_out_sb", [128, NC * D], BF16)
    bin_sb = sb("bin_sb", [128, 24], F32)
    bvrow_sb = sb("bvrow_sb", [1, D], BF16)
    bout_sb = sb("bout_sb", [128, 8], F32)
    maskA_sb = sb("maskA_sb", [128, 6 * QT], BF16)
    ones_col = sb("ones_col", [128, 1], F32)
    eps_row = sb("eps_row", [1, 1], F32)
    cshA_col = sb("cshA_col", [128, 1], F32)
    ones_row = sb("ones_row", [1, 128], F32)
    ones_rowb = sb("ones_rowb", [1, 128], BF16)

    banks = BankPool(pp, [es.enter_context(nc.psum_tensor(f"bk{i}", [128, 512], F32))
                          for i in range(8)])
    ld_s = pp.sem("ld_s"); ld_g = pp.sem("ld_g")
    st_s = pp.sem("st_s"); st_g = pp.sem("st_g")
    mm = pp.sem("mm"); ev = EvFacade(pp)
    sqsem = pp.sem("sq")
    sq_slots = Slots(pp, 2)
    xrl_slots = LSlots(pp, 2, "xrl")
    exp_slots = Slots(pp, 2)
    at_slots = Slots(pp, 3)
    x1_slots = Slots(pp, 1)

    mk = pp.block()
    pp.op("vector", "memset", ones_col[:], 1.0)
    pp.op("vector", "memset", eps_row[:], 1e-5)
    pp.op("vector", "memset", cshA_col[:], -C_SHIFT_A)
    pp.op("vector", "memset", ones_row[:], 1.0)
    pp.op("vector", "memset", ones_rowb[:], 1.0)
    pp.op("vector", "memset",
          v_ring.ap().rearrange("p (s h j) -> p s h j", s=12, h=NH, j=65)[:, :, :, 64:65], 1.0,
          inc=pp.sem("ev").inc())
    pp.op("sync", "dma_start", out=bin_sb[:], in_=inp["A_b_in_cols"], inc=ld_s.inc(16))
    pp.op("sync", "dma_start", out=bout_sb[:], in_=inp["A_b_out_cols"], inc=ld_s.inc(16))
    pp.op("sync", "dma_start", out=maskA_sb[:], in_=inp["maskA"], inc=ld_s.inc(16))
    pp.op("gpsimd", "dma_start", out=bvrow_sb[:], in_=inp["A_b_vrow"], inc=ld_g.inc(16))
    for kc in range(NC):
        pp.op("gpsimd", "dma_start", out=w_in_sb[:, kc * 3 * D:(kc + 1) * 3 * D],
              in_=inp["A_w_inT"][kc * 128:(kc + 1) * 128, :], inc=ld_g.inc(16))
        pp.op("gpsimd", "dma_start", out=w_out_sb[:, kc * D:(kc + 1) * D],
              in_=inp["A_w_outT"][kc * 128:(kc + 1) * 128, :], inc=ld_g.inc(16))
    pp.block_end(mk)

    xT = inp["xT"]; x1T = dram["x1T"]
    R = lambda r: rows[0:1, r * QT:(r + 1) * QT]

    def tr(c, tile):
        s = tile % 8
        return t_ring[:, c * 1024 + s * 128: c * 1024 + (s + 1) * 128]

    def tr_qt(c, qt):
        s = (qt * 4) % 8
        return t_ring[:, c * 1024 + s * 128: c * 1024 + s * 128 + 512]

    def qr(c, qt):
        s = qt % 2
        return q_ring[:, c * 2 * QT + s * QT: c * 2 * QT + (s + 1) * QT]

    def kr(c, tile):
        s = tile % 12
        return k_ring[:, c * 12 * 128 + s * 128: c * 12 * 128 + (s + 1) * 128]

    def kr_qt(c, qt):
        s = (qt * 4) % 12
        return k_ring[:, c * 12 * 128 + s * 128: c * 12 * 128 + s * 128 + 512]

    def vr(tile, h):
        s = tile % 12
        return v_ring[:, s * (NH * 65) + h * 65: s * (NH * 65) + h * 65 + 65]

    def vr_w(tile, g):
        s = tile % 12
        base = v_ring[:, s * (NH * 65) + 8 * g * 65: s * (NH * 65) + (8 * g + 8) * 65]
        return base.rearrange("p (h j) -> p h j", h=8, j=65)[:, :, 0:64]

    def emit_ln_qkv(qt):
        mk = pp.block()
        t0 = qt * QT
        for c in range(NC):
            pp.op("sync", "dma_start", out=x_sb[:, c * QT:(c + 1) * QT],
                  in_=xT[c * 128:(c + 1) * 128, t0:t0 + QT], inc=ld_s.inc(16))
        pp.wait("scalar", ld_s); pp.wait("scalar", ld_g)
        pp.wait("tensor", ld_s); pp.wait("tensor", ld_g)
        src = [x_sb[:, c * QT:(c + 1) * QT] for c in range(NC)]
        ln_stats_rows(pp, banks, sq_slots, sq_sb, ones_col, src, R, mm, ev, sqsem, eps_row[0:1, 0:1])
        brstd = banks.get(); bmmr = banks.get()
        pp.wait("tensor", ev)
        pp.op("tensor", "matmul", brstd.t[:], ones_row[:], R(3), start=True, stop=True)
        pp.op("tensor", "matmul", bmmr.t[:], ones_row[:], R(4), start=True, stop=True, inc=mm.inc())
        pp.wait("vector", mm)
        for c in range(NC):
            pp.op("vector", "tensor_mul", tmp_sb[:], x_sb[:, c * QT:(c + 1) * QT], brstd.t[:])
            pp.op("vector", "tensor_sub", tr_qt(c, qt), tmp_sb[:], bmmr.t[:],
                  inc=(ev.inc() if c == NC - 1 else None))
        banks.rel(brstd, ev); banks.rel(bmmr, ev)

        if "t" in DBG:
            pp.wait("gpsimd", ev)
            for c in range(NC):
                pp.op("gpsimd", "dma_start", out=dram["dbgT"][c * 128:(c + 1) * 128, t0:t0 + QT],
                      in_=tr_qt(c, qt), inc=st_g.inc(16))
        pp.wait("tensor", ev)
        for kind, base_oc in (("q", 0), ("k", 8)):
            for oc in range(NC):
                bo = banks.get()
                pairs = []
                for kc in range(NC):
                    l = w_in_sb[:, kc * 3 * D + (base_oc + oc) * 128:
                                kc * 3 * D + (base_oc + oc) * 128 + 128]
                    pairs.append((l, tr_qt(kc, qt)))
                mm_group(pp, bo.t[:], pairs, inc=mm.inc())
                pp.wait("scalar", mm)
                dst = qr(oc, qt) if kind == "q" else kr_qt(oc, qt)
                pp.op("scalar", "activation", dst, bo.t[:], AF.Identity,
                      bias=bin_sb[:, base_oc + oc: base_oc + oc + 1], inc=ev.inc())
                banks.rel(bo, ev)
        if "q" in DBG or "k" in DBG:
            pp.wait("gpsimd", ev)
            for c in range(NC):
                if "q" in DBG:
                    pp.op("gpsimd", "dma_start", out=dram["dbgT"][c * 128:(c + 1) * 128, t0:t0 + QT],
                          in_=qr(c, qt), inc=st_g.inc(16))
                if "k" in DBG:
                    pp.op("gpsimd", "dma_start", out=dram["dbgT"][D + c * 128:D + (c + 1) * 128, t0:t0 + QT],
                          in_=kr_qt(c, qt), inc=st_g.inc(16))
        for i in range(4):
            tile = qt * 4 + i
            for g in range(2):
                bo = banks.get()
                for kc in range(NC):
                    l = tr(kc, tile)
                    r = w_in_sb[:, kc * 3 * D + 2 * D + g * 512:
                                kc * 3 * D + 2 * D + (g + 1) * 512]
                    pp.op("tensor", "matmul", bo.t[:], l, r, start=(kc == 0), stop=False)
                pp.op("tensor", "matmul", bo.t[:], ones_rowb[:],
                      bvrow_sb[0:1, g * 512:(g + 1) * 512], start=False, stop=True, inc=mm.inc())
                pp.wait("vector", mm)
                pp.op("vector", "tensor_copy", vr_w(tile, g),
                      bo.t.ap().rearrange("p (h j) -> p h j", h=8, j=64), inc=ev.inc())
                banks.rel(bo, ev)
        if "v" in DBG:
            pp.wait("gpsimd", ev)
            for i in range(4):
                tile = qt * 4 + i
                if tile >= 3:
                    continue
                s = tile % 12
                pp.op("gpsimd", "dma_start",
                      out=dram["dbgT"][0:128, tile * 1040:(tile + 1) * 1040],
                      in_=v_ring[:, s * 1040:(s + 1) * 1040], inc=st_g.inc(16))
        pp.block_end(mk)

    def emit_attn(qt):
        mk = pp.block()
        t0 = qt * QT
        kbs_all = [(i, qt * 4 - 1 + i) for i in range(6)]
        kbs = [(i, kb) for i, kb in kbs_all if 0 <= kb < T // 128]
        for c in range(NC):
            bctxs = {}
            pipe = Pipe(depth=3)
            for hh in range(2):
                h = 2 * c + hh
                rowoff = hh * 64
                bctxs[hh] = banks.get()
                first = True
                for i, kb in kbs:
                    bs = banks.get()
                    pp.op("tensor", "matmul", bs.t[:], kr(c, kb)[rowoff:rowoff + 64, :],
                          qr(c, qt)[rowoff:rowoff + 64, :], start=True, stop=True, inc=mm.inc())
                    mmv = mm.n

                    def consume(bs=bs, h=h, hh=hh, i=i, kb=kb, mmv=mmv, first=first,
                                last=(i == kbs[-1][0])):
                        pp.wait("scalar", mm, mmv)
                        es_ = exp_slots.get("scalar")
                        eslot = exp_sb[:, es_ * QT:(es_ + 1) * QT]
                        pp.op("scalar", "activation", eslot, bs.t[:], AF.Exp,
                              scale=0.125, bias=cshA_col[:, 0:1], inc=ev.inc())
                        banks.rel(bs, ev)
                        pp.wait("vector", ev)
                        as_ = at_slots.get("vector")
                        aslot = attnT_sb[:, as_ * QT:(as_ + 1) * QT]
                        pp.op("vector", "scalar_tensor_tensor", aslot, eslot, 1e30,
                              maskA_sb[:, i * QT:(i + 1) * QT], ALU.min, ALU.mult, inc=ev.inc())
                        exp_slots.used(es_, ev)
                        pp.wait("tensor", ev)
                        pp.op("tensor", "matmul", bctxs[hh].t[0:65, :], vr(kb, h), aslot,
                              start=first, stop=last, inc=mm.inc())
                        at_slots.used(as_, mm)
                    pipe.push(consume)
                    first = False
            pipe.flush()
            for hh in range(2):
                rowoff = hh * 64
                bctx = bctxs[hh]
                pp.wait("scalar", mm)
                pp.op("scalar", "copy", ctxb_sb[rowoff:rowoff + 64, c * QT:(c + 1) * QT],
                      bctx.t[0:64, :], inc=ev.inc())
                pp.wait("vector", mm)
                pp.op("vector", "tensor_scalar_add", dnr_sb[0:1, hh * QT:(hh + 1) * QT],
                      bctx.t[64:65, :], 1e-30)
                pp.op("vector", "reciprocal", dnr_sb[0:1, hh * QT:(hh + 1) * QT],
                      dnr_sb[0:1, hh * QT:(hh + 1) * QT], inc=ev.inc())
                banks.rel(bctx, ev)
            br = banks.get()
            pp.wait("tensor", ev)
            pp.op("tensor", "matmul", br.t[0:64, :], ones_row[0:1, 0:64],
                  dnr_sb[0:1, 0:QT], start=True, stop=True)
            pp.op("tensor", "matmul", br.t[64:128, :], ones_row[0:1, 0:64],
                  dnr_sb[0:1, QT:2 * QT], start=True, stop=True, inc=mm.inc())
            pp.wait("vector", mm)
            pp.op("vector", "tensor_mul", ctxb_sb[:, c * QT:(c + 1) * QT],
                  ctxb_sb[:, c * QT:(c + 1) * QT], br.t[:], inc=ev.inc())
            banks.rel(br, ev)

        if "ctx" in DBG:
            pp.wait("gpsimd", ev)
            for c in range(NC):
                pp.op("gpsimd", "dma_start", out=dram["dbgT"][c * 128:(c + 1) * 128, t0:t0 + QT],
                      in_=ctxb_sb[:, c * QT:(c + 1) * QT], inc=st_g.inc(16))
        pp.wait("tensor", ev)
        for oc in range(NC):
            bo = banks.get()
            pairs = []
            for kc in range(NC):
                l = w_out_sb[:, kc * D + oc * 128: kc * D + oc * 128 + 128]
                pairs.append((l, ctxb_sb[:, kc * QT:(kc + 1) * QT]))
            mm_group(pp, bo.t[:], pairs, inc=mm.inc())
            pp.wait("vector", mm)
            pp.op("vector", "scalar_tensor_tensor", x_sb[:, oc * QT:(oc + 1) * QT],
                  bo.t[:], bout_sb[:, oc:oc + 1], tr_qt(oc, qt), ALU.add, ALU.add, inc=ev.inc())
            banks.rel(bo, ev)

        if "u" in DBG:
            pp.wait("gpsimd", ev)
            for c in range(NC):
                pp.op("gpsimd", "dma_start", out=dram["dbgT"][c * 128:(c + 1) * 128, t0:t0 + QT],
                      in_=x_sb[:, c * QT:(c + 1) * QT], inc=st_g.inc(16))
        pp.wait("scalar", ev)
        pp.wait("tensor", ev)
        src = [x_sb[:, c * QT:(c + 1) * QT] for c in range(NC)]
        ln_stats_rows(pp, banks, sq_slots, sq_sb, ones_col, src, R, mm, ev, sqsem, eps_row[0:1, 0:1])
        brstd = banks.get(); bmmr = banks.get()
        pp.wait("tensor", ev)
        pp.op("tensor", "matmul", brstd.t[:], ones_row[:], R(3), start=True, stop=True)
        pp.op("tensor", "matmul", bmmr.t[:], ones_row[:], R(4), start=True, stop=True, inc=mm.inc())
        pp.wait("vector", mm)
        pp.wait("sync", mm)
        for c in range(NC):
            xs_ = xrl_slots.get("sync")
            xsl = sq_sb[:, xs_ * QT:(xs_ + 1) * QT]
            pp.op("sync", "dma_start", out=xsl,
                  in_=xT[c * 128:(c + 1) * 128, t0:t0 + QT], inc=xrl_slots.linc(xs_))
            xrl_slots.wait_loaded("vector", xs_)
            s1_ = x1_slots.get("vector")
            x1sl = x1_sb[:, s1_ * QT:(s1_ + 1) * QT]
            pp.op("vector", "tensor_mul", tmp_sb[:], x_sb[:, c * QT:(c + 1) * QT], brstd.t[:])
            pp.op("vector", "tensor_sub", tmp_sb[:], tmp_sb[:], bmmr.t[:])
            pp.op("vector", "tensor_add", x1sl, tmp_sb[:], xsl, inc=ev.inc())
            xrl_slots.used(xs_, ev)
            if "x1" in DBG:
                pp.wait("gpsimd", ev)
                pp.op("gpsimd", "dma_start", out=dram["dbgT"][c * 128:(c + 1) * 128, t0:t0 + QT],
                      in_=x1sl, inc=st_g.inc(16))
            if "lnu" in DBG:
                pp.wait("gpsimd", ev)
                pp.op("gpsimd", "dma_start", out=dram["dbgT"][D + c * 128:D + (c + 1) * 128, t0:t0 + QT],
                      in_=xsl, inc=st_g.inc(16))
            pp.wait("sync", ev)
            pp.op("sync", "dma_start", out=x1T[c * 128:(c + 1) * 128, t0:t0 + QT],
                  in_=x1sl, inc=st_s.inc(16))
            x1_slots.used(s1_, st_s)
        banks.rel(brstd, ev); banks.rel(bmmr, ev)
        pp.block_end(mk)

    for qt in range(NQT):
        emit_ln_qkv(qt)
        if qt >= 1:
            emit_attn(qt - 1)
    emit_attn(NQT - 1)
    pp.wait("sync", st_s); pp.wait("gpsimd", st_g)


def build_program(phases=("A",)):
    nc = bass.Bass("TRN2", target_bir_lowering=False)
    inp = {}
    def I(name, shape, dt=F32):
        inp[name] = nc.dram_tensor(name, list(shape), dt, kind="ExternalInput").ap()
        return inp[name]

    I("xT", [D, T])
    I("A_w_inT", [D, 3 * D]); I("A_b_in_cols", [128, 24]); I("A_b_vrow", [1, D])
    I("A_w_outT", [D, D]); I("A_b_out_cols", [128, 8])
    I("maskA", [128, 6 * QT], BF16)
    I("B_WqT", [D, D]); I("B_WdkvT", [D, 512]); I("B_WukT", [512, D])
    I("B_WuvT", [512, D]); I("B_WoT", [D, D])
    I("B_n1w_cols", [128, 8]); I("B_lng_cols", [128, 4]); I("B_lnb_cols", [128, 4])
    I("maskB", [128, 4 * QT], BF16)
    I("maskE", [128, 6 * QT], BF16)
    I("ropeB_C", [128, T], BF16); I("ropeB_S", [128, T], BF16); I("permB", [128, 128], BF16)
    I("C_n2w_cols", [128, 8]); I("C_gwT", [D, 4]); I("C_gb_col", [4, 1])
    I("C_W1T", [4 * D, 4096]); I("C_W2T", [4 * 4096, D])
    I("C_b1_cols", [128, 4 * 32]); I("C_b2_cols", [128, 4 * 8])
    I("D_wqT", [D, D]); I("D_wkT", [D, D]); I("D_wvT", [D, D])
    I("D_wg1T", [D, 16]); I("D_bg1_col", [16, 1]); I("D_wg2T", [16, D])
    I("D_bg2_cols", [128, 8]); I("D_wogT", [D, D]); I("D_bog_cols", [128, 8])
    I("D_woT", [D, D]); I("D_wgateT", [D, 4096]); I("D_wupT", [D, 4096])
    I("D_wdownT", [4096, D]); I("D_s1_cols", [128, 8]); I("D_s2_cols", [128, 8])
    I("D_bd2", [128, 2], BF16); I("D_ex2", [2, 128], BF16); I("D_ex2f", [2, 128])
    I("D_smask", [128, 8]); I("D_imask", [128, 8]); I("D_wm2", [2, 8])
    I("E_Wqkv", [6 * D, 3 * D]); I("E_Wo", [6 * D, D])
    I("E_W1p", [6 * D, 5632]); I("E_W2p", [6 * 2816, D])
    I("E_qs_cols", [128, 6]); I("E_ks_cols", [128, 6])
    I("E_lng_cols", [128, 6 * 8]); I("E_lnb_cols", [128, 6 * 8])
    I("E_b1p_cols", [128, 6 * 44]); I("E_b2_cols", [128, 6 * 8])
    I("E_Cq", [128, T], BF16); I("E_Sq", [128, T], BF16)
    I("E_Ck", [128, T], BF16); I("E_Sk", [128, T], BF16)
    I("E_permE", [128, 128], BF16)

    out_name = {"A": "x1T", "B": "aT", "C": "x2T", "D": "x3T", "E": "x4T"}[phases[-1]]
    dram = {}
    def S(name, shape, dt=F32):
        kind = "ExternalOutput" if (name == out_name or name in EXTRA_OUT) else "Internal"
        if kind == "ExternalOutput":
            dram[name] = nc.dram_tensor(name, list(shape), dt, kind=kind).ap()
        else:
            dram[name] = nc.dram_tensor(name, list(shape), dt).ap()
        return dram[name]

    S("x1T", [D, T])
    S("t1T", [D, T], BF16); S("xnT", [D, T], BF16)
    S("kT", [D, T], BF16); S("vT", [T, NH * 65], BF16)
    S("aT", [D, T])
    S("mT", [D, T], BF16); S("accT", [D, T]); S("mrT", [4, T]); S("x2T", [D, T])
    S("t2T", [D, T], BF16); S("xn2T", [D, T], BF16); S("qTd", [D, T], BF16)
    S("kpub", [D, T], BF16); S("vpub", [D, T], BF16); S("apub", [D, T], F16)
    S("kG", [8 * D, T], BF16); S("vG", [8 * D, T], BF16); S("aG", [8 * D, T], F16)
    S("yT", [D, T]); S("hT", [4096, T], BF16); S("x3T", [D, T])
    S("xxA", [D, T]); S("xxB", [D, T]); S("x4T", [D, T])
    if DBG:
        dram["dbgT"] = nc.dram_tensor("dbgT", [2 * D, T], F32, kind="ExternalOutput").ap()

    ctx = dict(nc=nc, inp=inp, dram=dram)
    for ph in phases:
        es = ExitStack()
        pp = P(nc, es)
        {"A": emit_phaseA, "B": emit_phaseB, "C": emit_phaseC,
         "D": emit_phaseD, "E": emit_phaseE}[ph](pp, ctx)
        pp.replay_into()
        es.close()
        nc.reset()
    return nc


def emit_phaseB(pp, ctx):
    nc = ctx["nc"]; inp = ctx["inp"]; dram = ctx["dram"]
    es = pp.es
    sb = lambda name, shape, dt: es.enter_context(nc.sbuf_tensor("B" + name, shape, dt))

    banks = BankPool(pp, [es.enter_context(nc.psum_tensor(f"Bbk{i}", [128, 512], F32))
                          for i in range(8)])
    ld_s = pp.sem("ld_s"); ld_g = pp.sem("ld_g")
    st_s = pp.sem("st_s"); st_g = pp.sem("st_g")
    mm = pp.sem("mm"); ev = EvFacade(pp)
    sqsem = pp.sem("sq")

    ropeC = sb("ropeC", [128, T], BF16)
    ropeS = sb("ropeS", [128, T], BF16)
    perm = sb("perm", [128, 128], BF16)
    ones_col = sb("ones_col", [128, 1], F32)
    ones_row = sb("ones_row", [1, 128], F32)
    eps5 = sb("eps5", [1, 1], F32)
    eps6 = sb("eps6", [1, 1], F32)
    cshB = sb("cshB", [128, 1], F32)
    n1w_col = sb("n1w_col", [128, 8], F32)
    lng_col = sb("lng_col", [128, 4], F32)
    lnb_col = sb("lnb_col", [128, 4], F32)

    x1T = dram["x1T"]; t1T = dram["t1T"]; xnT = dram["xnT"]
    kT = dram["kT"]; vT = dram["vT"]; aT = dram["aT"]

    mk = pp.block()
    pp.op("vector", "memset", ones_col[:], 1.0)
    pp.op("vector", "memset", ones_row[:], 1.0)
    pp.op("vector", "memset", eps5[:], 1e-5)
    pp.op("vector", "memset", eps6[:], 1e-6)
    pp.op("vector", "memset", cshB[:], -C_SHIFT_MLA)
    pp.op("gpsimd", "dma_start", out=ropeC[:], in_=inp["ropeB_C"], inc=ld_g.inc(16))
    pp.op("gpsimd", "dma_start", out=ropeS[:], in_=inp["ropeB_S"], inc=ld_g.inc(16))
    pp.op("gpsimd", "dma_start", out=perm[:], in_=inp["permB"], inc=ld_g.inc(16))
    pp.op("sync", "dma_start", out=n1w_col[:], in_=inp["B_n1w_cols"], inc=ld_s.inc(16))
    pp.op("sync", "dma_start", out=lng_col[:], in_=inp["B_lng_cols"], inc=ld_s.inc(16))
    pp.op("sync", "dma_start", out=lnb_col[:], in_=inp["B_lnb_cols"], inc=ld_s.inc(16))
    pp.block_end(mk)

    def rope_chunk(dst_ap, src_bf_ap, t0, w, sgn=1.0):
        bp = banks.get()
        pp.op("tensor", "matmul", bp.t[:, 0:w], perm[:], src_bf_ap, start=True, stop=True,
              inc=mm.inc())
        pp.wait("vector", mm)
        pp.op("vector", "tensor_mul", rope_t1[:, 0:w], bp.t[:, 0:w], ropeS[:, t0:t0 + w])
        pp.op("vector", "tensor_mul", rope_t2[:, 0:w], src_bf_ap, ropeC[:, t0:t0 + w])
        pp.op("vector", "tensor_add", dst_ap, rope_t2[:, 0:w], rope_t1[:, 0:w], inc=ev.inc())
        banks.rel(bp, ev)

    es2 = ExitStack()
    sb2 = lambda name, shape, dt: es2.enter_context(nc.sbuf_tensor("B" + name, shape, dt))
    x1_sb = sb2("x1_sb", [128, 8 * QT], F32)
    sq_sb = sb2("sqB", [128, 2 * QT], F32)
    tmp_sb = sb2("tmpB", [128, QT], F32)
    rows = sb2("rowsB", [1, 8 * QT], F32)
    t1_sb = sb2("t1B", [128, 8 * QT], BF16)
    xn_sb = sb2("xnB", [128, 8 * QT], BF16)
    lat_f = sb2("latf", [128, 4 * QT], F32)
    lat_b = sb2("latb", [128, 4 * QT], BF16)
    kw_sb = sb2("kwB", [128, 8 * QT], BF16)
    vt_sb = sb2("vtB", [128, 2 * 1040], BF16)
    rope_t1 = sb2("ropet1", [128, QT], F32)
    rope_t2 = sb2("ropet2", [128, QT], F32)
    wdkv_sb = sb2("wdkv", [128, 8 * 512], BF16)
    wuk_sb = sb2("wuk", [128, 4 * D], BF16)
    wuv_sb = sb2("wuv", [128, 4 * D], BF16)

    sq_slots = Slots(pp, 2)
    vt_slots = Slots(pp, 2)
    R = lambda r: rows[0:1, r * QT:(r + 1) * QT]

    mk = pp.block()
    pp.op("vector", "memset",
          vt_sb.ap().rearrange("p (s h j) -> p s h j", s=2, h=NH, j=65)[:, :, :, 64:65], 1.0)
    for kc in range(NC):
        pp.op("gpsimd", "dma_start", out=wdkv_sb[:, kc * 512:(kc + 1) * 512],
              in_=inp["B_WdkvT"][kc * 128:(kc + 1) * 128, :], inc=ld_g.inc(16))
    for kc in range(4):
        pp.op("gpsimd", "dma_start", out=wuk_sb[:, kc * D:(kc + 1) * D],
              in_=inp["B_WukT"][kc * 128:(kc + 1) * 128, :], inc=ld_g.inc(16))
        pp.op("gpsimd", "dma_start", out=wuv_sb[:, kc * D:(kc + 1) * D],
              in_=inp["B_WuvT"][kc * 128:(kc + 1) * 128, :], inc=ld_g.inc(16))
    pp.block_end(mk)

    for qt in range(NQT):
        mk = pp.block()
        t0 = qt * QT
        for c in range(NC):
            pp.op("sync", "dma_start", out=x1_sb[:, c * QT:(c + 1) * QT],
                  in_=x1T[c * 128:(c + 1) * 128, t0:t0 + QT], inc=ld_s.inc(16))
        pp.wait("scalar", ld_s); pp.wait("scalar", ld_g)
        pp.wait("tensor", ld_s); pp.wait("tensor", ld_g)
        src = [x1_sb[:, c * QT:(c + 1) * QT] for c in range(NC)]
        ln_stats_rows(pp, banks, sq_slots, sq_sb, ones_col, src, R, mm, ev, sqsem,
                      eps5[0:1, 0:1])
        pp.op("vector", "tensor_scalar_add", R(5), R(2), 0.0)
        pp.op("vector", "tensor_single_scalar", R(6), R(2), 1e-5, ALU.add)
        pp.op("vector", "reciprocal", R(6), R(6))
        pp.op("vector", "tensor_mul", R(5), R(5), R(6), inc=ev.inc())
        pp.wait("scalar", ev)
        pp.op("scalar", "activation", R(5), R(5), AF.Ln, bias=eps6[0:1, 0:1])
        pp.op("scalar", "activation", R(5), R(5), AF.Exp, scale=-0.5, inc=ev.inc())
        brstd = banks.get(); bmmr = banks.get(); bs_ = banks.get()
        pp.wait("tensor", ev)
        pp.op("tensor", "matmul", brstd.t[:], ones_row[:], R(3), start=True, stop=True)
        pp.op("tensor", "matmul", bmmr.t[:], ones_row[:], R(4), start=True, stop=True)
        pp.op("tensor", "matmul", bs_.t[:], ones_row[:], R(5), start=True, stop=True, inc=mm.inc())
        pp.wait("vector", mm)
        for c in range(NC):
            t1sl = t1_sb[:, c * QT:(c + 1) * QT]
            pp.op("vector", "tensor_mul", tmp_sb[:], x1_sb[:, c * QT:(c + 1) * QT], brstd.t[:])
            pp.op("vector", "tensor_sub", t1sl, tmp_sb[:], bmmr.t[:])
            pp.op("vector", "scalar_tensor_tensor", xn_sb[:, c * QT:(c + 1) * QT],
                  t1sl, n1w_col[:, c:c + 1], bs_.t[:], ALU.mult, ALU.mult, inc=ev.inc())
            pp.wait("sync", ev)
            pp.op("sync", "dma_start", out=t1T[c * 128:(c + 1) * 128, t0:t0 + QT],
                  in_=t1sl, inc=st_s.inc(16))
            pp.op("sync", "dma_start", out=xnT[c * 128:(c + 1) * 128, t0:t0 + QT],
                  in_=xn_sb[:, c * QT:(c + 1) * QT], inc=st_s.inc(16))
        banks.rel(brstd, ev); banks.rel(bmmr, ev); banks.rel(bs_, ev)
        pp.block_end(mk)

    for qt in range(NQT):
        mk = pp.block()
        t0 = qt * QT
        for c in range(NC):
            pp.op("sync", "dma_start", out=xn_sb[:, c * QT:(c + 1) * QT],
                  in_=xnT[c * 128:(c + 1) * 128, t0:t0 + QT], inc=ld_s.inc(16))
        pp.wait("tensor", ld_s); pp.wait("tensor", ld_g)
        for oc in range(4):
            bo = banks.get()
            pairs = [(wdkv_sb[:, kc * 512 + oc * 128: kc * 512 + (oc + 1) * 128],
                      xn_sb[:, kc * QT:(kc + 1) * QT]) for kc in range(NC)]
            mm_group(pp, bo.t[:], pairs, inc=mm.inc())
            pp.wait("scalar", mm)
            pp.op("scalar", "copy", lat_f[:, oc * QT:(oc + 1) * QT], bo.t[:], inc=ev.inc())
            banks.rel(bo, ev)
        pp.wait("scalar", ev)
        pp.wait("tensor", ev)
        src = [lat_f[:, c * QT:(c + 1) * QT] for c in range(4)]
        ln_stats_rows(pp, banks, sq_slots, sq_sb, ones_col, src, R, mm, ev, sqsem,
                      eps5[0:1, 0:1])
        brstd = banks.get(); bmmr = banks.get()
        pp.wait("tensor", ev)
        pp.op("tensor", "matmul", brstd.t[:], ones_row[:], R(3), start=True, stop=True)
        pp.op("tensor", "matmul", bmmr.t[:], ones_row[:], R(4), start=True, stop=True, inc=mm.inc())
        pp.wait("vector", mm)
        for c in range(4):
            pp.op("vector", "tensor_mul", tmp_sb[:], lat_f[:, c * QT:(c + 1) * QT], brstd.t[:])
            pp.op("vector", "tensor_sub", tmp_sb[:], tmp_sb[:], bmmr.t[:])
            pp.op("vector", "tensor_scalar", lat_b[:, c * QT:(c + 1) * QT], tmp_sb[:],
                  lng_col[:, c:c + 1], lnb_col[:, c:c + 1], ALU.mult, ALU.add,
                  inc=ev.inc())
        banks.rel(brstd, ev); banks.rel(bmmr, ev)
        pp.wait("tensor", ev)
        for oc in range(NC):
            bo = banks.get()
            pairs = [(wuk_sb[:, kc * D + oc * 128: kc * D + (oc + 1) * 128],
                      lat_b[:, kc * QT:(kc + 1) * QT]) for kc in range(4)]
            mm_group(pp, bo.t[:], pairs, inc=mm.inc())
            pp.wait("scalar", mm)
            pp.op("scalar", "copy", kw_sb[:, oc * QT:(oc + 1) * QT], bo.t[:], inc=ev.inc())
            banks.rel(bo, ev)
        pp.wait("tensor", ev)
        pp.wait("vector", ev)
        for oc in range(NC):
            rope_chunk(kw_sb[:, oc * QT:(oc + 1) * QT], kw_sb[:, oc * QT:(oc + 1) * QT],
                       t0, QT)
            pp.wait("sync", ev)
            pp.op("sync", "dma_start", out=kT[oc * 128:(oc + 1) * 128, t0:t0 + QT],
                  in_=kw_sb[:, oc * QT:(oc + 1) * QT], inc=st_s.inc(16))
        for i in range(4):
            tile = qt * 4 + i
            vs_ = vt_slots.get("vector")
            for g in range(2):
                bo = banks.get()
                for kc in range(4):
                    pp.op("tensor", "matmul", bo.t[:],
                          lat_b[:, kc * QT + i * 128: kc * QT + (i + 1) * 128],
                          wuv_sb[:, kc * D + g * 512: kc * D + (g + 1) * 512],
                          start=(kc == 0), stop=(kc == 3), inc=(mm.inc() if kc == 3 else None))
                pp.wait("vector", mm)
                base = vt_sb[:, vs_ * 1040 + 8 * g * 65: vs_ * 1040 + (8 * g + 8) * 65]
                pp.op("vector", "tensor_copy", base.rearrange("p (h j) -> p h j", h=8, j=65)[:, :, 0:64],
                      bo.t.ap().rearrange("p (h j) -> p h j", h=8, j=64), inc=ev.inc())
                banks.rel(bo, ev)
            pp.wait("sync", ev)
            pp.op("sync", "dma_start", out=vT[tile * 128:(tile + 1) * 128, :],
                  in_=vt_sb[:, vs_ * 1040:(vs_ + 1) * 1040], inc=st_s.inc(16))
            vt_slots.used(vs_, st_s)
        pp.block_end(mk)

    es2.close()

    es3 = ExitStack()
    sb3 = lambda name, shape, dt: es3.enter_context(nc.sbuf_tensor("B" + name, shape, dt))
    k_full = sb3("kfull", [128, NC * T], BF16)
    v_ring = sb3("vringB", [128, 4 * 1040], BF16)
    xn_sl = sb3("xnsl", [128, 8 * QT], BF16)
    t1_sl = sb3("t1sl", [128, 8 * QT], BF16)
    q_fm = sb3("qfm", [128, 8 * QT], BF16)
    qa_sb = sb3("qa", [128, QT], BF16)
    rope_t1 = sb3("ropet1b", [128, QT], F32)
    rope_t2 = sb3("ropet2b", [128, QT], F32)
    wq_sb = sb3("wqB", [128, 8 * D], BF16)
    wo_sb = sb3("woB", [128, 8 * D], BF16)
    ctxb_sb = sb3("ctxbB", [128, 8 * QT], BF16)
    exp_sb = sb3("expB", [128, 3 * QT], F32)
    attnT_sb = sb3("attnTB", [128, 4 * QT], BF16)
    dnr_sb = sb3("dnrB", [1, 2 * QT], F32)
    maskB_sb = sb3("maskBs", [128, 4 * QT], BF16)
    a_sb = sb3("aB", [128, 2 * QT], F32)

    exp_slots = Slots(pp, 3)
    at_slots = Slots(pp, 4)
    a_slots = Slots(pp, 2)
    vr_slots = LSlots(pp, 4, "vld")

    mk = pp.block()
    pp.op("sync", "dma_start", out=maskB_sb[:], in_=inp["maskB"], inc=ld_s.inc(16))
    for kc in range(NC):
        pp.op("gpsimd", "dma_start", out=wq_sb[:, kc * D:(kc + 1) * D],
              in_=inp["B_WqT"][kc * 128:(kc + 1) * 128, :], inc=ld_g.inc(16))
        pp.op("gpsimd", "dma_start", out=wo_sb[:, kc * D:(kc + 1) * D],
              in_=inp["B_WoT"][kc * 128:(kc + 1) * 128, :], inc=ld_g.inc(16))
        pp.op("sync", "dma_start", out=k_full[:, kc * T:(kc + 1) * T],
              in_=kT[kc * 128:(kc + 1) * 128, :], inc=ld_s.inc(16))
    pp.block_end(mk)

    def kf(c, kb):
        return k_full[:, c * T + kb * 128: c * T + (kb + 1) * 128]

    HGROUPS = [list(range(0, 4)), list(range(4, 8)), list(range(8, 12)), list(range(12, 16))]

    for qt in range(NQT):
        mk = pp.block()
        t0 = qt * QT
        for c in range(NC):
            pp.op("sync", "dma_start", out=xn_sl[:, c * QT:(c + 1) * QT],
                  in_=xnT[c * 128:(c + 1) * 128, t0:t0 + QT], inc=ld_s.inc(16))
            pp.op("sync", "dma_start", out=t1_sl[:, c * QT:(c + 1) * QT],
                  in_=t1T[c * 128:(c + 1) * 128, t0:t0 + QT], inc=ld_s.inc(16))
        pp.wait("tensor", ld_s); pp.wait("tensor", ld_g)
        for oc in range(NC):
            bo = banks.get()
            pairs = [(wq_sb[:, kc * D + oc * 128: kc * D + (oc + 1) * 128],
                      xn_sl[:, kc * QT:(kc + 1) * QT]) for kc in range(NC)]
            mm_group(pp, bo.t[:], pairs, inc=mm.inc())
            pp.wait("scalar", mm)
            pp.op("scalar", "copy", qa_sb[:], bo.t[:], inc=ev.inc())
            banks.rel(bo, ev)
            pp.wait("tensor", ev)
            pp.wait("vector", ev)
            rope_chunk(q_fm[:, oc * QT:(oc + 1) * QT], qa_sb[:], t0, QT)
        if "Bq" in DBG and qt == 0:
            pp.wait("gpsimd", ev)
            for c in range(NC):
                pp.op("gpsimd", "dma_start", out=dram["dbgT"][c * 128:(c + 1) * 128, 0:QT],
                      in_=q_fm[:, c * QT:(c + 1) * QT], inc=pp.sem("st_g").inc(16))
        nkb = qt * 4 + 4
        pp.wait("tensor", ld_s); pp.wait("tensor", ld_g)
        for grp in HGROUPS:
            cb = {h: banks.get() for h in grp}
            first = {h: True for h in grp}
            pipe = Pipe(depth=2)
            vheld = []
            for kb in range(nkb):
                vs_ = vr_slots.get("gpsimd")
                pp.op("gpsimd", "dma_start", out=v_ring[:, vs_ * 1040:(vs_ + 1) * 1040],
                      in_=vT[kb * 128:(kb + 1) * 128, :], inc=vr_slots.linc(vs_))
                vr_slots.wait_loaded("tensor", vs_)
                diag = kb - qt * 4
                for h in grp:
                    c = h // 2; rowoff = (h % 2) * 64
                    bs = banks.get()
                    pp.op("tensor", "matmul", bs.t[:], kf(c, kb)[rowoff:rowoff + 64, :],
                          q_fm[:, c * QT:(c + 1) * QT][rowoff:rowoff + 64, :],
                          start=True, stop=True, inc=mm.inc())
                    mmv = mm.n

                    def consume(bs=bs, h=h, kb=kb, vs_=vs_, diag=diag, mmv=mmv,
                                first_h=first[h]):
                        pp.wait("scalar", mm, mmv)
                        as_ = at_slots.get("vector")
                        aslot = attnT_sb[:, as_ * QT:(as_ + 1) * QT]
                        es_ = exp_slots.get("scalar")
                        eslot = exp_sb[:, es_ * QT:(es_ + 1) * QT]
                        pp.op("scalar", "activation", eslot, bs.t[:], AF.Exp,
                              scale=0.125, bias=cshB[:, 0:1], inc=ev.inc())
                        banks.rel(bs, ev)
                        pp.wait("vector", ev)
                        if diag < 0:
                            pp.op("vector", "tensor_copy", aslot, eslot, inc=ev.inc())
                        else:
                            pp.op("vector", "scalar_tensor_tensor", aslot, eslot, 1e30,
                                  maskB_sb[:, diag * QT:(diag + 1) * QT], ALU.min, ALU.mult,
                                  inc=ev.inc())
                        exp_slots.used(es_, ev)
                        pp.wait("tensor", ev)
                        pp.op("tensor", "matmul", cb[h].t[0:65, :],
                              v_ring[:, vs_ * 1040 + h * 65: vs_ * 1040 + h * 65 + 65],
                              aslot, start=first_h, stop=(kb == nkb - 1), inc=mm.inc())
                        at_slots.used(as_, mm)
                    pipe.push(consume)
                    first[h] = False
                vheld.append(vs_)
                while len(vheld) > 2:
                    vr_slots.used(vheld.pop(0), mm)
            pipe.flush()
            for vs_ in vheld:
                vr_slots.used(vs_, mm)
            for h in grp:
                c = h // 2; rowoff = (h % 2) * 64; hh = h % 2
                pp.wait("scalar", mm)
                pp.op("scalar", "copy", ctxb_sb[rowoff:rowoff + 64, c * QT:(c + 1) * QT],
                      cb[h].t[0:64, :], inc=ev.inc())
                pp.wait("vector", mm)
                pp.op("vector", "tensor_scalar_add", dnr_sb[0:1, hh * QT:(hh + 1) * QT],
                      cb[h].t[64:65, :], 1e-30)
                pp.op("vector", "reciprocal", dnr_sb[0:1, hh * QT:(hh + 1) * QT],
                      dnr_sb[0:1, hh * QT:(hh + 1) * QT], inc=ev.inc())
                banks.rel(cb[h], ev)
                if hh == 1:
                    br = banks.get()
                    pp.wait("tensor", ev)
                    pp.op("tensor", "matmul", br.t[0:64, :], ones_row[0:1, 0:64],
                          dnr_sb[0:1, 0:QT], start=True, stop=True)
                    pp.op("tensor", "matmul", br.t[64:128, :], ones_row[0:1, 0:64],
                          dnr_sb[0:1, QT:2 * QT], start=True, stop=True, inc=mm.inc())
                    pp.wait("vector", mm)
                    pp.op("vector", "tensor_mul", ctxb_sb[:, c * QT:(c + 1) * QT],
                          ctxb_sb[:, c * QT:(c + 1) * QT], br.t[:], inc=ev.inc())
                    banks.rel(br, ev)
        if "Bctx" in DBG and qt == 0:
            pp.wait("gpsimd", ev)
            for c in range(NC):
                pp.op("gpsimd", "dma_start", out=dram["dbgT"][D + c * 128:D + (c + 1) * 128, 0:QT],
                      in_=ctxb_sb[:, c * QT:(c + 1) * QT], inc=pp.sem("st_g").inc(16))
        pp.wait("tensor", ev)
        for oc in range(NC):
            bo = banks.get()
            pairs = [(wo_sb[:, kc * D + oc * 128: kc * D + (oc + 1) * 128],
                      ctxb_sb[:, kc * QT:(kc + 1) * QT]) for kc in range(NC)]
            mm_group(pp, bo.t[:], pairs, inc=mm.inc())
            pp.wait("vector", mm)
            sa = a_slots.get("vector")
            asl = a_sb[:, sa * QT:(sa + 1) * QT]
            pp.op("vector", "tensor_add", asl, bo.t[:], t1_sl[:, oc * QT:(oc + 1) * QT],
                  inc=ev.inc())
            banks.rel(bo, ev)
            pp.wait("sync", ev)
            pp.op("sync", "dma_start", out=aT[oc * 128:(oc + 1) * 128, t0:t0 + QT],
                  in_=asl, inc=st_s.inc(16))
            a_slots.used(sa, st_s)
        pp.block_end(mk)

    es3.close()
    pp.wait("sync", st_s); pp.wait("gpsimd", st_g)


def emit_phaseC(pp, ctx):
    nc = ctx["nc"]; inp = ctx["inp"]; dram = ctx["dram"]
    es = pp.es
    sb = lambda name, shape, dt: es.enter_context(nc.sbuf_tensor("C" + name, shape, dt))

    banks = BankPool(pp, [es.enter_context(nc.psum_tensor(f"Cbk{i}", [128, 512], F32))
                          for i in range(8)])
    ld_s = pp.sem("ld_s"); ld_g = pp.sem("ld_g")
    st_s = pp.sem("st_s"); st_g = pp.sem("st_g")
    mm = pp.sem("mm"); ev = EvFacade(pp)
    sqsem = pp.sem("sq")

    aT = dram["aT"]; x1T = dram["x1T"]; mT = dram["mT"]
    accT = dram["accT"]; mrT = dram["mrT"]; x2T = dram["x2T"]

    ones_col = sb("ones_col", [128, 1], F32)
    ones_row = sb("ones_row", [1, 128], F32)
    eps5 = sb("eps5", [1, 1], F32)
    eps6 = sb("eps6", [1, 1], F32)
    n2w_col = sb("n2w", [128, 8], F32)
    gwT_sb = sb("gwT", [128, 8 * 4], F32)
    gb_col = sb("gb", [4, 1], F32)

    mk = pp.block()
    pp.op("vector", "memset", ones_col[:], 1.0)
    pp.op("vector", "memset", ones_row[:], 1.0)
    pp.op("vector", "memset", eps5[:], 1e-5)
    pp.op("vector", "memset", eps6[:], 1e-6)
    pp.op("sync", "dma_start", out=n2w_col[:], in_=inp["C_n2w_cols"], inc=ld_s.inc(16))
    pp.op("sync", "dma_start", out=gb_col[:], in_=inp["C_gb_col"], inc=ld_s.inc(16))
    for kc in range(NC):
        pp.op("sync", "dma_start", out=gwT_sb[:, kc * 4:(kc + 1) * 4],
              in_=inp["C_gwT"][kc * 128:(kc + 1) * 128, :], inc=ld_s.inc(16))
    pp.block_end(mk)

    es2 = ExitStack()
    sb2 = lambda name, shape, dt: es2.enter_context(nc.sbuf_tensor("C0" + name, shape, dt))
    a_sb = sb2("a", [128, 8 * QT], F32)
    sq_sb = sb2("sq", [128, 2 * QT], F32)
    za_sb = sb2("za", [128, 8 * QT], F32)
    m_sb = sb2("m", [128, 8 * QT], BF16)
    rows = sb2("rows", [1, 8 * QT], F32)
    lg_sb = sb2("lg", [4, 2 * QT], F32)
    mrow_sb = sb2("mrow", [1, 8 * QT], F32)
    sq_slots = Slots(pp, 2)
    R = lambda r: rows[0:1, r * QT:(r + 1) * QT]

    for qt in range(NQT):
        mk = pp.block()
        t0 = qt * QT
        for c in range(NC):
            pp.op("sync", "dma_start", out=a_sb[:, c * QT:(c + 1) * QT],
                  in_=aT[c * 128:(c + 1) * 128, t0:t0 + QT], inc=ld_s.inc(16))
        pp.wait("scalar", ld_s); pp.wait("scalar", ld_g)
        pp.wait("tensor", ld_s); pp.wait("tensor", ld_g)
        bsq = banks.get()
        for c in range(NC):
            s_ = sq_slots.get("scalar")
            ssl = sq_sb[:, s_ * QT:(s_ + 1) * QT]
            pp.op("scalar", "activation", ssl, a_sb[:, c * QT:(c + 1) * QT], AF.Square,
                  inc=sqsem.inc())
            pp.wait("tensor", sqsem)
            pp.op("tensor", "matmul", bsq.t[0:1, :], ones_col[:], ssl,
                  start=(c == 0), stop=(c == NC - 1), inc=mm.inc())
            sq_slots.used(s_, mm)
        pp.wait("vector", mm)
        pp.op("vector", "tensor_scalar_mul", R(0), bsq.t[0:1, :], 1.0 / D, inc=ev.inc())
        banks.rel(bsq, ev)
        pp.wait("scalar", ev)
        pp.op("scalar", "activation", R(1), R(0), AF.Ln, bias=eps6[0:1, 0:1])
        pp.op("scalar", "activation", R(1), R(1), AF.Exp, scale=-0.5, inc=ev.inc())
        bs_ = banks.get(); bs4 = banks.get()
        pp.wait("tensor", ev)
        pp.op("tensor", "matmul", bs_.t[:], ones_row[:], R(1), start=True, stop=True)
        pp.op("tensor", "matmul", bs4.t[0:4, :], ones_row[0:1, 0:4], R(1),
              start=True, stop=True, inc=mm.inc())
        pp.wait("vector", mm)
        for c in range(NC):
            pp.op("vector", "tensor_scalar_mul", za_sb[:, c * QT:(c + 1) * QT],
                  a_sb[:, c * QT:(c + 1) * QT], n2w_col[:, c:c + 1])
            pp.op("vector", "tensor_mul", m_sb[:, c * QT:(c + 1) * QT],
                  za_sb[:, c * QT:(c + 1) * QT], bs_.t[:], inc=ev.inc())
            pp.wait("sync", ev)
            pp.op("sync", "dma_start", out=mT[c * 128:(c + 1) * 128, t0:t0 + QT],
                  in_=m_sb[:, c * QT:(c + 1) * QT], inc=st_s.inc(16))
        banks.rel(bs_, ev)
        bl = banks.get()
        pp.wait("tensor", ev)
        pairs = [(gwT_sb[:, kc * 4:(kc + 1) * 4], za_sb[:, kc * QT:(kc + 1) * QT])
                 for kc in range(NC)]
        mm_group(pp, bl.t[0:4, :], pairs, inc=mm.inc())
        pp.wait("scalar", mm)
        pp.op("scalar", "copy", lg_sb[0:4, 0:QT], bl.t[0:4, :], inc=ev.inc())
        banks.rel(bl, ev)
        pp.wait("vector", ev)
        pp.op("vector", "tensor_mul", lg_sb[0:4, QT:2 * QT], lg_sb[0:4, 0:QT], bs4.t[0:4, :])
        pp.op("vector", "tensor_scalar_add", lg_sb[0:4, QT:2 * QT],
              lg_sb[0:4, QT:2 * QT], gb_col[0:4, 0:1], inc=ev.inc())
        banks.rel(bs4, ev)
        pp.wait("gpsimd", ev)
        for e in range(4):
            pp.op("gpsimd", "dma_start", out=mrow_sb[0:1, e * QT:(e + 1) * QT],
                  in_=lg_sb[e:e + 1, QT:2 * QT], inc=ld_g.inc(16))
        pp.wait("vector", ld_s); pp.wait("vector", ld_g)
        MX = mrow_sb[0:1, 4 * QT:5 * QT]
        pp.op("vector", "tensor_max", MX, mrow_sb[0:1, 0:QT], mrow_sb[0:1, QT:2 * QT])
        pp.op("vector", "tensor_max", mrow_sb[0:1, 5 * QT:6 * QT],
              mrow_sb[0:1, 2 * QT:3 * QT], mrow_sb[0:1, 3 * QT:4 * QT])
        pp.op("vector", "tensor_max", MX, MX, mrow_sb[0:1, 5 * QT:6 * QT])
        for e in range(4):
            pp.op("vector", "tensor_tensor", mrow_sb[0:1, e * QT:(e + 1) * QT],
                  mrow_sb[0:1, e * QT:(e + 1) * QT], MX, ALU.is_ge,
                  inc=(ev.inc() if e == 3 else None))
        pp.wait("sync", ev)
        for e in range(4):
            pp.op("sync", "dma_start", out=mrT[e:e + 1, t0:t0 + QT],
                  in_=mrow_sb[0:1, e * QT:(e + 1) * QT], inc=st_s.inc(16))
        pp.block_end(mk)
    es2.close()

    es3 = ExitStack()
    sb3 = lambda name, shape, dt: es3.enter_context(nc.sbuf_tensor("C1" + name, shape, dt))
    w1_sb = sb3("w1", [128, NC * 4096], BF16)
    w2_sb = sb3("w2", [128, 32 * D], BF16)
    m_sl = sb3("msl", [128, 8 * QT], BF16)
    h_sb = sb3("h", [128, 32 * QT], BF16)
    mrow2 = sb3("mrow2", [1, 2 * QT], F32)
    maskb_sb = sb3("maskb", [128, 2 * QT], F32)
    y_sb = sb3("y", [128, 2 * QT], F32)
    b1_col = sb3("b1c", [128, 4 * 32], F32)
    b2_col = sb3("b2c", [128, 4 * 8], F32)
    y_slots = Slots(pp, 2)
    kva_war = [None, None]
    kvald = [pp.sem("kvald0"), pp.sem("kvald1")]
    mr_slots = Slots(pp, 2)
    acc_war = {}

    mk = pp.block()
    pp.op("sync", "dma_start", out=b1_col[:], in_=inp["C_b1_cols"], inc=ld_s.inc(16))
    pp.op("sync", "dma_start", out=b2_col[:], in_=inp["C_b2_cols"], inc=ld_s.inc(16))
    pp.block_end(mk)

    for e in range(4):
        mk = pp.block()
        for kc in range(NC):
            pp.op("gpsimd", "dma_start", out=w1_sb[:, kc * 4096:(kc + 1) * 4096],
                  in_=inp["C_W1T"][e * D + kc * 128: e * D + (kc + 1) * 128, :],
                  inc=ld_g.inc(16))
        for kc in range(32):
            pp.op("gpsimd", "dma_start", out=w2_sb[:, kc * D:(kc + 1) * D],
                  in_=inp["C_W2T"][e * 4096 + kc * 128: e * 4096 + (kc + 1) * 128, :],
                  inc=ld_g.inc(16))
        pp.block_end(mk)
        for qt in range(NQT):
            mk = pp.block()
            t0 = qt * QT
            for c in range(NC):
                pp.op("sync", "dma_start", out=m_sl[:, c * QT:(c + 1) * QT],
                      in_=mT[c * 128:(c + 1) * 128, t0:t0 + QT], inc=ld_s.inc(16))
            mr_ = mr_slots.get("sync")
            pp.op("sync", "dma_start", out=mrow2[0:1, mr_ * QT:(mr_ + 1) * QT],
                  in_=mrT[e:e + 1, t0:t0 + QT], inc=ld_s.inc(16))
            pp.wait("tensor", ld_s); pp.wait("tensor", ld_g)
            bmk = banks.get()
            pp.op("tensor", "matmul", bmk.t[:], ones_row[:],
                  mrow2[0:1, mr_ * QT:(mr_ + 1) * QT], start=True, stop=True, inc=mm.inc())
            pp.wait("vector", mm)
            mb = maskb_sb[:, (qt % 2) * QT:(qt % 2 + 1) * QT]
            pp.op("vector", "tensor_copy", mb, bmk.t[:], inc=ev.inc())
            mr_slots.used(mr_, ev)
            banks.rel(bmk, ev)
            for oc in range(32):
                bo = banks.get()
                pairs = [(w1_sb[:, kc * 4096 + oc * 128: kc * 4096 + (oc + 1) * 128],
                          m_sl[:, kc * QT:(kc + 1) * QT]) for kc in range(NC)]
                mm_group(pp, bo.t[:], pairs, inc=mm.inc())
                pp.wait("scalar", mm)
                pp.op("scalar", "activation", h_sb[:, oc * QT:(oc + 1) * QT], bo.t[:],
                      AF.Relu, bias=b1_col[:, e * 32 + oc: e * 32 + oc + 1], inc=ev.inc())
                banks.rel(bo, ev)
            pp.wait("tensor", ev)
            for oc in range(NC):
                bo = banks.get()
                pairs = [(w2_sb[:, kc * D + oc * 128: kc * D + (oc + 1) * 128],
                          h_sb[:, kc * QT:(kc + 1) * QT]) for kc in range(32)]
                mm_group(pp, bo.t[:], pairs, inc=mm.inc())
                pp.wait("vector", mm)
                ys = y_slots.get("vector")
                ysl = y_sb[:, ys * QT:(ys + 1) * QT]
                pp.op("vector", "scalar_tensor_tensor", ysl, bo.t[:],
                      b2_col[:, e * 8 + oc: e * 8 + oc + 1], mb, ALU.add, ALU.mult,
                      inc=ev.inc())
                banks.rel(bo, ev)
                pp.wait("gpsimd", ev)
                key = (qt, oc)
                if key in acc_war:
                    pp.wait("gpsimd", acc_war[key][0], acc_war[key][1])
                if e == 0:
                    pp.op("gpsimd", "dma_start", out=accT[oc * 128:(oc + 1) * 128, t0:t0 + QT],
                          in_=ysl, inc=st_g.inc(16))
                else:
                    pp.op("gpsimd", "dma_start", out=accT[oc * 128:(oc + 1) * 128, t0:t0 + QT],
                          in_=ysl, accum_op=ALU.add, inc=st_g.inc(16))
                acc_war[key] = (st_g, st_g.n)
                y_slots.used(ys, st_s)
            pp.block_end(mk)
    es3.close()

    es4 = ExitStack()
    sb4 = lambda name, shape, dt: es4.enter_context(nc.sbuf_tensor("C2" + name, shape, dt))
    b_sb = sb4("b", [128, 8 * QT], F32)
    acc_sb = sb4("acc", [128, 8 * QT], F32)
    sq_sb2 = sb4("sq", [128, 2 * QT], F32)
    tmp_sb = sb4("tmp", [128, QT], F32)
    rows2 = sb4("rows", [1, 5 * QT], F32)
    x2_sb = sb4("x2", [128, 2 * QT], F32)
    sq2_slots = Slots(pp, 2)
    acc_slots = Slots(pp, 2)
    x2_slots = Slots(pp, 2)
    R2 = lambda r: rows2[0:1, r * QT:(r + 1) * QT]

    for qt in range(NQT):
        mk = pp.block()
        t0 = qt * QT
        for c in range(NC):
            pp.op("sync", "dma_start", out=acc_sb[:, c * QT:(c + 1) * QT],
                  in_=accT[c * 128:(c + 1) * 128, t0:t0 + QT], inc=ld_s.inc(16))
            pp.op("sync", "dma_start", out=b_sb[:, c * QT:(c + 1) * QT],
                  in_=aT[c * 128:(c + 1) * 128, t0:t0 + QT], inc=ld_s.inc(16))
        pp.wait("vector", ld_s); pp.wait("vector", ld_g)
        for c in range(NC):
            pp.op("vector", "tensor_add", b_sb[:, c * QT:(c + 1) * QT],
                  b_sb[:, c * QT:(c + 1) * QT], acc_sb[:, c * QT:(c + 1) * QT],
                  inc=ev.inc())
        pp.wait("scalar", ev)
        pp.wait("tensor", ev)
        src = [b_sb[:, c * QT:(c + 1) * QT] for c in range(NC)]
        ln_stats_rows(pp, banks, sq2_slots, sq_sb2, ones_col, src, R2, mm, ev, sqsem,
                      eps5[0:1, 0:1])
        brstd = banks.get(); bmmr = banks.get()
        pp.wait("tensor", ev)
        pp.op("tensor", "matmul", brstd.t[:], ones_row[:], R2(3), start=True, stop=True)
        pp.op("tensor", "matmul", bmmr.t[:], ones_row[:], R2(4), start=True, stop=True, inc=mm.inc())
        pp.wait("sync", ev)
        for c in range(NC):
            pp.op("sync", "dma_start", out=acc_sb[:, c * QT:(c + 1) * QT],
                  in_=x1T[c * 128:(c + 1) * 128, t0:t0 + QT], inc=ld_s.inc(16))
        pp.wait("vector", mm)
        pp.wait("vector", ld_s); pp.wait("vector", ld_g)
        for c in range(NC):
            s2_ = x2_slots.get("vector")
            x2sl = x2_sb[:, s2_ * QT:(s2_ + 1) * QT]
            pp.op("vector", "tensor_mul", tmp_sb[:], b_sb[:, c * QT:(c + 1) * QT], brstd.t[:])
            pp.op("vector", "tensor_sub", tmp_sb[:], tmp_sb[:], bmmr.t[:])
            pp.op("vector", "tensor_add", x2sl, tmp_sb[:], acc_sb[:, c * QT:(c + 1) * QT],
                  inc=ev.inc())
            pp.wait("sync", ev)
            pp.op("sync", "dma_start", out=x2T[c * 128:(c + 1) * 128, t0:t0 + QT],
                  in_=x2sl, inc=st_s.inc(16))
            x2_slots.used(s2_, st_s)
        banks.rel(brstd, ev); banks.rel(bmmr, ev)
        pp.block_end(mk)
    es4.close()
    pp.wait("sync", st_s); pp.wait("gpsimd", st_g)


def emit_phaseD(pp, ctx):
    nc = ctx["nc"]; inp = ctx["inp"]; dram = ctx["dram"]
    es = pp.es
    sb = lambda name, shape, dt: es.enter_context(nc.sbuf_tensor("D" + name, shape, dt))

    banks = BankPool(pp, [es.enter_context(nc.psum_tensor(f"Dbk{i}", [128, 512], F32))
                          for i in range(8)])
    ld_s = pp.sem("ld_s"); ld_g = pp.sem("ld_g")
    st_s = pp.sem("st_s"); st_g = pp.sem("st_g")
    mm = pp.sem("mm"); ev = EvFacade(pp)
    sqsem = pp.sem("sq")
    ccK = pp.sem("ccK"); ccV = pp.sem("ccV"); ccA = pp.sem("ccA")

    x2T = dram["x2T"]; t2T = dram["t2T"]; xn2T = dram["xn2T"]; qTd = dram["qTd"]
    kpub = dram["kpub"]; vpub = dram["vpub"]; apub = dram["apub"]
    kG = dram["kG"]; vG = dram["vG"]; aG = dram["aG"]
    yT = dram["yT"]; hT = dram["hT"]; x3T = dram["x3T"]

    ones_col = sb("ones_col", [128, 1], F32)
    ones_row = sb("ones_row", [1, 128], F32)
    eps5 = sb("eps5", [128, 1], F32)
    eps6r = sb("eps6r", [128, 1], F32)
    s1_col = sb("s1c", [128, 8], F32)
    s2_col = sb("s2c", [128, 8], F32)
    bd2 = sb("bd2", [128, 2], BF16)
    ex2 = sb("ex2", [2, 128], BF16)
    ex2f = sb("ex2f", [2, 128], F32)
    smask = sb("smask", [128, 8], F32)
    imask = sb("imask", [128, 8], F32)
    wm2 = sb("wm2", [2, 8], F32)

    mk = pp.block()
    pp.op("vector", "memset", ones_col[:], 1.0)
    pp.op("vector", "memset", ones_row[:], 1.0)
    pp.op("vector", "memset", eps5[:], 1e-5)
    pp.op("vector", "memset", eps6r[:], 1e-6)
    pp.op("sync", "dma_start", out=s1_col[:], in_=inp["D_s1_cols"], inc=ld_s.inc(16))
    pp.op("sync", "dma_start", out=s2_col[:], in_=inp["D_s2_cols"], inc=ld_s.inc(16))
    pp.op("gpsimd", "dma_start", out=bd2[:], in_=inp["D_bd2"], inc=ld_g.inc(16))
    pp.op("gpsimd", "dma_start", out=ex2[:], in_=inp["D_ex2"], inc=ld_g.inc(16))
    pp.op("gpsimd", "dma_start", out=ex2f[:], in_=inp["D_ex2f"], inc=ld_g.inc(16))
    pp.op("gpsimd", "dma_start", out=smask[:], in_=inp["D_smask"], inc=ld_g.inc(16))
    pp.op("gpsimd", "dma_start", out=imask[:], in_=inp["D_imask"], inc=ld_g.inc(16))
    pp.op("sync", "dma_start", out=wm2[:], in_=inp["D_wm2"], inc=ld_s.inc(16))
    pp.block_end(mk)

    es2 = ExitStack()
    sb2 = lambda name, shape, dt: es2.enter_context(nc.sbuf_tensor("D1" + name, shape, dt))
    x_sb = sb2("x", [128, 8 * QT], F32)
    sq_sb = sb2("sq", [128, 2 * QT], F32)
    tmp_sb = sb2("tmp", [128, QT], F32)
    rows = sb2("rows", [1, 8 * QT], F32)
    t2_sb = sb2("t2", [128, 8 * QT], BF16)
    xn_sb = sb2("xn", [128, 8 * QT], BF16)
    sq_slots = Slots(pp, 2)
    R = lambda r: rows[0:1, r * QT:(r + 1) * QT]

    for qt in range(NQT):
        mk = pp.block()
        t0 = qt * QT
        for c in range(NC):
            pp.op("sync", "dma_start", out=x_sb[:, c * QT:(c + 1) * QT],
                  in_=x2T[c * 128:(c + 1) * 128, t0:t0 + QT], inc=ld_s.inc(16))
        pp.wait("scalar", ld_s); pp.wait("scalar", ld_g)
        pp.wait("tensor", ld_s); pp.wait("tensor", ld_g)
        src = [x_sb[:, c * QT:(c + 1) * QT] for c in range(NC)]
        ln_stats_rows(pp, banks, sq_slots, sq_sb, ones_col, src, R, mm, ev, sqsem,
                      eps5[0:1, 0:1])
        pp.op("vector", "tensor_single_scalar", R(6), R(2), 1e-5, ALU.add)
        pp.op("vector", "reciprocal", R(6), R(6))
        pp.op("vector", "tensor_mul", R(5), R(2), R(6), inc=ev.inc())
        pp.wait("scalar", ev)
        pp.op("scalar", "activation", R(5), R(5), AF.Ln)
        pp.op("scalar", "activation", R(5), R(5), AF.Exp, scale=0.5, inc=ev.inc())
        pp.wait("vector", ev)
        pp.op("vector", "tensor_scalar_add", R(5), R(5), 1e-6)
        pp.op("vector", "reciprocal", R(5), R(5), inc=ev.inc())
        brstd = banks.get(); bmmr = banks.get(); bs_ = banks.get()
        pp.wait("tensor", ev)
        pp.op("tensor", "matmul", brstd.t[:], ones_row[:], R(3), start=True, stop=True)
        pp.op("tensor", "matmul", bmmr.t[:], ones_row[:], R(4), start=True, stop=True)
        pp.op("tensor", "matmul", bs_.t[:], ones_row[:], R(5), start=True, stop=True, inc=mm.inc())
        pp.wait("vector", mm)
        for c in range(NC):
            t2sl = t2_sb[:, c * QT:(c + 1) * QT]
            pp.op("vector", "tensor_mul", tmp_sb[:], x_sb[:, c * QT:(c + 1) * QT], brstd.t[:])
            pp.op("vector", "tensor_sub", t2sl, tmp_sb[:], bmmr.t[:])
            pp.op("vector", "scalar_tensor_tensor", xn_sb[:, c * QT:(c + 1) * QT],
                  t2sl, s1_col[:, c:c + 1], bs_.t[:], ALU.mult, ALU.mult, inc=ev.inc())
            pp.wait("sync", ev)
            pp.op("sync", "dma_start", out=t2T[c * 128:(c + 1) * 128, t0:t0 + QT],
                  in_=t2sl, inc=st_s.inc(16))
            pp.op("sync", "dma_start", out=xn2T[c * 128:(c + 1) * 128, t0:t0 + QT],
                  in_=xn_sb[:, c * QT:(c + 1) * QT], inc=st_s.inc(16))
        banks.rel(brstd, ev); banks.rel(bmmr, ev); banks.rel(bs_, ev)
        pp.block_end(mk)
    es2.close()

    es3 = ExitStack()
    sb3 = lambda name, shape, dt: es3.enter_context(nc.sbuf_tensor("D2" + name, shape, dt))
    xn_sl = sb3("xn", [128, 8 * QT], BF16)
    out_bf = sb3("obf", [128, 2 * QT], BF16)
    gh_sb = sb3("gh", [16, QT], BF16)
    al_sb = sb3("al", [128, 2 * QT], F16)
    wq_sb = sb3("wq", [128, 8 * D], BF16)
    wk_sb = sb3("wk", [128, 8 * D], BF16)
    wv_sb = sb3("wv", [128, 8 * D], BF16)
    wg1_sb = sb3("wg1", [128, 8 * 16], BF16)
    wg2_sb = sb3("wg2", [16, 8 * 128], BF16)
    bg1_col = sb3("bg1", [16, 1], F32)
    bg2s_col = sb3("bg2s", [128, 8], F32)
    obf_slots = Slots(pp, 2)
    al_slots = Slots(pp, 2)

    mk = pp.block()
    pp.op("sync", "dma_start", out=bg1_col[:], in_=inp["D_bg1_col"], inc=ld_s.inc(16))
    pp.op("sync", "dma_start", out=bg2s_col[:], in_=inp["D_bg2_cols"], inc=ld_s.inc(16))
    pp.wait("vector", ld_s); pp.wait("vector", ld_g)
    pp.op("vector", "tensor_scalar_mul", bg2s_col[:], bg2s_col[:], 1.0 / 16.0)
    pp.op("gpsimd", "dma_start", out=wg1_sb[:], in_=inp["D_wg1T"], inc=ld_g.inc(16))
    pp.op("gpsimd", "dma_start", out=wg2_sb[:], in_=inp["D_wg2T"], inc=ld_g.inc(16))
    for kc in range(NC):
        pp.op("gpsimd", "dma_start", out=wq_sb[:, kc * D:(kc + 1) * D],
              in_=inp["D_wqT"][kc * 128:(kc + 1) * 128, :], inc=ld_g.inc(16))
        pp.op("gpsimd", "dma_start", out=wk_sb[:, kc * D:(kc + 1) * D],
              in_=inp["D_wkT"][kc * 128:(kc + 1) * 128, :], inc=ld_g.inc(16))
        pp.op("gpsimd", "dma_start", out=wv_sb[:, kc * D:(kc + 1) * D],
              in_=inp["D_wvT"][kc * 128:(kc + 1) * 128, :], inc=ld_g.inc(16))
    pp.block_end(mk)

    for qt in range(NQT):
        mk = pp.block()
        t0 = qt * QT
        for c in range(NC):
            pp.op("sync", "dma_start", out=xn_sl[:, c * QT:(c + 1) * QT],
                  in_=xn2T[c * 128:(c + 1) * 128, t0:t0 + QT], inc=ld_s.inc(16))
        pp.wait("tensor", ld_s); pp.wait("tensor", ld_g)
        for wname, wsb, dstT in (("q", wq_sb, qTd), ("k", wk_sb, kpub), ("v", wv_sb, vpub)):
            for oc in range(NC):
                bo = banks.get()
                pairs = [(wsb[:, kc * D + oc * 128: kc * D + (oc + 1) * 128],
                          xn_sl[:, kc * QT:(kc + 1) * QT]) for kc in range(NC)]
                mm_group(pp, bo.t[:], pairs, inc=mm.inc())
                pp.wait("scalar", mm)
                os_ = obf_slots.get("scalar")
                osl = out_bf[:, os_ * QT:(os_ + 1) * QT]
                pp.op("scalar", "copy", osl, bo.t[:], inc=ev.inc())
                banks.rel(bo, ev)
                pp.wait("sync", ev)
                pp.op("sync", "dma_start", out=dstT[oc * 128:(oc + 1) * 128, t0:t0 + QT],
                      in_=osl, inc=st_s.inc(16))
                obf_slots.used(os_, st_s)
        bgh = banks.get()
        pairs = [(wg1_sb[:, kc * 16:(kc + 1) * 16], xn_sl[:, kc * QT:(kc + 1) * QT])
                 for kc in range(NC)]
        mm_group(pp, bgh.t[0:16, :], pairs, inc=mm.inc())
        pp.wait("scalar", mm)
        pp.op("scalar", "activation", gh_sb[:], bgh.t[0:16, :], AF.Silu,
              bias=bg1_col[0:16, 0:1], inc=ev.inc())
        banks.rel(bgh, ev)
        pp.wait("tensor", ev)
        for oc in range(NC):
            bo = banks.get()
            pp.op("tensor", "matmul", bo.t[:], wg2_sb[0:16, oc * 128:(oc + 1) * 128],
                  gh_sb[:], start=True, stop=True, inc=mm.inc())
            pp.wait("scalar", mm)
            as_ = al_slots.get("scalar")
            asl = al_sb[:, as_ * QT:(as_ + 1) * QT]
            pp.op("scalar", "activation", asl, bo.t[:], AF.Sigmoid,
                  scale=1.0 / 16.0, bias=bg2s_col[:, oc:oc + 1], inc=ev.inc())
            banks.rel(bo, ev)
            pp.wait("sync", ev)
            pp.op("sync", "dma_start", out=apub[oc * 128:(oc + 1) * 128, t0:t0 + QT],
                  in_=asl, inc=st_s.inc(16))
            al_slots.used(as_, st_s)
        pp.block_end(mk)
    es3.close()

    mk = pp.block()
    pp.wait("gpsimd", st_s); pp.wait("gpsimd", st_g)
    pp.op("gpsimd", "collective_compute", "AllGather", ALU.bypass,
          replica_groups=[list(range(8))], ins=[kpub], outs=[kG], inc=ccK.inc())
    pp.op("gpsimd", "collective_compute", "AllGather", ALU.bypass,
          replica_groups=[list(range(8))], ins=[vpub], outs=[vG], inc=ccV.inc())
    pp.op("gpsimd", "collective_compute", "AllGather", ALU.bypass,
          replica_groups=[list(range(8))], ins=[apub], outs=[aG], inc=ccA.inc())
    pp.wait("gpsimd", ccK, 1)
    pp.wait("gpsimd", ccV, 1)
    pp.wait("gpsimd", ccA, 1)
    pp.op("gpsimd", "nop")
    pp.block_end(mk)

    es4 = ExitStack()
    sb4 = lambda name, shape, dt: es4.enter_context(nc.sbuf_tensor("D4" + name, shape, dt))
    q_sl = sb4("q", [128, 8 * QT], BF16)
    k_sl = sb4("k", [128, 2 * 8 * QT], BF16)
    v_sl = sb4("v", [128, 2 * 8 * QT], BF16)
    a_sl = sb4("a", [128, 2 * 8 * QT], BF16)
    xg_sl = sb4("xg", [128, 8 * QT], BF16)
    Rrun = sb4("R", [128, 8 * QT], BF16)
    prod = sb4("prod", [128, QT], BF16)
    o_acc = sb4("o", [128, 8 * QT], F32)
    w_sb = sb4("w", [2, 8 * QT], BF16)
    tmp4 = sb4("tmp4", [128, QT], F32)
    sq4 = sb4("sq4", [128, QT], F32)
    rows4 = sb4("rows4", [2, 8 * QT], F32)
    og_sb = sb4("og", [128, 8 * QT], BF16)
    sqb4 = sb4("sqb4", [128, QT], BF16)
    obf4 = sb4("obf4", [128, QT], BF16)
    g_sb = sb4("g", [128, 8 * QT], BF16)
    t2_sl = sb4("t2s", [128, 8 * QT], BF16)
    y_sb = sb4("y", [128, 2 * QT], F32)
    wog_sb = sb4("wog", [128, 8 * D], BF16)
    wo_sb = sb4("wo", [128, 8 * D], BF16)
    bog_col = sb4("bog", [128, 8], F32)
    y_slots = Slots(pp, 2)
    kva_war = [None, None]
    kvald = [pp.sem("kvald0"), pp.sem("kvald1")]

    mk = pp.block()
    pp.op("sync", "dma_start", out=bog_col[:], in_=inp["D_bog_cols"], inc=ld_s.inc(16))
    for kc in range(NC):
        pp.op("gpsimd", "dma_start", out=wog_sb[:, kc * D:(kc + 1) * D],
              in_=inp["D_wogT"][kc * 128:(kc + 1) * 128, :], inc=ld_g.inc(16))
        pp.op("gpsimd", "dma_start", out=wo_sb[:, kc * D:(kc + 1) * D],
              in_=inp["D_woT"][kc * 128:(kc + 1) * 128, :], inc=ld_g.inc(16))
    pp.block_end(mk)

    for qt in range(NQT):
        mk = pp.block()
        t0 = qt * QT
        pp.op("vector", "memset", o_acc[:], 0.0)
        pp.op("vector", "memset", Rrun[:], 1.0, inc=ev.inc())
        for c in range(NC):
            pp.op("sync", "dma_start", out=q_sl[:, c * QT:(c + 1) * QT],
                  in_=qTd[c * 128:(c + 1) * 128, t0:t0 + QT], inc=ld_s.inc(16))
        for s in range(7, -1, -1):
            ks_ = (7 - s) % 2
            koff = ks_ * 8 * QT
            ksl = lambda c, koff=koff: k_sl[:, koff + c * QT: koff + (c + 1) * QT]
            vsl = lambda c, koff=koff: v_sl[:, koff + c * QT: koff + (c + 1) * QT]
            asl = lambda c, koff=koff: a_sl[:, koff + c * QT: koff + (c + 1) * QT]
            if kva_war[ks_] is not None:
                for (ss, vv) in kva_war[ks_]:
                    pp.wait("sync", ss, vv)
                    pp.wait("gpsimd", ss, vv)
                kva_war[ks_] = None
            for c in range(NC):
                pp.op("sync", "dma_start", out=ksl(c),
                      in_=kG[s * D + c * 128: s * D + (c + 1) * 128, t0:t0 + QT],
                      inc=kvald[ks_].inc(16))
                pp.op("sync", "dma_start", out=vsl(c),
                      in_=vG[s * D + c * 128: s * D + (c + 1) * 128, t0:t0 + QT],
                      inc=kvald[ks_].inc(16))
                pp.op("gpsimd", "dma_start", out=asl(c),
                      in_=aG[s * D + c * 128: s * D + (c + 1) * 128, t0:t0 + QT],
                      inc=kvald[ks_].inc(16))
            pp.wait("vector", kvald[ks_])
            if s == 7:
                pp.wait("vector", ld_s); pp.wait("vector", ld_g)
            bw = banks.get()
            for c in range(NC):
                pp.op("vector", "tensor_scalar", asl(c),
                      asl(c), smask[:, s:s + 1], imask[:, s:s + 1],
                      ALU.mult, ALU.add)
                pp.op("vector", "tensor_mul", prod[:], q_sl[:, c * QT:(c + 1) * QT],
                      Rrun[:, c * QT:(c + 1) * QT])
                pp.op("vector", "tensor_mul", prod[:], prod[:], ksl(c),
                      inc=ev.inc())
                pp.wait("tensor", ev)
                pp.op("tensor", "matmul", bw.t[0:2, 0:QT], bd2[:], prod[:],
                      start=True, stop=True, inc=mm.inc())
                pp.wait("vector", mm)
                pp.op("vector", "tensor_scalar_mul", w_sb[0:2, c * QT:(c + 1) * QT],
                      bw.t[0:2, 0:QT], wm2[0:2, s:s + 1], inc=ev.inc())
                pp.op("vector", "tensor_mul", Rrun[:, c * QT:(c + 1) * QT],
                      Rrun[:, c * QT:(c + 1) * QT], asl(c))
            banks.rel(bw, ev)
            pp.wait("tensor", ev)
            for c in range(NC):
                be = banks.get()
                pp.op("tensor", "matmul", be.t[:], ex2[:, :],
                      w_sb[0:2, c * QT:(c + 1) * QT], start=True, stop=True, inc=mm.inc())
                pp.wait("vector", mm)
                pp.op("vector", "tensor_mul", tmp4[:], be.t[:], vsl(c))
                pp.op("vector", "tensor_add", o_acc[:, c * QT:(c + 1) * QT],
                      o_acc[:, c * QT:(c + 1) * QT], tmp4[:],
                      inc=(ev.inc() if c == NC - 1 else None))
                banks.rel(be, ev)
            kva_war[ks_] = ev.now_list()
        if "Do" in DBG:
            pp.wait("gpsimd", ev)
            for c in range(NC):
                pp.op("gpsimd", "dma_start", out=dram["dbgT"][c * 128:(c + 1) * 128, t0:t0 + QT],
                      in_=o_acc[:, c * QT:(c + 1) * QT], inc=st_g.inc(16))
        for c in range(NC):
            bsum = banks.get(); bsq = banks.get()
            osl = o_acc[:, c * QT:(c + 1) * QT]
            pp.wait("scalar", ev)
            pp.op("scalar", "activation", sq4[:], osl, AF.Square, inc=sqsem.inc())
            pp.wait("tensor", ev)
            obf = obf4[:]
            pp.op("vector", "tensor_copy", obf, osl, inc=ev.inc())
            pp.wait("tensor", sqsem)
            pp.wait("tensor", ev)
            pp.op("tensor", "matmul", bsum.t[0:2, 0:QT], bd2[:], obf, start=True, stop=True)
            sq4b = sqb4[:]
            pp.wait("vector", sqsem)
            pp.op("vector", "tensor_copy", sq4b, sq4[:], inc=ev.inc())
            pp.wait("tensor", ev)
            pp.op("tensor", "matmul", bsq.t[0:2, 0:QT], bd2[:], sq4b, start=True, stop=True,
                  inc=mm.inc())
            RW = lambda r: rows4[0:2, r * QT:(r + 1) * QT]
            pp.wait("vector", mm)
            pp.op("vector", "tensor_scalar_mul", RW(0), bsum.t[0:2, 0:QT], 1.0 / 64)
            pp.op("vector", "tensor_scalar_mul", RW(1), bsq.t[0:2, 0:QT], 1.0 / 64, inc=ev.inc())
            banks.rel(bsum, ev); banks.rel(bsq, ev)
            pp.op("vector", "tensor_mul", RW(2), RW(0), RW(0))
            pp.op("vector", "tensor_sub", RW(2), RW(1), RW(2), inc=ev.inc())
            pp.wait("scalar", ev)
            pp.op("scalar", "activation", RW(3), RW(2), AF.Ln, bias=eps5[0:2, 0:1])
            pp.op("scalar", "activation", RW(3), RW(3), AF.Exp, scale=-0.5, inc=ev.inc())
            pp.wait("vector", ev)
            pp.op("vector", "tensor_mul", RW(4), RW(0), RW(3), inc=ev.inc())
            bmu = banks.get()
            pp.wait("tensor", ev)
            pp.op("tensor", "matmul", bmu.t[0:128, 0:QT], ex2f[:, :],
                  rows4[0:2, 3 * QT:4 * QT], start=True, stop=True)
            bmm_ = banks.get()
            pp.op("tensor", "matmul", bmm_.t[0:128, 0:QT], ex2f[:, :],
                  rows4[0:2, 4 * QT:5 * QT], start=True, stop=True, inc=mm.inc())
            pp.wait("vector", mm)
            pp.op("vector", "tensor_mul", tmp4[:], osl, bmu.t[:])
            pp.op("vector", "tensor_sub", g_sb[:, c * QT:(c + 1) * QT], tmp4[:], bmm_.t[:],
                  inc=ev.inc())
            banks.rel(bmu, ev); banks.rel(bmm_, ev)
        if "Doln" in DBG:
            pp.wait("gpsimd", ev)
            for c in range(NC):
                pp.op("gpsimd", "dma_start", out=dram["dbgT"][c * 128:(c + 1) * 128, t0:t0 + QT],
                      in_=g_sb[:, c * QT:(c + 1) * QT], inc=st_g.inc(16))
        for c in range(NC):
            pp.op("sync", "dma_start", out=xg_sl[:, c * QT:(c + 1) * QT],
                  in_=xn2T[c * 128:(c + 1) * 128, t0:t0 + QT], inc=ld_s.inc(16))
            pp.op("sync", "dma_start", out=t2_sl[:, c * QT:(c + 1) * QT],
                  in_=t2T[c * 128:(c + 1) * 128, t0:t0 + QT], inc=ld_s.inc(16))
        pp.wait("tensor", ld_s); pp.wait("tensor", ld_g)
        for oc in range(NC):
            bo = banks.get()
            pairs = [(wog_sb[:, kc * D + oc * 128: kc * D + (oc + 1) * 128],
                      xg_sl[:, kc * QT:(kc + 1) * QT]) for kc in range(NC)]
            mm_group(pp, bo.t[:], pairs, inc=mm.inc())
            pp.wait("scalar", mm)
            pp.op("scalar", "activation", og_sb[:, oc * QT:(oc + 1) * QT], bo.t[:],
                  AF.Silu, bias=bog_col[:, oc:oc + 1], inc=ev.inc())
            banks.rel(bo, ev)
        pp.wait("vector", ev)
        for c in range(NC):
            pp.op("vector", "tensor_mul", g_sb[:, c * QT:(c + 1) * QT],
                  g_sb[:, c * QT:(c + 1) * QT], og_sb[:, c * QT:(c + 1) * QT],
                  inc=(ev.inc() if c == NC - 1 else None))
        if "Dog" in DBG:
            pp.wait("gpsimd", ev)
            for c in range(NC):
                pp.op("gpsimd", "dma_start", out=dram["dbgT"][D + c * 128:D + (c + 1) * 128, t0:t0 + QT],
                      in_=og_sb[:, c * QT:(c + 1) * QT], inc=st_g.inc(16))
        pp.wait("tensor", ev)
        for oc in range(NC):
            bo = banks.get()
            pairs = [(wo_sb[:, kc * D + oc * 128: kc * D + (oc + 1) * 128],
                      g_sb[:, kc * QT:(kc + 1) * QT]) for kc in range(NC)]
            mm_group(pp, bo.t[:], pairs, inc=mm.inc())
            pp.wait("vector", mm)
            ys = y_slots.get("vector")
            ysl = y_sb[:, ys * QT:(ys + 1) * QT]
            pp.op("vector", "tensor_add", ysl, bo.t[:], t2_sl[:, oc * QT:(oc + 1) * QT],
                  inc=ev.inc())
            banks.rel(bo, ev)
            pp.wait("sync", ev)
            pp.op("sync", "dma_start", out=yT[oc * 128:(oc + 1) * 128, t0:t0 + QT],
                  in_=ysl, inc=st_s.inc(16))
            y_slots.used(ys, st_s)
        pp.block_end(mk)
    es4.close()

    es5 = ExitStack()
    sb5 = lambda name, shape, dt: es5.enter_context(nc.sbuf_tensor("D5" + name, shape, dt))
    y_sl = sb5("y", [128, 8 * QT], F32)
    sq5 = sb5("sq", [128, 2 * QT], F32)
    rows5 = sb5("rows", [1, 4 * QT], F32)
    nrm_sb = sb5("nrm", [128, 8 * QT], BF16)
    hg_sb = sb5("hg", [128, QT], BF16)
    h_sb = sb5("h", [128, 2 * QT], BF16)
    wgate_sb = sb5("wgate", [128, 8 * 4096], BF16)
    wup_sb = sb5("wup", [128, 8 * 4096], BF16)
    sq5_slots = Slots(pp, 2)
    h_slots = Slots(pp, 2)
    R5 = lambda r: rows5[0:1, r * QT:(r + 1) * QT]

    mk = pp.block()
    for kc in range(NC):
        pp.op("gpsimd", "dma_start", out=wgate_sb[:, kc * 4096:(kc + 1) * 4096],
              in_=inp["D_wgateT"][kc * 128:(kc + 1) * 128, :], inc=ld_g.inc(16))
        pp.op("gpsimd", "dma_start", out=wup_sb[:, kc * 4096:(kc + 1) * 4096],
              in_=inp["D_wupT"][kc * 128:(kc + 1) * 128, :], inc=ld_g.inc(16))
    pp.block_end(mk)

    for qt in range(NQT):
        mk = pp.block()
        t0 = qt * QT
        for c in range(NC):
            pp.op("sync", "dma_start", out=y_sl[:, c * QT:(c + 1) * QT],
                  in_=yT[c * 128:(c + 1) * 128, t0:t0 + QT], inc=ld_s.inc(16))
        pp.wait("scalar", ld_s); pp.wait("scalar", ld_g)
        pp.wait("tensor", ld_s); pp.wait("tensor", ld_g)
        bsq = banks.get()
        for c in range(NC):
            s_ = sq5_slots.get("scalar")
            ssl = sq5[:, s_ * QT:(s_ + 1) * QT]
            pp.op("scalar", "activation", ssl, y_sl[:, c * QT:(c + 1) * QT], AF.Square,
                  inc=sqsem.inc())
            pp.wait("tensor", sqsem)
            pp.op("tensor", "matmul", bsq.t[0:1, :], ones_col[:], ssl,
                  start=(c == 0), stop=(c == NC - 1), inc=mm.inc())
            sq5_slots.used(s_, mm)
        pp.wait("vector", mm)
        pp.op("vector", "tensor_scalar_mul", R5(0), bsq.t[0:1, :], 1.0 / D, inc=ev.inc())
        banks.rel(bsq, ev)
        pp.wait("scalar", ev)
        pp.op("scalar", "activation", R5(1), R5(0), AF.Ln)
        pp.op("scalar", "activation", R5(1), R5(1), AF.Exp, scale=0.5, inc=ev.inc())
        pp.wait("vector", ev)
        pp.op("vector", "tensor_scalar_add", R5(1), R5(1), 1e-6)
        pp.op("vector", "reciprocal", R5(1), R5(1), inc=ev.inc())
        bs_ = banks.get()
        pp.wait("tensor", ev)
        pp.op("tensor", "matmul", bs_.t[:], ones_row[:], R5(1), start=True, stop=True, inc=mm.inc())
        pp.wait("vector", mm)
        for c in range(NC):
            pp.op("vector", "scalar_tensor_tensor", nrm_sb[:, c * QT:(c + 1) * QT],
                  y_sl[:, c * QT:(c + 1) * QT], s2_col[:, c:c + 1], bs_.t[:],
                  ALU.mult, ALU.mult, inc=(ev.inc() if c == NC - 1 else None))
        banks.rel(bs_, ev)
        pp.wait("tensor", ev)
        for oc in range(32):
            bg = banks.get()
            pairs = [(wgate_sb[:, kc * 4096 + oc * 128: kc * 4096 + (oc + 1) * 128],
                      nrm_sb[:, kc * QT:(kc + 1) * QT]) for kc in range(NC)]
            mm_group(pp, bg.t[:], pairs, inc=mm.inc())
            pp.wait("scalar", mm)
            pp.op("scalar", "activation", hg_sb[:], bg.t[:], AF.Silu, inc=ev.inc())
            banks.rel(bg, ev)
            bu = banks.get()
            pairs = [(wup_sb[:, kc * 4096 + oc * 128: kc * 4096 + (oc + 1) * 128],
                      nrm_sb[:, kc * QT:(kc + 1) * QT]) for kc in range(NC)]
            mm_group(pp, bu.t[:], pairs, inc=mm.inc())
            hs = h_slots.get("vector")
            hsl = h_sb[:, hs * QT:(hs + 1) * QT]
            pp.wait("vector", mm)
            pp.wait("vector", ev)
            pp.op("vector", "tensor_mul", hsl, hg_sb[:], bu.t[:], inc=ev.inc())
            banks.rel(bu, ev)
            pp.wait("sync", ev)
            pp.op("sync", "dma_start", out=hT[oc * 128:(oc + 1) * 128, t0:t0 + QT],
                  in_=hsl, inc=st_s.inc(16))
            h_slots.used(hs, st_s)
        pp.block_end(mk)
    es5.close()

    es6 = ExitStack()
    sb6 = lambda name, shape, dt: es6.enter_context(nc.sbuf_tensor("D6" + name, shape, dt))
    h_sl = sb6("h", [128, 32 * QT], BF16)
    x2r6 = sb6("x2r", [128, 8 * QT], F32)
    y_sl2 = sb6("y", [128, 8 * QT], F32)
    z_sb = sb6("z", [128, 8 * QT], F32)
    sq6 = sb6("sq", [128, 2 * QT], F32)
    tmp6 = sb6("tmp", [128, QT], F32)
    rows6 = sb6("rows", [1, 5 * QT], F32)
    x3_sb = sb6("x3", [128, 2 * QT], F32)
    wdown_sb = sb6("wdown", [128, 32 * D], BF16)
    sq6_slots = Slots(pp, 2)
    x3_slots = Slots(pp, 2)
    R6 = lambda r: rows6[0:1, r * QT:(r + 1) * QT]

    mk = pp.block()
    for kc in range(32):
        pp.op("gpsimd", "dma_start", out=wdown_sb[:, kc * D:(kc + 1) * D],
              in_=inp["D_wdownT"][kc * 128:(kc + 1) * 128, :], inc=ld_g.inc(16))
    pp.block_end(mk)

    for qt in range(NQT):
        mk = pp.block()
        t0 = qt * QT
        for c in range(32):
            pp.op("sync", "dma_start", out=h_sl[:, c * QT:(c + 1) * QT],
                  in_=hT[c * 128:(c + 1) * 128, t0:t0 + QT], inc=ld_s.inc(16))
        for c in range(NC):
            pp.op("sync", "dma_start", out=y_sl2[:, c * QT:(c + 1) * QT],
                  in_=yT[c * 128:(c + 1) * 128, t0:t0 + QT], inc=ld_s.inc(16))
        pp.wait("tensor", ld_s); pp.wait("tensor", ld_g)
        for oc in range(NC):
            bo = banks.get()
            pairs = [(wdown_sb[:, kc * D + oc * 128: kc * D + (oc + 1) * 128],
                      h_sl[:, kc * QT:(kc + 1) * QT]) for kc in range(32)]
            mm_group(pp, bo.t[:], pairs, inc=mm.inc())
            pp.wait("vector", mm)
            pp.wait("vector", ld_s); pp.wait("vector", ld_g)
            pp.op("vector", "tensor_add", z_sb[:, oc * QT:(oc + 1) * QT], bo.t[:],
                  y_sl2[:, oc * QT:(oc + 1) * QT], inc=ev.inc())
            banks.rel(bo, ev)
        pp.wait("scalar", ev)
        pp.wait("tensor", ev)
        src = [z_sb[:, c * QT:(c + 1) * QT] for c in range(NC)]
        ln_stats_rows(pp, banks, sq6_slots, sq6, ones_col, src, R6, mm, ev, sqsem,
                      eps5[0:1, 0:1])
        brstd = banks.get(); bmmr = banks.get()
        pp.wait("tensor", ev)
        pp.op("tensor", "matmul", brstd.t[:], ones_row[:], R6(3), start=True, stop=True)
        pp.op("tensor", "matmul", bmmr.t[:], ones_row[:], R6(4), start=True, stop=True, inc=mm.inc())
        for c in range(NC):
            pp.op("sync", "dma_start", out=x2r6[:, c * QT:(c + 1) * QT],
                  in_=x2T[c * 128:(c + 1) * 128, t0:t0 + QT], inc=ld_s.inc(16))
        pp.wait("vector", mm)
        pp.wait("vector", ld_s); pp.wait("vector", ld_g)
        for c in range(NC):
            s3_ = x3_slots.get("vector")
            x3sl = x3_sb[:, s3_ * QT:(s3_ + 1) * QT]
            pp.op("vector", "tensor_mul", tmp6[:], z_sb[:, c * QT:(c + 1) * QT], brstd.t[:])
            pp.op("vector", "tensor_sub", tmp6[:], tmp6[:], bmmr.t[:])
            pp.op("vector", "tensor_add", x3sl, tmp6[:], x2r6[:, c * QT:(c + 1) * QT],
                  inc=ev.inc())
            pp.wait("sync", ev)
            pp.op("sync", "dma_start", out=x3T[c * 128:(c + 1) * 128, t0:t0 + QT],
                  in_=x3sl, inc=st_s.inc(16))
            x3_slots.used(s3_, st_s)
        banks.rel(brstd, ev); banks.rel(bmmr, ev)
        pp.block_end(mk)
    es6.close()
    pp.wait("sync", st_s); pp.wait("gpsimd", st_g)


def emit_phaseE(pp, ctx):
    nc = ctx["nc"]; inp = ctx["inp"]; dram = ctx["dram"]
    es = pp.es
    sb = lambda name, shape, dt: es.enter_context(nc.sbuf_tensor("E" + name, shape, dt))

    banks = BankPool(pp, [es.enter_context(nc.psum_tensor(f"Ebk{i}", [128, 512], F32))
                          for i in range(8)])
    ld_s = pp.sem("ld_s"); ld_g = pp.sem("ld_g")
    st_s = pp.sem("st_s"); st_g = pp.sem("st_g")
    mm = pp.sem("mm"); ev = EvFacade(pp)
    sqsem = pp.sem("sq")

    x3T = dram["x3T"]; xxA = dram["xxA"]; xxB = dram["xxB"]; hT = dram["hT"]
    x4T = dram["x4T"]

    ones_col = sb("ones_col", [128, 1], F32)
    ones_row = sb("ones_row", [1, 128], F32)
    eps5 = sb("eps5", [128, 1], F32)
    cshE = sb("cshE", [128, 1], F32)
    permE = sb("permE", [128, 128], BF16)
    bd2 = sb("bd2", [128, 2], BF16)
    ex2f = sb("ex2f", [2, 128], F32)
    Cq = sb("Cq", [128, T], BF16)
    Sq = sb("Sq", [128, T], BF16)
    Ck = sb("Ck", [128, T], BF16)
    Sk = sb("Sk", [128, T], BF16)
    maskE_sb = sb("maskE", [128, 6 * QT], BF16)
    qs_col = sb("qs", [128, 6], F32)
    ks_col = sb("ks", [128, 6], F32)
    lng_col = sb("lng", [128, 6 * 8], F32)
    lnb_col = sb("lnb", [128, 6 * 8], F32)
    b1_col = sb("b1", [128, 6 * 44], F32)
    b2_col = sb("b2", [128, 6 * 8], F32)

    mk = pp.block()
    pp.op("vector", "memset", ones_col[:], 1.0)
    pp.op("vector", "memset", ones_row[:], 1.0)
    pp.op("vector", "memset", eps5[:], 1e-5)
    pp.op("vector", "memset", cshE[:], -C_SHIFT_E)
    pp.op("gpsimd", "dma_start", out=permE[:], in_=inp["E_permE"], inc=ld_g.inc(16))
    pp.op("gpsimd", "dma_start", out=bd2[:], in_=inp["D_bd2"], inc=ld_g.inc(16))
    pp.op("gpsimd", "dma_start", out=ex2f[:], in_=inp["D_ex2f"], inc=ld_g.inc(16))
    pp.op("gpsimd", "dma_start", out=Cq[:], in_=inp["E_Cq"], inc=ld_g.inc(16))
    pp.op("gpsimd", "dma_start", out=Sq[:], in_=inp["E_Sq"], inc=ld_g.inc(16))
    pp.op("gpsimd", "dma_start", out=Ck[:], in_=inp["E_Ck"], inc=ld_g.inc(16))
    pp.op("gpsimd", "dma_start", out=Sk[:], in_=inp["E_Sk"], inc=ld_g.inc(16))
    pp.op("sync", "dma_start", out=maskE_sb[:], in_=inp["maskE"], inc=ld_s.inc(16))
    pp.op("sync", "dma_start", out=qs_col[:], in_=inp["E_qs_cols"], inc=ld_s.inc(16))
    pp.op("sync", "dma_start", out=ks_col[:], in_=inp["E_ks_cols"], inc=ld_s.inc(16))
    pp.op("sync", "dma_start", out=lng_col[:], in_=inp["E_lng_cols"], inc=ld_s.inc(16))
    pp.op("sync", "dma_start", out=lnb_col[:], in_=inp["E_lnb_cols"], inc=ld_s.inc(16))
    pp.op("sync", "dma_start", out=b1_col[:], in_=inp["E_b1p_cols"], inc=ld_s.inc(16))
    pp.op("sync", "dma_start", out=b2_col[:], in_=inp["E_b2_cols"], inc=ld_s.inc(16))
    pp.block_end(mk)

    def load_shifted_bf(dst_chunks, srcT, qt):
        t0 = qt * QT
        for c in range(NC):
            dsl = dst_chunks(c)
            if c < 4:
                pp.op("gpsimd", "dma_start", out=dsl,
                      in_=srcT[c * 128:(c + 1) * 128, t0:t0 + QT], inc=ld_g.inc(16))
            elif qt == 0:
                pp.op("vector", "memset", dsl[:, 0:1], 0.0, inc=ev.inc())
                pp.op("gpsimd", "dma_start", out=dsl[:, 1:QT],
                      in_=srcT[c * 128:(c + 1) * 128, 0:QT - 1], inc=ld_g.inc(16))
            else:
                pp.op("gpsimd", "dma_start", out=dsl,
                      in_=srcT[c * 128:(c + 1) * 128, t0 - 1:t0 + QT - 1], inc=ld_g.inc(16))

    es0 = ExitStack()
    sb0 = lambda name, shape, dt: es0.enter_context(nc.sbuf_tensor("E0" + name, shape, dt))
    x_sb = sb0("x", [128, 8 * QT], F32)
    sq_sb = sb0("sq", [128, 2 * QT], F32)
    tmp_sb = sb0("tmp", [128, QT], F32)
    rows = sb0("rows", [1, 5 * QT], F32)
    t3_sb = sb0("t3", [128, 2 * QT], F32)
    sq_slots = Slots(pp, 2)
    t3_slots = Slots(pp, 2)
    R = lambda r: rows[0:1, r * QT:(r + 1) * QT]
    for qt in range(NQT):
        mk = pp.block()
        t0 = qt * QT
        for c in range(NC):
            pp.op("sync", "dma_start", out=x_sb[:, c * QT:(c + 1) * QT],
                  in_=x3T[c * 128:(c + 1) * 128, t0:t0 + QT], inc=ld_s.inc(16))
        pp.wait("scalar", ld_s); pp.wait("scalar", ld_g)
        pp.wait("tensor", ld_s); pp.wait("tensor", ld_g)
        src = [x_sb[:, c * QT:(c + 1) * QT] for c in range(NC)]
        ln_stats_rows(pp, banks, sq_slots, sq_sb, ones_col, src, R, mm, ev, sqsem,
                      eps5[0:1, 0:1])
        brstd = banks.get(); bmmr = banks.get()
        pp.wait("tensor", ev)
        pp.op("tensor", "matmul", brstd.t[:], ones_row[:], R(3), start=True, stop=True)
        pp.op("tensor", "matmul", bmmr.t[:], ones_row[:], R(4), start=True, stop=True, inc=mm.inc())
        pp.wait("vector", mm)
        for c in range(NC):
            ts_ = t3_slots.get("vector")
            tsl = t3_sb[:, ts_ * QT:(ts_ + 1) * QT]
            pp.op("vector", "tensor_mul", tmp_sb[:], x_sb[:, c * QT:(c + 1) * QT], brstd.t[:])
            pp.op("vector", "tensor_sub", tsl, tmp_sb[:], bmmr.t[:], inc=ev.inc())
            pp.wait("sync", ev)
            pp.op("sync", "dma_start", out=xxA[c * 128:(c + 1) * 128, t0:t0 + QT],
                  in_=tsl, inc=st_s.inc(16))
            t3_slots.used(ts_, st_s)
        banks.rel(brstd, ev); banks.rel(bmmr, ev)
        pp.block_end(mk)
    es0.close()

    for li in range(E_LAYERS):
        esA = ExitStack()
        sbA = lambda name, shape, dt: esA.enter_context(
            nc.sbuf_tensor(f"EL{li}a" + name, shape, dt))
        xsb = sbA("xsb", [128, 8 * QT], BF16)
        xres = sbA("xres", [128, 8 * QT], F32)
        wqkv_sb = sbA("wqkv", [128, 8 * 3072], BF16)
        wo_sb = sbA("wo", [128, 8 * D], BF16)
        q_fm = sbA("qfm", [128, 8 * QT], BF16)
        k_ring = sbA("kring", [128, NC * 8 * 128], BF16)
        v_ring = sbA("vring", [128, 8 * 1040], BF16)
        qa_sb = sbA("qa", [128, 2 * QT], BF16)
        sqE = sbA("sqE", [128, QT], F32)
        sqEb = sbA("sqEb", [128, QT], BF16)
        rowsE = sbA("rowsE", [2, 2 * QT], F32)
        ropt1 = sbA("ropt1", [128, QT], F32)
        ropt2 = sbA("ropt2", [128, QT], F32)
        exp_sb = sbA("exp", [128, 2 * QT], F32)
        attnT_sb = sbA("attnT", [128, 3 * QT], BF16)
        ctxb_sb = sbA("ctxb", [128, 8 * QT], BF16)
        dnr_sb = sbA("dnr", [1, 2 * QT], F32)
        y_sb = sbA("y", [128, QT], F32)
        qa_slots = Slots(pp, 2)
        exp_slots = Slots(pp, 2)
        at_slots = Slots(pp, 3)
        y_slots = Slots(pp, 1)
        RE = lambda r: rowsE[0:2, r * QT:(r + 1) * QT]

        mk = pp.block()
        pp.op("vector", "memset",
              v_ring.ap().rearrange("p (s h j) -> p s h j", s=8, h=NH, j=65)[:, :, :, 64:65],
              1.0)
        for kc in range(NC):
            pp.op("gpsimd", "dma_start", out=wqkv_sb[:, kc * 3072:(kc + 1) * 3072],
                  in_=inp["E_Wqkv"][li * D + kc * 128: li * D + (kc + 1) * 128, :],
                  inc=ld_g.inc(16))
            pp.op("gpsimd", "dma_start", out=wo_sb[:, kc * D:(kc + 1) * D],
                  in_=inp["E_Wo"][li * D + kc * 128: li * D + (kc + 1) * 128, :],
                  inc=ld_g.inc(16))
        pp.block_end(mk)

        def tr_(c, tile):
            s = tile % 8
            return k_ring[:, c * 1024 + s * 128: c * 1024 + (s + 1) * 128]

        def kr_qt_(c, qt):
            s = (qt * 4) % 8
            return k_ring[:, c * 1024 + s * 128: c * 1024 + s * 128 + 512]

        def vr_(tile, h):
            s = tile % 8
            return v_ring[:, s * 1040 + h * 65: s * 1040 + h * 65 + 65]

        def vr_w_(tile, g):
            s = tile % 8
            base = v_ring[:, s * 1040 + 8 * g * 65: s * 1040 + (8 * g + 8) * 65]
            return base.rearrange("p (h j) -> p h j", h=8, j=65)[:, :, 0:64]

        def norm_rope(kind, oc, qt, bank, dst):
            t0 = qt * QT
            scol = (qs_col if kind == "q" else ks_col)[:, li:li + 1]
            Ct = (Cq if kind == "q" else Ck)
            St = (Sq if kind == "q" else Sk)
            qs_ = qa_slots.get("scalar")
            qasl = qa_sb[:, qs_ * QT:(qs_ + 1) * QT]
            pp.op("scalar", "copy", qasl, bank.t[:], inc=ev.inc())
            pp.wait("vector", ev)
            pp.op("vector", "tensor_mul", sqEb[:], qasl, qasl, inc=ev.inc())
            banks.rel(bank, ev)
            bsq = banks.get()
            pp.wait("tensor", ev)
            pp.op("tensor", "matmul", bsq.t[0:2, 0:QT], bd2[:], sqEb[:],
                  start=True, stop=True, inc=mm.inc())
            pp.wait("scalar", mm)
            pp.op("scalar", "activation", RE(0), bsq.t[0:2, 0:QT], AF.Ln, inc=ev.inc())
            banks.rel(bsq, ev)
            pp.op("scalar", "activation", RE(0), RE(0), AF.Exp, scale=0.5, inc=ev.inc())
            pp.wait("vector", ev)
            pp.op("vector", "tensor_scalar_max", RE(1), RE(0), 1e-12)
            pp.op("vector", "reciprocal", RE(1), RE(1), inc=ev.inc())
            bsc = banks.get()
            pp.wait("tensor", ev)
            pp.op("tensor", "matmul", bsc.t[:], ex2f[:], RE(1), start=True, stop=True,
                  inc=mm.inc())
            pp.wait("vector", mm)
            qa2_ = qa_slots.get("vector")
            zsl = qa_sb[:, qa2_ * QT:(qa2_ + 1) * QT]
            pp.op("vector", "scalar_tensor_tensor", zsl, qasl, scol, bsc.t[:],
                  ALU.mult, ALU.mult, inc=ev.inc())
            qa_slots.used(qs_, ev)
            banks.rel(bsc, ev)
            bp = banks.get()
            pp.wait("tensor", ev)
            pp.op("tensor", "matmul", bp.t[:], permE[:], zsl, start=True, stop=True,
                  inc=mm.inc())
            pp.wait("vector", mm)
            pp.op("vector", "tensor_mul", ropt1[:], bp.t[:], St[:, t0:t0 + QT])
            pp.op("vector", "tensor_mul", ropt2[:], zsl, Ct[:, t0:t0 + QT])
            pp.op("vector", "tensor_add", dst, ropt2[:], ropt1[:], inc=ev.inc())
            qa_slots.used(qa2_, ev)
            banks.rel(bp, ev)

        for qt in range(NQT):
            mk = pp.block()
            t0 = qt * QT
            load_shifted_bf(lambda c: xsb[:, c * QT:(c + 1) * QT], xxA, qt)
            for c in range(NC):
                pp.op("sync", "dma_start", out=xres[:, c * QT:(c + 1) * QT],
                      in_=xxA[c * 128:(c + 1) * 128, t0:t0 + QT], inc=ld_s.inc(16))
            pp.wait("tensor", ld_s); pp.wait("tensor", ld_g)
            pp.wait("tensor", ev)
            for oc in range(NC):
                bo = banks.get()
                pairs = [(wqkv_sb[:, kc * 3072 + oc * 128: kc * 3072 + (oc + 1) * 128],
                          xsb[:, kc * QT:(kc + 1) * QT]) for kc in range(NC)]
                mm_group(pp, bo.t[:], pairs, inc=mm.inc())
                pp.wait("scalar", mm)
                norm_rope("q", oc, qt, bo, q_fm[:, oc * QT:(oc + 1) * QT])
            for oc in range(NC):
                bo = banks.get()
                pairs = [(wqkv_sb[:, kc * 3072 + D + oc * 128: kc * 3072 + D + (oc + 1) * 128],
                          xsb[:, kc * QT:(kc + 1) * QT]) for kc in range(NC)]
                mm_group(pp, bo.t[:], pairs, inc=mm.inc())
                pp.wait("scalar", mm)
                norm_rope("k", oc, qt, bo, kr_qt_(oc, qt))
            for i4 in range(4):
                tile = qt * 4 + i4
                for g in range(2):
                    bo = banks.get()
                    for kc in range(NC):
                        pp.op("tensor", "matmul", bo.t[:],
                              xsb[:, kc * QT + i4 * 128: kc * QT + (i4 + 1) * 128],
                              wqkv_sb[:, kc * 3072 + 2 * D + g * 512:
                                      kc * 3072 + 2 * D + (g + 1) * 512],
                              start=(kc == 0), stop=(kc == NC - 1),
                              inc=(mm.inc() if kc == NC - 1 else None))
                    pp.wait("vector", mm)
                    pp.op("vector", "tensor_copy", vr_w_(tile, g),
                          bo.t.ap().rearrange("p (h j) -> p h j", h=8, j=64), inc=ev.inc())
                    banks.rel(bo, ev)
            kbs_all = [(i, qt * 4 - 2 + i) for i in range(6)]
            kbs = [(i, kb) for i, kb in kbs_all if kb >= 0]
            for c in range(NC):
                bctxs = {}
                pipe = Pipe(depth=3)
                for hh in range(2):
                    h = 2 * c + hh
                    rowoff = hh * 64
                    bctxs[hh] = banks.get()
                    first = True
                    for i, kb in kbs:
                        bs = banks.get()
                        pp.op("tensor", "matmul", bs.t[:],
                              tr_(c, kb)[rowoff:rowoff + 64, :],
                              q_fm[:, c * QT:(c + 1) * QT][rowoff:rowoff + 64, :],
                              start=True, stop=True, inc=mm.inc())
                        mmv = mm.n

                        def consume(bs=bs, h=h, hh=hh, i=i, kb=kb, mmv=mmv, first=first,
                                    last=(i == kbs[-1][0])):
                            pp.wait("scalar", mm, mmv)
                            es_ = exp_slots.get("scalar")
                            eslot = exp_sb[:, es_ * QT:(es_ + 1) * QT]
                            pp.op("scalar", "activation", eslot, bs.t[:], AF.Exp,
                                  scale=8.0, bias=cshE[:, 0:1], inc=ev.inc())
                            banks.rel(bs, ev)
                            pp.wait("vector", ev)
                            as_ = at_slots.get("vector")
                            aslot = attnT_sb[:, as_ * QT:(as_ + 1) * QT]
                            pp.op("vector", "scalar_tensor_tensor", aslot, eslot, 1e30,
                                  maskE_sb[:, i * QT:(i + 1) * QT], ALU.min, ALU.mult,
                                  inc=ev.inc())
                            exp_slots.used(es_, ev)
                            pp.wait("tensor", ev)
                            pp.op("tensor", "matmul", bctxs[hh].t[0:65, :], vr_(kb, h),
                                  aslot, start=first, stop=last, inc=mm.inc())
                            at_slots.used(as_, mm)
                        pipe.push(consume)
                        first = False
                pipe.flush()
                for hh in range(2):
                    rowoff = hh * 64
                    bctx = bctxs[hh]
                    pp.wait("scalar", mm)
                    pp.op("scalar", "copy",
                          ctxb_sb[rowoff:rowoff + 64, c * QT:(c + 1) * QT],
                          bctx.t[0:64, :], inc=ev.inc())
                    pp.wait("vector", mm)
                    pp.op("vector", "tensor_scalar_add", dnr_sb[0:1, hh * QT:(hh + 1) * QT],
                          bctx.t[64:65, :], 1e-30)
                    pp.op("vector", "reciprocal", dnr_sb[0:1, hh * QT:(hh + 1) * QT],
                          dnr_sb[0:1, hh * QT:(hh + 1) * QT], inc=ev.inc())
                    banks.rel(bctx, ev)
                br = banks.get()
                pp.wait("tensor", ev)
                pp.op("tensor", "matmul", br.t[0:64, :], ones_row[0:1, 0:64],
                      dnr_sb[0:1, 0:QT], start=True, stop=True)
                pp.op("tensor", "matmul", br.t[64:128, :], ones_row[0:1, 0:64],
                      dnr_sb[0:1, QT:2 * QT], start=True, stop=True, inc=mm.inc())
                pp.wait("vector", mm)
                pp.op("vector", "tensor_mul", ctxb_sb[:, c * QT:(c + 1) * QT],
                      ctxb_sb[:, c * QT:(c + 1) * QT], br.t[:], inc=ev.inc())
                banks.rel(br, ev)
            pp.wait("tensor", ev)
            for oc in range(NC):
                bo = banks.get()
                pairs = [(wo_sb[:, kc * D + oc * 128: kc * D + (oc + 1) * 128],
                          ctxb_sb[:, kc * QT:(kc + 1) * QT]) for kc in range(NC)]
                mm_group(pp, bo.t[:], pairs, inc=mm.inc())
                pp.wait("vector", mm)
                pp.wait("vector", ld_s); pp.wait("vector", ld_g)
                ys = y_slots.get("vector")
                ysl = y_sb[:, ys * QT:(ys + 1) * QT]
                pp.op("vector", "tensor_add", ysl, bo.t[:],
                      xres[:, oc * QT:(oc + 1) * QT], inc=ev.inc())
                banks.rel(bo, ev)
                pp.wait("sync", ev)
                pp.op("sync", "dma_start", out=xxB[oc * 128:(oc + 1) * 128, t0:t0 + QT],
                      in_=ysl, inc=st_s.inc(16))
                y_slots.used(ys, st_s)
            pp.block_end(mk)
        esA.close()

        esF = ExitStack()
        sbF = lambda name, shape, dt: esF.enter_context(
            nc.sbuf_tensor(f"EL{li}f" + name, shape, dt))
        xs_f = sbF("xs", [128, 8 * QT], F32)
        sqF = sbF("sq", [128, 2 * QT], F32)
        tmpF = sbF("tmp", [128, QT], F32)
        rowsF = sbF("rows", [1, 5 * QT], F32)
        xn_sb = sbF("xn", [128, 8 * QT], BF16)
        w1_sb = sbF("w1", [128, 8 * 5632], BF16)
        ga_sb = sbF("ga", [128, QT], BF16)
        gg_sb = sbF("gg", [128, QT], BF16)
        h_sb = sbF("h", [128, 2 * QT], BF16)
        sqF_slots = Slots(pp, 2)
        h_slots = Slots(pp, 2)
        RF = lambda r: rowsF[0:1, r * QT:(r + 1) * QT]

        mk = pp.block()
        for kc in range(NC):
            pp.op("gpsimd", "dma_start", out=w1_sb[:, kc * 5632:(kc + 1) * 5632],
                  in_=inp["E_W1p"][li * D + kc * 128: li * D + (kc + 1) * 128, :],
                  inc=ld_g.inc(16))
        pp.block_end(mk)

        def load_shifted_f32(dst_chunks, srcT, qt):
            t0 = qt * QT
            for c in range(NC):
                dsl = dst_chunks(c)
                if c < 4:
                    pp.op("sync", "dma_start", out=dsl,
                          in_=srcT[c * 128:(c + 1) * 128, t0:t0 + QT], inc=ld_s.inc(16))
                elif qt == 0:
                    pp.op("vector", "memset", dsl[:, 0:1], 0.0, inc=ev.inc())
                    pp.op("sync", "dma_start", out=dsl[:, 1:QT],
                          in_=srcT[c * 128:(c + 1) * 128, 0:QT - 1], inc=ld_s.inc(16))
                else:
                    pp.op("sync", "dma_start", out=dsl,
                          in_=srcT[c * 128:(c + 1) * 128, t0 - 1:t0 + QT - 1], inc=ld_s.inc(16))

        for qt in range(NQT):
            mk = pp.block()
            t0 = qt * QT
            load_shifted_f32(lambda c: xs_f[:, c * QT:(c + 1) * QT], xxB, qt)
            pp.wait("scalar", ld_s); pp.wait("scalar", ld_g)
            pp.wait("tensor", ld_s); pp.wait("tensor", ld_g)
            pp.wait("vector", ld_s); pp.wait("vector", ld_g)
            src = [xs_f[:, c * QT:(c + 1) * QT] for c in range(NC)]
            ln_stats_rows(pp, banks, sqF_slots, sqF, ones_col, src, RF, mm, ev, sqsem,
                          eps5[0:1, 0:1])
            brstd = banks.get(); bmmr = banks.get()
            pp.wait("tensor", ev)
            pp.op("tensor", "matmul", brstd.t[:], ones_row[:], RF(3), start=True, stop=True)
            pp.op("tensor", "matmul", bmmr.t[:], ones_row[:], RF(4), start=True, stop=True,
                  inc=mm.inc())
            pp.wait("vector", mm)
            for c in range(NC):
                pp.op("vector", "tensor_mul", tmpF[:], xs_f[:, c * QT:(c + 1) * QT], brstd.t[:])
                pp.op("vector", "tensor_sub", tmpF[:], tmpF[:], bmmr.t[:])
                pp.op("vector", "tensor_scalar", xn_sb[:, c * QT:(c + 1) * QT], tmpF[:],
                      lng_col[:, li * 8 + c: li * 8 + c + 1],
                      lnb_col[:, li * 8 + c: li * 8 + c + 1],
                      ALU.mult, ALU.add, inc=ev.inc())
            banks.rel(brstd, ev); banks.rel(bmmr, ev)
            pp.wait("tensor", ev)
            for j in range(22):
                ba = banks.get()
                pairs = [(w1_sb[:, kc * 5632 + j * 128: kc * 5632 + (j + 1) * 128],
                          xn_sb[:, kc * QT:(kc + 1) * QT]) for kc in range(NC)]
                mm_group(pp, ba.t[:], pairs, inc=mm.inc())
                pp.wait("vector", mm)
                pp.op("vector", "tensor_scalar_add", ga_sb[:], ba.t[:],
                      b1_col[:, li * 44 + j: li * 44 + j + 1], inc=ev.inc())
                banks.rel(ba, ev)
                bg = banks.get()
                pairs = [(w1_sb[:, kc * 5632 + 2816 + j * 128: kc * 5632 + 2816 + (j + 1) * 128],
                          xn_sb[:, kc * QT:(kc + 1) * QT]) for kc in range(NC)]
                mm_group(pp, bg.t[:], pairs, inc=mm.inc())
                pp.wait("scalar", mm)
                pp.op("scalar", "activation", gg_sb[:], bg.t[:], AF.Gelu,
                      bias=b1_col[:, li * 44 + 22 + j: li * 44 + 22 + j + 1], inc=ev.inc())
                banks.rel(bg, ev)
                hs = h_slots.get("vector")
                hsl = h_sb[:, hs * QT:(hs + 1) * QT]
                pp.wait("vector", ev)
                pp.op("vector", "tensor_mul", hsl, ga_sb[:], gg_sb[:], inc=ev.inc())
                pp.wait("sync", ev)
                pp.op("sync", "dma_start", out=hT[j * 128:(j + 1) * 128, t0:t0 + QT],
                      in_=hsl, inc=st_s.inc(16))
                h_slots.used(hs, st_s)
            pp.block_end(mk)
        esF.close()

        esG = ExitStack()
        sbG = lambda name, shape, dt: esG.enter_context(
            nc.sbuf_tensor(f"EL{li}g" + name, shape, dt))
        h_sl = sbG("h", [128, 22 * QT], BF16)
        xx_sl = sbG("xx", [128, 8 * QT], F32)
        z_sb = sbG("z", [128, 2 * QT], F32)
        w2_sb = sbG("w2", [128, 22 * D], BF16)
        z_slots = Slots(pp, 2)

        mk = pp.block()
        for kc in range(22):
            pp.op("gpsimd", "dma_start", out=w2_sb[:, kc * D:(kc + 1) * D],
                  in_=inp["E_W2p"][li * 2816 + kc * 128: li * 2816 + (kc + 1) * 128, :],
                  inc=ld_g.inc(16))
        pp.block_end(mk)

        for qt in range(NQT):
            mk = pp.block()
            t0 = qt * QT
            for c in range(22):
                pp.op("sync", "dma_start", out=h_sl[:, c * QT:(c + 1) * QT],
                      in_=hT[c * 128:(c + 1) * 128, t0:t0 + QT], inc=ld_s.inc(16))
            for c in range(NC):
                pp.op("sync", "dma_start", out=xx_sl[:, c * QT:(c + 1) * QT],
                      in_=xxB[c * 128:(c + 1) * 128, t0:t0 + QT], inc=ld_s.inc(16))
            pp.wait("tensor", ld_s); pp.wait("tensor", ld_g)
            for oc in range(NC):
                bo = banks.get()
                pairs = [(w2_sb[:, kc * D + oc * 128: kc * D + (oc + 1) * 128],
                          h_sl[:, kc * QT:(kc + 1) * QT]) for kc in range(22)]
                mm_group(pp, bo.t[:], pairs, inc=mm.inc())
                pp.wait("vector", mm)
                pp.wait("vector", ld_s); pp.wait("vector", ld_g)
                zs = z_slots.get("vector")
                zsl = z_sb[:, zs * QT:(zs + 1) * QT]
                pp.op("vector", "scalar_tensor_tensor", zsl, bo.t[:],
                      b2_col[:, li * 8 + oc: li * 8 + oc + 1],
                      xx_sl[:, oc * QT:(oc + 1) * QT], ALU.add, ALU.add, inc=ev.inc())
                banks.rel(bo, ev)
                pp.wait("sync", ev)
                pp.op("sync", "dma_start", out=xxA[oc * 128:(oc + 1) * 128, t0:t0 + QT],
                      in_=zsl, inc=st_s.inc(16))
                z_slots.used(zs, st_s)
            pp.block_end(mk)
        esG.close()

    esZ = ExitStack()
    sbZ = lambda name, shape, dt: esZ.enter_context(nc.sbuf_tensor("EZ" + name, shape, dt))
    z_sb2 = sbZ("z", [128, 8 * QT], F32)
    x3r = sbZ("x3r", [128, 8 * QT], F32)
    sqZ = sbZ("sq", [128, 2 * QT], F32)
    tmpZ = sbZ("tmp", [128, QT], F32)
    rowsZ = sbZ("rows", [1, 5 * QT], F32)
    x4_sb = sbZ("x4", [128, 2 * QT], F32)
    sqZ_slots = Slots(pp, 2)
    x4_slots = Slots(pp, 2)
    RZ = lambda r: rowsZ[0:1, r * QT:(r + 1) * QT]
    for qt in range(NQT):
        mk = pp.block()
        t0 = qt * QT
        for c in range(NC):
            pp.op("sync", "dma_start", out=z_sb2[:, c * QT:(c + 1) * QT],
                  in_=xxA[c * 128:(c + 1) * 128, t0:t0 + QT], inc=ld_s.inc(16))
        pp.wait("scalar", ld_s); pp.wait("scalar", ld_g)
        pp.wait("tensor", ld_s); pp.wait("tensor", ld_g)
        src = [z_sb2[:, c * QT:(c + 1) * QT] for c in range(NC)]
        ln_stats_rows(pp, banks, sqZ_slots, sqZ, ones_col, src, RZ, mm, ev, sqsem,
                      eps5[0:1, 0:1])
        brstd = banks.get(); bmmr = banks.get()
        pp.wait("tensor", ev)
        pp.op("tensor", "matmul", brstd.t[:], ones_row[:], RZ(3), start=True, stop=True)
        pp.op("tensor", "matmul", bmmr.t[:], ones_row[:], RZ(4), start=True, stop=True,
              inc=mm.inc())
        for c in range(NC):
            pp.op("sync", "dma_start", out=x3r[:, c * QT:(c + 1) * QT],
                  in_=x3T[c * 128:(c + 1) * 128, t0:t0 + QT], inc=ld_s.inc(16))
        pp.wait("vector", mm)
        pp.wait("vector", ld_s); pp.wait("vector", ld_g)
        for c in range(NC):
            s4_ = x4_slots.get("vector")
            x4sl = x4_sb[:, s4_ * QT:(s4_ + 1) * QT]
            pp.op("vector", "tensor_mul", tmpZ[:], z_sb2[:, c * QT:(c + 1) * QT], brstd.t[:])
            pp.op("vector", "tensor_sub", tmpZ[:], tmpZ[:], bmmr.t[:])
            pp.op("vector", "tensor_add", x4sl, tmpZ[:], x3r[:, c * QT:(c + 1) * QT],
                  inc=ev.inc())
            pp.wait("sync", ev)
            pp.op("sync", "dma_start", out=x4T[c * 128:(c + 1) * 128, t0:t0 + QT],
                  in_=x4sl, inc=st_s.inc(16))
            x4_slots.used(s4_, st_s)
        banks.rel(brstd, ev); banks.rel(bmmr, ev)
        pp.block_end(mk)
    esZ.close()
    pp.wait("sync", st_s); pp.wait("gpsimd", st_g)


def prep_shared_inputs(inputs):
    consts = host_constants()
    base = {
        "A_w_inT": np.ascontiguousarray(np.asarray(inputs["L0_attn_w_in"]).T),
        "A_b_in_cols": _cols(inputs["L0_attn_b_in"], 24),
        "A_b_vrow": np.ascontiguousarray(np.asarray(inputs["L0_attn_b_in"])[2048:3072].reshape(1, D)),
        "A_w_outT": np.ascontiguousarray(np.asarray(inputs["L0_attn_w_out"]).T),
        "A_b_out_cols": _cols(inputs["L0_attn_b_out"], 8),
        "maskA": consts["maskA"],
        "B_WqT": np.ascontiguousarray(np.asarray(inputs["L0_ffn_Wq"]).T),
        "B_WdkvT": np.ascontiguousarray(np.asarray(inputs["L0_ffn_Wdkv"]).T),
        "B_WukT": np.ascontiguousarray(np.asarray(inputs["L0_ffn_Wuk"]).T),
        "B_WuvT": np.ascontiguousarray(np.asarray(inputs["L0_ffn_Wuv"]).T),
        "B_WoT": np.ascontiguousarray(np.asarray(inputs["L0_ffn_Wo"]).T),
        "B_n1w_cols": _cols(inputs["L0_ffn_norm1_w"], 8),
        "B_lng_cols": _cols(inputs["L0_ffn_ln_g"], 4),
        "B_lnb_cols": _cols(inputs["L0_ffn_ln_b"], 4),
        "maskB": consts["maskB"],
        "maskE": consts["maskE"],
        "ropeB_C": consts["ropeB_C"], "ropeB_S": consts["ropeB_S"], "permB": consts["permB"],
        "C_n2w_cols": _cols(inputs["L0_ffn_norm2_w"], 8),
        "C_gwT": np.ascontiguousarray(np.asarray(inputs["L0_ffn_gate_w"]).T),
        "C_gb_col": np.ascontiguousarray(np.asarray(inputs["L0_ffn_gate_b"]).reshape(4, 1)),
        "C_W1T": np.ascontiguousarray(np.asarray(inputs["L0_ffn_W1"]).transpose(0, 2, 1).reshape(4 * D, 4096)),
        "C_W2T": np.ascontiguousarray(np.asarray(inputs["L0_ffn_W2"]).transpose(0, 2, 1).reshape(4 * 4096, D)),
        "C_b1_cols": np.concatenate([_cols(np.asarray(inputs["L0_ffn_b1"])[e], 32) for e in range(4)], axis=1),
        "C_b2_cols": np.concatenate([_cols(np.asarray(inputs["L0_ffn_b2"])[e], 8) for e in range(4)], axis=1),
        "D_wqT": np.ascontiguousarray(np.asarray(inputs["L1_attn_wq"]).T),
        "D_wkT": np.ascontiguousarray(np.asarray(inputs["L1_attn_wk"]).T),
        "D_wvT": np.ascontiguousarray(np.asarray(inputs["L1_attn_wv"]).T),
        "D_wg1T": np.ascontiguousarray(np.asarray(inputs["L1_attn_wg1"]).T),
        "D_bg1_col": np.ascontiguousarray(np.asarray(inputs["L1_attn_bg1"]).reshape(16, 1)),
        "D_wg2T": np.ascontiguousarray(np.asarray(inputs["L1_attn_wg2"]).T),
        "D_bg2_cols": _cols(inputs["L1_attn_bg2"], 8),
        "D_wogT": np.ascontiguousarray(np.asarray(inputs["L1_attn_wog"]).T),
        "D_bog_cols": _cols(inputs["L1_attn_bog"], 8),
        "D_woT": np.ascontiguousarray(np.asarray(inputs["L1_attn_wo"]).T),
        "D_wgateT": np.ascontiguousarray(np.asarray(inputs["L1_attn_w_gate"]).T),
        "D_wupT": np.ascontiguousarray(np.asarray(inputs["L1_attn_w_up"]).T),
        "D_wdownT": np.ascontiguousarray(np.asarray(inputs["L1_attn_w_down"]).T),
        "D_s1_cols": _cols(inputs["L1_attn_scale1"], 8),
        "D_s2_cols": _cols(inputs["L1_attn_scale2"], 8),
        "D_bd2": consts["bd2"], "D_ex2": consts["ex2"], "D_ex2f": consts["ex2f"],
        "E_Wqkv": np.ascontiguousarray(np.asarray(inputs["L1_ffn_Wqkv"]).reshape(6 * D, 3 * D)),
        "E_Wo": np.ascontiguousarray(np.asarray(inputs["L1_ffn_Wo"]).reshape(6 * D, D)),
        "E_qs_cols": np.ascontiguousarray(np.tile(np.asarray(inputs["L1_ffn_q_scale"]), (1, 2)).T),
        "E_ks_cols": np.ascontiguousarray(np.tile(np.asarray(inputs["L1_ffn_k_scale"]), (1, 2)).T),
        "E_lng_cols": np.concatenate([_cols(np.asarray(inputs["L1_ffn_ln_g"])[i], 8) for i in range(6)], axis=1),
        "E_lnb_cols": np.concatenate([_cols(np.asarray(inputs["L1_ffn_ln_b"])[i], 8) for i in range(6)], axis=1),
        "E_Cq": consts["E_Cq"], "E_Sq": consts["E_Sq"],
        "E_Ck": consts["E_Ck"], "E_Sk": consts["E_Sk"],
        "E_permE": consts["E_permE"],
    }
    DI = 2730
    W1 = np.asarray(inputs["L1_ffn_W1"])
    W1p = np.zeros((6, D, 5632), np.float32)
    W1p[:, :, 0:DI] = W1[:, :, 0:DI]
    W1p[:, :, 2816:2816 + DI] = W1[:, :, DI:2 * DI]
    base["E_W1p"] = np.ascontiguousarray(W1p.reshape(6 * D, 5632))
    W2 = np.asarray(inputs["L1_ffn_W2"])
    W2p = np.zeros((6, 2816, D), np.float32)
    W2p[:, 0:DI, :] = W2
    base["E_W2p"] = np.ascontiguousarray(W2p.reshape(6 * 2816, D))
    b1 = np.asarray(inputs["L1_ffn_b1"])
    b1p = np.zeros((6, 5632), np.float32)
    b1p[:, 0:DI] = b1[:, 0:DI]
    b1p[:, 2816:2816 + DI] = b1[:, DI:2 * DI]
    base["E_b1p_cols"] = np.concatenate([_cols(b1p[i], 44) for i in range(6)], axis=1)
    base["E_b2_cols"] = np.concatenate([_cols(np.asarray(inputs["L1_ffn_b2"])[i], 8) for i in range(6)], axis=1)
    return base


PHASES = ("A", "B", "C", "D", "E")
OUT_NAME = {"A": "x1T", "B": "aT", "C": "x2T", "D": "x3T", "E": "x4T"}

def kernel(**inputs):
    nc = build_program(phases=PHASES)
    base = prep_shared_inputs(inputs)
    x = np.asarray(inputs["L0_attn_x"])
    import ml_dtypes
    in_maps = []
    for b in range(8):
        m = dict(base)
        m["xT"] = np.ascontiguousarray(x[b].T)
        sm = (np.arange(8) <= b).astype(np.float32)
        m["D_smask"] = np.ascontiguousarray(np.tile(sm, (128, 1)))
        m["D_imask"] = np.ascontiguousarray(np.tile(1.0 - sm, (128, 1)))
        m["D_wm2"] = np.ascontiguousarray(np.tile(sm, (2, 1)))
        in_maps.append(m)
    res = run_bass_kernel_spmd(nc, in_maps, core_ids=list(range(8)))
    oname = OUT_NAME[PHASES[-1]]
    out = np.stack([np.ascontiguousarray(np.asarray(r[oname]).astype(np.float32).T)
                    for r in res.results], axis=0)
    return out


# version 08 (final)
# speedup vs baseline: 1.0345x; 1.0345x over previous
import numpy as np
from contextlib import ExitStack
import concourse.bass as bass
import concourse.mybir as mybir
from concourse.bass_utils import run_bass_kernel_spmd

F32 = mybir.dt.float32
BF16 = mybir.dt.bfloat16
F16 = mybir.dt.float16
AF = mybir.ActivationFunctionType
ALU = mybir.AluOpType
AX = mybir.AxisListType

D = 1024
T = 4096
NC = 8
NH = 16
HD = 64
QT = 512
NQT = T // QT
ENGS = ("sync", "tensor", "vector", "scalar", "gpsimd")

DBG = []
EXTRA_OUT = []
E_LAYERS = 6

C_SHIFT_A = 12.0
C_SHIFT_MLA = 16.0
C_SHIFT_E = 16.0


class Sem:
    __slots__ = ("h", "n", "name")
    def __init__(self, h, name):
        self.h = h; self.n = 0; self.name = name
    def inc(self, k=1):
        self.n += k
        return (self, k)


class EvFacade:
    def __init__(self, pp, name="ev"):
        self.pp = pp
        self.sems = {e: pp.sem(f"{name}_{e[0]}{e[1]}") for e in ENGS}
    def inc(self, k=1):
        return ("EVF", self, k)
    def now_list(self):
        return [(s, s.n) for s in self.sems.values() if s.n > 0]


class P:
    def __init__(self, nc, es):
        self.nc = nc
        self.es = es
        self.q = {e: [] for e in ENGS}
        self._sems = {}
        self._waited = {}
        self.psem = {e: self.sem("prog_" + e) for e in ENGS}

    def sem(self, name):
        if name not in self._sems:
            self._sems[name] = Sem(self.es.enter_context(self.nc.semaphore(name)), name)
        return self._sems[name]

    def op(self, eng, meth, *args, inc=None, **kw):
        incs = []
        if inc is not None:
            if isinstance(inc, tuple) and len(inc) == 3 and inc[0] == "EVF":
                incs.append(inc[1].sems[eng].inc(inc[2]))
            else:
                incs.append(inc)
        self.q[eng].append(["op", meth, args, kw, incs])

    def wait(self, eng, sem, val=None):
        if isinstance(sem, EvFacade):
            assert val is None
            for (s, v) in sem.now_list():
                self.wait(eng, s, v)
            return
        v = sem.n if val is None else val
        key = (eng, id(sem))
        if v > 0 and self._waited.get(key, 0) < v:
            self.q[eng].append(["wait", sem, v])
            self._waited[key] = v

    def extra_inc(self, eng, sem):
        for item in reversed(self.q[eng]):
            if item[0] == "op":
                if item[4]:
                    self.q[eng].append(["op", "drain", (), {}, [sem.inc(1)]])
                else:
                    k = 16 if item[1] in ("dma_start", "dma_start_transpose",
                                          "collective_compute") else 1
                    item[4].append(sem.inc(k))
                return True
        return False

    def block(self, engines=ENGS):
        snap = {e: self.psem[e].n for e in ENGS}
        marks = {e: len(self.q[e]) for e in ENGS}
        for e in engines:
            for o in ENGS:
                if o != e and snap[o] > 0:
                    self.wait(e, self.psem[o], snap[o])
        return marks

    def block_end(self, marks):
        fenced = set()
        for e, sname in (("sync", "st_s"), ("gpsimd", "st_g")):
            s = self._sems.get(sname)
            if s is not None and s.n > 0 and any(
                    it[0] == "op" and it[1] == "dma_start" and any(x[0] is s for x in it[4])
                    for it in self.q[e][marks[e]:]):
                self.wait(e, s, s.n)
                fenced.add(e)
        for e in ENGS:
            emitted = any(it[0] == "op" for it in self.q[e][marks[e]:])
            if emitted:
                if e in fenced:
                    self.q[e].append(["op", "drain", (), {}, [self.psem[e].inc(1)]])
                else:
                    self.extra_inc(e, self.psem[e])

    def replay_into(self):
        with self.nc.Block() as block:
            for name in ENGS:
                getattr(block, name)(lambda eng, nm=name: self._replay(eng, nm))

    def _replay(self, eng, name):
        for item in self.q[name]:
            if item[0] == "wait":
                _, sem, val = item
                eng.wait_ge(sem.h, val)
            else:
                _, meth, args, kw, incs = item
                r = getattr(eng, meth)(*args, **kw)
                for (sem, k) in incs:
                    r.then_inc(sem.h, k)


class Bank:
    __slots__ = ("t", "war", "held")
    def __init__(self, t):
        self.t = t; self.war = None; self.held = False


class BankPool:
    def __init__(self, pp, banks):
        self.pp = pp
        self.banks = [Bank(t) for t in banks]
        self.rr = 0

    def get(self, eng="tensor"):
        for _ in range(len(self.banks)):
            b = self.banks[self.rr % len(self.banks)]
            self.rr += 1
            if not b.held:
                break
        else:
            raise RuntimeError("all banks held")
        b.held = True
        if b.war is not None:
            for (s, v) in b.war:
                self.pp.wait(eng, s, v)
            b.war = None
        return b

    def rel(self, b, sem):
        b.war = sem.now_list() if isinstance(sem, EvFacade) else [(sem, sem.n)]
        b.held = False


class Slots:
    def __init__(self, pp, n):
        self.pp = pp; self.war = [None] * n; self.i = 0
    def get(self, writer_eng):
        s = self.i % len(self.war); self.i += 1
        if self.war[s] is not None:
            for (ss, v) in self.war[s]:
                self.pp.wait(writer_eng, ss, v)
            self.war[s] = None
        return s
    def used(self, s, sem):
        self.war[s] = sem.now_list() if isinstance(sem, EvFacade) else [(sem, sem.n)]


import os
PIPE_DEPTH_OVERRIDE = os.environ.get("PIPE_DEPTH")


class LSlots:
    def __init__(self, pp, n, name):
        self.pp = pp
        self.war = [None] * n
        self.lsem = [pp.sem(f"{name}{i}") for i in range(n)]
        self.i = 0
    def get(self, eng):
        s = self.i % len(self.war); self.i += 1
        if self.war[s] is not None:
            for (ss, v) in self.war[s]:
                self.pp.wait(eng, ss, v)
            self.war[s] = None
        return s
    def linc(self, s, k=16):
        return self.lsem[s].inc(k)
    def wait_loaded(self, eng, s):
        self.pp.wait(eng, self.lsem[s])
    def used(self, s, sem):
        self.war[s] = sem.now_list() if isinstance(sem, EvFacade) else [(sem, sem.n)]


class Pipe:
    def __init__(self, depth=3):
        if PIPE_DEPTH_OVERRIDE is not None:
            depth = int(PIPE_DEPTH_OVERRIDE)
        self.depth = depth
        self.q = []
    def push(self, fn):
        self.q.append(fn)
        if len(self.q) > self.depth:
            self.q.pop(0)()
    def flush(self):
        for fn in self.q:
            fn()
        self.q = []


def mm_group(pp, out_ap, pairs, inc=None):
    n = len(pairs)
    for i, (l, r) in enumerate(pairs):
        pp.op("tensor", "matmul", out_ap, l, r, start=(i == 0), stop=(i == n - 1),
              inc=(inc if i == n - 1 else None))


def _mask_tiles(deltas, qw, lo, hi):
    out = []
    kl = np.arange(128)[:, None]
    ql = np.arange(qw)[None, :]
    for d in deltas:
        diff = ql - (kl + d)
        out.append(((diff >= lo) & (diff <= hi)).astype(np.float32))
    return np.concatenate(out, axis=1)


def host_constants():
    import ml_dtypes
    c = {}
    c["maskA"] = _mask_tiles([-128, 0, 128, 256, 384, 512], QT, -128, 128).astype(ml_dtypes.bfloat16)
    c["maskB"] = _mask_tiles([0, 128, 256, 384], QT, 0, 10 ** 9).astype(ml_dtypes.bfloat16)
    c["maskE"] = _mask_tiles([-256, -128, 0, 128, 256, 384], QT, 0, 256).astype(ml_dtypes.bfloat16)
    half = 32
    pos = np.arange(T, dtype=np.float64)
    inv_freq = 1.0 / 10000.0 ** (np.arange(half, dtype=np.float64) / half)
    ang = pos[None, :] * inv_freq[:, None]
    cos = np.cos(ang); sin = np.sin(ang)
    C1 = np.concatenate([cos, cos], axis=0)
    S1 = np.concatenate([-sin, sin], axis=0)
    c["ropeB_C"] = np.tile(C1, (2, 1)).astype(ml_dtypes.bfloat16)
    c["ropeB_S"] = np.tile(S1, (2, 1)).astype(ml_dtypes.bfloat16)
    P = np.zeros((128, 128), np.float32)
    for base in (0, 64):
        for j in range(32):
            P[base + j + 32, base + j] = 1.0
            P[base + j, base + j + 32] = 1.0
    c["permB"] = P.astype(ml_dtypes.bfloat16)
    bd2 = np.zeros((128, 2), np.float32)
    bd2[0:64, 0] = 1.0; bd2[64:128, 1] = 1.0
    c["bd2"] = bd2.astype(ml_dtypes.bfloat16)
    ex2 = np.zeros((2, 128), np.float32)
    ex2[0, 0:64] = 1.0; ex2[1, 64:128] = 1.0
    c["ex2"] = ex2.astype(ml_dtypes.bfloat16)
    c["ex2f"] = ex2
    DH = 64
    half_idx = np.arange(0, DH, 2, dtype=np.float64)
    inv_freq = 1.0 / 10000.0 ** (half_idx / DH)
    t_ = np.arange(T, dtype=np.float64)
    freqs = t_[:, None] * inv_freq[None, :]
    freqs = np.concatenate([freqs, freqs], -1)
    base_scale = (half_idx + 0.4 * DH) / (1.4 * DH)
    power = (t_ - T // 2) / (256 // 2)
    scale = base_scale[None, :] ** power[:, None]
    scale = np.concatenate([scale, scale], -1)
    cosf = np.cos(freqs); sinf = np.sin(freqs)
    Cq_ = (cosf * scale).T; Sq_ = (sinf * scale).T
    Ck_ = (cosf / scale).T; Sk_ = (sinf / scale).T
    c["E_Cq"] = np.tile(Cq_, (2, 1)).astype(ml_dtypes.bfloat16)
    c["E_Sq"] = np.tile(Sq_, (2, 1)).astype(ml_dtypes.bfloat16)
    c["E_Ck"] = np.tile(Ck_, (2, 1)).astype(ml_dtypes.bfloat16)
    c["E_Sk"] = np.tile(Sk_, (2, 1)).astype(ml_dtypes.bfloat16)
    PE_ = np.zeros((128, 128), np.float32)
    for base in (0, 64):
        for j in range(32):
            PE_[base + j + 32, base + j] = -1.0
            PE_[base + j, base + j + 32] = 1.0
    c["E_permE"] = PE_.astype(ml_dtypes.bfloat16)
    return c


def _cols(vec, nchunks):
    return np.ascontiguousarray(np.asarray(vec).reshape(nchunks, 128).T)


def ln_stats_rows(pp, banks, sq_slots, sq_sb, ones_col, src_chunks, rowap, mm, ev, sqsem, eps_ap=None):
    nch = len(src_chunks)
    bsum = banks.get(); bsq = banks.get()
    for c, xc in enumerate(src_chunks):
        s = sq_slots.get("scalar")
        ssl = sq_sb[:, s * QT:(s + 1) * QT][:, 0:xc.shape[1]]
        pp.op("scalar", "activation", ssl, xc, AF.Square, inc=sqsem.inc())
        pp.op("tensor", "matmul", bsum.t[0:1, 0:xc.shape[1]], ones_col[:], xc,
              start=(c == 0), stop=(c == nch - 1), inc=(mm.inc() if c == nch - 1 else None))
        pp.wait("tensor", sqsem)
        pp.op("tensor", "matmul", bsq.t[0:1, 0:xc.shape[1]], ones_col[:], ssl,
              start=(c == 0), stop=(c == nch - 1), inc=mm.inc())
        sq_slots.used(s, mm)
    W = src_chunks[0].shape[1]
    nfeat = float(nch * 128)
    pp.wait("vector", mm)
    pp.op("vector", "tensor_scalar_mul", rowap(0), bsum.t[0:1, 0:W], 1.0 / nfeat)
    pp.op("vector", "tensor_scalar_mul", rowap(1), bsq.t[0:1, 0:W], 1.0 / nfeat, inc=ev.inc())
    banks.rel(bsum, ev); banks.rel(bsq, ev)
    pp.op("vector", "tensor_mul", rowap(2), rowap(0), rowap(0))
    pp.op("vector", "tensor_sub", rowap(2), rowap(1), rowap(2), inc=ev.inc())
    pp.wait("scalar", ev)
    pp.op("scalar", "activation", rowap(3), rowap(2), AF.Ln, bias=eps_ap)
    pp.op("scalar", "activation", rowap(3), rowap(3), AF.Exp, scale=-0.5, inc=ev.inc())
    pp.wait("vector", ev)
    pp.op("vector", "tensor_mul", rowap(4), rowap(0), rowap(3), inc=ev.inc())


def emit_phaseA(pp, ctx):
    nc = ctx["nc"]; inp = ctx["inp"]; dram = ctx["dram"]
    es = pp.es
    sb = lambda name, shape, dt: es.enter_context(nc.sbuf_tensor(name, shape, dt))

    x_sb = sb("x_sb", [128, 8 * QT], F32)
    sq_sb = sb("sq_sb", [128, 2 * QT], F32)
    tmp_sb = sb("tmp_sb", [128, QT], F32)
    t_ring = sb("t_ring", [128, NC * 8 * 128], BF16)
    q_ring = sb("q_ring", [128, NC * 2 * QT], BF16)
    k_ring = sb("k_ring", [128, NC * 12 * 128], BF16)
    v_ring = sb("v_ring", [128, 12 * (NH * 65)], BF16)
    exp_sb = sb("exp_sb", [128, 2 * QT], F32)
    attnT_sb = sb("attnT_sb", [128, 3 * QT], BF16)
    ctxb_sb = sb("ctxb_sb", [128, NC * QT], BF16)
    dnr_sb = sb("dnr_sb", [1, 2 * QT], F32)
    rows = sb("rows", [1, 5 * QT], F32)
    x1_sb = sb("x1_sb", [128, QT], F32)
    w_in_sb = sb("w_in_sb", [128, NC * 3 * D], BF16)
    w_out_sb = sb("w_out_sb", [128, NC * D], BF16)
    bin_sb = sb("bin_sb", [128, 24], F32)
    bvrow_sb = sb("bvrow_sb", [1, D], BF16)
    bout_sb = sb("bout_sb", [128, 8], F32)
    maskA_sb = sb("maskA_sb", [128, 6 * QT], BF16)
    ones_col = sb("ones_col", [128, 1], F32)
    eps_row = sb("eps_row", [1, 1], F32)
    cshA_col = sb("cshA_col", [128, 1], F32)
    ones_row = sb("ones_row", [1, 128], F32)
    ones_rowb = sb("ones_rowb", [1, 128], BF16)

    banks = BankPool(pp, [es.enter_context(nc.psum_tensor(f"bk{i}", [128, 512], F32))
                          for i in range(8)])
    ld_s = pp.sem("ld_s"); ld_g = pp.sem("ld_g")
    st_s = pp.sem("st_s"); st_g = pp.sem("st_g")
    mm = pp.sem("mm"); ev = EvFacade(pp)
    sqsem = pp.sem("sq")
    sq_slots = Slots(pp, 2)
    xrl_slots = LSlots(pp, 2, "xrl")
    exp_slots = Slots(pp, 2)
    at_slots = Slots(pp, 3)
    x1_slots = Slots(pp, 1)

    mk = pp.block()
    pp.op("vector", "memset", ones_col[:], 1.0)
    pp.op("vector", "memset", eps_row[:], 1e-5)
    pp.op("vector", "memset", cshA_col[:], -C_SHIFT_A)
    pp.op("vector", "memset", ones_row[:], 1.0)
    pp.op("vector", "memset", ones_rowb[:], 1.0)
    pp.op("vector", "memset",
          v_ring.ap().rearrange("p (s h j) -> p s h j", s=12, h=NH, j=65)[:, :, :, 64:65], 1.0,
          inc=pp.sem("ev").inc())
    pp.op("sync", "dma_start", out=bin_sb[:], in_=inp["A_b_in_cols"], inc=ld_s.inc(16))
    pp.op("sync", "dma_start", out=bout_sb[:], in_=inp["A_b_out_cols"], inc=ld_s.inc(16))
    pp.op("sync", "dma_start", out=maskA_sb[:], in_=inp["maskA"], inc=ld_s.inc(16))
    pp.op("gpsimd", "dma_start", out=bvrow_sb[:], in_=inp["A_b_vrow"], inc=ld_g.inc(16))
    for kc in range(NC):
        pp.op("gpsimd", "dma_start", out=w_in_sb[:, kc * 3 * D:(kc + 1) * 3 * D],
              in_=inp["A_w_inT"][kc * 128:(kc + 1) * 128, :], inc=ld_g.inc(16))
        pp.op("gpsimd", "dma_start", out=w_out_sb[:, kc * D:(kc + 1) * D],
              in_=inp["A_w_outT"][kc * 128:(kc + 1) * 128, :], inc=ld_g.inc(16))
    pp.block_end(mk)

    xT = inp["xT"]; x1T = dram["x1T"]
    R = lambda r: rows[0:1, r * QT:(r + 1) * QT]

    def tr(c, tile):
        s = tile % 8
        return t_ring[:, c * 1024 + s * 128: c * 1024 + (s + 1) * 128]

    def tr_qt(c, qt):
        s = (qt * 4) % 8
        return t_ring[:, c * 1024 + s * 128: c * 1024 + s * 128 + 512]

    def qr(c, qt):
        s = qt % 2
        return q_ring[:, c * 2 * QT + s * QT: c * 2 * QT + (s + 1) * QT]

    def kr(c, tile):
        s = tile % 12
        return k_ring[:, c * 12 * 128 + s * 128: c * 12 * 128 + (s + 1) * 128]

    def kr_qt(c, qt):
        s = (qt * 4) % 12
        return k_ring[:, c * 12 * 128 + s * 128: c * 12 * 128 + s * 128 + 512]

    def vr(tile, h):
        s = tile % 12
        return v_ring[:, s * (NH * 65) + h * 65: s * (NH * 65) + h * 65 + 65]

    def vr_w(tile, g):
        s = tile % 12
        base = v_ring[:, s * (NH * 65) + 8 * g * 65: s * (NH * 65) + (8 * g + 8) * 65]
        return base.rearrange("p (h j) -> p h j", h=8, j=65)[:, :, 0:64]

    def emit_ln_qkv(qt):
        mk = pp.block()
        t0 = qt * QT
        for c in range(NC):
            pp.op("sync", "dma_start", out=x_sb[:, c * QT:(c + 1) * QT],
                  in_=xT[c * 128:(c + 1) * 128, t0:t0 + QT], inc=ld_s.inc(16))
        pp.wait("scalar", ld_s); pp.wait("scalar", ld_g)
        pp.wait("tensor", ld_s); pp.wait("tensor", ld_g)
        src = [x_sb[:, c * QT:(c + 1) * QT] for c in range(NC)]
        ln_stats_rows(pp, banks, sq_slots, sq_sb, ones_col, src, R, mm, ev, sqsem, eps_row[0:1, 0:1])
        brstd = banks.get(); bmmr = banks.get()
        pp.wait("tensor", ev)
        pp.op("tensor", "matmul", brstd.t[:], ones_row[:], R(3), start=True, stop=True)
        pp.op("tensor", "matmul", bmmr.t[:], ones_row[:], R(4), start=True, stop=True, inc=mm.inc())
        pp.wait("vector", mm)
        for c in range(NC):
            pp.op("vector", "tensor_mul", tmp_sb[:], x_sb[:, c * QT:(c + 1) * QT], brstd.t[:])
            pp.op("vector", "tensor_sub", tr_qt(c, qt), tmp_sb[:], bmmr.t[:],
                  inc=(ev.inc() if c == NC - 1 else None))
        banks.rel(brstd, ev); banks.rel(bmmr, ev)

        if "t" in DBG:
            pp.wait("gpsimd", ev)
            for c in range(NC):
                pp.op("gpsimd", "dma_start", out=dram["dbgT"][c * 128:(c + 1) * 128, t0:t0 + QT],
                      in_=tr_qt(c, qt), inc=st_g.inc(16))
        pp.wait("tensor", ev)
        for kind, base_oc in (("q", 0), ("k", 8)):
            for oc in range(NC):
                bo = banks.get()
                pairs = []
                for kc in range(NC):
                    l = w_in_sb[:, kc * 3 * D + (base_oc + oc) * 128:
                                kc * 3 * D + (base_oc + oc) * 128 + 128]
                    pairs.append((l, tr_qt(kc, qt)))
                mm_group(pp, bo.t[:], pairs, inc=mm.inc())
                pp.wait("scalar", mm)
                dst = qr(oc, qt) if kind == "q" else kr_qt(oc, qt)
                pp.op("scalar", "activation", dst, bo.t[:], AF.Identity,
                      bias=bin_sb[:, base_oc + oc: base_oc + oc + 1], inc=ev.inc())
                banks.rel(bo, ev)
        if "q" in DBG or "k" in DBG:
            pp.wait("gpsimd", ev)
            for c in range(NC):
                if "q" in DBG:
                    pp.op("gpsimd", "dma_start", out=dram["dbgT"][c * 128:(c + 1) * 128, t0:t0 + QT],
                          in_=qr(c, qt), inc=st_g.inc(16))
                if "k" in DBG:
                    pp.op("gpsimd", "dma_start", out=dram["dbgT"][D + c * 128:D + (c + 1) * 128, t0:t0 + QT],
                          in_=kr_qt(c, qt), inc=st_g.inc(16))
        for i in range(4):
            tile = qt * 4 + i
            for g in range(2):
                bo = banks.get()
                for kc in range(NC):
                    l = tr(kc, tile)
                    r = w_in_sb[:, kc * 3 * D + 2 * D + g * 512:
                                kc * 3 * D + 2 * D + (g + 1) * 512]
                    pp.op("tensor", "matmul", bo.t[:], l, r, start=(kc == 0), stop=False)
                pp.op("tensor", "matmul", bo.t[:], ones_rowb[:],
                      bvrow_sb[0:1, g * 512:(g + 1) * 512], start=False, stop=True, inc=mm.inc())
                pp.wait("vector", mm)
                pp.op("vector", "tensor_copy", vr_w(tile, g),
                      bo.t.ap().rearrange("p (h j) -> p h j", h=8, j=64), inc=ev.inc())
                banks.rel(bo, ev)
        if "v" in DBG:
            pp.wait("gpsimd", ev)
            for i in range(4):
                tile = qt * 4 + i
                if tile >= 3:
                    continue
                s = tile % 12
                pp.op("gpsimd", "dma_start",
                      out=dram["dbgT"][0:128, tile * 1040:(tile + 1) * 1040],
                      in_=v_ring[:, s * 1040:(s + 1) * 1040], inc=st_g.inc(16))
        pp.block_end(mk)

    def emit_attn(qt):
        mk = pp.block()
        t0 = qt * QT
        kbs_all = [(i, qt * 4 - 1 + i) for i in range(6)]
        kbs = [(i, kb) for i, kb in kbs_all if 0 <= kb < T // 128]
        for c in range(NC):
            bctxs = {}
            pipe = Pipe(depth=3)
            for hh in range(2):
                h = 2 * c + hh
                rowoff = hh * 64
                bctxs[hh] = banks.get()
                first = True
                for i, kb in kbs:
                    bs = banks.get()
                    pp.op("tensor", "matmul", bs.t[:], kr(c, kb)[rowoff:rowoff + 64, :],
                          qr(c, qt)[rowoff:rowoff + 64, :], start=True, stop=True, inc=mm.inc())
                    mmv = mm.n

                    def consume(bs=bs, h=h, hh=hh, i=i, kb=kb, mmv=mmv, first=first,
                                last=(i == kbs[-1][0])):
                        pp.wait("scalar", mm, mmv)
                        es_ = exp_slots.get("scalar")
                        eslot = exp_sb[:, es_ * QT:(es_ + 1) * QT]
                        pp.op("scalar", "activation", eslot, bs.t[:], AF.Exp,
                              scale=0.125, bias=cshA_col[:, 0:1], inc=ev.inc())
                        banks.rel(bs, ev)
                        pp.wait("vector", ev)
                        as_ = at_slots.get("vector")
                        aslot = attnT_sb[:, as_ * QT:(as_ + 1) * QT]
                        pp.op("vector", "scalar_tensor_tensor", aslot, eslot, 1e30,
                              maskA_sb[:, i * QT:(i + 1) * QT], ALU.min, ALU.mult, inc=ev.inc())
                        exp_slots.used(es_, ev)
                        pp.wait("tensor", ev)
                        pp.op("tensor", "matmul", bctxs[hh].t[0:65, :], vr(kb, h), aslot,
                              start=first, stop=last, inc=mm.inc())
                        at_slots.used(as_, mm)
                    pipe.push(consume)
                    first = False
            pipe.flush()
            for hh in range(2):
                rowoff = hh * 64
                bctx = bctxs[hh]
                pp.wait("scalar", mm)
                pp.op("scalar", "copy", ctxb_sb[rowoff:rowoff + 64, c * QT:(c + 1) * QT],
                      bctx.t[0:64, :], inc=ev.inc())
                pp.wait("vector", mm)
                pp.op("vector", "tensor_scalar_add", dnr_sb[0:1, hh * QT:(hh + 1) * QT],
                      bctx.t[64:65, :], 1e-30)
                pp.op("vector", "reciprocal", dnr_sb[0:1, hh * QT:(hh + 1) * QT],
                      dnr_sb[0:1, hh * QT:(hh + 1) * QT], inc=ev.inc())
                banks.rel(bctx, ev)
            br = banks.get()
            pp.wait("tensor", ev)
            pp.op("tensor", "matmul", br.t[0:64, :], ones_row[0:1, 0:64],
                  dnr_sb[0:1, 0:QT], start=True, stop=True)
            pp.op("tensor", "matmul", br.t[64:128, :], ones_row[0:1, 0:64],
                  dnr_sb[0:1, QT:2 * QT], start=True, stop=True, inc=mm.inc())
            pp.wait("vector", mm)
            pp.op("vector", "tensor_mul", ctxb_sb[:, c * QT:(c + 1) * QT],
                  ctxb_sb[:, c * QT:(c + 1) * QT], br.t[:], inc=ev.inc())
            banks.rel(br, ev)

        if "ctx" in DBG:
            pp.wait("gpsimd", ev)
            for c in range(NC):
                pp.op("gpsimd", "dma_start", out=dram["dbgT"][c * 128:(c + 1) * 128, t0:t0 + QT],
                      in_=ctxb_sb[:, c * QT:(c + 1) * QT], inc=st_g.inc(16))
        pp.wait("tensor", ev)
        for oc in range(NC):
            bo = banks.get()
            pairs = []
            for kc in range(NC):
                l = w_out_sb[:, kc * D + oc * 128: kc * D + oc * 128 + 128]
                pairs.append((l, ctxb_sb[:, kc * QT:(kc + 1) * QT]))
            mm_group(pp, bo.t[:], pairs, inc=mm.inc())
            pp.wait("vector", mm)
            pp.op("vector", "scalar_tensor_tensor", x_sb[:, oc * QT:(oc + 1) * QT],
                  bo.t[:], bout_sb[:, oc:oc + 1], tr_qt(oc, qt), ALU.add, ALU.add, inc=ev.inc())
            banks.rel(bo, ev)

        if "u" in DBG:
            pp.wait("gpsimd", ev)
            for c in range(NC):
                pp.op("gpsimd", "dma_start", out=dram["dbgT"][c * 128:(c + 1) * 128, t0:t0 + QT],
                      in_=x_sb[:, c * QT:(c + 1) * QT], inc=st_g.inc(16))
        pp.wait("scalar", ev)
        pp.wait("tensor", ev)
        src = [x_sb[:, c * QT:(c + 1) * QT] for c in range(NC)]
        ln_stats_rows(pp, banks, sq_slots, sq_sb, ones_col, src, R, mm, ev, sqsem, eps_row[0:1, 0:1])
        brstd = banks.get(); bmmr = banks.get()
        pp.wait("tensor", ev)
        pp.op("tensor", "matmul", brstd.t[:], ones_row[:], R(3), start=True, stop=True)
        pp.op("tensor", "matmul", bmmr.t[:], ones_row[:], R(4), start=True, stop=True, inc=mm.inc())
        pp.wait("vector", mm)
        pp.wait("sync", mm)
        for c in range(NC):
            xs_ = xrl_slots.get("sync")
            xsl = sq_sb[:, xs_ * QT:(xs_ + 1) * QT]
            pp.op("sync", "dma_start", out=xsl,
                  in_=xT[c * 128:(c + 1) * 128, t0:t0 + QT], inc=xrl_slots.linc(xs_))
            xrl_slots.wait_loaded("vector", xs_)
            s1_ = x1_slots.get("vector")
            x1sl = x1_sb[:, s1_ * QT:(s1_ + 1) * QT]
            pp.op("vector", "tensor_mul", tmp_sb[:], x_sb[:, c * QT:(c + 1) * QT], brstd.t[:])
            pp.op("vector", "tensor_sub", tmp_sb[:], tmp_sb[:], bmmr.t[:])
            pp.op("vector", "tensor_add", x1sl, tmp_sb[:], xsl, inc=ev.inc())
            xrl_slots.used(xs_, ev)
            if "x1" in DBG:
                pp.wait("gpsimd", ev)
                pp.op("gpsimd", "dma_start", out=dram["dbgT"][c * 128:(c + 1) * 128, t0:t0 + QT],
                      in_=x1sl, inc=st_g.inc(16))
            if "lnu" in DBG:
                pp.wait("gpsimd", ev)
                pp.op("gpsimd", "dma_start", out=dram["dbgT"][D + c * 128:D + (c + 1) * 128, t0:t0 + QT],
                      in_=xsl, inc=st_g.inc(16))
            pp.wait("sync", ev)
            pp.op("sync", "dma_start", out=x1T[c * 128:(c + 1) * 128, t0:t0 + QT],
                  in_=x1sl, inc=st_s.inc(16))
            x1_slots.used(s1_, st_s)
        banks.rel(brstd, ev); banks.rel(bmmr, ev)
        pp.block_end(mk)

    for qt in range(NQT):
        emit_ln_qkv(qt)
        if qt >= 1:
            emit_attn(qt - 1)
    emit_attn(NQT - 1)
    pp.wait("sync", st_s); pp.wait("gpsimd", st_g)


def build_program(phases=("A",)):
    nc = bass.Bass("TRN2", target_bir_lowering=False)
    inp = {}
    def I(name, shape, dt=F32):
        inp[name] = nc.dram_tensor(name, list(shape), dt, kind="ExternalInput").ap()
        return inp[name]

    I("xT", [D, T])
    I("A_w_inT", [D, 3 * D]); I("A_b_in_cols", [128, 24]); I("A_b_vrow", [1, D])
    I("A_w_outT", [D, D]); I("A_b_out_cols", [128, 8])
    I("maskA", [128, 6 * QT], BF16)
    I("B_WqT", [D, D]); I("B_WdkvT", [D, 512]); I("B_WukT", [512, D])
    I("B_WuvT", [512, D]); I("B_WoT", [D, D])
    I("B_n1w_cols", [128, 8]); I("B_lng_cols", [128, 4]); I("B_lnb_cols", [128, 4])
    I("maskB", [128, 4 * QT], BF16)
    I("maskE", [128, 6 * QT], BF16)
    I("ropeB_C", [128, T], BF16); I("ropeB_S", [128, T], BF16); I("permB", [128, 128], BF16)
    I("C_n2w_cols", [128, 8]); I("C_gwT", [D, 4]); I("C_gb_col", [4, 1])
    I("C_W1T", [4 * D, 4096]); I("C_W2T", [4 * 4096, D])
    I("C_b1_cols", [128, 4 * 32]); I("C_b2_cols", [128, 4 * 8])
    I("D_wqT", [D, D]); I("D_wkT", [D, D]); I("D_wvT", [D, D])
    I("D_wg1T", [D, 16]); I("D_bg1_col", [16, 1]); I("D_wg2T", [16, D])
    I("D_bg2_cols", [128, 8]); I("D_wogT", [D, D]); I("D_bog_cols", [128, 8])
    I("D_woT", [D, D]); I("D_wgateT", [D, 4096]); I("D_wupT", [D, 4096])
    I("D_wdownT", [4096, D]); I("D_s1_cols", [128, 8]); I("D_s2_cols", [128, 8])
    I("D_bd2", [128, 2], BF16); I("D_ex2", [2, 128], BF16); I("D_ex2f", [2, 128])
    I("D_smask", [128, 8]); I("D_imask", [128, 8]); I("D_wm2", [2, 8])
    I("E_Wqkv", [6 * D, 3 * D]); I("E_Wo", [6 * D, D])
    I("E_W1p", [6 * D, 5632]); I("E_W2p", [6 * 2816, D])
    I("E_qs_cols", [128, 6]); I("E_ks_cols", [128, 6])
    I("E_lng_cols", [128, 6 * 8]); I("E_lnb_cols", [128, 6 * 8])
    I("E_b1p_cols", [128, 6 * 44]); I("E_b2_cols", [128, 6 * 8])
    I("E_Cq", [128, T], BF16); I("E_Sq", [128, T], BF16)
    I("E_Ck", [128, T], BF16); I("E_Sk", [128, T], BF16)
    I("E_permE", [128, 128], BF16)

    out_name = {"A": "x1T", "B": "aT", "C": "x2T", "D": "x3T", "E": "x4T"}[phases[-1]]
    dram = {}
    def S(name, shape, dt=F32):
        kind = "ExternalOutput" if (name == out_name or name in EXTRA_OUT) else "Internal"
        if kind == "ExternalOutput":
            dram[name] = nc.dram_tensor(name, list(shape), dt, kind=kind).ap()
        else:
            dram[name] = nc.dram_tensor(name, list(shape), dt).ap()
        return dram[name]

    S("x1T", [D, T])
    S("t1T", [D, T], BF16); S("xnT", [D, T], BF16)
    S("kT", [D, T], BF16); S("vT", [T, NH * 65], BF16)
    S("aT", [D, T])
    S("mT", [D, T], BF16); S("accT", [D, T]); S("mrT", [4, T]); S("x2T", [D, T])
    S("t2T", [D, T], BF16); S("xn2T", [D, T], BF16); S("qTd", [D, T], BF16)
    S("kpub", [D, T], BF16); S("vpub", [D, T], BF16); S("apub", [D, T], F16)
    S("kG", [8 * D, T], BF16); S("vG", [8 * D, T], BF16); S("aG", [8 * D, T], F16)
    S("yT", [D, T]); S("hT", [4096, T], BF16); S("x3T", [D, T])
    S("xxA", [D, T]); S("xxB", [D, T]); S("x4T", [D, T])
    if DBG:
        dram["dbgT"] = nc.dram_tensor("dbgT", [2 * D, T], F32, kind="ExternalOutput").ap()

    ctx = dict(nc=nc, inp=inp, dram=dram)
    for ph in phases:
        es = ExitStack()
        pp = P(nc, es)
        {"A": emit_phaseA, "B": emit_phaseB, "C": emit_phaseC,
         "D": emit_phaseD, "E": emit_phaseE}[ph](pp, ctx)
        pp.replay_into()
        es.close()
        nc.reset()
    return nc


def emit_phaseB(pp, ctx):
    nc = ctx["nc"]; inp = ctx["inp"]; dram = ctx["dram"]
    es = pp.es
    sb = lambda name, shape, dt: es.enter_context(nc.sbuf_tensor("B" + name, shape, dt))

    banks = BankPool(pp, [es.enter_context(nc.psum_tensor(f"Bbk{i}", [128, 512], F32))
                          for i in range(8)])
    ld_s = pp.sem("ld_s"); ld_g = pp.sem("ld_g")
    st_s = pp.sem("st_s"); st_g = pp.sem("st_g")
    mm = pp.sem("mm"); ev = EvFacade(pp)
    sqsem = pp.sem("sq")

    ropeC = sb("ropeC", [128, T], BF16)
    ropeS = sb("ropeS", [128, T], BF16)
    perm = sb("perm", [128, 128], BF16)
    ones_col = sb("ones_col", [128, 1], F32)
    ones_row = sb("ones_row", [1, 128], F32)
    eps5 = sb("eps5", [1, 1], F32)
    eps6 = sb("eps6", [1, 1], F32)
    cshB = sb("cshB", [128, 1], F32)
    n1w_col = sb("n1w_col", [128, 8], F32)
    lng_col = sb("lng_col", [128, 4], F32)
    lnb_col = sb("lnb_col", [128, 4], F32)

    x1T = dram["x1T"]; t1T = dram["t1T"]; xnT = dram["xnT"]
    kT = dram["kT"]; vT = dram["vT"]; aT = dram["aT"]

    mk = pp.block()
    pp.op("vector", "memset", ones_col[:], 1.0)
    pp.op("vector", "memset", ones_row[:], 1.0)
    pp.op("vector", "memset", eps5[:], 1e-5)
    pp.op("vector", "memset", eps6[:], 1e-6)
    pp.op("vector", "memset", cshB[:], -C_SHIFT_MLA)
    pp.op("gpsimd", "dma_start", out=ropeC[:], in_=inp["ropeB_C"], inc=ld_g.inc(16))
    pp.op("gpsimd", "dma_start", out=ropeS[:], in_=inp["ropeB_S"], inc=ld_g.inc(16))
    pp.op("gpsimd", "dma_start", out=perm[:], in_=inp["permB"], inc=ld_g.inc(16))
    pp.op("sync", "dma_start", out=n1w_col[:], in_=inp["B_n1w_cols"], inc=ld_s.inc(16))
    pp.op("sync", "dma_start", out=lng_col[:], in_=inp["B_lng_cols"], inc=ld_s.inc(16))
    pp.op("sync", "dma_start", out=lnb_col[:], in_=inp["B_lnb_cols"], inc=ld_s.inc(16))
    pp.block_end(mk)

    def rope_chunk(dst_ap, src_bf_ap, t0, w, sgn=1.0):
        bp = banks.get()
        pp.op("tensor", "matmul", bp.t[:, 0:w], perm[:], src_bf_ap, start=True, stop=True,
              inc=mm.inc())
        pp.wait("vector", mm)
        pp.op("vector", "tensor_mul", rope_t1[:, 0:w], bp.t[:, 0:w], ropeS[:, t0:t0 + w])
        pp.op("vector", "tensor_mul", rope_t2[:, 0:w], src_bf_ap, ropeC[:, t0:t0 + w])
        pp.op("vector", "tensor_add", dst_ap, rope_t2[:, 0:w], rope_t1[:, 0:w], inc=ev.inc())
        banks.rel(bp, ev)

    es2 = ExitStack()
    sb2 = lambda name, shape, dt: es2.enter_context(nc.sbuf_tensor("B" + name, shape, dt))
    x1_sb = sb2("x1_sb", [128, 8 * QT], F32)
    sq_sb = sb2("sqB", [128, 2 * QT], F32)
    tmp_sb = sb2("tmpB", [128, QT], F32)
    rows = sb2("rowsB", [1, 8 * QT], F32)
    t1_sb = sb2("t1B", [128, 8 * QT], BF16)
    xn_sb = sb2("xnB", [128, 8 * QT], BF16)
    lat_f = sb2("latf", [128, 4 * QT], F32)
    lat_b = sb2("latb", [128, 4 * QT], BF16)
    kw_sb = sb2("kwB", [128, 8 * QT], BF16)
    vt_sb = sb2("vtB", [128, 2 * 1040], BF16)
    rope_t1 = sb2("ropet1", [128, QT], F32)
    rope_t2 = sb2("ropet2", [128, QT], F32)
    wdkv_sb = sb2("wdkv", [128, 8 * 512], BF16)
    wuk_sb = sb2("wuk", [128, 4 * D], BF16)
    wuv_sb = sb2("wuv", [128, 4 * D], BF16)

    sq_slots = Slots(pp, 2)
    vt_slots = Slots(pp, 2)
    R = lambda r: rows[0:1, r * QT:(r + 1) * QT]

    mk = pp.block()
    pp.op("vector", "memset",
          vt_sb.ap().rearrange("p (s h j) -> p s h j", s=2, h=NH, j=65)[:, :, :, 64:65], 1.0)
    for kc in range(NC):
        pp.op("gpsimd", "dma_start", out=wdkv_sb[:, kc * 512:(kc + 1) * 512],
              in_=inp["B_WdkvT"][kc * 128:(kc + 1) * 128, :], inc=ld_g.inc(16))
    for kc in range(4):
        pp.op("gpsimd", "dma_start", out=wuk_sb[:, kc * D:(kc + 1) * D],
              in_=inp["B_WukT"][kc * 128:(kc + 1) * 128, :], inc=ld_g.inc(16))
        pp.op("gpsimd", "dma_start", out=wuv_sb[:, kc * D:(kc + 1) * D],
              in_=inp["B_WuvT"][kc * 128:(kc + 1) * 128, :], inc=ld_g.inc(16))
    pp.block_end(mk)

    for qt in range(NQT):
        mk = pp.block()
        t0 = qt * QT
        for c in range(NC):
            pp.op("sync", "dma_start", out=x1_sb[:, c * QT:(c + 1) * QT],
                  in_=x1T[c * 128:(c + 1) * 128, t0:t0 + QT], inc=ld_s.inc(16))
        pp.wait("scalar", ld_s); pp.wait("scalar", ld_g)
        pp.wait("tensor", ld_s); pp.wait("tensor", ld_g)
        src = [x1_sb[:, c * QT:(c + 1) * QT] for c in range(NC)]
        ln_stats_rows(pp, banks, sq_slots, sq_sb, ones_col, src, R, mm, ev, sqsem,
                      eps5[0:1, 0:1])
        pp.op("vector", "tensor_scalar_add", R(5), R(2), 0.0)
        pp.op("vector", "tensor_single_scalar", R(6), R(2), 1e-5, ALU.add)
        pp.op("vector", "reciprocal", R(6), R(6))
        pp.op("vector", "tensor_mul", R(5), R(5), R(6), inc=ev.inc())
        pp.wait("scalar", ev)
        pp.op("scalar", "activation", R(5), R(5), AF.Ln, bias=eps6[0:1, 0:1])
        pp.op("scalar", "activation", R(5), R(5), AF.Exp, scale=-0.5, inc=ev.inc())
        brstd = banks.get(); bmmr = banks.get(); bs_ = banks.get()
        pp.wait("tensor", ev)
        pp.op("tensor", "matmul", brstd.t[:], ones_row[:], R(3), start=True, stop=True)
        pp.op("tensor", "matmul", bmmr.t[:], ones_row[:], R(4), start=True, stop=True)
        pp.op("tensor", "matmul", bs_.t[:], ones_row[:], R(5), start=True, stop=True, inc=mm.inc())
        pp.wait("vector", mm)
        for c in range(NC):
            t1sl = t1_sb[:, c * QT:(c + 1) * QT]
            pp.op("vector", "tensor_mul", tmp_sb[:], x1_sb[:, c * QT:(c + 1) * QT], brstd.t[:])
            pp.op("vector", "tensor_sub", t1sl, tmp_sb[:], bmmr.t[:])
            pp.op("vector", "scalar_tensor_tensor", xn_sb[:, c * QT:(c + 1) * QT],
                  t1sl, n1w_col[:, c:c + 1], bs_.t[:], ALU.mult, ALU.mult, inc=ev.inc())
            pp.wait("sync", ev)
            pp.op("sync", "dma_start", out=t1T[c * 128:(c + 1) * 128, t0:t0 + QT],
                  in_=t1sl, inc=st_s.inc(16))
            pp.op("sync", "dma_start", out=xnT[c * 128:(c + 1) * 128, t0:t0 + QT],
                  in_=xn_sb[:, c * QT:(c + 1) * QT], inc=st_s.inc(16))
        banks.rel(brstd, ev); banks.rel(bmmr, ev); banks.rel(bs_, ev)
        pp.block_end(mk)

    for qt in range(NQT):
        mk = pp.block()
        t0 = qt * QT
        for c in range(NC):
            pp.op("sync", "dma_start", out=xn_sb[:, c * QT:(c + 1) * QT],
                  in_=xnT[c * 128:(c + 1) * 128, t0:t0 + QT], inc=ld_s.inc(16))
        pp.wait("tensor", ld_s); pp.wait("tensor", ld_g)
        for oc in range(4):
            bo = banks.get()
            pairs = [(wdkv_sb[:, kc * 512 + oc * 128: kc * 512 + (oc + 1) * 128],
                      xn_sb[:, kc * QT:(kc + 1) * QT]) for kc in range(NC)]
            mm_group(pp, bo.t[:], pairs, inc=mm.inc())
            pp.wait("scalar", mm)
            pp.op("scalar", "copy", lat_f[:, oc * QT:(oc + 1) * QT], bo.t[:], inc=ev.inc())
            banks.rel(bo, ev)
        pp.wait("scalar", ev)
        pp.wait("tensor", ev)
        src = [lat_f[:, c * QT:(c + 1) * QT] for c in range(4)]
        ln_stats_rows(pp, banks, sq_slots, sq_sb, ones_col, src, R, mm, ev, sqsem,
                      eps5[0:1, 0:1])
        brstd = banks.get(); bmmr = banks.get()
        pp.wait("tensor", ev)
        pp.op("tensor", "matmul", brstd.t[:], ones_row[:], R(3), start=True, stop=True)
        pp.op("tensor", "matmul", bmmr.t[:], ones_row[:], R(4), start=True, stop=True, inc=mm.inc())
        pp.wait("vector", mm)
        for c in range(4):
            pp.op("vector", "tensor_mul", tmp_sb[:], lat_f[:, c * QT:(c + 1) * QT], brstd.t[:])
            pp.op("vector", "tensor_sub", tmp_sb[:], tmp_sb[:], bmmr.t[:])
            pp.op("vector", "tensor_scalar", lat_b[:, c * QT:(c + 1) * QT], tmp_sb[:],
                  lng_col[:, c:c + 1], lnb_col[:, c:c + 1], ALU.mult, ALU.add,
                  inc=ev.inc())
        banks.rel(brstd, ev); banks.rel(bmmr, ev)
        pp.wait("tensor", ev)
        for oc in range(NC):
            bo = banks.get()
            pairs = [(wuk_sb[:, kc * D + oc * 128: kc * D + (oc + 1) * 128],
                      lat_b[:, kc * QT:(kc + 1) * QT]) for kc in range(4)]
            mm_group(pp, bo.t[:], pairs, inc=mm.inc())
            pp.wait("scalar", mm)
            pp.op("scalar", "copy", kw_sb[:, oc * QT:(oc + 1) * QT], bo.t[:], inc=ev.inc())
            banks.rel(bo, ev)
        pp.wait("tensor", ev)
        pp.wait("vector", ev)
        for oc in range(NC):
            rope_chunk(kw_sb[:, oc * QT:(oc + 1) * QT], kw_sb[:, oc * QT:(oc + 1) * QT],
                       t0, QT)
            pp.wait("sync", ev)
            pp.op("sync", "dma_start", out=kT[oc * 128:(oc + 1) * 128, t0:t0 + QT],
                  in_=kw_sb[:, oc * QT:(oc + 1) * QT], inc=st_s.inc(16))
        for i in range(4):
            tile = qt * 4 + i
            vs_ = vt_slots.get("vector")
            for g in range(2):
                bo = banks.get()
                for kc in range(4):
                    pp.op("tensor", "matmul", bo.t[:],
                          lat_b[:, kc * QT + i * 128: kc * QT + (i + 1) * 128],
                          wuv_sb[:, kc * D + g * 512: kc * D + (g + 1) * 512],
                          start=(kc == 0), stop=(kc == 3), inc=(mm.inc() if kc == 3 else None))
                pp.wait("vector", mm)
                base = vt_sb[:, vs_ * 1040 + 8 * g * 65: vs_ * 1040 + (8 * g + 8) * 65]
                pp.op("vector", "tensor_copy", base.rearrange("p (h j) -> p h j", h=8, j=65)[:, :, 0:64],
                      bo.t.ap().rearrange("p (h j) -> p h j", h=8, j=64), inc=ev.inc())
                banks.rel(bo, ev)
            pp.wait("sync", ev)
            pp.op("sync", "dma_start", out=vT[tile * 128:(tile + 1) * 128, :],
                  in_=vt_sb[:, vs_ * 1040:(vs_ + 1) * 1040], inc=st_s.inc(16))
            vt_slots.used(vs_, st_s)
        pp.block_end(mk)

    es2.close()

    es3 = ExitStack()
    sb3 = lambda name, shape, dt: es3.enter_context(nc.sbuf_tensor("B" + name, shape, dt))
    k_full = sb3("kfull", [128, NC * T], BF16)
    v_ring = sb3("vringB", [128, 4 * 1040], BF16)
    xn_sl = sb3("xnsl", [128, 8 * QT], BF16)
    t1_sl = sb3("t1sl", [128, 8 * QT], BF16)
    q_fm = sb3("qfm", [128, 8 * QT], BF16)
    qa_sb = sb3("qa", [128, QT], BF16)
    rope_t1 = sb3("ropet1b", [128, QT], F32)
    rope_t2 = sb3("ropet2b", [128, QT], F32)
    wq_sb = sb3("wqB", [128, 8 * D], BF16)
    wo_sb = sb3("woB", [128, 8 * D], BF16)
    ctxb_sb = sb3("ctxbB", [128, 8 * QT], BF16)
    exp_sb = sb3("expB", [128, 3 * QT], F32)
    attnT_sb = sb3("attnTB", [128, 4 * QT], BF16)
    dnr_sb = sb3("dnrB", [1, 2 * QT], F32)
    maskB_sb = sb3("maskBs", [128, 4 * QT], BF16)
    a_sb = sb3("aB", [128, 2 * QT], F32)

    exp_slots = Slots(pp, 3)
    at_slots = Slots(pp, 4)
    a_slots = Slots(pp, 2)
    vr_slots = LSlots(pp, 4, "vld")

    mk = pp.block()
    pp.op("sync", "dma_start", out=maskB_sb[:], in_=inp["maskB"], inc=ld_s.inc(16))
    for kc in range(NC):
        pp.op("gpsimd", "dma_start", out=wq_sb[:, kc * D:(kc + 1) * D],
              in_=inp["B_WqT"][kc * 128:(kc + 1) * 128, :], inc=ld_g.inc(16))
        pp.op("gpsimd", "dma_start", out=wo_sb[:, kc * D:(kc + 1) * D],
              in_=inp["B_WoT"][kc * 128:(kc + 1) * 128, :], inc=ld_g.inc(16))
        pp.op("sync", "dma_start", out=k_full[:, kc * T:(kc + 1) * T],
              in_=kT[kc * 128:(kc + 1) * 128, :], inc=ld_s.inc(16))
    pp.block_end(mk)

    def kf(c, kb):
        return k_full[:, c * T + kb * 128: c * T + (kb + 1) * 128]

    HGROUPS = [list(range(0, 4)), list(range(4, 8)), list(range(8, 12)), list(range(12, 16))]

    for qt in range(NQT):
        mk = pp.block()
        t0 = qt * QT
        for c in range(NC):
            pp.op("sync", "dma_start", out=xn_sl[:, c * QT:(c + 1) * QT],
                  in_=xnT[c * 128:(c + 1) * 128, t0:t0 + QT], inc=ld_s.inc(16))
            pp.op("sync", "dma_start", out=t1_sl[:, c * QT:(c + 1) * QT],
                  in_=t1T[c * 128:(c + 1) * 128, t0:t0 + QT], inc=ld_s.inc(16))
        pp.wait("tensor", ld_s); pp.wait("tensor", ld_g)
        for oc in range(NC):
            bo = banks.get()
            pairs = [(wq_sb[:, kc * D + oc * 128: kc * D + (oc + 1) * 128],
                      xn_sl[:, kc * QT:(kc + 1) * QT]) for kc in range(NC)]
            mm_group(pp, bo.t[:], pairs, inc=mm.inc())
            pp.wait("scalar", mm)
            pp.op("scalar", "copy", qa_sb[:], bo.t[:], inc=ev.inc())
            banks.rel(bo, ev)
            pp.wait("tensor", ev)
            pp.wait("vector", ev)
            rope_chunk(q_fm[:, oc * QT:(oc + 1) * QT], qa_sb[:], t0, QT)
        if "Bq" in DBG and qt == 0:
            pp.wait("gpsimd", ev)
            for c in range(NC):
                pp.op("gpsimd", "dma_start", out=dram["dbgT"][c * 128:(c + 1) * 128, 0:QT],
                      in_=q_fm[:, c * QT:(c + 1) * QT], inc=pp.sem("st_g").inc(16))
        nkb = qt * 4 + 4
        pp.wait("tensor", ld_s); pp.wait("tensor", ld_g)
        for grp in HGROUPS:
            cb = {h: banks.get() for h in grp}
            first = {h: True for h in grp}
            pipe = Pipe(depth=3)
            vheld = []
            for kb in range(nkb):
                vs_ = vr_slots.get("gpsimd")
                pp.op("gpsimd", "dma_start", out=v_ring[:, vs_ * 1040:(vs_ + 1) * 1040],
                      in_=vT[kb * 128:(kb + 1) * 128, :], inc=vr_slots.linc(vs_))
                vr_slots.wait_loaded("tensor", vs_)
                diag = kb - qt * 4
                for h in grp:
                    c = h // 2; rowoff = (h % 2) * 64
                    bs = banks.get()
                    pp.op("tensor", "matmul", bs.t[:], kf(c, kb)[rowoff:rowoff + 64, :],
                          q_fm[:, c * QT:(c + 1) * QT][rowoff:rowoff + 64, :],
                          start=True, stop=True, inc=mm.inc())
                    mmv = mm.n

                    def consume(bs=bs, h=h, kb=kb, vs_=vs_, diag=diag, mmv=mmv,
                                first_h=first[h]):
                        pp.wait("scalar", mm, mmv)
                        as_ = at_slots.get("vector")
                        aslot = attnT_sb[:, as_ * QT:(as_ + 1) * QT]
                        es_ = exp_slots.get("scalar")
                        eslot = exp_sb[:, es_ * QT:(es_ + 1) * QT]
                        pp.op("scalar", "activation", eslot, bs.t[:], AF.Exp,
                              scale=0.125, bias=cshB[:, 0:1], inc=ev.inc())
                        banks.rel(bs, ev)
                        pp.wait("vector", ev)
                        if diag < 0:
                            pp.op("vector", "tensor_copy", aslot, eslot, inc=ev.inc())
                        else:
                            pp.op("vector", "scalar_tensor_tensor", aslot, eslot, 1e30,
                                  maskB_sb[:, diag * QT:(diag + 1) * QT], ALU.min, ALU.mult,
                                  inc=ev.inc())
                        exp_slots.used(es_, ev)
                        pp.wait("tensor", ev)
                        pp.op("tensor", "matmul", cb[h].t[0:65, :],
                              v_ring[:, vs_ * 1040 + h * 65: vs_ * 1040 + h * 65 + 65],
                              aslot, start=first_h, stop=(kb == nkb - 1), inc=mm.inc())
                        at_slots.used(as_, mm)
                    pipe.push(consume)
                    first[h] = False
                vheld.append(vs_)
                while len(vheld) > 2:
                    vr_slots.used(vheld.pop(0), mm)
            pipe.flush()
            for vs_ in vheld:
                vr_slots.used(vs_, mm)
            for h in grp:
                c = h // 2; rowoff = (h % 2) * 64; hh = h % 2
                pp.wait("scalar", mm)
                pp.op("scalar", "copy", ctxb_sb[rowoff:rowoff + 64, c * QT:(c + 1) * QT],
                      cb[h].t[0:64, :], inc=ev.inc())
                pp.wait("vector", mm)
                pp.op("vector", "tensor_scalar_add", dnr_sb[0:1, hh * QT:(hh + 1) * QT],
                      cb[h].t[64:65, :], 1e-30)
                pp.op("vector", "reciprocal", dnr_sb[0:1, hh * QT:(hh + 1) * QT],
                      dnr_sb[0:1, hh * QT:(hh + 1) * QT], inc=ev.inc())
                banks.rel(cb[h], ev)
                if hh == 1:
                    br = banks.get()
                    pp.wait("tensor", ev)
                    pp.op("tensor", "matmul", br.t[0:64, :], ones_row[0:1, 0:64],
                          dnr_sb[0:1, 0:QT], start=True, stop=True)
                    pp.op("tensor", "matmul", br.t[64:128, :], ones_row[0:1, 0:64],
                          dnr_sb[0:1, QT:2 * QT], start=True, stop=True, inc=mm.inc())
                    pp.wait("vector", mm)
                    pp.op("vector", "tensor_mul", ctxb_sb[:, c * QT:(c + 1) * QT],
                          ctxb_sb[:, c * QT:(c + 1) * QT], br.t[:], inc=ev.inc())
                    banks.rel(br, ev)
        if "Bctx" in DBG and qt == 0:
            pp.wait("gpsimd", ev)
            for c in range(NC):
                pp.op("gpsimd", "dma_start", out=dram["dbgT"][D + c * 128:D + (c + 1) * 128, 0:QT],
                      in_=ctxb_sb[:, c * QT:(c + 1) * QT], inc=pp.sem("st_g").inc(16))
        pp.wait("tensor", ev)
        for oc in range(NC):
            bo = banks.get()
            pairs = [(wo_sb[:, kc * D + oc * 128: kc * D + (oc + 1) * 128],
                      ctxb_sb[:, kc * QT:(kc + 1) * QT]) for kc in range(NC)]
            mm_group(pp, bo.t[:], pairs, inc=mm.inc())
            pp.wait("vector", mm)
            sa = a_slots.get("vector")
            asl = a_sb[:, sa * QT:(sa + 1) * QT]
            pp.op("vector", "tensor_add", asl, bo.t[:], t1_sl[:, oc * QT:(oc + 1) * QT],
                  inc=ev.inc())
            banks.rel(bo, ev)
            pp.wait("sync", ev)
            pp.op("sync", "dma_start", out=aT[oc * 128:(oc + 1) * 128, t0:t0 + QT],
                  in_=asl, inc=st_s.inc(16))
            a_slots.used(sa, st_s)
        pp.block_end(mk)

    es3.close()
    pp.wait("sync", st_s); pp.wait("gpsimd", st_g)


def emit_phaseC(pp, ctx):
    nc = ctx["nc"]; inp = ctx["inp"]; dram = ctx["dram"]
    es = pp.es
    sb = lambda name, shape, dt: es.enter_context(nc.sbuf_tensor("C" + name, shape, dt))

    banks = BankPool(pp, [es.enter_context(nc.psum_tensor(f"Cbk{i}", [128, 512], F32))
                          for i in range(8)])
    ld_s = pp.sem("ld_s"); ld_g = pp.sem("ld_g")
    st_s = pp.sem("st_s"); st_g = pp.sem("st_g")
    mm = pp.sem("mm"); ev = EvFacade(pp)
    sqsem = pp.sem("sq")

    aT = dram["aT"]; x1T = dram["x1T"]; mT = dram["mT"]
    accT = dram["accT"]; mrT = dram["mrT"]; x2T = dram["x2T"]

    ones_col = sb("ones_col", [128, 1], F32)
    ones_row = sb("ones_row", [1, 128], F32)
    eps5 = sb("eps5", [1, 1], F32)
    eps6 = sb("eps6", [1, 1], F32)
    n2w_col = sb("n2w", [128, 8], F32)
    gwT_sb = sb("gwT", [128, 8 * 4], F32)
    gb_col = sb("gb", [4, 1], F32)

    mk = pp.block()
    pp.op("vector", "memset", ones_col[:], 1.0)
    pp.op("vector", "memset", ones_row[:], 1.0)
    pp.op("vector", "memset", eps5[:], 1e-5)
    pp.op("vector", "memset", eps6[:], 1e-6)
    pp.op("sync", "dma_start", out=n2w_col[:], in_=inp["C_n2w_cols"], inc=ld_s.inc(16))
    pp.op("sync", "dma_start", out=gb_col[:], in_=inp["C_gb_col"], inc=ld_s.inc(16))
    for kc in range(NC):
        pp.op("sync", "dma_start", out=gwT_sb[:, kc * 4:(kc + 1) * 4],
              in_=inp["C_gwT"][kc * 128:(kc + 1) * 128, :], inc=ld_s.inc(16))
    pp.block_end(mk)

    es2 = ExitStack()
    sb2 = lambda name, shape, dt: es2.enter_context(nc.sbuf_tensor("C0" + name, shape, dt))
    a_sb = sb2("a", [128, 8 * QT], F32)
    sq_sb = sb2("sq", [128, 2 * QT], F32)
    za_sb = sb2("za", [128, 8 * QT], F32)
    m_sb = sb2("m", [128, 8 * QT], BF16)
    rows = sb2("rows", [1, 8 * QT], F32)
    lg_sb = sb2("lg", [4, 2 * QT], F32)
    mrow_sb = sb2("mrow", [1, 8 * QT], F32)
    sq_slots = Slots(pp, 2)
    R = lambda r: rows[0:1, r * QT:(r + 1) * QT]

    for qt in range(NQT):
        mk = pp.block()
        t0 = qt * QT
        for c in range(NC):
            pp.op("sync", "dma_start", out=a_sb[:, c * QT:(c + 1) * QT],
                  in_=aT[c * 128:(c + 1) * 128, t0:t0 + QT], inc=ld_s.inc(16))
        pp.wait("scalar", ld_s); pp.wait("scalar", ld_g)
        pp.wait("tensor", ld_s); pp.wait("tensor", ld_g)
        bsq = banks.get()
        for c in range(NC):
            s_ = sq_slots.get("scalar")
            ssl = sq_sb[:, s_ * QT:(s_ + 1) * QT]
            pp.op("scalar", "activation", ssl, a_sb[:, c * QT:(c + 1) * QT], AF.Square,
                  inc=sqsem.inc())
            pp.wait("tensor", sqsem)
            pp.op("tensor", "matmul", bsq.t[0:1, :], ones_col[:], ssl,
                  start=(c == 0), stop=(c == NC - 1), inc=mm.inc())
            sq_slots.used(s_, mm)
        pp.wait("vector", mm)
        pp.op("vector", "tensor_scalar_mul", R(0), bsq.t[0:1, :], 1.0 / D, inc=ev.inc())
        banks.rel(bsq, ev)
        pp.wait("scalar", ev)
        pp.op("scalar", "activation", R(1), R(0), AF.Ln, bias=eps6[0:1, 0:1])
        pp.op("scalar", "activation", R(1), R(1), AF.Exp, scale=-0.5, inc=ev.inc())
        bs_ = banks.get(); bs4 = banks.get()
        pp.wait("tensor", ev)
        pp.op("tensor", "matmul", bs_.t[:], ones_row[:], R(1), start=True, stop=True)
        pp.op("tensor", "matmul", bs4.t[0:4, :], ones_row[0:1, 0:4], R(1),
              start=True, stop=True, inc=mm.inc())
        pp.wait("vector", mm)
        for c in range(NC):
            pp.op("vector", "tensor_scalar_mul", za_sb[:, c * QT:(c + 1) * QT],
                  a_sb[:, c * QT:(c + 1) * QT], n2w_col[:, c:c + 1])
            pp.op("vector", "tensor_mul", m_sb[:, c * QT:(c + 1) * QT],
                  za_sb[:, c * QT:(c + 1) * QT], bs_.t[:], inc=ev.inc())
            pp.wait("sync", ev)
            pp.op("sync", "dma_start", out=mT[c * 128:(c + 1) * 128, t0:t0 + QT],
                  in_=m_sb[:, c * QT:(c + 1) * QT], inc=st_s.inc(16))
        banks.rel(bs_, ev)
        bl = banks.get()
        pp.wait("tensor", ev)
        pairs = [(gwT_sb[:, kc * 4:(kc + 1) * 4], za_sb[:, kc * QT:(kc + 1) * QT])
                 for kc in range(NC)]
        mm_group(pp, bl.t[0:4, :], pairs, inc=mm.inc())
        pp.wait("scalar", mm)
        pp.op("scalar", "copy", lg_sb[0:4, 0:QT], bl.t[0:4, :], inc=ev.inc())
        banks.rel(bl, ev)
        pp.wait("vector", ev)
        pp.op("vector", "tensor_mul", lg_sb[0:4, QT:2 * QT], lg_sb[0:4, 0:QT], bs4.t[0:4, :])
        pp.op("vector", "tensor_scalar_add", lg_sb[0:4, QT:2 * QT],
              lg_sb[0:4, QT:2 * QT], gb_col[0:4, 0:1], inc=ev.inc())
        banks.rel(bs4, ev)
        pp.wait("gpsimd", ev)
        for e in range(4):
            pp.op("gpsimd", "dma_start", out=mrow_sb[0:1, e * QT:(e + 1) * QT],
                  in_=lg_sb[e:e + 1, QT:2 * QT], inc=ld_g.inc(16))
        pp.wait("vector", ld_s); pp.wait("vector", ld_g)
        MX = mrow_sb[0:1, 4 * QT:5 * QT]
        pp.op("vector", "tensor_max", MX, mrow_sb[0:1, 0:QT], mrow_sb[0:1, QT:2 * QT])
        pp.op("vector", "tensor_max", mrow_sb[0:1, 5 * QT:6 * QT],
              mrow_sb[0:1, 2 * QT:3 * QT], mrow_sb[0:1, 3 * QT:4 * QT])
        pp.op("vector", "tensor_max", MX, MX, mrow_sb[0:1, 5 * QT:6 * QT])
        for e in range(4):
            pp.op("vector", "tensor_tensor", mrow_sb[0:1, e * QT:(e + 1) * QT],
                  mrow_sb[0:1, e * QT:(e + 1) * QT], MX, ALU.is_ge,
                  inc=(ev.inc() if e == 3 else None))
        pp.wait("sync", ev)
        for e in range(4):
            pp.op("sync", "dma_start", out=mrT[e:e + 1, t0:t0 + QT],
                  in_=mrow_sb[0:1, e * QT:(e + 1) * QT], inc=st_s.inc(16))
        pp.block_end(mk)
    es2.close()

    es3 = ExitStack()
    sb3 = lambda name, shape, dt: es3.enter_context(nc.sbuf_tensor("C1" + name, shape, dt))
    w1_sb = sb3("w1", [128, NC * 4096], BF16)
    w2_sb = sb3("w2", [128, 32 * D], BF16)
    m_sl = sb3("msl", [128, 8 * QT], BF16)
    h_sb = sb3("h", [128, 32 * QT], BF16)
    mrow2 = sb3("mrow2", [1, 2 * QT], F32)
    maskb_sb = sb3("maskb", [128, 2 * QT], F32)
    y_sb = sb3("y", [128, 2 * QT], F32)
    b1_col = sb3("b1c", [128, 4 * 32], F32)
    b2_col = sb3("b2c", [128, 4 * 8], F32)
    y_slots = Slots(pp, 2)
    kva_war = [None, None]
    kvald = [pp.sem("kvald0"), pp.sem("kvald1")]
    mr_slots = Slots(pp, 2)
    acc_war = {}

    mk = pp.block()
    pp.op("sync", "dma_start", out=b1_col[:], in_=inp["C_b1_cols"], inc=ld_s.inc(16))
    pp.op("sync", "dma_start", out=b2_col[:], in_=inp["C_b2_cols"], inc=ld_s.inc(16))
    pp.block_end(mk)

    for e in range(4):
        mk = pp.block()
        for kc in range(NC):
            pp.op("gpsimd", "dma_start", out=w1_sb[:, kc * 4096:(kc + 1) * 4096],
                  in_=inp["C_W1T"][e * D + kc * 128: e * D + (kc + 1) * 128, :],
                  inc=ld_g.inc(16))
        for kc in range(32):
            pp.op("gpsimd", "dma_start", out=w2_sb[:, kc * D:(kc + 1) * D],
                  in_=inp["C_W2T"][e * 4096 + kc * 128: e * 4096 + (kc + 1) * 128, :],
                  inc=ld_g.inc(16))
        pp.block_end(mk)
        for qt in range(NQT):
            mk = pp.block()
            t0 = qt * QT
            for c in range(NC):
                pp.op("sync", "dma_start", out=m_sl[:, c * QT:(c + 1) * QT],
                      in_=mT[c * 128:(c + 1) * 128, t0:t0 + QT], inc=ld_s.inc(16))
            mr_ = mr_slots.get("sync")
            pp.op("sync", "dma_start", out=mrow2[0:1, mr_ * QT:(mr_ + 1) * QT],
                  in_=mrT[e:e + 1, t0:t0 + QT], inc=ld_s.inc(16))
            pp.wait("tensor", ld_s); pp.wait("tensor", ld_g)
            bmk = banks.get()
            pp.op("tensor", "matmul", bmk.t[:], ones_row[:],
                  mrow2[0:1, mr_ * QT:(mr_ + 1) * QT], start=True, stop=True, inc=mm.inc())
            pp.wait("vector", mm)
            mb = maskb_sb[:, (qt % 2) * QT:(qt % 2 + 1) * QT]
            pp.op("vector", "tensor_copy", mb, bmk.t[:], inc=ev.inc())
            mr_slots.used(mr_, ev)
            banks.rel(bmk, ev)
            for oc in range(32):
                bo = banks.get()
                pairs = [(w1_sb[:, kc * 4096 + oc * 128: kc * 4096 + (oc + 1) * 128],
                          m_sl[:, kc * QT:(kc + 1) * QT]) for kc in range(NC)]
                mm_group(pp, bo.t[:], pairs, inc=mm.inc())
                pp.wait("scalar", mm)
                pp.op("scalar", "activation", h_sb[:, oc * QT:(oc + 1) * QT], bo.t[:],
                      AF.Relu, bias=b1_col[:, e * 32 + oc: e * 32 + oc + 1], inc=ev.inc())
                banks.rel(bo, ev)
            pp.wait("tensor", ev)
            for oc in range(NC):
                bo = banks.get()
                pairs = [(w2_sb[:, kc * D + oc * 128: kc * D + (oc + 1) * 128],
                          h_sb[:, kc * QT:(kc + 1) * QT]) for kc in range(32)]
                mm_group(pp, bo.t[:], pairs, inc=mm.inc())
                pp.wait("vector", mm)
                ys = y_slots.get("vector")
                ysl = y_sb[:, ys * QT:(ys + 1) * QT]
                pp.op("vector", "scalar_tensor_tensor", ysl, bo.t[:],
                      b2_col[:, e * 8 + oc: e * 8 + oc + 1], mb, ALU.add, ALU.mult,
                      inc=ev.inc())
                banks.rel(bo, ev)
                pp.wait("gpsimd", ev)
                key = (qt, oc)
                if key in acc_war:
                    pp.wait("gpsimd", acc_war[key][0], acc_war[key][1])
                if e == 0:
                    pp.op("gpsimd", "dma_start", out=accT[oc * 128:(oc + 1) * 128, t0:t0 + QT],
                          in_=ysl, inc=st_g.inc(16))
                else:
                    pp.op("gpsimd", "dma_start", out=accT[oc * 128:(oc + 1) * 128, t0:t0 + QT],
                          in_=ysl, accum_op=ALU.add, inc=st_g.inc(16))
                acc_war[key] = (st_g, st_g.n)
                y_slots.used(ys, st_s)
            pp.block_end(mk)
    es3.close()

    es4 = ExitStack()
    sb4 = lambda name, shape, dt: es4.enter_context(nc.sbuf_tensor("C2" + name, shape, dt))
    b_sb = sb4("b", [128, 8 * QT], F32)
    acc_sb = sb4("acc", [128, 8 * QT], F32)
    sq_sb2 = sb4("sq", [128, 2 * QT], F32)
    tmp_sb = sb4("tmp", [128, QT], F32)
    rows2 = sb4("rows", [1, 5 * QT], F32)
    x2_sb = sb4("x2", [128, 2 * QT], F32)
    sq2_slots = Slots(pp, 2)
    acc_slots = Slots(pp, 2)
    x2_slots = Slots(pp, 2)
    R2 = lambda r: rows2[0:1, r * QT:(r + 1) * QT]

    for qt in range(NQT):
        mk = pp.block()
        t0 = qt * QT
        for c in range(NC):
            pp.op("sync", "dma_start", out=acc_sb[:, c * QT:(c + 1) * QT],
                  in_=accT[c * 128:(c + 1) * 128, t0:t0 + QT], inc=ld_s.inc(16))
            pp.op("sync", "dma_start", out=b_sb[:, c * QT:(c + 1) * QT],
                  in_=aT[c * 128:(c + 1) * 128, t0:t0 + QT], inc=ld_s.inc(16))
        pp.wait("vector", ld_s); pp.wait("vector", ld_g)
        for c in range(NC):
            pp.op("vector", "tensor_add", b_sb[:, c * QT:(c + 1) * QT],
                  b_sb[:, c * QT:(c + 1) * QT], acc_sb[:, c * QT:(c + 1) * QT],
                  inc=ev.inc())
        pp.wait("scalar", ev)
        pp.wait("tensor", ev)
        src = [b_sb[:, c * QT:(c + 1) * QT] for c in range(NC)]
        ln_stats_rows(pp, banks, sq2_slots, sq_sb2, ones_col, src, R2, mm, ev, sqsem,
                      eps5[0:1, 0:1])
        brstd = banks.get(); bmmr = banks.get()
        pp.wait("tensor", ev)
        pp.op("tensor", "matmul", brstd.t[:], ones_row[:], R2(3), start=True, stop=True)
        pp.op("tensor", "matmul", bmmr.t[:], ones_row[:], R2(4), start=True, stop=True, inc=mm.inc())
        pp.wait("sync", ev)
        for c in range(NC):
            pp.op("sync", "dma_start", out=acc_sb[:, c * QT:(c + 1) * QT],
                  in_=x1T[c * 128:(c + 1) * 128, t0:t0 + QT], inc=ld_s.inc(16))
        pp.wait("vector", mm)
        pp.wait("vector", ld_s); pp.wait("vector", ld_g)
        for c in range(NC):
            s2_ = x2_slots.get("vector")
            x2sl = x2_sb[:, s2_ * QT:(s2_ + 1) * QT]
            pp.op("vector", "tensor_mul", tmp_sb[:], b_sb[:, c * QT:(c + 1) * QT], brstd.t[:])
            pp.op("vector", "tensor_sub", tmp_sb[:], tmp_sb[:], bmmr.t[:])
            pp.op("vector", "tensor_add", x2sl, tmp_sb[:], acc_sb[:, c * QT:(c + 1) * QT],
                  inc=ev.inc())
            pp.wait("sync", ev)
            pp.op("sync", "dma_start", out=x2T[c * 128:(c + 1) * 128, t0:t0 + QT],
                  in_=x2sl, inc=st_s.inc(16))
            x2_slots.used(s2_, st_s)
        banks.rel(brstd, ev); banks.rel(bmmr, ev)
        pp.block_end(mk)
    es4.close()
    pp.wait("sync", st_s); pp.wait("gpsimd", st_g)


def emit_phaseD(pp, ctx):
    nc = ctx["nc"]; inp = ctx["inp"]; dram = ctx["dram"]
    es = pp.es
    sb = lambda name, shape, dt: es.enter_context(nc.sbuf_tensor("D" + name, shape, dt))

    banks = BankPool(pp, [es.enter_context(nc.psum_tensor(f"Dbk{i}", [128, 512], F32))
                          for i in range(8)])
    ld_s = pp.sem("ld_s"); ld_g = pp.sem("ld_g")
    st_s = pp.sem("st_s"); st_g = pp.sem("st_g")
    mm = pp.sem("mm"); ev = EvFacade(pp)
    sqsem = pp.sem("sq")
    ccK = pp.sem("ccK"); ccV = pp.sem("ccV"); ccA = pp.sem("ccA")

    x2T = dram["x2T"]; t2T = dram["t2T"]; xn2T = dram["xn2T"]; qTd = dram["qTd"]
    kpub = dram["kpub"]; vpub = dram["vpub"]; apub = dram["apub"]
    kG = dram["kG"]; vG = dram["vG"]; aG = dram["aG"]
    yT = dram["yT"]; hT = dram["hT"]; x3T = dram["x3T"]

    ones_col = sb("ones_col", [128, 1], F32)
    ones_row = sb("ones_row", [1, 128], F32)
    eps5 = sb("eps5", [128, 1], F32)
    eps6r = sb("eps6r", [128, 1], F32)
    s1_col = sb("s1c", [128, 8], F32)
    s2_col = sb("s2c", [128, 8], F32)
    bd2 = sb("bd2", [128, 2], BF16)
    ex2 = sb("ex2", [2, 128], BF16)
    ex2f = sb("ex2f", [2, 128], F32)
    smask = sb("smask", [128, 8], F32)
    imask = sb("imask", [128, 8], F32)
    wm2 = sb("wm2", [2, 8], F32)

    mk = pp.block()
    pp.op("vector", "memset", ones_col[:], 1.0)
    pp.op("vector", "memset", ones_row[:], 1.0)
    pp.op("vector", "memset", eps5[:], 1e-5)
    pp.op("vector", "memset", eps6r[:], 1e-6)
    pp.op("sync", "dma_start", out=s1_col[:], in_=inp["D_s1_cols"], inc=ld_s.inc(16))
    pp.op("sync", "dma_start", out=s2_col[:], in_=inp["D_s2_cols"], inc=ld_s.inc(16))
    pp.op("gpsimd", "dma_start", out=bd2[:], in_=inp["D_bd2"], inc=ld_g.inc(16))
    pp.op("gpsimd", "dma_start", out=ex2[:], in_=inp["D_ex2"], inc=ld_g.inc(16))
    pp.op("gpsimd", "dma_start", out=ex2f[:], in_=inp["D_ex2f"], inc=ld_g.inc(16))
    pp.op("gpsimd", "dma_start", out=smask[:], in_=inp["D_smask"], inc=ld_g.inc(16))
    pp.op("gpsimd", "dma_start", out=imask[:], in_=inp["D_imask"], inc=ld_g.inc(16))
    pp.op("sync", "dma_start", out=wm2[:], in_=inp["D_wm2"], inc=ld_s.inc(16))
    pp.block_end(mk)

    es2 = ExitStack()
    sb2 = lambda name, shape, dt: es2.enter_context(nc.sbuf_tensor("D1" + name, shape, dt))
    x_sb = sb2("x", [128, 8 * QT], F32)
    sq_sb = sb2("sq", [128, 2 * QT], F32)
    tmp_sb = sb2("tmp", [128, QT], F32)
    rows = sb2("rows", [1, 8 * QT], F32)
    t2_sb = sb2("t2", [128, 8 * QT], BF16)
    xn_sb = sb2("xn", [128, 8 * QT], BF16)
    sq_slots = Slots(pp, 2)
    R = lambda r: rows[0:1, r * QT:(r + 1) * QT]

    for qt in range(NQT):
        mk = pp.block()
        t0 = qt * QT
        for c in range(NC):
            pp.op("sync", "dma_start", out=x_sb[:, c * QT:(c + 1) * QT],
                  in_=x2T[c * 128:(c + 1) * 128, t0:t0 + QT], inc=ld_s.inc(16))
        pp.wait("scalar", ld_s); pp.wait("scalar", ld_g)
        pp.wait("tensor", ld_s); pp.wait("tensor", ld_g)
        src = [x_sb[:, c * QT:(c + 1) * QT] for c in range(NC)]
        ln_stats_rows(pp, banks, sq_slots, sq_sb, ones_col, src, R, mm, ev, sqsem,
                      eps5[0:1, 0:1])
        pp.op("vector", "tensor_single_scalar", R(6), R(2), 1e-5, ALU.add)
        pp.op("vector", "reciprocal", R(6), R(6))
        pp.op("vector", "tensor_mul", R(5), R(2), R(6), inc=ev.inc())
        pp.wait("scalar", ev)
        pp.op("scalar", "activation", R(5), R(5), AF.Ln)
        pp.op("scalar", "activation", R(5), R(5), AF.Exp, scale=0.5, inc=ev.inc())
        pp.wait("vector", ev)
        pp.op("vector", "tensor_scalar_add", R(5), R(5), 1e-6)
        pp.op("vector", "reciprocal", R(5), R(5), inc=ev.inc())
        brstd = banks.get(); bmmr = banks.get(); bs_ = banks.get()
        pp.wait("tensor", ev)
        pp.op("tensor", "matmul", brstd.t[:], ones_row[:], R(3), start=True, stop=True)
        pp.op("tensor", "matmul", bmmr.t[:], ones_row[:], R(4), start=True, stop=True)
        pp.op("tensor", "matmul", bs_.t[:], ones_row[:], R(5), start=True, stop=True, inc=mm.inc())
        pp.wait("vector", mm)
        for c in range(NC):
            t2sl = t2_sb[:, c * QT:(c + 1) * QT]
            pp.op("vector", "tensor_mul", tmp_sb[:], x_sb[:, c * QT:(c + 1) * QT], brstd.t[:])
            pp.op("vector", "tensor_sub", t2sl, tmp_sb[:], bmmr.t[:])
            pp.op("vector", "scalar_tensor_tensor", xn_sb[:, c * QT:(c + 1) * QT],
                  t2sl, s1_col[:, c:c + 1], bs_.t[:], ALU.mult, ALU.mult, inc=ev.inc())
            pp.wait("sync", ev)
            pp.op("sync", "dma_start", out=t2T[c * 128:(c + 1) * 128, t0:t0 + QT],
                  in_=t2sl, inc=st_s.inc(16))
            pp.op("sync", "dma_start", out=xn2T[c * 128:(c + 1) * 128, t0:t0 + QT],
                  in_=xn_sb[:, c * QT:(c + 1) * QT], inc=st_s.inc(16))
        banks.rel(brstd, ev); banks.rel(bmmr, ev); banks.rel(bs_, ev)
        pp.block_end(mk)
    es2.close()

    es3 = ExitStack()
    sb3 = lambda name, shape, dt: es3.enter_context(nc.sbuf_tensor("D2" + name, shape, dt))
    xn_sl = sb3("xn", [128, 8 * QT], BF16)
    out_bf = sb3("obf", [128, 2 * QT], BF16)
    gh_sb = sb3("gh", [16, QT], BF16)
    al_sb = sb3("al", [128, 2 * QT], F16)
    wq_sb = sb3("wq", [128, 8 * D], BF16)
    wk_sb = sb3("wk", [128, 8 * D], BF16)
    wv_sb = sb3("wv", [128, 8 * D], BF16)
    wg1_sb = sb3("wg1", [128, 8 * 16], BF16)
    wg2_sb = sb3("wg2", [16, 8 * 128], BF16)
    bg1_col = sb3("bg1", [16, 1], F32)
    bg2s_col = sb3("bg2s", [128, 8], F32)
    obf_slots = Slots(pp, 2)
    al_slots = Slots(pp, 2)

    mk = pp.block()
    pp.op("sync", "dma_start", out=bg1_col[:], in_=inp["D_bg1_col"], inc=ld_s.inc(16))
    pp.op("sync", "dma_start", out=bg2s_col[:], in_=inp["D_bg2_cols"], inc=ld_s.inc(16))
    pp.wait("vector", ld_s); pp.wait("vector", ld_g)
    pp.op("vector", "tensor_scalar_mul", bg2s_col[:], bg2s_col[:], 1.0 / 16.0)
    pp.op("gpsimd", "dma_start", out=wg1_sb[:], in_=inp["D_wg1T"], inc=ld_g.inc(16))
    pp.op("gpsimd", "dma_start", out=wg2_sb[:], in_=inp["D_wg2T"], inc=ld_g.inc(16))
    for kc in range(NC):
        pp.op("gpsimd", "dma_start", out=wq_sb[:, kc * D:(kc + 1) * D],
              in_=inp["D_wqT"][kc * 128:(kc + 1) * 128, :], inc=ld_g.inc(16))
        pp.op("gpsimd", "dma_start", out=wk_sb[:, kc * D:(kc + 1) * D],
              in_=inp["D_wkT"][kc * 128:(kc + 1) * 128, :], inc=ld_g.inc(16))
        pp.op("gpsimd", "dma_start", out=wv_sb[:, kc * D:(kc + 1) * D],
              in_=inp["D_wvT"][kc * 128:(kc + 1) * 128, :], inc=ld_g.inc(16))
    pp.block_end(mk)

    for qt in range(NQT):
        mk = pp.block()
        t0 = qt * QT
        for c in range(NC):
            pp.op("sync", "dma_start", out=xn_sl[:, c * QT:(c + 1) * QT],
                  in_=xn2T[c * 128:(c + 1) * 128, t0:t0 + QT], inc=ld_s.inc(16))
        pp.wait("tensor", ld_s); pp.wait("tensor", ld_g)
        for wname, wsb, dstT in (("q", wq_sb, qTd), ("k", wk_sb, kpub), ("v", wv_sb, vpub)):
            for oc in range(NC):
                bo = banks.get()
                pairs = [(wsb[:, kc * D + oc * 128: kc * D + (oc + 1) * 128],
                          xn_sl[:, kc * QT:(kc + 1) * QT]) for kc in range(NC)]
                mm_group(pp, bo.t[:], pairs, inc=mm.inc())
                pp.wait("scalar", mm)
                os_ = obf_slots.get("scalar")
                osl = out_bf[:, os_ * QT:(os_ + 1) * QT]
                pp.op("scalar", "copy", osl, bo.t[:], inc=ev.inc())
                banks.rel(bo, ev)
                pp.wait("sync", ev)
                pp.op("sync", "dma_start", out=dstT[oc * 128:(oc + 1) * 128, t0:t0 + QT],
                      in_=osl, inc=st_s.inc(16))
                obf_slots.used(os_, st_s)
        bgh = banks.get()
        pairs = [(wg1_sb[:, kc * 16:(kc + 1) * 16], xn_sl[:, kc * QT:(kc + 1) * QT])
                 for kc in range(NC)]
        mm_group(pp, bgh.t[0:16, :], pairs, inc=mm.inc())
        pp.wait("scalar", mm)
        pp.op("scalar", "activation", gh_sb[:], bgh.t[0:16, :], AF.Silu,
              bias=bg1_col[0:16, 0:1], inc=ev.inc())
        banks.rel(bgh, ev)
        pp.wait("tensor", ev)
        for oc in range(NC):
            bo = banks.get()
            pp.op("tensor", "matmul", bo.t[:], wg2_sb[0:16, oc * 128:(oc + 1) * 128],
                  gh_sb[:], start=True, stop=True, inc=mm.inc())
            pp.wait("scalar", mm)
            as_ = al_slots.get("scalar")
            asl = al_sb[:, as_ * QT:(as_ + 1) * QT]
            pp.op("scalar", "activation", asl, bo.t[:], AF.Sigmoid,
                  scale=1.0 / 16.0, bias=bg2s_col[:, oc:oc + 1], inc=ev.inc())
            banks.rel(bo, ev)
            pp.wait("sync", ev)
            pp.op("sync", "dma_start", out=apub[oc * 128:(oc + 1) * 128, t0:t0 + QT],
                  in_=asl, inc=st_s.inc(16))
            al_slots.used(as_, st_s)
        pp.block_end(mk)
    es3.close()

    mk = pp.block()
    pp.wait("gpsimd", st_s); pp.wait("gpsimd", st_g)
    pp.op("gpsimd", "collective_compute", "AllGather", ALU.bypass,
          replica_groups=[list(range(8))], ins=[kpub], outs=[kG], inc=ccK.inc())
    pp.op("gpsimd", "collective_compute", "AllGather", ALU.bypass,
          replica_groups=[list(range(8))], ins=[vpub], outs=[vG], inc=ccV.inc())
    pp.op("gpsimd", "collective_compute", "AllGather", ALU.bypass,
          replica_groups=[list(range(8))], ins=[apub], outs=[aG], inc=ccA.inc())
    pp.wait("gpsimd", ccK, 1)
    pp.wait("gpsimd", ccV, 1)
    pp.wait("gpsimd", ccA, 1)
    pp.op("gpsimd", "nop")
    pp.block_end(mk)

    es4 = ExitStack()
    sb4 = lambda name, shape, dt: es4.enter_context(nc.sbuf_tensor("D4" + name, shape, dt))
    q_sl = sb4("q", [128, 8 * QT], BF16)
    k_sl = sb4("k", [128, 2 * 8 * QT], BF16)
    v_sl = sb4("v", [128, 2 * 8 * QT], BF16)
    a_sl = sb4("a", [128, 2 * 8 * QT], BF16)
    xg_sl = sb4("xg", [128, 8 * QT], BF16)
    Rrun = sb4("R", [128, 8 * QT], BF16)
    prod = sb4("prod", [128, QT], BF16)
    o_acc = sb4("o", [128, 8 * QT], F32)
    w_sb = sb4("w", [2, 8 * QT], BF16)
    tmp4 = sb4("tmp4", [128, QT], F32)
    sq4 = sb4("sq4", [128, QT], F32)
    rows4 = sb4("rows4", [2, 8 * QT], F32)
    og_sb = sb4("og", [128, 8 * QT], BF16)
    sqb4 = sb4("sqb4", [128, QT], BF16)
    obf4 = sb4("obf4", [128, QT], BF16)
    g_sb = sb4("g", [128, 8 * QT], BF16)
    t2_sl = sb4("t2s", [128, 8 * QT], BF16)
    y_sb = sb4("y", [128, 2 * QT], F32)
    wog_sb = sb4("wog", [128, 8 * D], BF16)
    wo_sb = sb4("wo", [128, 8 * D], BF16)
    bog_col = sb4("bog", [128, 8], F32)
    y_slots = Slots(pp, 2)
    kva_war = [None, None]
    kvald = [pp.sem("kvald0"), pp.sem("kvald1")]

    mk = pp.block()
    pp.op("sync", "dma_start", out=bog_col[:], in_=inp["D_bog_cols"], inc=ld_s.inc(16))
    for kc in range(NC):
        pp.op("gpsimd", "dma_start", out=wog_sb[:, kc * D:(kc + 1) * D],
              in_=inp["D_wogT"][kc * 128:(kc + 1) * 128, :], inc=ld_g.inc(16))
        pp.op("gpsimd", "dma_start", out=wo_sb[:, kc * D:(kc + 1) * D],
              in_=inp["D_woT"][kc * 128:(kc + 1) * 128, :], inc=ld_g.inc(16))
    pp.block_end(mk)

    for qt in range(NQT):
        mk = pp.block()
        t0 = qt * QT
        pp.op("vector", "memset", o_acc[:], 0.0)
        pp.op("vector", "memset", Rrun[:], 1.0, inc=ev.inc())
        for c in range(NC):
            pp.op("sync", "dma_start", out=q_sl[:, c * QT:(c + 1) * QT],
                  in_=qTd[c * 128:(c + 1) * 128, t0:t0 + QT], inc=ld_s.inc(16))
        for s in range(7, -1, -1):
            ks_ = (7 - s) % 2
            koff = ks_ * 8 * QT
            ksl = lambda c, koff=koff: k_sl[:, koff + c * QT: koff + (c + 1) * QT]
            vsl = lambda c, koff=koff: v_sl[:, koff + c * QT: koff + (c + 1) * QT]
            asl = lambda c, koff=koff: a_sl[:, koff + c * QT: koff + (c + 1) * QT]
            if kva_war[ks_] is not None:
                for (ss, vv) in kva_war[ks_]:
                    pp.wait("sync", ss, vv)
                    pp.wait("gpsimd", ss, vv)
                kva_war[ks_] = None
            for c in range(NC):
                pp.op("sync", "dma_start", out=ksl(c),
                      in_=kG[s * D + c * 128: s * D + (c + 1) * 128, t0:t0 + QT],
                      inc=kvald[ks_].inc(16))
                pp.op("sync", "dma_start", out=vsl(c),
                      in_=vG[s * D + c * 128: s * D + (c + 1) * 128, t0:t0 + QT],
                      inc=kvald[ks_].inc(16))
                pp.op("gpsimd", "dma_start", out=asl(c),
                      in_=aG[s * D + c * 128: s * D + (c + 1) * 128, t0:t0 + QT],
                      inc=kvald[ks_].inc(16))
            pp.wait("vector", kvald[ks_])
            if s == 7:
                pp.wait("vector", ld_s); pp.wait("vector", ld_g)
            bw = banks.get()
            for c in range(NC):
                pp.op("vector", "tensor_scalar", asl(c),
                      asl(c), smask[:, s:s + 1], imask[:, s:s + 1],
                      ALU.mult, ALU.add)
                pp.op("vector", "tensor_mul", prod[:], q_sl[:, c * QT:(c + 1) * QT],
                      Rrun[:, c * QT:(c + 1) * QT])
                pp.op("vector", "tensor_mul", prod[:], prod[:], ksl(c),
                      inc=ev.inc())
                pp.wait("tensor", ev)
                pp.op("tensor", "matmul", bw.t[0:2, 0:QT], bd2[:], prod[:],
                      start=True, stop=True, inc=mm.inc())
                pp.wait("vector", mm)
                pp.op("vector", "tensor_scalar_mul", w_sb[0:2, c * QT:(c + 1) * QT],
                      bw.t[0:2, 0:QT], wm2[0:2, s:s + 1], inc=ev.inc())
                pp.op("vector", "tensor_mul", Rrun[:, c * QT:(c + 1) * QT],
                      Rrun[:, c * QT:(c + 1) * QT], asl(c))
            banks.rel(bw, ev)
            pp.wait("tensor", ev)
            for c in range(NC):
                be = banks.get()
                pp.op("tensor", "matmul", be.t[:], ex2[:, :],
                      w_sb[0:2, c * QT:(c + 1) * QT], start=True, stop=True, inc=mm.inc())
                pp.wait("vector", mm)
                pp.op("vector", "tensor_mul", tmp4[:], be.t[:], vsl(c))
                pp.op("vector", "tensor_add", o_acc[:, c * QT:(c + 1) * QT],
                      o_acc[:, c * QT:(c + 1) * QT], tmp4[:],
                      inc=(ev.inc() if c == NC - 1 else None))
                banks.rel(be, ev)
            kva_war[ks_] = ev.now_list()
        if "Do" in DBG:
            pp.wait("gpsimd", ev)
            for c in range(NC):
                pp.op("gpsimd", "dma_start", out=dram["dbgT"][c * 128:(c + 1) * 128, t0:t0 + QT],
                      in_=o_acc[:, c * QT:(c + 1) * QT], inc=st_g.inc(16))
        for c in range(NC):
            bsum = banks.get(); bsq = banks.get()
            osl = o_acc[:, c * QT:(c + 1) * QT]
            pp.wait("scalar", ev)
            pp.op("scalar", "activation", sq4[:], osl, AF.Square, inc=sqsem.inc())
            pp.wait("tensor", ev)
            obf = obf4[:]
            pp.op("vector", "tensor_copy", obf, osl, inc=ev.inc())
            pp.wait("tensor", sqsem)
            pp.wait("tensor", ev)
            pp.op("tensor", "matmul", bsum.t[0:2, 0:QT], bd2[:], obf, start=True, stop=True)
            sq4b = sqb4[:]
            pp.wait("vector", sqsem)
            pp.op("vector", "tensor_copy", sq4b, sq4[:], inc=ev.inc())
            pp.wait("tensor", ev)
            pp.op("tensor", "matmul", bsq.t[0:2, 0:QT], bd2[:], sq4b, start=True, stop=True,
                  inc=mm.inc())
            RW = lambda r: rows4[0:2, r * QT:(r + 1) * QT]
            pp.wait("vector", mm)
            pp.op("vector", "tensor_scalar_mul", RW(0), bsum.t[0:2, 0:QT], 1.0 / 64)
            pp.op("vector", "tensor_scalar_mul", RW(1), bsq.t[0:2, 0:QT], 1.0 / 64, inc=ev.inc())
            banks.rel(bsum, ev); banks.rel(bsq, ev)
            pp.op("vector", "tensor_mul", RW(2), RW(0), RW(0))
            pp.op("vector", "tensor_sub", RW(2), RW(1), RW(2), inc=ev.inc())
            pp.wait("scalar", ev)
            pp.op("scalar", "activation", RW(3), RW(2), AF.Ln, bias=eps5[0:2, 0:1])
            pp.op("scalar", "activation", RW(3), RW(3), AF.Exp, scale=-0.5, inc=ev.inc())
            pp.wait("vector", ev)
            pp.op("vector", "tensor_mul", RW(4), RW(0), RW(3), inc=ev.inc())
            bmu = banks.get()
            pp.wait("tensor", ev)
            pp.op("tensor", "matmul", bmu.t[0:128, 0:QT], ex2f[:, :],
                  rows4[0:2, 3 * QT:4 * QT], start=True, stop=True)
            bmm_ = banks.get()
            pp.op("tensor", "matmul", bmm_.t[0:128, 0:QT], ex2f[:, :],
                  rows4[0:2, 4 * QT:5 * QT], start=True, stop=True, inc=mm.inc())
            pp.wait("vector", mm)
            pp.op("vector", "tensor_mul", tmp4[:], osl, bmu.t[:])
            pp.op("vector", "tensor_sub", g_sb[:, c * QT:(c + 1) * QT], tmp4[:], bmm_.t[:],
                  inc=ev.inc())
            banks.rel(bmu, ev); banks.rel(bmm_, ev)
        if "Doln" in DBG:
            pp.wait("gpsimd", ev)
            for c in range(NC):
                pp.op("gpsimd", "dma_start", out=dram["dbgT"][c * 128:(c + 1) * 128, t0:t0 + QT],
                      in_=g_sb[:, c * QT:(c + 1) * QT], inc=st_g.inc(16))
        for c in range(NC):
            pp.op("sync", "dma_start", out=xg_sl[:, c * QT:(c + 1) * QT],
                  in_=xn2T[c * 128:(c + 1) * 128, t0:t0 + QT], inc=ld_s.inc(16))
            pp.op("sync", "dma_start", out=t2_sl[:, c * QT:(c + 1) * QT],
                  in_=t2T[c * 128:(c + 1) * 128, t0:t0 + QT], inc=ld_s.inc(16))
        pp.wait("tensor", ld_s); pp.wait("tensor", ld_g)
        for oc in range(NC):
            bo = banks.get()
            pairs = [(wog_sb[:, kc * D + oc * 128: kc * D + (oc + 1) * 128],
                      xg_sl[:, kc * QT:(kc + 1) * QT]) for kc in range(NC)]
            mm_group(pp, bo.t[:], pairs, inc=mm.inc())
            pp.wait("scalar", mm)
            pp.op("scalar", "activation", og_sb[:, oc * QT:(oc + 1) * QT], bo.t[:],
                  AF.Silu, bias=bog_col[:, oc:oc + 1], inc=ev.inc())
            banks.rel(bo, ev)
        pp.wait("vector", ev)
        for c in range(NC):
            pp.op("vector", "tensor_mul", g_sb[:, c * QT:(c + 1) * QT],
                  g_sb[:, c * QT:(c + 1) * QT], og_sb[:, c * QT:(c + 1) * QT],
                  inc=(ev.inc() if c == NC - 1 else None))
        if "Dog" in DBG:
            pp.wait("gpsimd", ev)
            for c in range(NC):
                pp.op("gpsimd", "dma_start", out=dram["dbgT"][D + c * 128:D + (c + 1) * 128, t0:t0 + QT],
                      in_=og_sb[:, c * QT:(c + 1) * QT], inc=st_g.inc(16))
        pp.wait("tensor", ev)
        for oc in range(NC):
            bo = banks.get()
            pairs = [(wo_sb[:, kc * D + oc * 128: kc * D + (oc + 1) * 128],
                      g_sb[:, kc * QT:(kc + 1) * QT]) for kc in range(NC)]
            mm_group(pp, bo.t[:], pairs, inc=mm.inc())
            pp.wait("vector", mm)
            ys = y_slots.get("vector")
            ysl = y_sb[:, ys * QT:(ys + 1) * QT]
            pp.op("vector", "tensor_add", ysl, bo.t[:], t2_sl[:, oc * QT:(oc + 1) * QT],
                  inc=ev.inc())
            banks.rel(bo, ev)
            pp.wait("sync", ev)
            pp.op("sync", "dma_start", out=yT[oc * 128:(oc + 1) * 128, t0:t0 + QT],
                  in_=ysl, inc=st_s.inc(16))
            y_slots.used(ys, st_s)
        pp.block_end(mk)
    es4.close()

    es5 = ExitStack()
    sb5 = lambda name, shape, dt: es5.enter_context(nc.sbuf_tensor("D5" + name, shape, dt))
    y_sl = sb5("y", [128, 8 * QT], F32)
    sq5 = sb5("sq", [128, 2 * QT], F32)
    rows5 = sb5("rows", [1, 4 * QT], F32)
    nrm_sb = sb5("nrm", [128, 8 * QT], BF16)
    hg_sb = sb5("hg", [128, QT], BF16)
    h_sb = sb5("h", [128, 2 * QT], BF16)
    wgate_sb = sb5("wgate", [128, 8 * 4096], BF16)
    wup_sb = sb5("wup", [128, 8 * 4096], BF16)
    sq5_slots = Slots(pp, 2)
    h_slots = Slots(pp, 2)
    R5 = lambda r: rows5[0:1, r * QT:(r + 1) * QT]

    mk = pp.block()
    for kc in range(NC):
        pp.op("gpsimd", "dma_start", out=wgate_sb[:, kc * 4096:(kc + 1) * 4096],
              in_=inp["D_wgateT"][kc * 128:(kc + 1) * 128, :], inc=ld_g.inc(16))
        pp.op("gpsimd", "dma_start", out=wup_sb[:, kc * 4096:(kc + 1) * 4096],
              in_=inp["D_wupT"][kc * 128:(kc + 1) * 128, :], inc=ld_g.inc(16))
    pp.block_end(mk)

    for qt in range(NQT):
        mk = pp.block()
        t0 = qt * QT
        for c in range(NC):
            pp.op("sync", "dma_start", out=y_sl[:, c * QT:(c + 1) * QT],
                  in_=yT[c * 128:(c + 1) * 128, t0:t0 + QT], inc=ld_s.inc(16))
        pp.wait("scalar", ld_s); pp.wait("scalar", ld_g)
        pp.wait("tensor", ld_s); pp.wait("tensor", ld_g)
        bsq = banks.get()
        for c in range(NC):
            s_ = sq5_slots.get("scalar")
            ssl = sq5[:, s_ * QT:(s_ + 1) * QT]
            pp.op("scalar", "activation", ssl, y_sl[:, c * QT:(c + 1) * QT], AF.Square,
                  inc=sqsem.inc())
            pp.wait("tensor", sqsem)
            pp.op("tensor", "matmul", bsq.t[0:1, :], ones_col[:], ssl,
                  start=(c == 0), stop=(c == NC - 1), inc=mm.inc())
            sq5_slots.used(s_, mm)
        pp.wait("vector", mm)
        pp.op("vector", "tensor_scalar_mul", R5(0), bsq.t[0:1, :], 1.0 / D, inc=ev.inc())
        banks.rel(bsq, ev)
        pp.wait("scalar", ev)
        pp.op("scalar", "activation", R5(1), R5(0), AF.Ln)
        pp.op("scalar", "activation", R5(1), R5(1), AF.Exp, scale=0.5, inc=ev.inc())
        pp.wait("vector", ev)
        pp.op("vector", "tensor_scalar_add", R5(1), R5(1), 1e-6)
        pp.op("vector", "reciprocal", R5(1), R5(1), inc=ev.inc())
        bs_ = banks.get()
        pp.wait("tensor", ev)
        pp.op("tensor", "matmul", bs_.t[:], ones_row[:], R5(1), start=True, stop=True, inc=mm.inc())
        pp.wait("vector", mm)
        for c in range(NC):
            pp.op("vector", "scalar_tensor_tensor", nrm_sb[:, c * QT:(c + 1) * QT],
                  y_sl[:, c * QT:(c + 1) * QT], s2_col[:, c:c + 1], bs_.t[:],
                  ALU.mult, ALU.mult, inc=(ev.inc() if c == NC - 1 else None))
        banks.rel(bs_, ev)
        pp.wait("tensor", ev)
        for oc in range(32):
            bg = banks.get()
            pairs = [(wgate_sb[:, kc * 4096 + oc * 128: kc * 4096 + (oc + 1) * 128],
                      nrm_sb[:, kc * QT:(kc + 1) * QT]) for kc in range(NC)]
            mm_group(pp, bg.t[:], pairs, inc=mm.inc())
            pp.wait("scalar", mm)
            pp.op("scalar", "activation", hg_sb[:], bg.t[:], AF.Silu, inc=ev.inc())
            banks.rel(bg, ev)
            bu = banks.get()
            pairs = [(wup_sb[:, kc * 4096 + oc * 128: kc * 4096 + (oc + 1) * 128],
                      nrm_sb[:, kc * QT:(kc + 1) * QT]) for kc in range(NC)]
            mm_group(pp, bu.t[:], pairs, inc=mm.inc())
            hs = h_slots.get("vector")
            hsl = h_sb[:, hs * QT:(hs + 1) * QT]
            pp.wait("vector", mm)
            pp.wait("vector", ev)
            pp.op("vector", "tensor_mul", hsl, hg_sb[:], bu.t[:], inc=ev.inc())
            banks.rel(bu, ev)
            pp.wait("sync", ev)
            pp.op("sync", "dma_start", out=hT[oc * 128:(oc + 1) * 128, t0:t0 + QT],
                  in_=hsl, inc=st_s.inc(16))
            h_slots.used(hs, st_s)
        pp.block_end(mk)
    es5.close()

    es6 = ExitStack()
    sb6 = lambda name, shape, dt: es6.enter_context(nc.sbuf_tensor("D6" + name, shape, dt))
    h_sl = sb6("h", [128, 32 * QT], BF16)
    x2r6 = sb6("x2r", [128, 8 * QT], F32)
    y_sl2 = sb6("y", [128, 8 * QT], F32)
    z_sb = sb6("z", [128, 8 * QT], F32)
    sq6 = sb6("sq", [128, 2 * QT], F32)
    tmp6 = sb6("tmp", [128, QT], F32)
    rows6 = sb6("rows", [1, 5 * QT], F32)
    x3_sb = sb6("x3", [128, 2 * QT], F32)
    wdown_sb = sb6("wdown", [128, 32 * D], BF16)
    sq6_slots = Slots(pp, 2)
    x3_slots = Slots(pp, 2)
    R6 = lambda r: rows6[0:1, r * QT:(r + 1) * QT]

    mk = pp.block()
    for kc in range(32):
        pp.op("gpsimd", "dma_start", out=wdown_sb[:, kc * D:(kc + 1) * D],
              in_=inp["D_wdownT"][kc * 128:(kc + 1) * 128, :], inc=ld_g.inc(16))
    pp.block_end(mk)

    for qt in range(NQT):
        mk = pp.block()
        t0 = qt * QT
        for c in range(32):
            pp.op("sync", "dma_start", out=h_sl[:, c * QT:(c + 1) * QT],
                  in_=hT[c * 128:(c + 1) * 128, t0:t0 + QT], inc=ld_s.inc(16))
        for c in range(NC):
            pp.op("sync", "dma_start", out=y_sl2[:, c * QT:(c + 1) * QT],
                  in_=yT[c * 128:(c + 1) * 128, t0:t0 + QT], inc=ld_s.inc(16))
        pp.wait("tensor", ld_s); pp.wait("tensor", ld_g)
        for oc in range(NC):
            bo = banks.get()
            pairs = [(wdown_sb[:, kc * D + oc * 128: kc * D + (oc + 1) * 128],
                      h_sl[:, kc * QT:(kc + 1) * QT]) for kc in range(32)]
            mm_group(pp, bo.t[:], pairs, inc=mm.inc())
            pp.wait("vector", mm)
            pp.wait("vector", ld_s); pp.wait("vector", ld_g)
            pp.op("vector", "tensor_add", z_sb[:, oc * QT:(oc + 1) * QT], bo.t[:],
                  y_sl2[:, oc * QT:(oc + 1) * QT], inc=ev.inc())
            banks.rel(bo, ev)
        pp.wait("scalar", ev)
        pp.wait("tensor", ev)
        src = [z_sb[:, c * QT:(c + 1) * QT] for c in range(NC)]
        ln_stats_rows(pp, banks, sq6_slots, sq6, ones_col, src, R6, mm, ev, sqsem,
                      eps5[0:1, 0:1])
        brstd = banks.get(); bmmr = banks.get()
        pp.wait("tensor", ev)
        pp.op("tensor", "matmul", brstd.t[:], ones_row[:], R6(3), start=True, stop=True)
        pp.op("tensor", "matmul", bmmr.t[:], ones_row[:], R6(4), start=True, stop=True, inc=mm.inc())
        for c in range(NC):
            pp.op("sync", "dma_start", out=x2r6[:, c * QT:(c + 1) * QT],
                  in_=x2T[c * 128:(c + 1) * 128, t0:t0 + QT], inc=ld_s.inc(16))
        pp.wait("vector", mm)
        pp.wait("vector", ld_s); pp.wait("vector", ld_g)
        for c in range(NC):
            s3_ = x3_slots.get("vector")
            x3sl = x3_sb[:, s3_ * QT:(s3_ + 1) * QT]
            pp.op("vector", "tensor_mul", tmp6[:], z_sb[:, c * QT:(c + 1) * QT], brstd.t[:])
            pp.op("vector", "tensor_sub", tmp6[:], tmp6[:], bmmr.t[:])
            pp.op("vector", "tensor_add", x3sl, tmp6[:], x2r6[:, c * QT:(c + 1) * QT],
                  inc=ev.inc())
            pp.wait("sync", ev)
            pp.op("sync", "dma_start", out=x3T[c * 128:(c + 1) * 128, t0:t0 + QT],
                  in_=x3sl, inc=st_s.inc(16))
            x3_slots.used(s3_, st_s)
        banks.rel(brstd, ev); banks.rel(bmmr, ev)
        pp.block_end(mk)
    es6.close()
    pp.wait("sync", st_s); pp.wait("gpsimd", st_g)


def emit_phaseE(pp, ctx):
    nc = ctx["nc"]; inp = ctx["inp"]; dram = ctx["dram"]
    es = pp.es
    sb = lambda name, shape, dt: es.enter_context(nc.sbuf_tensor("E" + name, shape, dt))

    banks = BankPool(pp, [es.enter_context(nc.psum_tensor(f"Ebk{i}", [128, 512], F32))
                          for i in range(8)])
    ld_s = pp.sem("ld_s"); ld_g = pp.sem("ld_g")
    st_s = pp.sem("st_s"); st_g = pp.sem("st_g")
    mm = pp.sem("mm"); ev = EvFacade(pp)
    sqsem = pp.sem("sq")

    x3T = dram["x3T"]; xxA = dram["xxA"]; xxB = dram["xxB"]; hT = dram["hT"]
    x4T = dram["x4T"]

    ones_col = sb("ones_col", [128, 1], F32)
    ones_row = sb("ones_row", [1, 128], F32)
    eps5 = sb("eps5", [128, 1], F32)
    cshE = sb("cshE", [128, 1], F32)
    permE = sb("permE", [128, 128], BF16)
    bd2 = sb("bd2", [128, 2], BF16)
    ex2f = sb("ex2f", [2, 128], F32)
    Cq = sb("Cq", [128, T], BF16)
    Sq = sb("Sq", [128, T], BF16)
    Ck = sb("Ck", [128, T], BF16)
    Sk = sb("Sk", [128, T], BF16)
    maskE_sb = sb("maskE", [128, 6 * QT], BF16)
    qs_col = sb("qs", [128, 6], F32)
    ks_col = sb("ks", [128, 6], F32)
    lng_col = sb("lng", [128, 6 * 8], F32)
    lnb_col = sb("lnb", [128, 6 * 8], F32)
    b1_col = sb("b1", [128, 6 * 44], F32)
    b2_col = sb("b2", [128, 6 * 8], F32)

    mk = pp.block()
    pp.op("vector", "memset", ones_col[:], 1.0)
    pp.op("vector", "memset", ones_row[:], 1.0)
    pp.op("vector", "memset", eps5[:], 1e-5)
    pp.op("vector", "memset", cshE[:], -C_SHIFT_E)
    pp.op("gpsimd", "dma_start", out=permE[:], in_=inp["E_permE"], inc=ld_g.inc(16))
    pp.op("gpsimd", "dma_start", out=bd2[:], in_=inp["D_bd2"], inc=ld_g.inc(16))
    pp.op("gpsimd", "dma_start", out=ex2f[:], in_=inp["D_ex2f"], inc=ld_g.inc(16))
    pp.op("gpsimd", "dma_start", out=Cq[:], in_=inp["E_Cq"], inc=ld_g.inc(16))
    pp.op("gpsimd", "dma_start", out=Sq[:], in_=inp["E_Sq"], inc=ld_g.inc(16))
    pp.op("gpsimd", "dma_start", out=Ck[:], in_=inp["E_Ck"], inc=ld_g.inc(16))
    pp.op("gpsimd", "dma_start", out=Sk[:], in_=inp["E_Sk"], inc=ld_g.inc(16))
    pp.op("sync", "dma_start", out=maskE_sb[:], in_=inp["maskE"], inc=ld_s.inc(16))
    pp.op("sync", "dma_start", out=qs_col[:], in_=inp["E_qs_cols"], inc=ld_s.inc(16))
    pp.op("sync", "dma_start", out=ks_col[:], in_=inp["E_ks_cols"], inc=ld_s.inc(16))
    pp.op("sync", "dma_start", out=lng_col[:], in_=inp["E_lng_cols"], inc=ld_s.inc(16))
    pp.op("sync", "dma_start", out=lnb_col[:], in_=inp["E_lnb_cols"], inc=ld_s.inc(16))
    pp.op("sync", "dma_start", out=b1_col[:], in_=inp["E_b1p_cols"], inc=ld_s.inc(16))
    pp.op("sync", "dma_start", out=b2_col[:], in_=inp["E_b2_cols"], inc=ld_s.inc(16))
    pp.block_end(mk)

    def load_shifted_bf(dst_chunks, srcT, qt):
        t0 = qt * QT
        for c in range(NC):
            dsl = dst_chunks(c)
            if c < 4:
                pp.op("gpsimd", "dma_start", out=dsl,
                      in_=srcT[c * 128:(c + 1) * 128, t0:t0 + QT], inc=ld_g.inc(16))
            elif qt == 0:
                pp.op("vector", "memset", dsl[:, 0:1], 0.0, inc=ev.inc())
                pp.op("gpsimd", "dma_start", out=dsl[:, 1:QT],
                      in_=srcT[c * 128:(c + 1) * 128, 0:QT - 1], inc=ld_g.inc(16))
            else:
                pp.op("gpsimd", "dma_start", out=dsl,
                      in_=srcT[c * 128:(c + 1) * 128, t0 - 1:t0 + QT - 1], inc=ld_g.inc(16))

    es0 = ExitStack()
    sb0 = lambda name, shape, dt: es0.enter_context(nc.sbuf_tensor("E0" + name, shape, dt))
    x_sb = sb0("x", [128, 8 * QT], F32)
    sq_sb = sb0("sq", [128, 2 * QT], F32)
    tmp_sb = sb0("tmp", [128, QT], F32)
    rows = sb0("rows", [1, 5 * QT], F32)
    t3_sb = sb0("t3", [128, 2 * QT], F32)
    sq_slots = Slots(pp, 2)
    t3_slots = Slots(pp, 2)
    R = lambda r: rows[0:1, r * QT:(r + 1) * QT]
    for qt in range(NQT):
        mk = pp.block()
        t0 = qt * QT
        for c in range(NC):
            pp.op("sync", "dma_start", out=x_sb[:, c * QT:(c + 1) * QT],
                  in_=x3T[c * 128:(c + 1) * 128, t0:t0 + QT], inc=ld_s.inc(16))
        pp.wait("scalar", ld_s); pp.wait("scalar", ld_g)
        pp.wait("tensor", ld_s); pp.wait("tensor", ld_g)
        src = [x_sb[:, c * QT:(c + 1) * QT] for c in range(NC)]
        ln_stats_rows(pp, banks, sq_slots, sq_sb, ones_col, src, R, mm, ev, sqsem,
                      eps5[0:1, 0:1])
        brstd = banks.get(); bmmr = banks.get()
        pp.wait("tensor", ev)
        pp.op("tensor", "matmul", brstd.t[:], ones_row[:], R(3), start=True, stop=True)
        pp.op("tensor", "matmul", bmmr.t[:], ones_row[:], R(4), start=True, stop=True, inc=mm.inc())
        pp.wait("vector", mm)
        for c in range(NC):
            ts_ = t3_slots.get("vector")
            tsl = t3_sb[:, ts_ * QT:(ts_ + 1) * QT]
            pp.op("vector", "tensor_mul", tmp_sb[:], x_sb[:, c * QT:(c + 1) * QT], brstd.t[:])
            pp.op("vector", "tensor_sub", tsl, tmp_sb[:], bmmr.t[:], inc=ev.inc())
            pp.wait("sync", ev)
            pp.op("sync", "dma_start", out=xxA[c * 128:(c + 1) * 128, t0:t0 + QT],
                  in_=tsl, inc=st_s.inc(16))
            t3_slots.used(ts_, st_s)
        banks.rel(brstd, ev); banks.rel(bmmr, ev)
        pp.block_end(mk)
    es0.close()

    for li in range(E_LAYERS):
        esA = ExitStack()
        sbA = lambda name, shape, dt: esA.enter_context(
            nc.sbuf_tensor(f"EL{li}a" + name, shape, dt))
        xsb = sbA("xsb", [128, 8 * QT], BF16)
        xres = sbA("xres", [128, 8 * QT], F32)
        wqkv_sb = sbA("wqkv", [128, 8 * 3072], BF16)
        wo_sb = sbA("wo", [128, 8 * D], BF16)
        q_fm = sbA("qfm", [128, 8 * QT], BF16)
        k_ring = sbA("kring", [128, NC * 8 * 128], BF16)
        v_ring = sbA("vring", [128, 8 * 1040], BF16)
        qa_sb = sbA("qa", [128, 2 * QT], BF16)
        sqE = sbA("sqE", [128, QT], F32)
        sqEb = sbA("sqEb", [128, QT], BF16)
        rowsE = sbA("rowsE", [2, 2 * QT], F32)
        ropt1 = sbA("ropt1", [128, QT], F32)
        ropt2 = sbA("ropt2", [128, QT], F32)
        exp_sb = sbA("exp", [128, 2 * QT], F32)
        attnT_sb = sbA("attnT", [128, 3 * QT], BF16)
        ctxb_sb = sbA("ctxb", [128, 8 * QT], BF16)
        dnr_sb = sbA("dnr", [1, 2 * QT], F32)
        y_sb = sbA("y", [128, QT], F32)
        qa_slots = Slots(pp, 2)
        exp_slots = Slots(pp, 2)
        at_slots = Slots(pp, 3)
        y_slots = Slots(pp, 1)
        RE = lambda r: rowsE[0:2, r * QT:(r + 1) * QT]

        mk = pp.block()
        pp.op("vector", "memset",
              v_ring.ap().rearrange("p (s h j) -> p s h j", s=8, h=NH, j=65)[:, :, :, 64:65],
              1.0)
        for kc in range(NC):
            pp.op("gpsimd", "dma_start", out=wqkv_sb[:, kc * 3072:(kc + 1) * 3072],
                  in_=inp["E_Wqkv"][li * D + kc * 128: li * D + (kc + 1) * 128, :],
                  inc=ld_g.inc(16))
            pp.op("gpsimd", "dma_start", out=wo_sb[:, kc * D:(kc + 1) * D],
                  in_=inp["E_Wo"][li * D + kc * 128: li * D + (kc + 1) * 128, :],
                  inc=ld_g.inc(16))
        pp.block_end(mk)

        def tr_(c, tile):
            s = tile % 8
            return k_ring[:, c * 1024 + s * 128: c * 1024 + (s + 1) * 128]

        def kr_qt_(c, qt):
            s = (qt * 4) % 8
            return k_ring[:, c * 1024 + s * 128: c * 1024 + s * 128 + 512]

        def vr_(tile, h):
            s = tile % 8
            return v_ring[:, s * 1040 + h * 65: s * 1040 + h * 65 + 65]

        def vr_w_(tile, g):
            s = tile % 8
            base = v_ring[:, s * 1040 + 8 * g * 65: s * 1040 + (8 * g + 8) * 65]
            return base.rearrange("p (h j) -> p h j", h=8, j=65)[:, :, 0:64]

        def norm_rope(kind, oc, qt, bank, dst):
            t0 = qt * QT
            scol = (qs_col if kind == "q" else ks_col)[:, li:li + 1]
            Ct = (Cq if kind == "q" else Ck)
            St = (Sq if kind == "q" else Sk)
            qs_ = qa_slots.get("scalar")
            qasl = qa_sb[:, qs_ * QT:(qs_ + 1) * QT]
            pp.op("scalar", "copy", qasl, bank.t[:], inc=ev.inc())
            pp.wait("vector", ev)
            pp.op("vector", "tensor_mul", sqEb[:], qasl, qasl, inc=ev.inc())
            banks.rel(bank, ev)
            bsq = banks.get()
            pp.wait("tensor", ev)
            pp.op("tensor", "matmul", bsq.t[0:2, 0:QT], bd2[:], sqEb[:],
                  start=True, stop=True, inc=mm.inc())
            pp.wait("scalar", mm)
            pp.op("scalar", "activation", RE(0), bsq.t[0:2, 0:QT], AF.Ln, inc=ev.inc())
            banks.rel(bsq, ev)
            pp.op("scalar", "activation", RE(0), RE(0), AF.Exp, scale=0.5, inc=ev.inc())
            pp.wait("vector", ev)
            pp.op("vector", "tensor_scalar_max", RE(1), RE(0), 1e-12)
            pp.op("vector", "reciprocal", RE(1), RE(1), inc=ev.inc())
            bsc = banks.get()
            pp.wait("tensor", ev)
            pp.op("tensor", "matmul", bsc.t[:], ex2f[:], RE(1), start=True, stop=True,
                  inc=mm.inc())
            pp.wait("vector", mm)
            qa2_ = qa_slots.get("vector")
            zsl = qa_sb[:, qa2_ * QT:(qa2_ + 1) * QT]
            pp.op("vector", "scalar_tensor_tensor", zsl, qasl, scol, bsc.t[:],
                  ALU.mult, ALU.mult, inc=ev.inc())
            qa_slots.used(qs_, ev)
            banks.rel(bsc, ev)
            bp = banks.get()
            pp.wait("tensor", ev)
            pp.op("tensor", "matmul", bp.t[:], permE[:], zsl, start=True, stop=True,
                  inc=mm.inc())
            pp.wait("vector", mm)
            pp.op("vector", "tensor_mul", ropt1[:], bp.t[:], St[:, t0:t0 + QT])
            pp.op("vector", "tensor_mul", ropt2[:], zsl, Ct[:, t0:t0 + QT])
            pp.op("vector", "tensor_add", dst, ropt2[:], ropt1[:], inc=ev.inc())
            qa_slots.used(qa2_, ev)
            banks.rel(bp, ev)

        for qt in range(NQT):
            mk = pp.block()
            t0 = qt * QT
            load_shifted_bf(lambda c: xsb[:, c * QT:(c + 1) * QT], xxA, qt)
            for c in range(NC):
                pp.op("sync", "dma_start", out=xres[:, c * QT:(c + 1) * QT],
                      in_=xxA[c * 128:(c + 1) * 128, t0:t0 + QT], inc=ld_s.inc(16))
            pp.wait("tensor", ld_s); pp.wait("tensor", ld_g)
            pp.wait("tensor", ev)
            for oc in range(NC):
                bo = banks.get()
                pairs = [(wqkv_sb[:, kc * 3072 + oc * 128: kc * 3072 + (oc + 1) * 128],
                          xsb[:, kc * QT:(kc + 1) * QT]) for kc in range(NC)]
                mm_group(pp, bo.t[:], pairs, inc=mm.inc())
                pp.wait("scalar", mm)
                norm_rope("q", oc, qt, bo, q_fm[:, oc * QT:(oc + 1) * QT])
            for oc in range(NC):
                bo = banks.get()
                pairs = [(wqkv_sb[:, kc * 3072 + D + oc * 128: kc * 3072 + D + (oc + 1) * 128],
                          xsb[:, kc * QT:(kc + 1) * QT]) for kc in range(NC)]
                mm_group(pp, bo.t[:], pairs, inc=mm.inc())
                pp.wait("scalar", mm)
                norm_rope("k", oc, qt, bo, kr_qt_(oc, qt))
            for i4 in range(4):
                tile = qt * 4 + i4
                for g in range(2):
                    bo = banks.get()
                    for kc in range(NC):
                        pp.op("tensor", "matmul", bo.t[:],
                              xsb[:, kc * QT + i4 * 128: kc * QT + (i4 + 1) * 128],
                              wqkv_sb[:, kc * 3072 + 2 * D + g * 512:
                                      kc * 3072 + 2 * D + (g + 1) * 512],
                              start=(kc == 0), stop=(kc == NC - 1),
                              inc=(mm.inc() if kc == NC - 1 else None))
                    pp.wait("vector", mm)
                    pp.op("vector", "tensor_copy", vr_w_(tile, g),
                          bo.t.ap().rearrange("p (h j) -> p h j", h=8, j=64), inc=ev.inc())
                    banks.rel(bo, ev)
            kbs_all = [(i, qt * 4 - 2 + i) for i in range(6)]
            kbs = [(i, kb) for i, kb in kbs_all if kb >= 0]
            for c in range(NC):
                bctxs = {}
                pipe = Pipe(depth=3)
                for hh in range(2):
                    h = 2 * c + hh
                    rowoff = hh * 64
                    bctxs[hh] = banks.get()
                    first = True
                    for i, kb in kbs:
                        bs = banks.get()
                        pp.op("tensor", "matmul", bs.t[:],
                              tr_(c, kb)[rowoff:rowoff + 64, :],
                              q_fm[:, c * QT:(c + 1) * QT][rowoff:rowoff + 64, :],
                              start=True, stop=True, inc=mm.inc())
                        mmv = mm.n

                        def consume(bs=bs, h=h, hh=hh, i=i, kb=kb, mmv=mmv, first=first,
                                    last=(i == kbs[-1][0])):
                            pp.wait("scalar", mm, mmv)
                            es_ = exp_slots.get("scalar")
                            eslot = exp_sb[:, es_ * QT:(es_ + 1) * QT]
                            pp.op("scalar", "activation", eslot, bs.t[:], AF.Exp,
                                  scale=8.0, bias=cshE[:, 0:1], inc=ev.inc())
                            banks.rel(bs, ev)
                            pp.wait("vector", ev)
                            as_ = at_slots.get("vector")
                            aslot = attnT_sb[:, as_ * QT:(as_ + 1) * QT]
                            pp.op("vector", "scalar_tensor_tensor", aslot, eslot, 1e30,
                                  maskE_sb[:, i * QT:(i + 1) * QT], ALU.min, ALU.mult,
                                  inc=ev.inc())
                            exp_slots.used(es_, ev)
                            pp.wait("tensor", ev)
                            pp.op("tensor", "matmul", bctxs[hh].t[0:65, :], vr_(kb, h),
                                  aslot, start=first, stop=last, inc=mm.inc())
                            at_slots.used(as_, mm)
                        pipe.push(consume)
                        first = False
                pipe.flush()
                for hh in range(2):
                    rowoff = hh * 64
                    bctx = bctxs[hh]
                    pp.wait("scalar", mm)
                    pp.op("scalar", "copy",
                          ctxb_sb[rowoff:rowoff + 64, c * QT:(c + 1) * QT],
                          bctx.t[0:64, :], inc=ev.inc())
                    pp.wait("vector", mm)
                    pp.op("vector", "tensor_scalar_add", dnr_sb[0:1, hh * QT:(hh + 1) * QT],
                          bctx.t[64:65, :], 1e-30)
                    pp.op("vector", "reciprocal", dnr_sb[0:1, hh * QT:(hh + 1) * QT],
                          dnr_sb[0:1, hh * QT:(hh + 1) * QT], inc=ev.inc())
                    banks.rel(bctx, ev)
                br = banks.get()
                pp.wait("tensor", ev)
                pp.op("tensor", "matmul", br.t[0:64, :], ones_row[0:1, 0:64],
                      dnr_sb[0:1, 0:QT], start=True, stop=True)
                pp.op("tensor", "matmul", br.t[64:128, :], ones_row[0:1, 0:64],
                      dnr_sb[0:1, QT:2 * QT], start=True, stop=True, inc=mm.inc())
                pp.wait("vector", mm)
                pp.op("vector", "tensor_mul", ctxb_sb[:, c * QT:(c + 1) * QT],
                      ctxb_sb[:, c * QT:(c + 1) * QT], br.t[:], inc=ev.inc())
                banks.rel(br, ev)
            pp.wait("tensor", ev)
            for oc in range(NC):
                bo = banks.get()
                pairs = [(wo_sb[:, kc * D + oc * 128: kc * D + (oc + 1) * 128],
                          ctxb_sb[:, kc * QT:(kc + 1) * QT]) for kc in range(NC)]
                mm_group(pp, bo.t[:], pairs, inc=mm.inc())
                pp.wait("vector", mm)
                pp.wait("vector", ld_s); pp.wait("vector", ld_g)
                ys = y_slots.get("vector")
                ysl = y_sb[:, ys * QT:(ys + 1) * QT]
                pp.op("vector", "tensor_add", ysl, bo.t[:],
                      xres[:, oc * QT:(oc + 1) * QT], inc=ev.inc())
                banks.rel(bo, ev)
                pp.wait("sync", ev)
                pp.op("sync", "dma_start", out=xxB[oc * 128:(oc + 1) * 128, t0:t0 + QT],
                      in_=ysl, inc=st_s.inc(16))
                y_slots.used(ys, st_s)
            pp.block_end(mk)
        esA.close()

        esF = ExitStack()
        sbF = lambda name, shape, dt: esF.enter_context(
            nc.sbuf_tensor(f"EL{li}f" + name, shape, dt))
        xs_f = sbF("xs", [128, 8 * QT], F32)
        sqF = sbF("sq", [128, 2 * QT], F32)
        tmpF = sbF("tmp", [128, QT], F32)
        rowsF = sbF("rows", [1, 5 * QT], F32)
        xn_sb = sbF("xn", [128, 8 * QT], BF16)
        w1_sb = sbF("w1", [128, 8 * 5632], BF16)
        ga_sb = sbF("ga", [128, QT], BF16)
        gg_sb = sbF("gg", [128, QT], BF16)
        h_sb = sbF("h", [128, 2 * QT], BF16)
        sqF_slots = Slots(pp, 2)
        h_slots = Slots(pp, 2)
        RF = lambda r: rowsF[0:1, r * QT:(r + 1) * QT]

        mk = pp.block()
        for kc in range(NC):
            pp.op("gpsimd", "dma_start", out=w1_sb[:, kc * 5632:(kc + 1) * 5632],
                  in_=inp["E_W1p"][li * D + kc * 128: li * D + (kc + 1) * 128, :],
                  inc=ld_g.inc(16))
        pp.block_end(mk)

        def load_shifted_f32(dst_chunks, srcT, qt):
            t0 = qt * QT
            for c in range(NC):
                dsl = dst_chunks(c)
                if c < 4:
                    pp.op("sync", "dma_start", out=dsl,
                          in_=srcT[c * 128:(c + 1) * 128, t0:t0 + QT], inc=ld_s.inc(16))
                elif qt == 0:
                    pp.op("vector", "memset", dsl[:, 0:1], 0.0, inc=ev.inc())
                    pp.op("sync", "dma_start", out=dsl[:, 1:QT],
                          in_=srcT[c * 128:(c + 1) * 128, 0:QT - 1], inc=ld_s.inc(16))
                else:
                    pp.op("sync", "dma_start", out=dsl,
                          in_=srcT[c * 128:(c + 1) * 128, t0 - 1:t0 + QT - 1], inc=ld_s.inc(16))

        for qt in range(NQT):
            mk = pp.block()
            t0 = qt * QT
            load_shifted_f32(lambda c: xs_f[:, c * QT:(c + 1) * QT], xxB, qt)
            pp.wait("scalar", ld_s); pp.wait("scalar", ld_g)
            pp.wait("tensor", ld_s); pp.wait("tensor", ld_g)
            pp.wait("vector", ld_s); pp.wait("vector", ld_g)
            src = [xs_f[:, c * QT:(c + 1) * QT] for c in range(NC)]
            ln_stats_rows(pp, banks, sqF_slots, sqF, ones_col, src, RF, mm, ev, sqsem,
                          eps5[0:1, 0:1])
            brstd = banks.get(); bmmr = banks.get()
            pp.wait("tensor", ev)
            pp.op("tensor", "matmul", brstd.t[:], ones_row[:], RF(3), start=True, stop=True)
            pp.op("tensor", "matmul", bmmr.t[:], ones_row[:], RF(4), start=True, stop=True,
                  inc=mm.inc())
            pp.wait("vector", mm)
            for c in range(NC):
                pp.op("vector", "tensor_mul", tmpF[:], xs_f[:, c * QT:(c + 1) * QT], brstd.t[:])
                pp.op("vector", "tensor_sub", tmpF[:], tmpF[:], bmmr.t[:])
                pp.op("vector", "tensor_scalar", xn_sb[:, c * QT:(c + 1) * QT], tmpF[:],
                      lng_col[:, li * 8 + c: li * 8 + c + 1],
                      lnb_col[:, li * 8 + c: li * 8 + c + 1],
                      ALU.mult, ALU.add, inc=ev.inc())
            banks.rel(brstd, ev); banks.rel(bmmr, ev)
            pp.wait("tensor", ev)
            for j in range(22):
                ba = banks.get()
                pairs = [(w1_sb[:, kc * 5632 + j * 128: kc * 5632 + (j + 1) * 128],
                          xn_sb[:, kc * QT:(kc + 1) * QT]) for kc in range(NC)]
                mm_group(pp, ba.t[:], pairs, inc=mm.inc())
                pp.wait("vector", mm)
                pp.op("vector", "tensor_scalar_add", ga_sb[:], ba.t[:],
                      b1_col[:, li * 44 + j: li * 44 + j + 1], inc=ev.inc())
                banks.rel(ba, ev)
                bg = banks.get()
                pairs = [(w1_sb[:, kc * 5632 + 2816 + j * 128: kc * 5632 + 2816 + (j + 1) * 128],
                          xn_sb[:, kc * QT:(kc + 1) * QT]) for kc in range(NC)]
                mm_group(pp, bg.t[:], pairs, inc=mm.inc())
                pp.wait("scalar", mm)
                pp.op("scalar", "activation", gg_sb[:], bg.t[:], AF.Gelu,
                      bias=b1_col[:, li * 44 + 22 + j: li * 44 + 22 + j + 1], inc=ev.inc())
                banks.rel(bg, ev)
                hs = h_slots.get("vector")
                hsl = h_sb[:, hs * QT:(hs + 1) * QT]
                pp.wait("vector", ev)
                pp.op("vector", "tensor_mul", hsl, ga_sb[:], gg_sb[:], inc=ev.inc())
                pp.wait("sync", ev)
                pp.op("sync", "dma_start", out=hT[j * 128:(j + 1) * 128, t0:t0 + QT],
                      in_=hsl, inc=st_s.inc(16))
                h_slots.used(hs, st_s)
            pp.block_end(mk)
        esF.close()

        esG = ExitStack()
        sbG = lambda name, shape, dt: esG.enter_context(
            nc.sbuf_tensor(f"EL{li}g" + name, shape, dt))
        h_sl = sbG("h", [128, 22 * QT], BF16)
        xx_sl = sbG("xx", [128, 8 * QT], F32)
        z_sb = sbG("z", [128, 2 * QT], F32)
        w2_sb = sbG("w2", [128, 22 * D], BF16)
        z_slots = Slots(pp, 2)

        mk = pp.block()
        for kc in range(22):
            pp.op("gpsimd", "dma_start", out=w2_sb[:, kc * D:(kc + 1) * D],
                  in_=inp["E_W2p"][li * 2816 + kc * 128: li * 2816 + (kc + 1) * 128, :],
                  inc=ld_g.inc(16))
        pp.block_end(mk)

        for qt in range(NQT):
            mk = pp.block()
            t0 = qt * QT
            for c in range(22):
                pp.op("sync", "dma_start", out=h_sl[:, c * QT:(c + 1) * QT],
                      in_=hT[c * 128:(c + 1) * 128, t0:t0 + QT], inc=ld_s.inc(16))
            for c in range(NC):
                pp.op("sync", "dma_start", out=xx_sl[:, c * QT:(c + 1) * QT],
                      in_=xxB[c * 128:(c + 1) * 128, t0:t0 + QT], inc=ld_s.inc(16))
            pp.wait("tensor", ld_s); pp.wait("tensor", ld_g)
            for oc in range(NC):
                bo = banks.get()
                pairs = [(w2_sb[:, kc * D + oc * 128: kc * D + (oc + 1) * 128],
                          h_sl[:, kc * QT:(kc + 1) * QT]) for kc in range(22)]
                mm_group(pp, bo.t[:], pairs, inc=mm.inc())
                pp.wait("vector", mm)
                pp.wait("vector", ld_s); pp.wait("vector", ld_g)
                zs = z_slots.get("vector")
                zsl = z_sb[:, zs * QT:(zs + 1) * QT]
                pp.op("vector", "scalar_tensor_tensor", zsl, bo.t[:],
                      b2_col[:, li * 8 + oc: li * 8 + oc + 1],
                      xx_sl[:, oc * QT:(oc + 1) * QT], ALU.add, ALU.add, inc=ev.inc())
                banks.rel(bo, ev)
                pp.wait("sync", ev)
                pp.op("sync", "dma_start", out=xxA[oc * 128:(oc + 1) * 128, t0:t0 + QT],
                      in_=zsl, inc=st_s.inc(16))
                z_slots.used(zs, st_s)
            pp.block_end(mk)
        esG.close()

    esZ = ExitStack()
    sbZ = lambda name, shape, dt: esZ.enter_context(nc.sbuf_tensor("EZ" + name, shape, dt))
    z_sb2 = sbZ("z", [128, 8 * QT], F32)
    x3r = sbZ("x3r", [128, 8 * QT], F32)
    sqZ = sbZ("sq", [128, 2 * QT], F32)
    tmpZ = sbZ("tmp", [128, QT], F32)
    rowsZ = sbZ("rows", [1, 5 * QT], F32)
    x4_sb = sbZ("x4", [128, 2 * QT], F32)
    sqZ_slots = Slots(pp, 2)
    x4_slots = Slots(pp, 2)
    RZ = lambda r: rowsZ[0:1, r * QT:(r + 1) * QT]
    for qt in range(NQT):
        mk = pp.block()
        t0 = qt * QT
        for c in range(NC):
            pp.op("sync", "dma_start", out=z_sb2[:, c * QT:(c + 1) * QT],
                  in_=xxA[c * 128:(c + 1) * 128, t0:t0 + QT], inc=ld_s.inc(16))
        pp.wait("scalar", ld_s); pp.wait("scalar", ld_g)
        pp.wait("tensor", ld_s); pp.wait("tensor", ld_g)
        src = [z_sb2[:, c * QT:(c + 1) * QT] for c in range(NC)]
        ln_stats_rows(pp, banks, sqZ_slots, sqZ, ones_col, src, RZ, mm, ev, sqsem,
                      eps5[0:1, 0:1])
        brstd = banks.get(); bmmr = banks.get()
        pp.wait("tensor", ev)
        pp.op("tensor", "matmul", brstd.t[:], ones_row[:], RZ(3), start=True, stop=True)
        pp.op("tensor", "matmul", bmmr.t[:], ones_row[:], RZ(4), start=True, stop=True,
              inc=mm.inc())
        for c in range(NC):
            pp.op("sync", "dma_start", out=x3r[:, c * QT:(c + 1) * QT],
                  in_=x3T[c * 128:(c + 1) * 128, t0:t0 + QT], inc=ld_s.inc(16))
        pp.wait("vector", mm)
        pp.wait("vector", ld_s); pp.wait("vector", ld_g)
        for c in range(NC):
            s4_ = x4_slots.get("vector")
            x4sl = x4_sb[:, s4_ * QT:(s4_ + 1) * QT]
            pp.op("vector", "tensor_mul", tmpZ[:], z_sb2[:, c * QT:(c + 1) * QT], brstd.t[:])
            pp.op("vector", "tensor_sub", tmpZ[:], tmpZ[:], bmmr.t[:])
            pp.op("vector", "tensor_add", x4sl, tmpZ[:], x3r[:, c * QT:(c + 1) * QT],
                  inc=ev.inc())
            pp.wait("sync", ev)
            pp.op("sync", "dma_start", out=x4T[c * 128:(c + 1) * 128, t0:t0 + QT],
                  in_=x4sl, inc=st_s.inc(16))
            x4_slots.used(s4_, st_s)
        banks.rel(brstd, ev); banks.rel(bmmr, ev)
        pp.block_end(mk)
    esZ.close()
    pp.wait("sync", st_s); pp.wait("gpsimd", st_g)


def prep_shared_inputs(inputs):
    consts = host_constants()
    base = {
        "A_w_inT": np.ascontiguousarray(np.asarray(inputs["L0_attn_w_in"]).T),
        "A_b_in_cols": _cols(inputs["L0_attn_b_in"], 24),
        "A_b_vrow": np.ascontiguousarray(np.asarray(inputs["L0_attn_b_in"])[2048:3072].reshape(1, D)),
        "A_w_outT": np.ascontiguousarray(np.asarray(inputs["L0_attn_w_out"]).T),
        "A_b_out_cols": _cols(inputs["L0_attn_b_out"], 8),
        "maskA": consts["maskA"],
        "B_WqT": np.ascontiguousarray(np.asarray(inputs["L0_ffn_Wq"]).T),
        "B_WdkvT": np.ascontiguousarray(np.asarray(inputs["L0_ffn_Wdkv"]).T),
        "B_WukT": np.ascontiguousarray(np.asarray(inputs["L0_ffn_Wuk"]).T),
        "B_WuvT": np.ascontiguousarray(np.asarray(inputs["L0_ffn_Wuv"]).T),
        "B_WoT": np.ascontiguousarray(np.asarray(inputs["L0_ffn_Wo"]).T),
        "B_n1w_cols": _cols(inputs["L0_ffn_norm1_w"], 8),
        "B_lng_cols": _cols(inputs["L0_ffn_ln_g"], 4),
        "B_lnb_cols": _cols(inputs["L0_ffn_ln_b"], 4),
        "maskB": consts["maskB"],
        "maskE": consts["maskE"],
        "ropeB_C": consts["ropeB_C"], "ropeB_S": consts["ropeB_S"], "permB": consts["permB"],
        "C_n2w_cols": _cols(inputs["L0_ffn_norm2_w"], 8),
        "C_gwT": np.ascontiguousarray(np.asarray(inputs["L0_ffn_gate_w"]).T),
        "C_gb_col": np.ascontiguousarray(np.asarray(inputs["L0_ffn_gate_b"]).reshape(4, 1)),
        "C_W1T": np.ascontiguousarray(np.asarray(inputs["L0_ffn_W1"]).transpose(0, 2, 1).reshape(4 * D, 4096)),
        "C_W2T": np.ascontiguousarray(np.asarray(inputs["L0_ffn_W2"]).transpose(0, 2, 1).reshape(4 * 4096, D)),
        "C_b1_cols": np.concatenate([_cols(np.asarray(inputs["L0_ffn_b1"])[e], 32) for e in range(4)], axis=1),
        "C_b2_cols": np.concatenate([_cols(np.asarray(inputs["L0_ffn_b2"])[e], 8) for e in range(4)], axis=1),
        "D_wqT": np.ascontiguousarray(np.asarray(inputs["L1_attn_wq"]).T),
        "D_wkT": np.ascontiguousarray(np.asarray(inputs["L1_attn_wk"]).T),
        "D_wvT": np.ascontiguousarray(np.asarray(inputs["L1_attn_wv"]).T),
        "D_wg1T": np.ascontiguousarray(np.asarray(inputs["L1_attn_wg1"]).T),
        "D_bg1_col": np.ascontiguousarray(np.asarray(inputs["L1_attn_bg1"]).reshape(16, 1)),
        "D_wg2T": np.ascontiguousarray(np.asarray(inputs["L1_attn_wg2"]).T),
        "D_bg2_cols": _cols(inputs["L1_attn_bg2"], 8),
        "D_wogT": np.ascontiguousarray(np.asarray(inputs["L1_attn_wog"]).T),
        "D_bog_cols": _cols(inputs["L1_attn_bog"], 8),
        "D_woT": np.ascontiguousarray(np.asarray(inputs["L1_attn_wo"]).T),
        "D_wgateT": np.ascontiguousarray(np.asarray(inputs["L1_attn_w_gate"]).T),
        "D_wupT": np.ascontiguousarray(np.asarray(inputs["L1_attn_w_up"]).T),
        "D_wdownT": np.ascontiguousarray(np.asarray(inputs["L1_attn_w_down"]).T),
        "D_s1_cols": _cols(inputs["L1_attn_scale1"], 8),
        "D_s2_cols": _cols(inputs["L1_attn_scale2"], 8),
        "D_bd2": consts["bd2"], "D_ex2": consts["ex2"], "D_ex2f": consts["ex2f"],
        "E_Wqkv": np.ascontiguousarray(np.asarray(inputs["L1_ffn_Wqkv"]).reshape(6 * D, 3 * D)),
        "E_Wo": np.ascontiguousarray(np.asarray(inputs["L1_ffn_Wo"]).reshape(6 * D, D)),
        "E_qs_cols": np.ascontiguousarray(np.tile(np.asarray(inputs["L1_ffn_q_scale"]), (1, 2)).T),
        "E_ks_cols": np.ascontiguousarray(np.tile(np.asarray(inputs["L1_ffn_k_scale"]), (1, 2)).T),
        "E_lng_cols": np.concatenate([_cols(np.asarray(inputs["L1_ffn_ln_g"])[i], 8) for i in range(6)], axis=1),
        "E_lnb_cols": np.concatenate([_cols(np.asarray(inputs["L1_ffn_ln_b"])[i], 8) for i in range(6)], axis=1),
        "E_Cq": consts["E_Cq"], "E_Sq": consts["E_Sq"],
        "E_Ck": consts["E_Ck"], "E_Sk": consts["E_Sk"],
        "E_permE": consts["E_permE"],
    }
    DI = 2730
    W1 = np.asarray(inputs["L1_ffn_W1"])
    W1p = np.zeros((6, D, 5632), np.float32)
    W1p[:, :, 0:DI] = W1[:, :, 0:DI]
    W1p[:, :, 2816:2816 + DI] = W1[:, :, DI:2 * DI]
    base["E_W1p"] = np.ascontiguousarray(W1p.reshape(6 * D, 5632))
    W2 = np.asarray(inputs["L1_ffn_W2"])
    W2p = np.zeros((6, 2816, D), np.float32)
    W2p[:, 0:DI, :] = W2
    base["E_W2p"] = np.ascontiguousarray(W2p.reshape(6 * 2816, D))
    b1 = np.asarray(inputs["L1_ffn_b1"])
    b1p = np.zeros((6, 5632), np.float32)
    b1p[:, 0:DI] = b1[:, 0:DI]
    b1p[:, 2816:2816 + DI] = b1[:, DI:2 * DI]
    base["E_b1p_cols"] = np.concatenate([_cols(b1p[i], 44) for i in range(6)], axis=1)
    base["E_b2_cols"] = np.concatenate([_cols(np.asarray(inputs["L1_ffn_b2"])[i], 8) for i in range(6)], axis=1)
    return base


PHASES = ("A", "B", "C", "D", "E")
OUT_NAME = {"A": "x1T", "B": "aT", "C": "x2T", "D": "x3T", "E": "x4T"}

def kernel(**inputs):
    nc = build_program(phases=PHASES)
    base = prep_shared_inputs(inputs)
    x = np.asarray(inputs["L0_attn_x"])
    import ml_dtypes
    in_maps = []
    for b in range(8):
        m = dict(base)
        m["xT"] = np.ascontiguousarray(x[b].T)
        sm = (np.arange(8) <= b).astype(np.float32)
        m["D_smask"] = np.ascontiguousarray(np.tile(sm, (128, 1)))
        m["D_imask"] = np.ascontiguousarray(np.tile(1.0 - sm, (128, 1)))
        m["D_wm2"] = np.ascontiguousarray(np.tile(sm, (2, 1)))
        in_maps.append(m)
    res = run_bass_kernel_spmd(nc, in_maps, core_ids=list(range(8)))
    oname = OUT_NAME[PHASES[-1]]
    out = np.stack([np.ascontiguousarray(np.asarray(r[oname]).astype(np.float32).T)
                    for r in res.results], axis=0)
    return out


# version 09
# speedup vs baseline: 1.0560x; 1.0208x over previous
import numpy as np
from contextlib import ExitStack
import concourse.bass as bass
import concourse.mybir as mybir
from concourse.bass_utils import run_bass_kernel_spmd

F32 = mybir.dt.float32
BF16 = mybir.dt.bfloat16
F16 = mybir.dt.float16
AF = mybir.ActivationFunctionType
ALU = mybir.AluOpType
AX = mybir.AxisListType

D = 1024
T = 4096
NC = 8
NH = 16
HD = 64
QT = 512
NQT = T // QT
ENGS = ("sync", "tensor", "vector", "scalar", "gpsimd")

DBG = []
EXTRA_OUT = []
E_LAYERS = 6

C_SHIFT_A = 12.0
C_SHIFT_MLA = 16.0
C_SHIFT_E = 16.0


class Sem:
    __slots__ = ("h", "n", "name")
    def __init__(self, h, name):
        self.h = h; self.n = 0; self.name = name
    def inc(self, k=1):
        self.n += k
        return (self, k)


class EvFacade:
    def __init__(self, pp, name="ev"):
        self.pp = pp
        self.sems = {e: pp.sem(f"{name}_{e[0]}{e[1]}") for e in ENGS}
    def inc(self, k=1):
        return ("EVF", self, k)
    def now_list(self):
        return [(s, s.n) for s in self.sems.values() if s.n > 0]


class P:
    def __init__(self, nc, es):
        self.nc = nc
        self.es = es
        self.q = {e: [] for e in ENGS}
        self._sems = {}
        self._waited = {}
        self.psem = {e: self.sem("prog_" + e) for e in ENGS}

    def sem(self, name):
        if name not in self._sems:
            self._sems[name] = Sem(self.es.enter_context(self.nc.semaphore(name)), name)
        return self._sems[name]

    def op(self, eng, meth, *args, inc=None, **kw):
        incs = []
        if inc is not None:
            if isinstance(inc, tuple) and len(inc) == 3 and inc[0] == "EVF":
                incs.append(inc[1].sems[eng].inc(inc[2]))
            else:
                incs.append(inc)
        self.q[eng].append(["op", meth, args, kw, incs])

    def wait(self, eng, sem, val=None):
        if isinstance(sem, EvFacade):
            assert val is None
            for (s, v) in sem.now_list():
                self.wait(eng, s, v)
            return
        v = sem.n if val is None else val
        key = (eng, id(sem))
        if v > 0 and self._waited.get(key, 0) < v:
            self.q[eng].append(["wait", sem, v])
            self._waited[key] = v

    def extra_inc(self, eng, sem):
        for item in reversed(self.q[eng]):
            if item[0] == "op":
                if item[4]:
                    self.q[eng].append(["op", "drain", (), {}, [sem.inc(1)]])
                else:
                    k = 16 if item[1] in ("dma_start", "dma_start_transpose",
                                          "collective_compute") else 1
                    item[4].append(sem.inc(k))
                return True
        return False

    def block(self, engines=ENGS):
        snap = {e: self.psem[e].n for e in ENGS}
        marks = {e: len(self.q[e]) for e in ENGS}
        for e in engines:
            for o in ENGS:
                if o != e and snap[o] > 0:
                    self.wait(e, self.psem[o], snap[o])
        return marks

    def block_end(self, marks):
        fenced = set()
        for e, sname in (("sync", "st_s"), ("gpsimd", "st_g")):
            s = self._sems.get(sname)
            if s is not None and s.n > 0 and any(
                    it[0] == "op" and it[1] == "dma_start" and any(x[0] is s for x in it[4])
                    for it in self.q[e][marks[e]:]):
                self.wait(e, s, s.n)
                fenced.add(e)
        for e in ENGS:
            emitted = any(it[0] == "op" for it in self.q[e][marks[e]:])
            if emitted:
                if e in fenced:
                    self.q[e].append(["op", "drain", (), {}, [self.psem[e].inc(1)]])
                else:
                    self.extra_inc(e, self.psem[e])

    def replay_into(self):
        with self.nc.Block() as block:
            for name in ENGS:
                getattr(block, name)(lambda eng, nm=name: self._replay(eng, nm))

    def _replay(self, eng, name):
        for item in self.q[name]:
            if item[0] == "wait":
                _, sem, val = item
                eng.wait_ge(sem.h, val)
            else:
                _, meth, args, kw, incs = item
                r = getattr(eng, meth)(*args, **kw)
                for (sem, k) in incs:
                    r.then_inc(sem.h, k)


class Bank:
    __slots__ = ("t", "war", "held")
    def __init__(self, t):
        self.t = t; self.war = None; self.held = False


class BankPool:
    def __init__(self, pp, banks):
        self.pp = pp
        self.banks = [Bank(t) for t in banks]
        self.rr = 0

    def get(self, eng="tensor"):
        for _ in range(len(self.banks)):
            b = self.banks[self.rr % len(self.banks)]
            self.rr += 1
            if not b.held:
                break
        else:
            raise RuntimeError("all banks held")
        b.held = True
        if b.war is not None:
            for (s, v) in b.war:
                self.pp.wait(eng, s, v)
            b.war = None
        return b

    def rel(self, b, sem):
        b.war = sem.now_list() if isinstance(sem, EvFacade) else [(sem, sem.n)]
        b.held = False


class Slots:
    def __init__(self, pp, n):
        self.pp = pp; self.war = [None] * n; self.i = 0
    def get(self, writer_eng):
        s = self.i % len(self.war); self.i += 1
        if self.war[s] is not None:
            for (ss, v) in self.war[s]:
                self.pp.wait(writer_eng, ss, v)
            self.war[s] = None
        return s
    def used(self, s, sem):
        self.war[s] = sem.now_list() if isinstance(sem, EvFacade) else [(sem, sem.n)]


import os
PIPE_DEPTH_OVERRIDE = os.environ.get("PIPE_DEPTH")


class LSlots:
    def __init__(self, pp, n, name):
        self.pp = pp
        self.war = [None] * n
        self.lsem = [pp.sem(f"{name}{i}") for i in range(n)]
        self.i = 0
    def get(self, eng):
        s = self.i % len(self.war); self.i += 1
        if self.war[s] is not None:
            for (ss, v) in self.war[s]:
                self.pp.wait(eng, ss, v)
            self.war[s] = None
        return s
    def linc(self, s, k=16):
        return self.lsem[s].inc(k)
    def wait_loaded(self, eng, s):
        self.pp.wait(eng, self.lsem[s])
    def used(self, s, sem):
        self.war[s] = sem.now_list() if isinstance(sem, EvFacade) else [(sem, sem.n)]


class Pipe:
    def __init__(self, depth=3):
        if PIPE_DEPTH_OVERRIDE is not None:
            depth = int(PIPE_DEPTH_OVERRIDE)
        self.depth = depth
        self.q = []
    def push(self, fn):
        self.q.append(fn)
        if len(self.q) > self.depth:
            self.q.pop(0)()
    def flush(self):
        for fn in self.q:
            fn()
        self.q = []


def mm_group(pp, out_ap, pairs, inc=None):
    n = len(pairs)
    for i, (l, r) in enumerate(pairs):
        pp.op("tensor", "matmul", out_ap, l, r, start=(i == 0), stop=(i == n - 1),
              inc=(inc if i == n - 1 else None))


def _mask_tiles(deltas, qw, lo, hi):
    out = []
    kl = np.arange(128)[:, None]
    ql = np.arange(qw)[None, :]
    for d in deltas:
        diff = ql - (kl + d)
        out.append(((diff >= lo) & (diff <= hi)).astype(np.float32))
    return np.concatenate(out, axis=1)


def host_constants():
    import ml_dtypes
    c = {}
    c["maskA"] = _mask_tiles([-128, 0, 128, 256, 384, 512], QT, -128, 128).astype(ml_dtypes.bfloat16)
    c["maskB"] = _mask_tiles([0, 128, 256, 384], QT, 0, 10 ** 9).astype(ml_dtypes.bfloat16)
    c["maskE"] = _mask_tiles([-256, -128, 0, 128, 256, 384], QT, 0, 256).astype(ml_dtypes.bfloat16)
    half = 32
    pos = np.arange(T, dtype=np.float64)
    inv_freq = 1.0 / 10000.0 ** (np.arange(half, dtype=np.float64) / half)
    ang = pos[None, :] * inv_freq[:, None]
    cos = np.cos(ang); sin = np.sin(ang)
    C1 = np.concatenate([cos, cos], axis=0)
    S1 = np.concatenate([-sin, sin], axis=0)
    c["ropeB_C"] = np.tile(C1, (2, 1)).astype(ml_dtypes.bfloat16)
    c["ropeB_S"] = np.tile(S1, (2, 1)).astype(ml_dtypes.bfloat16)
    P = np.zeros((128, 128), np.float32)
    for base in (0, 64):
        for j in range(32):
            P[base + j + 32, base + j] = 1.0
            P[base + j, base + j + 32] = 1.0
    c["permB"] = P.astype(ml_dtypes.bfloat16)
    bd2 = np.zeros((128, 2), np.float32)
    bd2[0:64, 0] = 1.0; bd2[64:128, 1] = 1.0
    c["bd2"] = bd2.astype(ml_dtypes.bfloat16)
    ex2 = np.zeros((2, 128), np.float32)
    ex2[0, 0:64] = 1.0; ex2[1, 64:128] = 1.0
    c["ex2"] = ex2.astype(ml_dtypes.bfloat16)
    c["ex2f"] = ex2
    DH = 64
    half_idx = np.arange(0, DH, 2, dtype=np.float64)
    inv_freq = 1.0 / 10000.0 ** (half_idx / DH)
    t_ = np.arange(T, dtype=np.float64)
    freqs = t_[:, None] * inv_freq[None, :]
    freqs = np.concatenate([freqs, freqs], -1)
    base_scale = (half_idx + 0.4 * DH) / (1.4 * DH)
    power = (t_ - T // 2) / (256 // 2)
    scale = base_scale[None, :] ** power[:, None]
    scale = np.concatenate([scale, scale], -1)
    cosf = np.cos(freqs); sinf = np.sin(freqs)
    Cq_ = (cosf * scale).T; Sq_ = (sinf * scale).T
    Ck_ = (cosf / scale).T; Sk_ = (sinf / scale).T
    c["E_Cq"] = np.tile(Cq_, (2, 1)).astype(ml_dtypes.bfloat16)
    c["E_Sq"] = np.tile(Sq_, (2, 1)).astype(ml_dtypes.bfloat16)
    c["E_Ck"] = np.tile(Ck_, (2, 1)).astype(ml_dtypes.bfloat16)
    c["E_Sk"] = np.tile(Sk_, (2, 1)).astype(ml_dtypes.bfloat16)
    PE_ = np.zeros((128, 128), np.float32)
    for base in (0, 64):
        for j in range(32):
            PE_[base + j + 32, base + j] = -1.0
            PE_[base + j, base + j + 32] = 1.0
    c["E_permE"] = PE_.astype(ml_dtypes.bfloat16)
    return c


def _cols(vec, nchunks):
    return np.ascontiguousarray(np.asarray(vec).reshape(nchunks, 128).T)


def ln_stats_rows(pp, banks, sq_slots, sq_sb, ones_col, src_chunks, rowap, mm, ev, sqsem, eps_ap=None):
    nch = len(src_chunks)
    bsum = banks.get(); bsq = banks.get()
    for c, xc in enumerate(src_chunks):
        s = sq_slots.get("scalar")
        ssl = sq_sb[:, s * QT:(s + 1) * QT][:, 0:xc.shape[1]]
        pp.op("scalar", "activation", ssl, xc, AF.Square, inc=sqsem.inc())
        pp.op("tensor", "matmul", bsum.t[0:1, 0:xc.shape[1]], ones_col[:], xc,
              start=(c == 0), stop=(c == nch - 1), inc=(mm.inc() if c == nch - 1 else None))
        pp.wait("tensor", sqsem)
        pp.op("tensor", "matmul", bsq.t[0:1, 0:xc.shape[1]], ones_col[:], ssl,
              start=(c == 0), stop=(c == nch - 1), inc=mm.inc())
        sq_slots.used(s, mm)
    W = src_chunks[0].shape[1]
    nfeat = float(nch * 128)
    pp.wait("vector", mm)
    pp.op("vector", "tensor_scalar_mul", rowap(0), bsum.t[0:1, 0:W], 1.0 / nfeat)
    pp.op("vector", "tensor_scalar_mul", rowap(1), bsq.t[0:1, 0:W], 1.0 / nfeat, inc=ev.inc())
    banks.rel(bsum, ev); banks.rel(bsq, ev)
    pp.op("vector", "tensor_mul", rowap(2), rowap(0), rowap(0))
    pp.op("vector", "tensor_sub", rowap(2), rowap(1), rowap(2), inc=ev.inc())
    pp.wait("scalar", ev)
    pp.op("scalar", "activation", rowap(3), rowap(2), AF.Ln, bias=eps_ap)
    pp.op("scalar", "activation", rowap(3), rowap(3), AF.Exp, scale=-0.5, inc=ev.inc())
    pp.wait("vector", ev)
    pp.op("vector", "tensor_mul", rowap(4), rowap(0), rowap(3), inc=ev.inc())


def emit_phaseA(pp, ctx):
    nc = ctx["nc"]; inp = ctx["inp"]; dram = ctx["dram"]
    es = pp.es
    sb = lambda name, shape, dt: es.enter_context(nc.sbuf_tensor(name, shape, dt))

    x_sb = sb("x_sb", [128, 8 * QT], F32)
    sq_sb = sb("sq_sb", [128, 2 * QT], F32)
    tmp_sb = sb("tmp_sb", [128, QT], F32)
    t_ring = sb("t_ring", [128, NC * 8 * 128], BF16)
    q_ring = sb("q_ring", [128, NC * 2 * QT], BF16)
    k_ring = sb("k_ring", [128, NC * 12 * 128], BF16)
    v_ring = sb("v_ring", [128, 12 * (NH * 65)], BF16)
    exp_sb = sb("exp_sb", [128, 2 * QT], F32)
    attnT_sb = sb("attnT_sb", [128, 3 * QT], BF16)
    ctxb_sb = sb("ctxb_sb", [128, NC * QT], BF16)
    dnr_sb = sb("dnr_sb", [1, 2 * QT], F32)
    rows = sb("rows", [1, 5 * QT], F32)
    x1_sb = sb("x1_sb", [128, QT], F32)
    w_in_sb = sb("w_in_sb", [128, NC * 3 * D], BF16)
    w_out_sb = sb("w_out_sb", [128, NC * D], BF16)
    bin_sb = sb("bin_sb", [128, 24], F32)
    bvrow_sb = sb("bvrow_sb", [1, D], BF16)
    bout_sb = sb("bout_sb", [128, 8], F32)
    maskA_sb = sb("maskA_sb", [128, 6 * QT], BF16)
    ones_col = sb("ones_col", [128, 1], F32)
    eps_row = sb("eps_row", [1, 1], F32)
    cshA_col = sb("cshA_col", [128, 1], F32)
    ones_row = sb("ones_row", [1, 128], F32)
    ones_rowb = sb("ones_rowb", [1, 128], BF16)

    banks = BankPool(pp, [es.enter_context(nc.psum_tensor(f"bk{i}", [128, 512], F32))
                          for i in range(8)])
    ld_s = pp.sem("ld_s"); ld_g = pp.sem("ld_g")
    st_s = pp.sem("st_s"); st_g = pp.sem("st_g")
    mm = pp.sem("mm"); ev = EvFacade(pp)
    sqsem = pp.sem("sq")
    sq_slots = Slots(pp, 2)
    xrl_slots = LSlots(pp, 2, "xrl")
    exp_slots = Slots(pp, 2)
    at_slots = Slots(pp, 3)
    x1_slots = Slots(pp, 1)

    mk = pp.block()
    pp.op("vector", "memset", ones_col[:], 1.0)
    pp.op("vector", "memset", eps_row[:], 1e-5)
    pp.op("vector", "memset", cshA_col[:], -C_SHIFT_A)
    pp.op("vector", "memset", ones_row[:], 1.0)
    pp.op("vector", "memset", ones_rowb[:], 1.0)
    pp.op("vector", "memset",
          v_ring.ap().rearrange("p (s h j) -> p s h j", s=12, h=NH, j=65)[:, :, :, 64:65], 1.0,
          inc=pp.sem("ev").inc())
    pp.op("sync", "dma_start", out=bin_sb[:], in_=inp["A_b_in_cols"], inc=ld_s.inc(16))
    pp.op("sync", "dma_start", out=bout_sb[:], in_=inp["A_b_out_cols"], inc=ld_s.inc(16))
    pp.op("sync", "dma_start", out=maskA_sb[:], in_=inp["maskA"], inc=ld_s.inc(16))
    pp.op("gpsimd", "dma_start", out=bvrow_sb[:], in_=inp["A_b_vrow"], inc=ld_g.inc(16))
    for kc in range(NC):
        pp.op("gpsimd", "dma_start", out=w_in_sb[:, kc * 3 * D:(kc + 1) * 3 * D],
              in_=inp["A_w_inT"][kc * 128:(kc + 1) * 128, :], inc=ld_g.inc(16))
        pp.op("gpsimd", "dma_start", out=w_out_sb[:, kc * D:(kc + 1) * D],
              in_=inp["A_w_outT"][kc * 128:(kc + 1) * 128, :], inc=ld_g.inc(16))
    pp.block_end(mk)

    xT = inp["xT"]; x1T = dram["x1T"]
    R = lambda r: rows[0:1, r * QT:(r + 1) * QT]

    def tr(c, tile):
        s = tile % 8
        return t_ring[:, c * 1024 + s * 128: c * 1024 + (s + 1) * 128]

    def tr_qt(c, qt):
        s = (qt * 4) % 8
        return t_ring[:, c * 1024 + s * 128: c * 1024 + s * 128 + 512]

    def qr(c, qt):
        s = qt % 2
        return q_ring[:, c * 2 * QT + s * QT: c * 2 * QT + (s + 1) * QT]

    def kr(c, tile):
        s = tile % 12
        return k_ring[:, c * 12 * 128 + s * 128: c * 12 * 128 + (s + 1) * 128]

    def kr_qt(c, qt):
        s = (qt * 4) % 12
        return k_ring[:, c * 12 * 128 + s * 128: c * 12 * 128 + s * 128 + 512]

    def vr(tile, h):
        s = tile % 12
        return v_ring[:, s * (NH * 65) + h * 65: s * (NH * 65) + h * 65 + 65]

    def vr_w(tile, g):
        s = tile % 12
        base = v_ring[:, s * (NH * 65) + 8 * g * 65: s * (NH * 65) + (8 * g + 8) * 65]
        return base.rearrange("p (h j) -> p h j", h=8, j=65)[:, :, 0:64]

    def emit_ln_qkv(qt):
        mk = pp.block()
        t0 = qt * QT
        for c in range(NC):
            pp.op("sync", "dma_start", out=x_sb[:, c * QT:(c + 1) * QT],
                  in_=xT[c * 128:(c + 1) * 128, t0:t0 + QT], inc=ld_s.inc(16))
        pp.wait("scalar", ld_s); pp.wait("scalar", ld_g)
        pp.wait("tensor", ld_s); pp.wait("tensor", ld_g)
        src = [x_sb[:, c * QT:(c + 1) * QT] for c in range(NC)]
        ln_stats_rows(pp, banks, sq_slots, sq_sb, ones_col, src, R, mm, ev, sqsem, eps_row[0:1, 0:1])
        brstd = banks.get(); bmmr = banks.get()
        pp.wait("tensor", ev)
        pp.op("tensor", "matmul", brstd.t[:], ones_row[:], R(3), start=True, stop=True)
        pp.op("tensor", "matmul", bmmr.t[:], ones_row[:], R(4), start=True, stop=True, inc=mm.inc())
        pp.wait("vector", mm)
        for c in range(NC):
            pp.op("vector", "tensor_mul", tmp_sb[:], x_sb[:, c * QT:(c + 1) * QT], brstd.t[:])
            pp.op("vector", "tensor_sub", tr_qt(c, qt), tmp_sb[:], bmmr.t[:],
                  inc=(ev.inc() if c == NC - 1 else None))
        banks.rel(brstd, ev); banks.rel(bmmr, ev)

        if "t" in DBG:
            pp.wait("gpsimd", ev)
            for c in range(NC):
                pp.op("gpsimd", "dma_start", out=dram["dbgT"][c * 128:(c + 1) * 128, t0:t0 + QT],
                      in_=tr_qt(c, qt), inc=st_g.inc(16))
        pp.wait("tensor", ev)
        for kind, base_oc in (("q", 0), ("k", 8)):
            for oc in range(NC):
                bo = banks.get()
                pairs = []
                for kc in range(NC):
                    l = w_in_sb[:, kc * 3 * D + (base_oc + oc) * 128:
                                kc * 3 * D + (base_oc + oc) * 128 + 128]
                    pairs.append((l, tr_qt(kc, qt)))
                mm_group(pp, bo.t[:], pairs, inc=mm.inc())
                pp.wait("scalar", mm)
                dst = qr(oc, qt) if kind == "q" else kr_qt(oc, qt)
                pp.op("scalar", "activation", dst, bo.t[:], AF.Identity,
                      bias=bin_sb[:, base_oc + oc: base_oc + oc + 1], inc=ev.inc())
                banks.rel(bo, ev)
        if "q" in DBG or "k" in DBG:
            pp.wait("gpsimd", ev)
            for c in range(NC):
                if "q" in DBG:
                    pp.op("gpsimd", "dma_start", out=dram["dbgT"][c * 128:(c + 1) * 128, t0:t0 + QT],
                          in_=qr(c, qt), inc=st_g.inc(16))
                if "k" in DBG:
                    pp.op("gpsimd", "dma_start", out=dram["dbgT"][D + c * 128:D + (c + 1) * 128, t0:t0 + QT],
                          in_=kr_qt(c, qt), inc=st_g.inc(16))
        for i in range(4):
            tile = qt * 4 + i
            for g in range(2):
                bo = banks.get()
                for kc in range(NC):
                    l = tr(kc, tile)
                    r = w_in_sb[:, kc * 3 * D + 2 * D + g * 512:
                                kc * 3 * D + 2 * D + (g + 1) * 512]
                    pp.op("tensor", "matmul", bo.t[:], l, r, start=(kc == 0), stop=False)
                pp.op("tensor", "matmul", bo.t[:], ones_rowb[:],
                      bvrow_sb[0:1, g * 512:(g + 1) * 512], start=False, stop=True, inc=mm.inc())
                pp.wait("vector", mm)
                pp.op("vector", "tensor_copy", vr_w(tile, g),
                      bo.t.ap().rearrange("p (h j) -> p h j", h=8, j=64), inc=ev.inc())
                banks.rel(bo, ev)
        if "v" in DBG:
            pp.wait("gpsimd", ev)
            for i in range(4):
                tile = qt * 4 + i
                if tile >= 3:
                    continue
                s = tile % 12
                pp.op("gpsimd", "dma_start",
                      out=dram["dbgT"][0:128, tile * 1040:(tile + 1) * 1040],
                      in_=v_ring[:, s * 1040:(s + 1) * 1040], inc=st_g.inc(16))
        pp.block_end(mk)

    def emit_attn(qt):
        mk = pp.block()
        t0 = qt * QT
        kbs_all = [(i, qt * 4 - 1 + i) for i in range(6)]
        kbs = [(i, kb) for i, kb in kbs_all if 0 <= kb < T // 128]
        for c in range(NC):
            bctxs = {}
            pipe = Pipe(depth=3)
            for hh in range(2):
                h = 2 * c + hh
                rowoff = hh * 64
                bctxs[hh] = banks.get()
                first = True
                for i, kb in kbs:
                    bs = banks.get()
                    pp.op("tensor", "matmul", bs.t[:], kr(c, kb)[rowoff:rowoff + 64, :],
                          qr(c, qt)[rowoff:rowoff + 64, :], start=True, stop=True, inc=mm.inc())
                    mmv = mm.n

                    def consume(bs=bs, h=h, hh=hh, i=i, kb=kb, mmv=mmv, first=first,
                                last=(i == kbs[-1][0])):
                        pp.wait("scalar", mm, mmv)
                        es_ = exp_slots.get("scalar")
                        eslot = exp_sb[:, es_ * QT:(es_ + 1) * QT]
                        pp.op("scalar", "activation", eslot, bs.t[:], AF.Exp,
                              scale=0.125, bias=cshA_col[:, 0:1], inc=ev.inc())
                        banks.rel(bs, ev)
                        pp.wait("vector", ev)
                        as_ = at_slots.get("vector")
                        aslot = attnT_sb[:, as_ * QT:(as_ + 1) * QT]
                        pp.op("vector", "scalar_tensor_tensor", aslot, eslot, 1e30,
                              maskA_sb[:, i * QT:(i + 1) * QT], ALU.min, ALU.mult, inc=ev.inc())
                        exp_slots.used(es_, ev)
                        pp.wait("tensor", ev)
                        pp.op("tensor", "matmul", bctxs[hh].t[0:65, :], vr(kb, h), aslot,
                              start=first, stop=last, inc=mm.inc())
                        at_slots.used(as_, mm)
                    pipe.push(consume)
                    first = False
            pipe.flush()
            for hh in range(2):
                rowoff = hh * 64
                bctx = bctxs[hh]
                pp.wait("scalar", mm)
                pp.op("scalar", "copy", ctxb_sb[rowoff:rowoff + 64, c * QT:(c + 1) * QT],
                      bctx.t[0:64, :], inc=ev.inc())
                pp.wait("vector", mm)
                pp.op("vector", "tensor_scalar_add", dnr_sb[0:1, hh * QT:(hh + 1) * QT],
                      bctx.t[64:65, :], 1e-30)
                pp.op("vector", "reciprocal", dnr_sb[0:1, hh * QT:(hh + 1) * QT],
                      dnr_sb[0:1, hh * QT:(hh + 1) * QT], inc=ev.inc())
                banks.rel(bctx, ev)
            br = banks.get()
            pp.wait("tensor", ev)
            pp.op("tensor", "matmul", br.t[0:64, :], ones_row[0:1, 0:64],
                  dnr_sb[0:1, 0:QT], start=True, stop=True)
            pp.op("tensor", "matmul", br.t[64:128, :], ones_row[0:1, 0:64],
                  dnr_sb[0:1, QT:2 * QT], start=True, stop=True, inc=mm.inc())
            pp.wait("vector", mm)
            pp.op("vector", "tensor_mul", ctxb_sb[:, c * QT:(c + 1) * QT],
                  ctxb_sb[:, c * QT:(c + 1) * QT], br.t[:], inc=ev.inc())
            banks.rel(br, ev)

        if "ctx" in DBG:
            pp.wait("gpsimd", ev)
            for c in range(NC):
                pp.op("gpsimd", "dma_start", out=dram["dbgT"][c * 128:(c + 1) * 128, t0:t0 + QT],
                      in_=ctxb_sb[:, c * QT:(c + 1) * QT], inc=st_g.inc(16))
        pp.wait("tensor", ev)
        for oc in range(NC):
            bo = banks.get()
            pairs = []
            for kc in range(NC):
                l = w_out_sb[:, kc * D + oc * 128: kc * D + oc * 128 + 128]
                pairs.append((l, ctxb_sb[:, kc * QT:(kc + 1) * QT]))
            mm_group(pp, bo.t[:], pairs, inc=mm.inc())
            pp.wait("vector", mm)
            pp.op("vector", "scalar_tensor_tensor", x_sb[:, oc * QT:(oc + 1) * QT],
                  bo.t[:], bout_sb[:, oc:oc + 1], tr_qt(oc, qt), ALU.add, ALU.add, inc=ev.inc())
            banks.rel(bo, ev)

        if "u" in DBG:
            pp.wait("gpsimd", ev)
            for c in range(NC):
                pp.op("gpsimd", "dma_start", out=dram["dbgT"][c * 128:(c + 1) * 128, t0:t0 + QT],
                      in_=x_sb[:, c * QT:(c + 1) * QT], inc=st_g.inc(16))
        pp.wait("scalar", ev)
        pp.wait("tensor", ev)
        src = [x_sb[:, c * QT:(c + 1) * QT] for c in range(NC)]
        ln_stats_rows(pp, banks, sq_slots, sq_sb, ones_col, src, R, mm, ev, sqsem, eps_row[0:1, 0:1])
        brstd = banks.get(); bmmr = banks.get()
        pp.wait("tensor", ev)
        pp.op("tensor", "matmul", brstd.t[:], ones_row[:], R(3), start=True, stop=True)
        pp.op("tensor", "matmul", bmmr.t[:], ones_row[:], R(4), start=True, stop=True, inc=mm.inc())
        pp.wait("vector", mm)
        pp.wait("sync", mm)
        for c in range(NC):
            xs_ = xrl_slots.get("sync")
            xsl = sq_sb[:, xs_ * QT:(xs_ + 1) * QT]
            pp.op("sync", "dma_start", out=xsl,
                  in_=xT[c * 128:(c + 1) * 128, t0:t0 + QT], inc=xrl_slots.linc(xs_))
            xrl_slots.wait_loaded("vector", xs_)
            s1_ = x1_slots.get("vector")
            x1sl = x1_sb[:, s1_ * QT:(s1_ + 1) * QT]
            pp.op("vector", "tensor_mul", tmp_sb[:], x_sb[:, c * QT:(c + 1) * QT], brstd.t[:])
            pp.op("vector", "tensor_sub", tmp_sb[:], tmp_sb[:], bmmr.t[:])
            pp.op("vector", "tensor_add", x1sl, tmp_sb[:], xsl, inc=ev.inc())
            xrl_slots.used(xs_, ev)
            if "x1" in DBG:
                pp.wait("gpsimd", ev)
                pp.op("gpsimd", "dma_start", out=dram["dbgT"][c * 128:(c + 1) * 128, t0:t0 + QT],
                      in_=x1sl, inc=st_g.inc(16))
            if "lnu" in DBG:
                pp.wait("gpsimd", ev)
                pp.op("gpsimd", "dma_start", out=dram["dbgT"][D + c * 128:D + (c + 1) * 128, t0:t0 + QT],
                      in_=xsl, inc=st_g.inc(16))
            pp.wait("sync", ev)
            pp.op("sync", "dma_start", out=x1T[c * 128:(c + 1) * 128, t0:t0 + QT],
                  in_=x1sl, inc=st_s.inc(16))
            x1_slots.used(s1_, st_s)
        banks.rel(brstd, ev); banks.rel(bmmr, ev)
        pp.block_end(mk)

    for qt in range(NQT):
        emit_ln_qkv(qt)
        if qt >= 1:
            emit_attn(qt - 1)
    emit_attn(NQT - 1)
    pp.wait("sync", st_s); pp.wait("gpsimd", st_g)


def build_program(phases=("A",)):
    nc = bass.Bass("TRN2", target_bir_lowering=False)
    inp = {}
    def I(name, shape, dt=F32):
        inp[name] = nc.dram_tensor(name, list(shape), dt, kind="ExternalInput").ap()
        return inp[name]

    I("xT", [D, T])
    I("A_w_inT", [D, 3 * D]); I("A_b_in_cols", [128, 24]); I("A_b_vrow", [1, D])
    I("A_w_outT", [D, D]); I("A_b_out_cols", [128, 8])
    I("maskA", [128, 6 * QT], BF16)
    I("B_WqT", [D, D]); I("B_WdkvT", [D, 512]); I("B_WukT", [512, D])
    I("B_WuvT", [512, D]); I("B_WoT", [D, D])
    I("B_n1w_cols", [128, 8]); I("B_lng_cols", [128, 4]); I("B_lnb_cols", [128, 4])
    I("maskB", [128, 4 * QT], BF16)
    I("maskE", [128, 6 * QT], BF16)
    I("ropeB_C", [128, T], BF16); I("ropeB_S", [128, T], BF16); I("permB", [128, 128], BF16)
    I("C_n2w_cols", [128, 8]); I("C_gwT", [D, 4]); I("C_gb_col", [4, 1])
    I("C_W1T", [4 * D, 4096]); I("C_W2T", [4 * 4096, D])
    I("C_b1_cols", [128, 4 * 32]); I("C_b2_cols", [128, 4 * 8])
    I("D_wqT", [D, D]); I("D_wkT", [D, D]); I("D_wvT", [D, D])
    I("D_wg1T", [D, 16]); I("D_bg1_col", [16, 1]); I("D_wg2T", [16, D])
    I("D_bg2_cols", [128, 8]); I("D_wogT", [D, D]); I("D_bog_cols", [128, 8])
    I("D_woT", [D, D]); I("D_wgateT", [D, 4096]); I("D_wupT", [D, 4096])
    I("D_wdownT", [4096, D]); I("D_s1_cols", [128, 8]); I("D_s2_cols", [128, 8])
    I("D_bd2", [128, 2], BF16); I("D_ex2", [2, 128], BF16); I("D_ex2f", [2, 128])
    I("D_smask", [128, 8]); I("D_imask", [128, 8]); I("D_wm2", [2, 8])
    I("E_Wqkv", [6 * D, 3 * D]); I("E_Wo", [6 * D, D])
    I("E_W1p", [6 * D, 5632]); I("E_W2p", [6 * 2816, D])
    I("E_qs_cols", [128, 6]); I("E_ks_cols", [128, 6])
    I("E_lng_cols", [128, 6 * 8]); I("E_lnb_cols", [128, 6 * 8])
    I("E_b1p_cols", [128, 6 * 44]); I("E_b2_cols", [128, 6 * 8])
    I("E_Cq", [128, T], BF16); I("E_Sq", [128, T], BF16)
    I("E_Ck", [128, T], BF16); I("E_Sk", [128, T], BF16)
    I("E_permE", [128, 128], BF16)

    out_name = {"A": "x1T", "B": "aT", "C": "x2T", "D": "x3T", "E": "x4T"}[phases[-1]]
    dram = {}
    def S(name, shape, dt=F32):
        kind = "ExternalOutput" if (name == out_name or name in EXTRA_OUT) else "Internal"
        if kind == "ExternalOutput":
            dram[name] = nc.dram_tensor(name, list(shape), dt, kind=kind).ap()
        else:
            dram[name] = nc.dram_tensor(name, list(shape), dt).ap()
        return dram[name]

    S("x1T", [D, T])
    S("t1T", [D, T], BF16); S("xnT", [D, T], BF16)
    S("kT", [D, T], BF16); S("vT", [T, NH * 65], BF16)
    S("aT", [D, T])
    S("mT", [D, T], BF16); S("accT", [D, T]); S("mrT", [4, T]); S("x2T", [D, T])
    S("t2T", [D, T], BF16); S("xn2T", [D, T], BF16); S("qTd", [D, T], BF16)
    S("kpub", [D, T], BF16); S("vpub", [D, T], BF16); S("apub", [D, T], F16)
    S("kG", [8 * D, T], BF16); S("vG", [8 * D, T], BF16); S("aG", [8 * D, T], F16)
    S("yT", [D, T]); S("hT", [4096, T], BF16); S("x3T", [D, T])
    S("xxA", [D, T]); S("xxB", [D, T]); S("x4T", [D, T])
    if DBG:
        dram["dbgT"] = nc.dram_tensor("dbgT", [2 * D, T], F32, kind="ExternalOutput").ap()

    ctx = dict(nc=nc, inp=inp, dram=dram)
    for ph in phases:
        es = ExitStack()
        pp = P(nc, es)
        {"A": emit_phaseA, "B": emit_phaseB, "C": emit_phaseC,
         "D": emit_phaseD, "E": emit_phaseE}[ph](pp, ctx)
        pp.replay_into()
        es.close()
        nc.reset()
    return nc


def emit_phaseB(pp, ctx):
    nc = ctx["nc"]; inp = ctx["inp"]; dram = ctx["dram"]
    es = pp.es
    sb = lambda name, shape, dt: es.enter_context(nc.sbuf_tensor("B" + name, shape, dt))

    banks = BankPool(pp, [es.enter_context(nc.psum_tensor(f"Bbk{i}", [128, 512], F32))
                          for i in range(8)])
    ld_s = pp.sem("ld_s"); ld_g = pp.sem("ld_g")
    st_s = pp.sem("st_s"); st_g = pp.sem("st_g")
    mm = pp.sem("mm"); ev = EvFacade(pp)
    sqsem = pp.sem("sq")

    ropeC = sb("ropeC", [128, T], BF16)
    ropeS = sb("ropeS", [128, T], BF16)
    perm = sb("perm", [128, 128], BF16)
    ones_col = sb("ones_col", [128, 1], F32)
    ones_row = sb("ones_row", [1, 128], F32)
    eps5 = sb("eps5", [1, 1], F32)
    eps6 = sb("eps6", [1, 1], F32)
    cshB = sb("cshB", [128, 1], F32)
    n1w_col = sb("n1w_col", [128, 8], F32)
    lng_col = sb("lng_col", [128, 4], F32)
    lnb_col = sb("lnb_col", [128, 4], F32)

    x1T = dram["x1T"]; t1T = dram["t1T"]; xnT = dram["xnT"]
    kT = dram["kT"]; vT = dram["vT"]; aT = dram["aT"]

    mk = pp.block()
    pp.op("vector", "memset", ones_col[:], 1.0)
    pp.op("vector", "memset", ones_row[:], 1.0)
    pp.op("vector", "memset", eps5[:], 1e-5)
    pp.op("vector", "memset", eps6[:], 1e-6)
    pp.op("vector", "memset", cshB[:], -C_SHIFT_MLA)
    pp.op("gpsimd", "dma_start", out=ropeC[:], in_=inp["ropeB_C"], inc=ld_g.inc(16))
    pp.op("gpsimd", "dma_start", out=ropeS[:], in_=inp["ropeB_S"], inc=ld_g.inc(16))
    pp.op("gpsimd", "dma_start", out=perm[:], in_=inp["permB"], inc=ld_g.inc(16))
    pp.op("sync", "dma_start", out=n1w_col[:], in_=inp["B_n1w_cols"], inc=ld_s.inc(16))
    pp.op("sync", "dma_start", out=lng_col[:], in_=inp["B_lng_cols"], inc=ld_s.inc(16))
    pp.op("sync", "dma_start", out=lnb_col[:], in_=inp["B_lnb_cols"], inc=ld_s.inc(16))
    pp.block_end(mk)

    def rope_chunk(dst_ap, src_bf_ap, t0, w, sgn=1.0):
        bp = banks.get()
        pp.op("tensor", "matmul", bp.t[:, 0:w], perm[:], src_bf_ap, start=True, stop=True,
              inc=mm.inc())
        pp.wait("vector", mm)
        pp.op("vector", "tensor_mul", rope_t1[:, 0:w], bp.t[:, 0:w], ropeS[:, t0:t0 + w])
        pp.op("vector", "tensor_mul", rope_t2[:, 0:w], src_bf_ap, ropeC[:, t0:t0 + w])
        pp.op("vector", "tensor_add", dst_ap, rope_t2[:, 0:w], rope_t1[:, 0:w], inc=ev.inc())
        banks.rel(bp, ev)

    es2 = ExitStack()
    sb2 = lambda name, shape, dt: es2.enter_context(nc.sbuf_tensor("B" + name, shape, dt))
    x1_sb = sb2("x1_sb", [128, 8 * QT], F32)
    sq_sb = sb2("sqB", [128, 2 * QT], F32)
    tmp_sb = sb2("tmpB", [128, QT], F32)
    rows = sb2("rowsB", [1, 8 * QT], F32)
    t1_sb = sb2("t1B", [128, 8 * QT], BF16)
    xn_sb = sb2("xnB", [128, 8 * QT], BF16)
    lat_f = sb2("latf", [128, 4 * QT], F32)
    lat_b = sb2("latb", [128, 4 * QT], BF16)
    kw_sb = sb2("kwB", [128, 8 * QT], BF16)
    vt_sb = sb2("vtB", [128, 2 * 1040], BF16)
    rope_t1 = sb2("ropet1", [128, QT], F32)
    rope_t2 = sb2("ropet2", [128, QT], F32)
    wdkv_sb = sb2("wdkv", [128, 8 * 512], BF16)
    wuk_sb = sb2("wuk", [128, 4 * D], BF16)
    wuv_sb = sb2("wuv", [128, 4 * D], BF16)

    sq_slots = Slots(pp, 2)
    vt_slots = Slots(pp, 2)
    R = lambda r: rows[0:1, r * QT:(r + 1) * QT]

    mk = pp.block()
    pp.op("vector", "memset",
          vt_sb.ap().rearrange("p (s h j) -> p s h j", s=2, h=NH, j=65)[:, :, :, 64:65], 1.0)
    for kc in range(NC):
        pp.op("gpsimd", "dma_start", out=wdkv_sb[:, kc * 512:(kc + 1) * 512],
              in_=inp["B_WdkvT"][kc * 128:(kc + 1) * 128, :], inc=ld_g.inc(16))
    for kc in range(4):
        pp.op("gpsimd", "dma_start", out=wuk_sb[:, kc * D:(kc + 1) * D],
              in_=inp["B_WukT"][kc * 128:(kc + 1) * 128, :], inc=ld_g.inc(16))
        pp.op("gpsimd", "dma_start", out=wuv_sb[:, kc * D:(kc + 1) * D],
              in_=inp["B_WuvT"][kc * 128:(kc + 1) * 128, :], inc=ld_g.inc(16))
    pp.block_end(mk)

    for qt in range(NQT):
        mk = pp.block()
        t0 = qt * QT
        for c in range(NC):
            pp.op("sync", "dma_start", out=x1_sb[:, c * QT:(c + 1) * QT],
                  in_=x1T[c * 128:(c + 1) * 128, t0:t0 + QT], inc=ld_s.inc(16))
        pp.wait("scalar", ld_s); pp.wait("scalar", ld_g)
        pp.wait("tensor", ld_s); pp.wait("tensor", ld_g)
        src = [x1_sb[:, c * QT:(c + 1) * QT] for c in range(NC)]
        ln_stats_rows(pp, banks, sq_slots, sq_sb, ones_col, src, R, mm, ev, sqsem,
                      eps5[0:1, 0:1])
        pp.op("vector", "tensor_scalar_add", R(5), R(2), 0.0)
        pp.op("vector", "tensor_single_scalar", R(6), R(2), 1e-5, ALU.add)
        pp.op("vector", "reciprocal", R(6), R(6))
        pp.op("vector", "tensor_mul", R(5), R(5), R(6), inc=ev.inc())
        pp.wait("scalar", ev)
        pp.op("scalar", "activation", R(5), R(5), AF.Ln, bias=eps6[0:1, 0:1])
        pp.op("scalar", "activation", R(5), R(5), AF.Exp, scale=-0.5, inc=ev.inc())
        brstd = banks.get(); bmmr = banks.get(); bs_ = banks.get()
        pp.wait("tensor", ev)
        pp.op("tensor", "matmul", brstd.t[:], ones_row[:], R(3), start=True, stop=True)
        pp.op("tensor", "matmul", bmmr.t[:], ones_row[:], R(4), start=True, stop=True)
        pp.op("tensor", "matmul", bs_.t[:], ones_row[:], R(5), start=True, stop=True, inc=mm.inc())
        pp.wait("vector", mm)
        for c in range(NC):
            t1sl = t1_sb[:, c * QT:(c + 1) * QT]
            pp.op("vector", "tensor_mul", tmp_sb[:], x1_sb[:, c * QT:(c + 1) * QT], brstd.t[:])
            pp.op("vector", "tensor_sub", t1sl, tmp_sb[:], bmmr.t[:])
            pp.op("vector", "scalar_tensor_tensor", xn_sb[:, c * QT:(c + 1) * QT],
                  t1sl, n1w_col[:, c:c + 1], bs_.t[:], ALU.mult, ALU.mult, inc=ev.inc())
            pp.wait("sync", ev)
            pp.op("sync", "dma_start", out=t1T[c * 128:(c + 1) * 128, t0:t0 + QT],
                  in_=t1sl, inc=st_s.inc(16))
            pp.op("sync", "dma_start", out=xnT[c * 128:(c + 1) * 128, t0:t0 + QT],
                  in_=xn_sb[:, c * QT:(c + 1) * QT], inc=st_s.inc(16))
        banks.rel(brstd, ev); banks.rel(bmmr, ev); banks.rel(bs_, ev)
        pp.block_end(mk)

    for qt in range(NQT):
        mk = pp.block()
        t0 = qt * QT
        for c in range(NC):
            pp.op("sync", "dma_start", out=xn_sb[:, c * QT:(c + 1) * QT],
                  in_=xnT[c * 128:(c + 1) * 128, t0:t0 + QT], inc=ld_s.inc(16))
        pp.wait("tensor", ld_s); pp.wait("tensor", ld_g)
        for oc in range(4):
            bo = banks.get()
            pairs = [(wdkv_sb[:, kc * 512 + oc * 128: kc * 512 + (oc + 1) * 128],
                      xn_sb[:, kc * QT:(kc + 1) * QT]) for kc in range(NC)]
            mm_group(pp, bo.t[:], pairs, inc=mm.inc())
            pp.wait("scalar", mm)
            pp.op("scalar", "copy", lat_f[:, oc * QT:(oc + 1) * QT], bo.t[:], inc=ev.inc())
            banks.rel(bo, ev)
        pp.wait("scalar", ev)
        pp.wait("tensor", ev)
        src = [lat_f[:, c * QT:(c + 1) * QT] for c in range(4)]
        ln_stats_rows(pp, banks, sq_slots, sq_sb, ones_col, src, R, mm, ev, sqsem,
                      eps5[0:1, 0:1])
        brstd = banks.get(); bmmr = banks.get()
        pp.wait("tensor", ev)
        pp.op("tensor", "matmul", brstd.t[:], ones_row[:], R(3), start=True, stop=True)
        pp.op("tensor", "matmul", bmmr.t[:], ones_row[:], R(4), start=True, stop=True, inc=mm.inc())
        pp.wait("vector", mm)
        for c in range(4):
            pp.op("vector", "tensor_mul", tmp_sb[:], lat_f[:, c * QT:(c + 1) * QT], brstd.t[:])
            pp.op("vector", "tensor_sub", tmp_sb[:], tmp_sb[:], bmmr.t[:])
            pp.op("vector", "tensor_scalar", lat_b[:, c * QT:(c + 1) * QT], tmp_sb[:],
                  lng_col[:, c:c + 1], lnb_col[:, c:c + 1], ALU.mult, ALU.add,
                  inc=ev.inc())
        banks.rel(brstd, ev); banks.rel(bmmr, ev)
        pp.wait("tensor", ev)
        for oc in range(NC):
            bo = banks.get()
            pairs = [(wuk_sb[:, kc * D + oc * 128: kc * D + (oc + 1) * 128],
                      lat_b[:, kc * QT:(kc + 1) * QT]) for kc in range(4)]
            mm_group(pp, bo.t[:], pairs, inc=mm.inc())
            pp.wait("scalar", mm)
            pp.op("scalar", "copy", kw_sb[:, oc * QT:(oc + 1) * QT], bo.t[:], inc=ev.inc())
            banks.rel(bo, ev)
        pp.wait("tensor", ev)
        pp.wait("vector", ev)
        for oc in range(NC):
            rope_chunk(kw_sb[:, oc * QT:(oc + 1) * QT], kw_sb[:, oc * QT:(oc + 1) * QT],
                       t0, QT)
            pp.wait("sync", ev)
            pp.op("sync", "dma_start", out=kT[oc * 128:(oc + 1) * 128, t0:t0 + QT],
                  in_=kw_sb[:, oc * QT:(oc + 1) * QT], inc=st_s.inc(16))
        for i in range(4):
            tile = qt * 4 + i
            vs_ = vt_slots.get("vector")
            for g in range(2):
                bo = banks.get()
                for kc in range(4):
                    pp.op("tensor", "matmul", bo.t[:],
                          lat_b[:, kc * QT + i * 128: kc * QT + (i + 1) * 128],
                          wuv_sb[:, kc * D + g * 512: kc * D + (g + 1) * 512],
                          start=(kc == 0), stop=(kc == 3), inc=(mm.inc() if kc == 3 else None))
                pp.wait("vector", mm)
                base = vt_sb[:, vs_ * 1040 + 8 * g * 65: vs_ * 1040 + (8 * g + 8) * 65]
                pp.op("vector", "tensor_copy", base.rearrange("p (h j) -> p h j", h=8, j=65)[:, :, 0:64],
                      bo.t.ap().rearrange("p (h j) -> p h j", h=8, j=64), inc=ev.inc())
                banks.rel(bo, ev)
            pp.wait("sync", ev)
            pp.op("sync", "dma_start", out=vT[tile * 128:(tile + 1) * 128, :],
                  in_=vt_sb[:, vs_ * 1040:(vs_ + 1) * 1040], inc=st_s.inc(16))
            vt_slots.used(vs_, st_s)
        pp.block_end(mk)

    es2.close()

    es3 = ExitStack()
    sb3 = lambda name, shape, dt: es3.enter_context(nc.sbuf_tensor("B" + name, shape, dt))
    k_full = sb3("kfull", [128, NC * T], BF16)
    v_ring = sb3("vringB", [128, 4 * 1040], BF16)
    xn_sl = sb3("xnsl", [128, 8 * QT], BF16)
    t1_sl = sb3("t1sl", [128, 8 * QT], BF16)
    q_fm = sb3("qfm", [128, 8 * QT], BF16)
    qa_sb = sb3("qa", [128, 2 * QT], BF16)
    rope_t1 = sb3("ropet1b", [128, QT], F32)
    rope_t2 = sb3("ropet2b", [128, QT], F32)
    wq_sb = sb3("wqB", [128, 8 * D], BF16)
    wo_sb = sb3("woB", [128, 8 * D], BF16)
    ctxb_sb = sb3("ctxbB", [128, 8 * QT], BF16)
    exp_sb = sb3("expB", [128, 3 * QT], F32)
    attnT_sb = sb3("attnTB", [128, 4 * QT], BF16)
    dnr_sb = sb3("dnrB", [1, 2 * QT], F32)
    maskB_sb = sb3("maskBs", [128, 4 * QT], BF16)
    a_sb = sb3("aB", [128, 2 * QT], F32)

    exp_slots = Slots(pp, 3)
    at_slots = Slots(pp, 4)
    a_slots = Slots(pp, 2)
    qa3_slots = Slots(pp, 2)
    vr_slots = LSlots(pp, 4, "vld")

    mk = pp.block()
    pp.op("sync", "dma_start", out=maskB_sb[:], in_=inp["maskB"], inc=ld_s.inc(16))
    for kc in range(NC):
        pp.op("gpsimd", "dma_start", out=wq_sb[:, kc * D:(kc + 1) * D],
              in_=inp["B_WqT"][kc * 128:(kc + 1) * 128, :], inc=ld_g.inc(16))
        pp.op("gpsimd", "dma_start", out=wo_sb[:, kc * D:(kc + 1) * D],
              in_=inp["B_WoT"][kc * 128:(kc + 1) * 128, :], inc=ld_g.inc(16))
        pp.op("sync", "dma_start", out=k_full[:, kc * T:(kc + 1) * T],
              in_=kT[kc * 128:(kc + 1) * 128, :], inc=ld_s.inc(16))
    pp.block_end(mk)

    def kf(c, kb):
        return k_full[:, c * T + kb * 128: c * T + (kb + 1) * 128]

    HGROUPS = [list(range(0, 4)), list(range(4, 8)), list(range(8, 12)), list(range(12, 16))]

    for qt in range(NQT):
        mk = pp.block()
        t0 = qt * QT
        for c in range(NC):
            pp.op("sync", "dma_start", out=xn_sl[:, c * QT:(c + 1) * QT],
                  in_=xnT[c * 128:(c + 1) * 128, t0:t0 + QT], inc=ld_s.inc(16))
            pp.op("sync", "dma_start", out=t1_sl[:, c * QT:(c + 1) * QT],
                  in_=t1T[c * 128:(c + 1) * 128, t0:t0 + QT], inc=ld_s.inc(16))
        pp.wait("tensor", ld_s); pp.wait("tensor", ld_g)
        pipeq = Pipe(depth=1)
        for oc in range(NC):
            bo = banks.get()
            pairs = [(wq_sb[:, kc * D + oc * 128: kc * D + (oc + 1) * 128],
                      xn_sl[:, kc * QT:(kc + 1) * QT]) for kc in range(NC)]
            mm_group(pp, bo.t[:], pairs, inc=mm.inc())
            mmv = mm.n

            def consumeq(oc=oc, bo=bo, mmv=mmv):
                pp.wait("scalar", mm, mmv)
                qs_ = qa3_slots.get("scalar")
                qsl = qa_sb[:, qs_ * QT:(qs_ + 1) * QT]
                pp.op("scalar", "copy", qsl, bo.t[:], inc=ev.inc())
                banks.rel(bo, ev)
                pp.wait("tensor", ev)
                pp.wait("vector", ev)
                rope_chunk(q_fm[:, oc * QT:(oc + 1) * QT], qsl, t0, QT)
                qa3_slots.used(qs_, ev)
            pipeq.push(consumeq)
        pipeq.flush()
        if "Bq" in DBG and qt == 0:
            pp.wait("gpsimd", ev)
            for c in range(NC):
                pp.op("gpsimd", "dma_start", out=dram["dbgT"][c * 128:(c + 1) * 128, 0:QT],
                      in_=q_fm[:, c * QT:(c + 1) * QT], inc=pp.sem("st_g").inc(16))
        nkb = qt * 4 + 4
        pp.wait("tensor", ld_s); pp.wait("tensor", ld_g)
        for grp in HGROUPS:
            cb = {h: banks.get() for h in grp}
            first = {h: True for h in grp}
            pipe = Pipe(depth=3)
            vheld = []
            for kb in range(nkb):
                vs_ = vr_slots.get("gpsimd")
                pp.op("gpsimd", "dma_start", out=v_ring[:, vs_ * 1040:(vs_ + 1) * 1040],
                      in_=vT[kb * 128:(kb + 1) * 128, :], inc=vr_slots.linc(vs_))
                vr_slots.wait_loaded("tensor", vs_)
                diag = kb - qt * 4
                for h in grp:
                    c = h // 2; rowoff = (h % 2) * 64
                    bs = banks.get()
                    pp.op("tensor", "matmul", bs.t[:], kf(c, kb)[rowoff:rowoff + 64, :],
                          q_fm[:, c * QT:(c + 1) * QT][rowoff:rowoff + 64, :],
                          start=True, stop=True, inc=mm.inc())
                    mmv = mm.n

                    def consume(bs=bs, h=h, kb=kb, vs_=vs_, diag=diag, mmv=mmv,
                                first_h=first[h]):
                        pp.wait("scalar", mm, mmv)
                        as_ = at_slots.get("vector")
                        aslot = attnT_sb[:, as_ * QT:(as_ + 1) * QT]
                        es_ = exp_slots.get("scalar")
                        eslot = exp_sb[:, es_ * QT:(es_ + 1) * QT]
                        pp.op("scalar", "activation", eslot, bs.t[:], AF.Exp,
                              scale=0.125, bias=cshB[:, 0:1], inc=ev.inc())
                        banks.rel(bs, ev)
                        pp.wait("vector", ev)
                        if diag < 0:
                            pp.op("vector", "tensor_copy", aslot, eslot, inc=ev.inc())
                        else:
                            pp.op("vector", "scalar_tensor_tensor", aslot, eslot, 1e30,
                                  maskB_sb[:, diag * QT:(diag + 1) * QT], ALU.min, ALU.mult,
                                  inc=ev.inc())
                        exp_slots.used(es_, ev)
                        pp.wait("tensor", ev)
                        pp.op("tensor", "matmul", cb[h].t[0:65, :],
                              v_ring[:, vs_ * 1040 + h * 65: vs_ * 1040 + h * 65 + 65],
                              aslot, start=first_h, stop=(kb == nkb - 1), inc=mm.inc())
                        at_slots.used(as_, mm)
                    pipe.push(consume)
                    first[h] = False
                vheld.append(vs_)
                while len(vheld) > 2:
                    vr_slots.used(vheld.pop(0), mm)
            pipe.flush()
            for vs_ in vheld:
                vr_slots.used(vs_, mm)
            for h in grp:
                c = h // 2; rowoff = (h % 2) * 64; hh = h % 2
                pp.wait("scalar", mm)
                pp.op("scalar", "copy", ctxb_sb[rowoff:rowoff + 64, c * QT:(c + 1) * QT],
                      cb[h].t[0:64, :], inc=ev.inc())
                pp.wait("vector", mm)
                pp.op("vector", "tensor_scalar_add", dnr_sb[0:1, hh * QT:(hh + 1) * QT],
                      cb[h].t[64:65, :], 1e-30)
                pp.op("vector", "reciprocal", dnr_sb[0:1, hh * QT:(hh + 1) * QT],
                      dnr_sb[0:1, hh * QT:(hh + 1) * QT], inc=ev.inc())
                banks.rel(cb[h], ev)
                if hh == 1:
                    br = banks.get()
                    pp.wait("tensor", ev)
                    pp.op("tensor", "matmul", br.t[0:64, :], ones_row[0:1, 0:64],
                          dnr_sb[0:1, 0:QT], start=True, stop=True)
                    pp.op("tensor", "matmul", br.t[64:128, :], ones_row[0:1, 0:64],
                          dnr_sb[0:1, QT:2 * QT], start=True, stop=True, inc=mm.inc())
                    pp.wait("vector", mm)
                    pp.op("vector", "tensor_mul", ctxb_sb[:, c * QT:(c + 1) * QT],
                          ctxb_sb[:, c * QT:(c + 1) * QT], br.t[:], inc=ev.inc())
                    banks.rel(br, ev)
        if "Bctx" in DBG and qt == 0:
            pp.wait("gpsimd", ev)
            for c in range(NC):
                pp.op("gpsimd", "dma_start", out=dram["dbgT"][D + c * 128:D + (c + 1) * 128, 0:QT],
                      in_=ctxb_sb[:, c * QT:(c + 1) * QT], inc=pp.sem("st_g").inc(16))
        pp.wait("tensor", ev)
        for oc in range(NC):
            bo = banks.get()
            pairs = [(wo_sb[:, kc * D + oc * 128: kc * D + (oc + 1) * 128],
                      ctxb_sb[:, kc * QT:(kc + 1) * QT]) for kc in range(NC)]
            mm_group(pp, bo.t[:], pairs, inc=mm.inc())
            pp.wait("vector", mm)
            sa = a_slots.get("vector")
            asl = a_sb[:, sa * QT:(sa + 1) * QT]
            pp.op("vector", "tensor_add", asl, bo.t[:], t1_sl[:, oc * QT:(oc + 1) * QT],
                  inc=ev.inc())
            banks.rel(bo, ev)
            pp.wait("sync", ev)
            pp.op("sync", "dma_start", out=aT[oc * 128:(oc + 1) * 128, t0:t0 + QT],
                  in_=asl, inc=st_s.inc(16))
            a_slots.used(sa, st_s)
        pp.block_end(mk)

    es3.close()
    pp.wait("sync", st_s); pp.wait("gpsimd", st_g)


def emit_phaseC(pp, ctx):
    nc = ctx["nc"]; inp = ctx["inp"]; dram = ctx["dram"]
    es = pp.es
    sb = lambda name, shape, dt: es.enter_context(nc.sbuf_tensor("C" + name, shape, dt))

    banks = BankPool(pp, [es.enter_context(nc.psum_tensor(f"Cbk{i}", [128, 512], F32))
                          for i in range(8)])
    ld_s = pp.sem("ld_s"); ld_g = pp.sem("ld_g")
    st_s = pp.sem("st_s"); st_g = pp.sem("st_g")
    mm = pp.sem("mm"); ev = EvFacade(pp)
    sqsem = pp.sem("sq")

    aT = dram["aT"]; x1T = dram["x1T"]; mT = dram["mT"]
    accT = dram["accT"]; mrT = dram["mrT"]; x2T = dram["x2T"]

    ones_col = sb("ones_col", [128, 1], F32)
    ones_row = sb("ones_row", [1, 128], F32)
    eps5 = sb("eps5", [1, 1], F32)
    eps6 = sb("eps6", [1, 1], F32)
    n2w_col = sb("n2w", [128, 8], F32)
    gwT_sb = sb("gwT", [128, 8 * 4], F32)
    gb_col = sb("gb", [4, 1], F32)

    mk = pp.block()
    pp.op("vector", "memset", ones_col[:], 1.0)
    pp.op("vector", "memset", ones_row[:], 1.0)
    pp.op("vector", "memset", eps5[:], 1e-5)
    pp.op("vector", "memset", eps6[:], 1e-6)
    pp.op("sync", "dma_start", out=n2w_col[:], in_=inp["C_n2w_cols"], inc=ld_s.inc(16))
    pp.op("sync", "dma_start", out=gb_col[:], in_=inp["C_gb_col"], inc=ld_s.inc(16))
    for kc in range(NC):
        pp.op("sync", "dma_start", out=gwT_sb[:, kc * 4:(kc + 1) * 4],
              in_=inp["C_gwT"][kc * 128:(kc + 1) * 128, :], inc=ld_s.inc(16))
    pp.block_end(mk)

    es2 = ExitStack()
    sb2 = lambda name, shape, dt: es2.enter_context(nc.sbuf_tensor("C0" + name, shape, dt))
    a_sb = sb2("a", [128, 8 * QT], F32)
    sq_sb = sb2("sq", [128, 2 * QT], F32)
    za_sb = sb2("za", [128, 8 * QT], F32)
    m_sb = sb2("m", [128, 8 * QT], BF16)
    rows = sb2("rows", [1, 8 * QT], F32)
    lg_sb = sb2("lg", [4, 2 * QT], F32)
    mrow_sb = sb2("mrow", [1, 8 * QT], F32)
    sq_slots = Slots(pp, 2)
    R = lambda r: rows[0:1, r * QT:(r + 1) * QT]

    for qt in range(NQT):
        mk = pp.block()
        t0 = qt * QT
        for c in range(NC):
            pp.op("sync", "dma_start", out=a_sb[:, c * QT:(c + 1) * QT],
                  in_=aT[c * 128:(c + 1) * 128, t0:t0 + QT], inc=ld_s.inc(16))
        pp.wait("scalar", ld_s); pp.wait("scalar", ld_g)
        pp.wait("tensor", ld_s); pp.wait("tensor", ld_g)
        bsq = banks.get()
        for c in range(NC):
            s_ = sq_slots.get("scalar")
            ssl = sq_sb[:, s_ * QT:(s_ + 1) * QT]
            pp.op("scalar", "activation", ssl, a_sb[:, c * QT:(c + 1) * QT], AF.Square,
                  inc=sqsem.inc())
            pp.wait("tensor", sqsem)
            pp.op("tensor", "matmul", bsq.t[0:1, :], ones_col[:], ssl,
                  start=(c == 0), stop=(c == NC - 1), inc=mm.inc())
            sq_slots.used(s_, mm)
        pp.wait("vector", mm)
        pp.op("vector", "tensor_scalar_mul", R(0), bsq.t[0:1, :], 1.0 / D, inc=ev.inc())
        banks.rel(bsq, ev)
        pp.wait("scalar", ev)
        pp.op("scalar", "activation", R(1), R(0), AF.Ln, bias=eps6[0:1, 0:1])
        pp.op("scalar", "activation", R(1), R(1), AF.Exp, scale=-0.5, inc=ev.inc())
        bs_ = banks.get(); bs4 = banks.get()
        pp.wait("tensor", ev)
        pp.op("tensor", "matmul", bs_.t[:], ones_row[:], R(1), start=True, stop=True)
        pp.op("tensor", "matmul", bs4.t[0:4, :], ones_row[0:1, 0:4], R(1),
              start=True, stop=True, inc=mm.inc())
        pp.wait("vector", mm)
        for c in range(NC):
            pp.op("vector", "tensor_scalar_mul", za_sb[:, c * QT:(c + 1) * QT],
                  a_sb[:, c * QT:(c + 1) * QT], n2w_col[:, c:c + 1])
            pp.op("vector", "tensor_mul", m_sb[:, c * QT:(c + 1) * QT],
                  za_sb[:, c * QT:(c + 1) * QT], bs_.t[:], inc=ev.inc())
            pp.wait("sync", ev)
            pp.op("sync", "dma_start", out=mT[c * 128:(c + 1) * 128, t0:t0 + QT],
                  in_=m_sb[:, c * QT:(c + 1) * QT], inc=st_s.inc(16))
        banks.rel(bs_, ev)
        bl = banks.get()
        pp.wait("tensor", ev)
        pairs = [(gwT_sb[:, kc * 4:(kc + 1) * 4], za_sb[:, kc * QT:(kc + 1) * QT])
                 for kc in range(NC)]
        mm_group(pp, bl.t[0:4, :], pairs, inc=mm.inc())
        pp.wait("scalar", mm)
        pp.op("scalar", "copy", lg_sb[0:4, 0:QT], bl.t[0:4, :], inc=ev.inc())
        banks.rel(bl, ev)
        pp.wait("vector", ev)
        pp.op("vector", "tensor_mul", lg_sb[0:4, QT:2 * QT], lg_sb[0:4, 0:QT], bs4.t[0:4, :])
        pp.op("vector", "tensor_scalar_add", lg_sb[0:4, QT:2 * QT],
              lg_sb[0:4, QT:2 * QT], gb_col[0:4, 0:1], inc=ev.inc())
        banks.rel(bs4, ev)
        pp.wait("gpsimd", ev)
        for e in range(4):
            pp.op("gpsimd", "dma_start", out=mrow_sb[0:1, e * QT:(e + 1) * QT],
                  in_=lg_sb[e:e + 1, QT:2 * QT], inc=ld_g.inc(16))
        pp.wait("vector", ld_s); pp.wait("vector", ld_g)
        MX = mrow_sb[0:1, 4 * QT:5 * QT]
        pp.op("vector", "tensor_max", MX, mrow_sb[0:1, 0:QT], mrow_sb[0:1, QT:2 * QT])
        pp.op("vector", "tensor_max", mrow_sb[0:1, 5 * QT:6 * QT],
              mrow_sb[0:1, 2 * QT:3 * QT], mrow_sb[0:1, 3 * QT:4 * QT])
        pp.op("vector", "tensor_max", MX, MX, mrow_sb[0:1, 5 * QT:6 * QT])
        for e in range(4):
            pp.op("vector", "tensor_tensor", mrow_sb[0:1, e * QT:(e + 1) * QT],
                  mrow_sb[0:1, e * QT:(e + 1) * QT], MX, ALU.is_ge,
                  inc=(ev.inc() if e == 3 else None))
        pp.wait("sync", ev)
        for e in range(4):
            pp.op("sync", "dma_start", out=mrT[e:e + 1, t0:t0 + QT],
                  in_=mrow_sb[0:1, e * QT:(e + 1) * QT], inc=st_s.inc(16))
        pp.block_end(mk)
    es2.close()

    es3 = ExitStack()
    sb3 = lambda name, shape, dt: es3.enter_context(nc.sbuf_tensor("C1" + name, shape, dt))
    w1_sb = sb3("w1", [128, NC * 4096], BF16)
    w2_sb = sb3("w2", [128, 32 * D], BF16)
    m_sl = sb3("msl", [128, 8 * QT], BF16)
    h_sb = sb3("h", [128, 32 * QT], BF16)
    mrow2 = sb3("mrow2", [1, 2 * QT], F32)
    maskb_sb = sb3("maskb", [128, 2 * QT], F32)
    y_sb = sb3("y", [128, 2 * QT], F32)
    b1_col = sb3("b1c", [128, 4 * 32], F32)
    b2_col = sb3("b2c", [128, 4 * 8], F32)
    y_slots = Slots(pp, 2)
    kva_war = [None, None]
    kvald = [pp.sem("kvald0"), pp.sem("kvald1")]
    mr_slots = Slots(pp, 2)
    acc_war = {}

    mk = pp.block()
    pp.op("sync", "dma_start", out=b1_col[:], in_=inp["C_b1_cols"], inc=ld_s.inc(16))
    pp.op("sync", "dma_start", out=b2_col[:], in_=inp["C_b2_cols"], inc=ld_s.inc(16))
    pp.block_end(mk)

    for e in range(4):
        mk = pp.block()
        for kc in range(NC):
            pp.op("gpsimd", "dma_start", out=w1_sb[:, kc * 4096:(kc + 1) * 4096],
                  in_=inp["C_W1T"][e * D + kc * 128: e * D + (kc + 1) * 128, :],
                  inc=ld_g.inc(16))
        for kc in range(32):
            pp.op("gpsimd", "dma_start", out=w2_sb[:, kc * D:(kc + 1) * D],
                  in_=inp["C_W2T"][e * 4096 + kc * 128: e * 4096 + (kc + 1) * 128, :],
                  inc=ld_g.inc(16))
        pp.block_end(mk)
        for qt in range(NQT):
            mk = pp.block()
            t0 = qt * QT
            for c in range(NC):
                pp.op("sync", "dma_start", out=m_sl[:, c * QT:(c + 1) * QT],
                      in_=mT[c * 128:(c + 1) * 128, t0:t0 + QT], inc=ld_s.inc(16))
            mr_ = mr_slots.get("sync")
            pp.op("sync", "dma_start", out=mrow2[0:1, mr_ * QT:(mr_ + 1) * QT],
                  in_=mrT[e:e + 1, t0:t0 + QT], inc=ld_s.inc(16))
            pp.wait("tensor", ld_s); pp.wait("tensor", ld_g)
            bmk = banks.get()
            pp.op("tensor", "matmul", bmk.t[:], ones_row[:],
                  mrow2[0:1, mr_ * QT:(mr_ + 1) * QT], start=True, stop=True, inc=mm.inc())
            pp.wait("vector", mm)
            mb = maskb_sb[:, (qt % 2) * QT:(qt % 2 + 1) * QT]
            pp.op("vector", "tensor_copy", mb, bmk.t[:], inc=ev.inc())
            mr_slots.used(mr_, ev)
            banks.rel(bmk, ev)
            for oc in range(32):
                bo = banks.get()
                pairs = [(w1_sb[:, kc * 4096 + oc * 128: kc * 4096 + (oc + 1) * 128],
                          m_sl[:, kc * QT:(kc + 1) * QT]) for kc in range(NC)]
                mm_group(pp, bo.t[:], pairs, inc=mm.inc())
                pp.wait("scalar", mm)
                pp.op("scalar", "activation", h_sb[:, oc * QT:(oc + 1) * QT], bo.t[:],
                      AF.Relu, bias=b1_col[:, e * 32 + oc: e * 32 + oc + 1], inc=ev.inc())
                banks.rel(bo, ev)
            pp.wait("tensor", ev)
            for oc in range(NC):
                bo = banks.get()
                pairs = [(w2_sb[:, kc * D + oc * 128: kc * D + (oc + 1) * 128],
                          h_sb[:, kc * QT:(kc + 1) * QT]) for kc in range(32)]
                mm_group(pp, bo.t[:], pairs, inc=mm.inc())
                pp.wait("vector", mm)
                ys = y_slots.get("vector")
                ysl = y_sb[:, ys * QT:(ys + 1) * QT]
                pp.op("vector", "scalar_tensor_tensor", ysl, bo.t[:],
                      b2_col[:, e * 8 + oc: e * 8 + oc + 1], mb, ALU.add, ALU.mult,
                      inc=ev.inc())
                banks.rel(bo, ev)
                pp.wait("gpsimd", ev)
                key = (qt, oc)
                if key in acc_war:
                    pp.wait("gpsimd", acc_war[key][0], acc_war[key][1])
                if e == 0:
                    pp.op("gpsimd", "dma_start", out=accT[oc * 128:(oc + 1) * 128, t0:t0 + QT],
                          in_=ysl, inc=st_g.inc(16))
                else:
                    pp.op("gpsimd", "dma_start", out=accT[oc * 128:(oc + 1) * 128, t0:t0 + QT],
                          in_=ysl, accum_op=ALU.add, inc=st_g.inc(16))
                acc_war[key] = (st_g, st_g.n)
                y_slots.used(ys, st_s)
            pp.block_end(mk)
    es3.close()

    es4 = ExitStack()
    sb4 = lambda name, shape, dt: es4.enter_context(nc.sbuf_tensor("C2" + name, shape, dt))
    b_sb = sb4("b", [128, 8 * QT], F32)
    acc_sb = sb4("acc", [128, 8 * QT], F32)
    sq_sb2 = sb4("sq", [128, 2 * QT], F32)
    tmp_sb = sb4("tmp", [128, QT], F32)
    rows2 = sb4("rows", [1, 5 * QT], F32)
    x2_sb = sb4("x2", [128, 2 * QT], F32)
    sq2_slots = Slots(pp, 2)
    acc_slots = Slots(pp, 2)
    x2_slots = Slots(pp, 2)
    R2 = lambda r: rows2[0:1, r * QT:(r + 1) * QT]

    for qt in range(NQT):
        mk = pp.block()
        t0 = qt * QT
        for c in range(NC):
            pp.op("sync", "dma_start", out=acc_sb[:, c * QT:(c + 1) * QT],
                  in_=accT[c * 128:(c + 1) * 128, t0:t0 + QT], inc=ld_s.inc(16))
            pp.op("sync", "dma_start", out=b_sb[:, c * QT:(c + 1) * QT],
                  in_=aT[c * 128:(c + 1) * 128, t0:t0 + QT], inc=ld_s.inc(16))
        pp.wait("vector", ld_s); pp.wait("vector", ld_g)
        for c in range(NC):
            pp.op("vector", "tensor_add", b_sb[:, c * QT:(c + 1) * QT],
                  b_sb[:, c * QT:(c + 1) * QT], acc_sb[:, c * QT:(c + 1) * QT],
                  inc=ev.inc())
        pp.wait("scalar", ev)
        pp.wait("tensor", ev)
        src = [b_sb[:, c * QT:(c + 1) * QT] for c in range(NC)]
        ln_stats_rows(pp, banks, sq2_slots, sq_sb2, ones_col, src, R2, mm, ev, sqsem,
                      eps5[0:1, 0:1])
        brstd = banks.get(); bmmr = banks.get()
        pp.wait("tensor", ev)
        pp.op("tensor", "matmul", brstd.t[:], ones_row[:], R2(3), start=True, stop=True)
        pp.op("tensor", "matmul", bmmr.t[:], ones_row[:], R2(4), start=True, stop=True, inc=mm.inc())
        pp.wait("sync", ev)
        for c in range(NC):
            pp.op("sync", "dma_start", out=acc_sb[:, c * QT:(c + 1) * QT],
                  in_=x1T[c * 128:(c + 1) * 128, t0:t0 + QT], inc=ld_s.inc(16))
        pp.wait("vector", mm)
        pp.wait("vector", ld_s); pp.wait("vector", ld_g)
        for c in range(NC):
            s2_ = x2_slots.get("vector")
            x2sl = x2_sb[:, s2_ * QT:(s2_ + 1) * QT]
            pp.op("vector", "tensor_mul", tmp_sb[:], b_sb[:, c * QT:(c + 1) * QT], brstd.t[:])
            pp.op("vector", "tensor_sub", tmp_sb[:], tmp_sb[:], bmmr.t[:])
            pp.op("vector", "tensor_add", x2sl, tmp_sb[:], acc_sb[:, c * QT:(c + 1) * QT],
                  inc=ev.inc())
            pp.wait("sync", ev)
            pp.op("sync", "dma_start", out=x2T[c * 128:(c + 1) * 128, t0:t0 + QT],
                  in_=x2sl, inc=st_s.inc(16))
            x2_slots.used(s2_, st_s)
        banks.rel(brstd, ev); banks.rel(bmmr, ev)
        pp.block_end(mk)
    es4.close()
    pp.wait("sync", st_s); pp.wait("gpsimd", st_g)


def emit_phaseD(pp, ctx):
    nc = ctx["nc"]; inp = ctx["inp"]; dram = ctx["dram"]
    es = pp.es
    sb = lambda name, shape, dt: es.enter_context(nc.sbuf_tensor("D" + name, shape, dt))

    banks = BankPool(pp, [es.enter_context(nc.psum_tensor(f"Dbk{i}", [128, 512], F32))
                          for i in range(8)])
    ld_s = pp.sem("ld_s"); ld_g = pp.sem("ld_g")
    st_s = pp.sem("st_s"); st_g = pp.sem("st_g")
    mm = pp.sem("mm"); ev = EvFacade(pp)
    sqsem = pp.sem("sq")
    ccK = pp.sem("ccK"); ccV = pp.sem("ccV"); ccA = pp.sem("ccA")

    x2T = dram["x2T"]; t2T = dram["t2T"]; xn2T = dram["xn2T"]; qTd = dram["qTd"]
    kpub = dram["kpub"]; vpub = dram["vpub"]; apub = dram["apub"]
    kG = dram["kG"]; vG = dram["vG"]; aG = dram["aG"]
    yT = dram["yT"]; hT = dram["hT"]; x3T = dram["x3T"]

    ones_col = sb("ones_col", [128, 1], F32)
    ones_row = sb("ones_row", [1, 128], F32)
    eps5 = sb("eps5", [128, 1], F32)
    eps6r = sb("eps6r", [128, 1], F32)
    s1_col = sb("s1c", [128, 8], F32)
    s2_col = sb("s2c", [128, 8], F32)
    bd2 = sb("bd2", [128, 2], BF16)
    ex2 = sb("ex2", [2, 128], BF16)
    ex2f = sb("ex2f", [2, 128], F32)
    smask = sb("smask", [128, 8], F32)
    imask = sb("imask", [128, 8], F32)
    wm2 = sb("wm2", [2, 8], F32)

    mk = pp.block()
    pp.op("vector", "memset", ones_col[:], 1.0)
    pp.op("vector", "memset", ones_row[:], 1.0)
    pp.op("vector", "memset", eps5[:], 1e-5)
    pp.op("vector", "memset", eps6r[:], 1e-6)
    pp.op("sync", "dma_start", out=s1_col[:], in_=inp["D_s1_cols"], inc=ld_s.inc(16))
    pp.op("sync", "dma_start", out=s2_col[:], in_=inp["D_s2_cols"], inc=ld_s.inc(16))
    pp.op("gpsimd", "dma_start", out=bd2[:], in_=inp["D_bd2"], inc=ld_g.inc(16))
    pp.op("gpsimd", "dma_start", out=ex2[:], in_=inp["D_ex2"], inc=ld_g.inc(16))
    pp.op("gpsimd", "dma_start", out=ex2f[:], in_=inp["D_ex2f"], inc=ld_g.inc(16))
    pp.op("gpsimd", "dma_start", out=smask[:], in_=inp["D_smask"], inc=ld_g.inc(16))
    pp.op("gpsimd", "dma_start", out=imask[:], in_=inp["D_imask"], inc=ld_g.inc(16))
    pp.op("sync", "dma_start", out=wm2[:], in_=inp["D_wm2"], inc=ld_s.inc(16))
    pp.block_end(mk)

    es2 = ExitStack()
    sb2 = lambda name, shape, dt: es2.enter_context(nc.sbuf_tensor("D1" + name, shape, dt))
    x_sb = sb2("x", [128, 8 * QT], F32)
    sq_sb = sb2("sq", [128, 2 * QT], F32)
    tmp_sb = sb2("tmp", [128, QT], F32)
    rows = sb2("rows", [1, 8 * QT], F32)
    t2_sb = sb2("t2", [128, 8 * QT], BF16)
    xn_sb = sb2("xn", [128, 8 * QT], BF16)
    sq_slots = Slots(pp, 2)
    R = lambda r: rows[0:1, r * QT:(r + 1) * QT]

    for qt in range(NQT):
        mk = pp.block()
        t0 = qt * QT
        for c in range(NC):
            pp.op("sync", "dma_start", out=x_sb[:, c * QT:(c + 1) * QT],
                  in_=x2T[c * 128:(c + 1) * 128, t0:t0 + QT], inc=ld_s.inc(16))
        pp.wait("scalar", ld_s); pp.wait("scalar", ld_g)
        pp.wait("tensor", ld_s); pp.wait("tensor", ld_g)
        src = [x_sb[:, c * QT:(c + 1) * QT] for c in range(NC)]
        ln_stats_rows(pp, banks, sq_slots, sq_sb, ones_col, src, R, mm, ev, sqsem,
                      eps5[0:1, 0:1])
        pp.op("vector", "tensor_single_scalar", R(6), R(2), 1e-5, ALU.add)
        pp.op("vector", "reciprocal", R(6), R(6))
        pp.op("vector", "tensor_mul", R(5), R(2), R(6), inc=ev.inc())
        pp.wait("scalar", ev)
        pp.op("scalar", "activation", R(5), R(5), AF.Ln)
        pp.op("scalar", "activation", R(5), R(5), AF.Exp, scale=0.5, inc=ev.inc())
        pp.wait("vector", ev)
        pp.op("vector", "tensor_scalar_add", R(5), R(5), 1e-6)
        pp.op("vector", "reciprocal", R(5), R(5), inc=ev.inc())
        brstd = banks.get(); bmmr = banks.get(); bs_ = banks.get()
        pp.wait("tensor", ev)
        pp.op("tensor", "matmul", brstd.t[:], ones_row[:], R(3), start=True, stop=True)
        pp.op("tensor", "matmul", bmmr.t[:], ones_row[:], R(4), start=True, stop=True)
        pp.op("tensor", "matmul", bs_.t[:], ones_row[:], R(5), start=True, stop=True, inc=mm.inc())
        pp.wait("vector", mm)
        for c in range(NC):
            t2sl = t2_sb[:, c * QT:(c + 1) * QT]
            pp.op("vector", "tensor_mul", tmp_sb[:], x_sb[:, c * QT:(c + 1) * QT], brstd.t[:])
            pp.op("vector", "tensor_sub", t2sl, tmp_sb[:], bmmr.t[:])
            pp.op("vector", "scalar_tensor_tensor", xn_sb[:, c * QT:(c + 1) * QT],
                  t2sl, s1_col[:, c:c + 1], bs_.t[:], ALU.mult, ALU.mult, inc=ev.inc())
            pp.wait("sync", ev)
            pp.op("sync", "dma_start", out=t2T[c * 128:(c + 1) * 128, t0:t0 + QT],
                  in_=t2sl, inc=st_s.inc(16))
            pp.op("sync", "dma_start", out=xn2T[c * 128:(c + 1) * 128, t0:t0 + QT],
                  in_=xn_sb[:, c * QT:(c + 1) * QT], inc=st_s.inc(16))
        banks.rel(brstd, ev); banks.rel(bmmr, ev); banks.rel(bs_, ev)
        pp.block_end(mk)
    es2.close()

    es3 = ExitStack()
    sb3 = lambda name, shape, dt: es3.enter_context(nc.sbuf_tensor("D2" + name, shape, dt))
    xn_sl = sb3("xn", [128, 8 * QT], BF16)
    out_bf = sb3("obf", [128, 2 * QT], BF16)
    gh_sb = sb3("gh", [16, QT], BF16)
    al_sb = sb3("al", [128, 2 * QT], F16)
    wq_sb = sb3("wq", [128, 8 * D], BF16)
    wk_sb = sb3("wk", [128, 8 * D], BF16)
    wv_sb = sb3("wv", [128, 8 * D], BF16)
    wg1_sb = sb3("wg1", [128, 8 * 16], BF16)
    wg2_sb = sb3("wg2", [16, 8 * 128], BF16)
    bg1_col = sb3("bg1", [16, 1], F32)
    bg2s_col = sb3("bg2s", [128, 8], F32)
    obf_slots = Slots(pp, 2)
    al_slots = Slots(pp, 2)

    mk = pp.block()
    pp.op("sync", "dma_start", out=bg1_col[:], in_=inp["D_bg1_col"], inc=ld_s.inc(16))
    pp.op("sync", "dma_start", out=bg2s_col[:], in_=inp["D_bg2_cols"], inc=ld_s.inc(16))
    pp.wait("vector", ld_s); pp.wait("vector", ld_g)
    pp.op("vector", "tensor_scalar_mul", bg2s_col[:], bg2s_col[:], 1.0 / 16.0)
    pp.op("gpsimd", "dma_start", out=wg1_sb[:], in_=inp["D_wg1T"], inc=ld_g.inc(16))
    pp.op("gpsimd", "dma_start", out=wg2_sb[:], in_=inp["D_wg2T"], inc=ld_g.inc(16))
    for kc in range(NC):
        pp.op("gpsimd", "dma_start", out=wq_sb[:, kc * D:(kc + 1) * D],
              in_=inp["D_wqT"][kc * 128:(kc + 1) * 128, :], inc=ld_g.inc(16))
        pp.op("gpsimd", "dma_start", out=wk_sb[:, kc * D:(kc + 1) * D],
              in_=inp["D_wkT"][kc * 128:(kc + 1) * 128, :], inc=ld_g.inc(16))
        pp.op("gpsimd", "dma_start", out=wv_sb[:, kc * D:(kc + 1) * D],
              in_=inp["D_wvT"][kc * 128:(kc + 1) * 128, :], inc=ld_g.inc(16))
    pp.block_end(mk)

    for qt in range(NQT):
        mk = pp.block()
        t0 = qt * QT
        for c in range(NC):
            pp.op("sync", "dma_start", out=xn_sl[:, c * QT:(c + 1) * QT],
                  in_=xn2T[c * 128:(c + 1) * 128, t0:t0 + QT], inc=ld_s.inc(16))
        pp.wait("tensor", ld_s); pp.wait("tensor", ld_g)
        for wname, wsb, dstT in (("q", wq_sb, qTd), ("k", wk_sb, kpub), ("v", wv_sb, vpub)):
            for oc in range(NC):
                bo = banks.get()
                pairs = [(wsb[:, kc * D + oc * 128: kc * D + (oc + 1) * 128],
                          xn_sl[:, kc * QT:(kc + 1) * QT]) for kc in range(NC)]
                mm_group(pp, bo.t[:], pairs, inc=mm.inc())
                pp.wait("scalar", mm)
                os_ = obf_slots.get("scalar")
                osl = out_bf[:, os_ * QT:(os_ + 1) * QT]
                pp.op("scalar", "copy", osl, bo.t[:], inc=ev.inc())
                banks.rel(bo, ev)
                pp.wait("sync", ev)
                pp.op("sync", "dma_start", out=dstT[oc * 128:(oc + 1) * 128, t0:t0 + QT],
                      in_=osl, inc=st_s.inc(16))
                obf_slots.used(os_, st_s)
        bgh = banks.get()
        pairs = [(wg1_sb[:, kc * 16:(kc + 1) * 16], xn_sl[:, kc * QT:(kc + 1) * QT])
                 for kc in range(NC)]
        mm_group(pp, bgh.t[0:16, :], pairs, inc=mm.inc())
        pp.wait("scalar", mm)
        pp.op("scalar", "activation", gh_sb[:], bgh.t[0:16, :], AF.Silu,
              bias=bg1_col[0:16, 0:1], inc=ev.inc())
        banks.rel(bgh, ev)
        pp.wait("tensor", ev)
        for oc in range(NC):
            bo = banks.get()
            pp.op("tensor", "matmul", bo.t[:], wg2_sb[0:16, oc * 128:(oc + 1) * 128],
                  gh_sb[:], start=True, stop=True, inc=mm.inc())
            pp.wait("scalar", mm)
            as_ = al_slots.get("scalar")
            asl = al_sb[:, as_ * QT:(as_ + 1) * QT]
            pp.op("scalar", "activation", asl, bo.t[:], AF.Sigmoid,
                  scale=1.0 / 16.0, bias=bg2s_col[:, oc:oc + 1], inc=ev.inc())
            banks.rel(bo, ev)
            pp.wait("sync", ev)
            pp.op("sync", "dma_start", out=apub[oc * 128:(oc + 1) * 128, t0:t0 + QT],
                  in_=asl, inc=st_s.inc(16))
            al_slots.used(as_, st_s)
        pp.block_end(mk)
    es3.close()

    mk = pp.block()
    pp.wait("gpsimd", st_s); pp.wait("gpsimd", st_g)
    pp.op("gpsimd", "collective_compute", "AllGather", ALU.bypass,
          replica_groups=[list(range(8))], ins=[kpub], outs=[kG], inc=ccK.inc())
    pp.op("gpsimd", "collective_compute", "AllGather", ALU.bypass,
          replica_groups=[list(range(8))], ins=[vpub], outs=[vG], inc=ccV.inc())
    pp.op("gpsimd", "collective_compute", "AllGather", ALU.bypass,
          replica_groups=[list(range(8))], ins=[apub], outs=[aG], inc=ccA.inc())
    pp.wait("gpsimd", ccK, 1)
    pp.wait("gpsimd", ccV, 1)
    pp.wait("gpsimd", ccA, 1)
    pp.op("gpsimd", "nop")
    pp.block_end(mk)

    es4 = ExitStack()
    sb4 = lambda name, shape, dt: es4.enter_context(nc.sbuf_tensor("D4" + name, shape, dt))
    q_sl = sb4("q", [128, 8 * QT], BF16)
    k_sl = sb4("k", [128, 2 * 8 * QT], BF16)
    v_sl = sb4("v", [128, 2 * 8 * QT], BF16)
    a_sl = sb4("a", [128, 2 * 8 * QT], BF16)
    xg_sl = sb4("xg", [128, 8 * QT], BF16)
    Rrun = sb4("R", [128, 8 * QT], BF16)
    prod = sb4("prod", [128, QT], BF16)
    o_acc = sb4("o", [128, 8 * QT], F32)
    w_sb = sb4("w", [2, 8 * QT], BF16)
    tmp4 = sb4("tmp4", [128, QT], F32)
    sq4 = sb4("sq4", [128, QT], F32)
    rows4 = sb4("rows4", [2, 8 * QT], F32)
    og_sb = sb4("og", [128, 8 * QT], BF16)
    sqb4 = sb4("sqb4", [128, QT], BF16)
    obf4 = sb4("obf4", [128, QT], BF16)
    g_sb = sb4("g", [128, 8 * QT], BF16)
    t2_sl = sb4("t2s", [128, 8 * QT], BF16)
    y_sb = sb4("y", [128, 2 * QT], F32)
    wog_sb = sb4("wog", [128, 8 * D], BF16)
    wo_sb = sb4("wo", [128, 8 * D], BF16)
    bog_col = sb4("bog", [128, 8], F32)
    y_slots = Slots(pp, 2)
    kva_war = [None, None]
    kvald = [pp.sem("kvald0"), pp.sem("kvald1")]

    mk = pp.block()
    pp.op("sync", "dma_start", out=bog_col[:], in_=inp["D_bog_cols"], inc=ld_s.inc(16))
    for kc in range(NC):
        pp.op("gpsimd", "dma_start", out=wog_sb[:, kc * D:(kc + 1) * D],
              in_=inp["D_wogT"][kc * 128:(kc + 1) * 128, :], inc=ld_g.inc(16))
        pp.op("gpsimd", "dma_start", out=wo_sb[:, kc * D:(kc + 1) * D],
              in_=inp["D_woT"][kc * 128:(kc + 1) * 128, :], inc=ld_g.inc(16))
    pp.block_end(mk)

    for qt in range(NQT):
        mk = pp.block()
        t0 = qt * QT
        pp.op("vector", "memset", o_acc[:], 0.0)
        pp.op("vector", "memset", Rrun[:], 1.0, inc=ev.inc())
        for c in range(NC):
            pp.op("sync", "dma_start", out=q_sl[:, c * QT:(c + 1) * QT],
                  in_=qTd[c * 128:(c + 1) * 128, t0:t0 + QT], inc=ld_s.inc(16))
        for s in range(7, -1, -1):
            ks_ = (7 - s) % 2
            koff = ks_ * 8 * QT
            ksl = lambda c, koff=koff: k_sl[:, koff + c * QT: koff + (c + 1) * QT]
            vsl = lambda c, koff=koff: v_sl[:, koff + c * QT: koff + (c + 1) * QT]
            asl = lambda c, koff=koff: a_sl[:, koff + c * QT: koff + (c + 1) * QT]
            if kva_war[ks_] is not None:
                for (ss, vv) in kva_war[ks_]:
                    pp.wait("sync", ss, vv)
                    pp.wait("gpsimd", ss, vv)
                kva_war[ks_] = None
            for c in range(NC):
                pp.op("sync", "dma_start", out=ksl(c),
                      in_=kG[s * D + c * 128: s * D + (c + 1) * 128, t0:t0 + QT],
                      inc=kvald[ks_].inc(16))
                pp.op("sync", "dma_start", out=vsl(c),
                      in_=vG[s * D + c * 128: s * D + (c + 1) * 128, t0:t0 + QT],
                      inc=kvald[ks_].inc(16))
                pp.op("gpsimd", "dma_start", out=asl(c),
                      in_=aG[s * D + c * 128: s * D + (c + 1) * 128, t0:t0 + QT],
                      inc=kvald[ks_].inc(16))
            pp.wait("vector", kvald[ks_])
            if s == 7:
                pp.wait("vector", ld_s); pp.wait("vector", ld_g)
            bw = banks.get()
            for c in range(NC):
                pp.op("vector", "tensor_scalar", asl(c),
                      asl(c), smask[:, s:s + 1], imask[:, s:s + 1],
                      ALU.mult, ALU.add)
                pp.op("vector", "tensor_mul", prod[:], q_sl[:, c * QT:(c + 1) * QT],
                      Rrun[:, c * QT:(c + 1) * QT])
                pp.op("vector", "tensor_mul", prod[:], prod[:], ksl(c),
                      inc=ev.inc())
                pp.wait("tensor", ev)
                pp.op("tensor", "matmul", bw.t[0:2, 0:QT], bd2[:], prod[:],
                      start=True, stop=True, inc=mm.inc())
                pp.wait("vector", mm)
                pp.op("vector", "tensor_scalar_mul", w_sb[0:2, c * QT:(c + 1) * QT],
                      bw.t[0:2, 0:QT], wm2[0:2, s:s + 1], inc=ev.inc())
                pp.op("vector", "tensor_mul", Rrun[:, c * QT:(c + 1) * QT],
                      Rrun[:, c * QT:(c + 1) * QT], asl(c))
            banks.rel(bw, ev)
            pp.wait("tensor", ev)
            for c in range(NC):
                be = banks.get()
                pp.op("tensor", "matmul", be.t[:], ex2[:, :],
                      w_sb[0:2, c * QT:(c + 1) * QT], start=True, stop=True, inc=mm.inc())
                pp.wait("vector", mm)
                pp.op("vector", "tensor_mul", tmp4[:], be.t[:], vsl(c))
                pp.op("vector", "tensor_add", o_acc[:, c * QT:(c + 1) * QT],
                      o_acc[:, c * QT:(c + 1) * QT], tmp4[:],
                      inc=(ev.inc() if c == NC - 1 else None))
                banks.rel(be, ev)
            kva_war[ks_] = ev.now_list()
        if "Do" in DBG:
            pp.wait("gpsimd", ev)
            for c in range(NC):
                pp.op("gpsimd", "dma_start", out=dram["dbgT"][c * 128:(c + 1) * 128, t0:t0 + QT],
                      in_=o_acc[:, c * QT:(c + 1) * QT], inc=st_g.inc(16))
        for c in range(NC):
            bsum = banks.get(); bsq = banks.get()
            osl = o_acc[:, c * QT:(c + 1) * QT]
            pp.wait("scalar", ev)
            pp.op("scalar", "activation", sq4[:], osl, AF.Square, inc=sqsem.inc())
            pp.wait("tensor", ev)
            obf = obf4[:]
            pp.op("vector", "tensor_copy", obf, osl, inc=ev.inc())
            pp.wait("tensor", sqsem)
            pp.wait("tensor", ev)
            pp.op("tensor", "matmul", bsum.t[0:2, 0:QT], bd2[:], obf, start=True, stop=True)
            sq4b = sqb4[:]
            pp.wait("vector", sqsem)
            pp.op("vector", "tensor_copy", sq4b, sq4[:], inc=ev.inc())
            pp.wait("tensor", ev)
            pp.op("tensor", "matmul", bsq.t[0:2, 0:QT], bd2[:], sq4b, start=True, stop=True,
                  inc=mm.inc())
            RW = lambda r: rows4[0:2, r * QT:(r + 1) * QT]
            pp.wait("vector", mm)
            pp.op("vector", "tensor_scalar_mul", RW(0), bsum.t[0:2, 0:QT], 1.0 / 64)
            pp.op("vector", "tensor_scalar_mul", RW(1), bsq.t[0:2, 0:QT], 1.0 / 64, inc=ev.inc())
            banks.rel(bsum, ev); banks.rel(bsq, ev)
            pp.op("vector", "tensor_mul", RW(2), RW(0), RW(0))
            pp.op("vector", "tensor_sub", RW(2), RW(1), RW(2), inc=ev.inc())
            pp.wait("scalar", ev)
            pp.op("scalar", "activation", RW(3), RW(2), AF.Ln, bias=eps5[0:2, 0:1])
            pp.op("scalar", "activation", RW(3), RW(3), AF.Exp, scale=-0.5, inc=ev.inc())
            pp.wait("vector", ev)
            pp.op("vector", "tensor_mul", RW(4), RW(0), RW(3), inc=ev.inc())
            bmu = banks.get()
            pp.wait("tensor", ev)
            pp.op("tensor", "matmul", bmu.t[0:128, 0:QT], ex2f[:, :],
                  rows4[0:2, 3 * QT:4 * QT], start=True, stop=True)
            bmm_ = banks.get()
            pp.op("tensor", "matmul", bmm_.t[0:128, 0:QT], ex2f[:, :],
                  rows4[0:2, 4 * QT:5 * QT], start=True, stop=True, inc=mm.inc())
            pp.wait("vector", mm)
            pp.op("vector", "tensor_mul", tmp4[:], osl, bmu.t[:])
            pp.op("vector", "tensor_sub", g_sb[:, c * QT:(c + 1) * QT], tmp4[:], bmm_.t[:],
                  inc=ev.inc())
            banks.rel(bmu, ev); banks.rel(bmm_, ev)
        if "Doln" in DBG:
            pp.wait("gpsimd", ev)
            for c in range(NC):
                pp.op("gpsimd", "dma_start", out=dram["dbgT"][c * 128:(c + 1) * 128, t0:t0 + QT],
                      in_=g_sb[:, c * QT:(c + 1) * QT], inc=st_g.inc(16))
        for c in range(NC):
            pp.op("sync", "dma_start", out=xg_sl[:, c * QT:(c + 1) * QT],
                  in_=xn2T[c * 128:(c + 1) * 128, t0:t0 + QT], inc=ld_s.inc(16))
            pp.op("sync", "dma_start", out=t2_sl[:, c * QT:(c + 1) * QT],
                  in_=t2T[c * 128:(c + 1) * 128, t0:t0 + QT], inc=ld_s.inc(16))
        pp.wait("tensor", ld_s); pp.wait("tensor", ld_g)
        for oc in range(NC):
            bo = banks.get()
            pairs = [(wog_sb[:, kc * D + oc * 128: kc * D + (oc + 1) * 128],
                      xg_sl[:, kc * QT:(kc + 1) * QT]) for kc in range(NC)]
            mm_group(pp, bo.t[:], pairs, inc=mm.inc())
            pp.wait("scalar", mm)
            pp.op("scalar", "activation", og_sb[:, oc * QT:(oc + 1) * QT], bo.t[:],
                  AF.Silu, bias=bog_col[:, oc:oc + 1], inc=ev.inc())
            banks.rel(bo, ev)
        pp.wait("vector", ev)
        for c in range(NC):
            pp.op("vector", "tensor_mul", g_sb[:, c * QT:(c + 1) * QT],
                  g_sb[:, c * QT:(c + 1) * QT], og_sb[:, c * QT:(c + 1) * QT],
                  inc=(ev.inc() if c == NC - 1 else None))
        if "Dog" in DBG:
            pp.wait("gpsimd", ev)
            for c in range(NC):
                pp.op("gpsimd", "dma_start", out=dram["dbgT"][D + c * 128:D + (c + 1) * 128, t0:t0 + QT],
                      in_=og_sb[:, c * QT:(c + 1) * QT], inc=st_g.inc(16))
        pp.wait("tensor", ev)
        for oc in range(NC):
            bo = banks.get()
            pairs = [(wo_sb[:, kc * D + oc * 128: kc * D + (oc + 1) * 128],
                      g_sb[:, kc * QT:(kc + 1) * QT]) for kc in range(NC)]
            mm_group(pp, bo.t[:], pairs, inc=mm.inc())
            pp.wait("vector", mm)
            ys = y_slots.get("vector")
            ysl = y_sb[:, ys * QT:(ys + 1) * QT]
            pp.op("vector", "tensor_add", ysl, bo.t[:], t2_sl[:, oc * QT:(oc + 1) * QT],
                  inc=ev.inc())
            banks.rel(bo, ev)
            pp.wait("sync", ev)
            pp.op("sync", "dma_start", out=yT[oc * 128:(oc + 1) * 128, t0:t0 + QT],
                  in_=ysl, inc=st_s.inc(16))
            y_slots.used(ys, st_s)
        pp.block_end(mk)
    es4.close()

    es5 = ExitStack()
    sb5 = lambda name, shape, dt: es5.enter_context(nc.sbuf_tensor("D5" + name, shape, dt))
    y_sl = sb5("y", [128, 8 * QT], F32)
    sq5 = sb5("sq", [128, 2 * QT], F32)
    rows5 = sb5("rows", [1, 4 * QT], F32)
    nrm_sb = sb5("nrm", [128, 8 * QT], BF16)
    hg_sb = sb5("hg", [128, QT], BF16)
    h_sb = sb5("h", [128, 2 * QT], BF16)
    wgate_sb = sb5("wgate", [128, 8 * 4096], BF16)
    wup_sb = sb5("wup", [128, 8 * 4096], BF16)
    sq5_slots = Slots(pp, 2)
    h_slots = Slots(pp, 2)
    R5 = lambda r: rows5[0:1, r * QT:(r + 1) * QT]

    mk = pp.block()
    for kc in range(NC):
        pp.op("gpsimd", "dma_start", out=wgate_sb[:, kc * 4096:(kc + 1) * 4096],
              in_=inp["D_wgateT"][kc * 128:(kc + 1) * 128, :], inc=ld_g.inc(16))
        pp.op("gpsimd", "dma_start", out=wup_sb[:, kc * 4096:(kc + 1) * 4096],
              in_=inp["D_wupT"][kc * 128:(kc + 1) * 128, :], inc=ld_g.inc(16))
    pp.block_end(mk)

    for qt in range(NQT):
        mk = pp.block()
        t0 = qt * QT
        for c in range(NC):
            pp.op("sync", "dma_start", out=y_sl[:, c * QT:(c + 1) * QT],
                  in_=yT[c * 128:(c + 1) * 128, t0:t0 + QT], inc=ld_s.inc(16))
        pp.wait("scalar", ld_s); pp.wait("scalar", ld_g)
        pp.wait("tensor", ld_s); pp.wait("tensor", ld_g)
        bsq = banks.get()
        for c in range(NC):
            s_ = sq5_slots.get("scalar")
            ssl = sq5[:, s_ * QT:(s_ + 1) * QT]
            pp.op("scalar", "activation", ssl, y_sl[:, c * QT:(c + 1) * QT], AF.Square,
                  inc=sqsem.inc())
            pp.wait("tensor", sqsem)
            pp.op("tensor", "matmul", bsq.t[0:1, :], ones_col[:], ssl,
                  start=(c == 0), stop=(c == NC - 1), inc=mm.inc())
            sq5_slots.used(s_, mm)
        pp.wait("vector", mm)
        pp.op("vector", "tensor_scalar_mul", R5(0), bsq.t[0:1, :], 1.0 / D, inc=ev.inc())
        banks.rel(bsq, ev)
        pp.wait("scalar", ev)
        pp.op("scalar", "activation", R5(1), R5(0), AF.Ln)
        pp.op("scalar", "activation", R5(1), R5(1), AF.Exp, scale=0.5, inc=ev.inc())
        pp.wait("vector", ev)
        pp.op("vector", "tensor_scalar_add", R5(1), R5(1), 1e-6)
        pp.op("vector", "reciprocal", R5(1), R5(1), inc=ev.inc())
        bs_ = banks.get()
        pp.wait("tensor", ev)
        pp.op("tensor", "matmul", bs_.t[:], ones_row[:], R5(1), start=True, stop=True, inc=mm.inc())
        pp.wait("vector", mm)
        for c in range(NC):
            pp.op("vector", "scalar_tensor_tensor", nrm_sb[:, c * QT:(c + 1) * QT],
                  y_sl[:, c * QT:(c + 1) * QT], s2_col[:, c:c + 1], bs_.t[:],
                  ALU.mult, ALU.mult, inc=(ev.inc() if c == NC - 1 else None))
        banks.rel(bs_, ev)
        pp.wait("tensor", ev)
        for oc in range(32):
            bg = banks.get()
            pairs = [(wgate_sb[:, kc * 4096 + oc * 128: kc * 4096 + (oc + 1) * 128],
                      nrm_sb[:, kc * QT:(kc + 1) * QT]) for kc in range(NC)]
            mm_group(pp, bg.t[:], pairs, inc=mm.inc())
            pp.wait("scalar", mm)
            pp.op("scalar", "activation", hg_sb[:], bg.t[:], AF.Silu, inc=ev.inc())
            banks.rel(bg, ev)
            bu = banks.get()
            pairs = [(wup_sb[:, kc * 4096 + oc * 128: kc * 4096 + (oc + 1) * 128],
                      nrm_sb[:, kc * QT:(kc + 1) * QT]) for kc in range(NC)]
            mm_group(pp, bu.t[:], pairs, inc=mm.inc())
            hs = h_slots.get("vector")
            hsl = h_sb[:, hs * QT:(hs + 1) * QT]
            pp.wait("vector", mm)
            pp.wait("vector", ev)
            pp.op("vector", "tensor_mul", hsl, hg_sb[:], bu.t[:], inc=ev.inc())
            banks.rel(bu, ev)
            pp.wait("sync", ev)
            pp.op("sync", "dma_start", out=hT[oc * 128:(oc + 1) * 128, t0:t0 + QT],
                  in_=hsl, inc=st_s.inc(16))
            h_slots.used(hs, st_s)
        pp.block_end(mk)
    es5.close()

    es6 = ExitStack()
    sb6 = lambda name, shape, dt: es6.enter_context(nc.sbuf_tensor("D6" + name, shape, dt))
    h_sl = sb6("h", [128, 32 * QT], BF16)
    x2r6 = sb6("x2r", [128, 8 * QT], F32)
    y_sl2 = sb6("y", [128, 8 * QT], F32)
    z_sb = sb6("z", [128, 8 * QT], F32)
    sq6 = sb6("sq", [128, 2 * QT], F32)
    tmp6 = sb6("tmp", [128, QT], F32)
    rows6 = sb6("rows", [1, 5 * QT], F32)
    x3_sb = sb6("x3", [128, 2 * QT], F32)
    wdown_sb = sb6("wdown", [128, 32 * D], BF16)
    sq6_slots = Slots(pp, 2)
    x3_slots = Slots(pp, 2)
    R6 = lambda r: rows6[0:1, r * QT:(r + 1) * QT]

    mk = pp.block()
    for kc in range(32):
        pp.op("gpsimd", "dma_start", out=wdown_sb[:, kc * D:(kc + 1) * D],
              in_=inp["D_wdownT"][kc * 128:(kc + 1) * 128, :], inc=ld_g.inc(16))
    pp.block_end(mk)

    for qt in range(NQT):
        mk = pp.block()
        t0 = qt * QT
        for c in range(32):
            pp.op("sync", "dma_start", out=h_sl[:, c * QT:(c + 1) * QT],
                  in_=hT[c * 128:(c + 1) * 128, t0:t0 + QT], inc=ld_s.inc(16))
        for c in range(NC):
            pp.op("sync", "dma_start", out=y_sl2[:, c * QT:(c + 1) * QT],
                  in_=yT[c * 128:(c + 1) * 128, t0:t0 + QT], inc=ld_s.inc(16))
        pp.wait("tensor", ld_s); pp.wait("tensor", ld_g)
        for oc in range(NC):
            bo = banks.get()
            pairs = [(wdown_sb[:, kc * D + oc * 128: kc * D + (oc + 1) * 128],
                      h_sl[:, kc * QT:(kc + 1) * QT]) for kc in range(32)]
            mm_group(pp, bo.t[:], pairs, inc=mm.inc())
            pp.wait("vector", mm)
            pp.wait("vector", ld_s); pp.wait("vector", ld_g)
            pp.op("vector", "tensor_add", z_sb[:, oc * QT:(oc + 1) * QT], bo.t[:],
                  y_sl2[:, oc * QT:(oc + 1) * QT], inc=ev.inc())
            banks.rel(bo, ev)
        pp.wait("scalar", ev)
        pp.wait("tensor", ev)
        src = [z_sb[:, c * QT:(c + 1) * QT] for c in range(NC)]
        ln_stats_rows(pp, banks, sq6_slots, sq6, ones_col, src, R6, mm, ev, sqsem,
                      eps5[0:1, 0:1])
        brstd = banks.get(); bmmr = banks.get()
        pp.wait("tensor", ev)
        pp.op("tensor", "matmul", brstd.t[:], ones_row[:], R6(3), start=True, stop=True)
        pp.op("tensor", "matmul", bmmr.t[:], ones_row[:], R6(4), start=True, stop=True, inc=mm.inc())
        for c in range(NC):
            pp.op("sync", "dma_start", out=x2r6[:, c * QT:(c + 1) * QT],
                  in_=x2T[c * 128:(c + 1) * 128, t0:t0 + QT], inc=ld_s.inc(16))
        pp.wait("vector", mm)
        pp.wait("vector", ld_s); pp.wait("vector", ld_g)
        for c in range(NC):
            s3_ = x3_slots.get("vector")
            x3sl = x3_sb[:, s3_ * QT:(s3_ + 1) * QT]
            pp.op("vector", "tensor_mul", tmp6[:], z_sb[:, c * QT:(c + 1) * QT], brstd.t[:])
            pp.op("vector", "tensor_sub", tmp6[:], tmp6[:], bmmr.t[:])
            pp.op("vector", "tensor_add", x3sl, tmp6[:], x2r6[:, c * QT:(c + 1) * QT],
                  inc=ev.inc())
            pp.wait("sync", ev)
            pp.op("sync", "dma_start", out=x3T[c * 128:(c + 1) * 128, t0:t0 + QT],
                  in_=x3sl, inc=st_s.inc(16))
            x3_slots.used(s3_, st_s)
        banks.rel(brstd, ev); banks.rel(bmmr, ev)
        pp.block_end(mk)
    es6.close()
    pp.wait("sync", st_s); pp.wait("gpsimd", st_g)


def emit_phaseE(pp, ctx):
    nc = ctx["nc"]; inp = ctx["inp"]; dram = ctx["dram"]
    es = pp.es
    sb = lambda name, shape, dt: es.enter_context(nc.sbuf_tensor("E" + name, shape, dt))

    banks = BankPool(pp, [es.enter_context(nc.psum_tensor(f"Ebk{i}", [128, 512], F32))
                          for i in range(8)])
    ld_s = pp.sem("ld_s"); ld_g = pp.sem("ld_g")
    st_s = pp.sem("st_s"); st_g = pp.sem("st_g")
    mm = pp.sem("mm"); ev = EvFacade(pp)
    sqsem = pp.sem("sq")

    x3T = dram["x3T"]; xxA = dram["xxA"]; xxB = dram["xxB"]; hT = dram["hT"]
    x4T = dram["x4T"]

    ones_col = sb("ones_col", [128, 1], F32)
    ones_row = sb("ones_row", [1, 128], F32)
    eps5 = sb("eps5", [128, 1], F32)
    cshE = sb("cshE", [128, 1], F32)
    permE = sb("permE", [128, 128], BF16)
    bd2 = sb("bd2", [128, 2], BF16)
    ex2f = sb("ex2f", [2, 128], F32)
    Cq = sb("Cq", [128, T], BF16)
    Sq = sb("Sq", [128, T], BF16)
    Ck = sb("Ck", [128, T], BF16)
    Sk = sb("Sk", [128, T], BF16)
    maskE_sb = sb("maskE", [128, 6 * QT], BF16)
    qs_col = sb("qs", [128, 6], F32)
    ks_col = sb("ks", [128, 6], F32)
    lng_col = sb("lng", [128, 6 * 8], F32)
    lnb_col = sb("lnb", [128, 6 * 8], F32)
    b1_col = sb("b1", [128, 6 * 44], F32)
    b2_col = sb("b2", [128, 6 * 8], F32)

    mk = pp.block()
    pp.op("vector", "memset", ones_col[:], 1.0)
    pp.op("vector", "memset", ones_row[:], 1.0)
    pp.op("vector", "memset", eps5[:], 1e-5)
    pp.op("vector", "memset", cshE[:], -C_SHIFT_E)
    pp.op("gpsimd", "dma_start", out=permE[:], in_=inp["E_permE"], inc=ld_g.inc(16))
    pp.op("gpsimd", "dma_start", out=bd2[:], in_=inp["D_bd2"], inc=ld_g.inc(16))
    pp.op("gpsimd", "dma_start", out=ex2f[:], in_=inp["D_ex2f"], inc=ld_g.inc(16))
    pp.op("gpsimd", "dma_start", out=Cq[:], in_=inp["E_Cq"], inc=ld_g.inc(16))
    pp.op("gpsimd", "dma_start", out=Sq[:], in_=inp["E_Sq"], inc=ld_g.inc(16))
    pp.op("gpsimd", "dma_start", out=Ck[:], in_=inp["E_Ck"], inc=ld_g.inc(16))
    pp.op("gpsimd", "dma_start", out=Sk[:], in_=inp["E_Sk"], inc=ld_g.inc(16))
    pp.op("sync", "dma_start", out=maskE_sb[:], in_=inp["maskE"], inc=ld_s.inc(16))
    pp.op("sync", "dma_start", out=qs_col[:], in_=inp["E_qs_cols"], inc=ld_s.inc(16))
    pp.op("sync", "dma_start", out=ks_col[:], in_=inp["E_ks_cols"], inc=ld_s.inc(16))
    pp.op("sync", "dma_start", out=lng_col[:], in_=inp["E_lng_cols"], inc=ld_s.inc(16))
    pp.op("sync", "dma_start", out=lnb_col[:], in_=inp["E_lnb_cols"], inc=ld_s.inc(16))
    pp.op("sync", "dma_start", out=b1_col[:], in_=inp["E_b1p_cols"], inc=ld_s.inc(16))
    pp.op("sync", "dma_start", out=b2_col[:], in_=inp["E_b2_cols"], inc=ld_s.inc(16))
    pp.block_end(mk)

    def load_shifted_bf(dst_chunks, srcT, qt):
        t0 = qt * QT
        for c in range(NC):
            dsl = dst_chunks(c)
            if c < 4:
                pp.op("gpsimd", "dma_start", out=dsl,
                      in_=srcT[c * 128:(c + 1) * 128, t0:t0 + QT], inc=ld_g.inc(16))
            elif qt == 0:
                pp.op("vector", "memset", dsl[:, 0:1], 0.0, inc=ev.inc())
                pp.op("gpsimd", "dma_start", out=dsl[:, 1:QT],
                      in_=srcT[c * 128:(c + 1) * 128, 0:QT - 1], inc=ld_g.inc(16))
            else:
                pp.op("gpsimd", "dma_start", out=dsl,
                      in_=srcT[c * 128:(c + 1) * 128, t0 - 1:t0 + QT - 1], inc=ld_g.inc(16))

    es0 = ExitStack()
    sb0 = lambda name, shape, dt: es0.enter_context(nc.sbuf_tensor("E0" + name, shape, dt))
    x_sb = sb0("x", [128, 8 * QT], F32)
    sq_sb = sb0("sq", [128, 2 * QT], F32)
    tmp_sb = sb0("tmp", [128, QT], F32)
    rows = sb0("rows", [1, 5 * QT], F32)
    t3_sb = sb0("t3", [128, 2 * QT], F32)
    sq_slots = Slots(pp, 2)
    t3_slots = Slots(pp, 2)
    R = lambda r: rows[0:1, r * QT:(r + 1) * QT]
    for qt in range(NQT):
        mk = pp.block()
        t0 = qt * QT
        for c in range(NC):
            pp.op("sync", "dma_start", out=x_sb[:, c * QT:(c + 1) * QT],
                  in_=x3T[c * 128:(c + 1) * 128, t0:t0 + QT], inc=ld_s.inc(16))
        pp.wait("scalar", ld_s); pp.wait("scalar", ld_g)
        pp.wait("tensor", ld_s); pp.wait("tensor", ld_g)
        src = [x_sb[:, c * QT:(c + 1) * QT] for c in range(NC)]
        ln_stats_rows(pp, banks, sq_slots, sq_sb, ones_col, src, R, mm, ev, sqsem,
                      eps5[0:1, 0:1])
        brstd = banks.get(); bmmr = banks.get()
        pp.wait("tensor", ev)
        pp.op("tensor", "matmul", brstd.t[:], ones_row[:], R(3), start=True, stop=True)
        pp.op("tensor", "matmul", bmmr.t[:], ones_row[:], R(4), start=True, stop=True, inc=mm.inc())
        pp.wait("vector", mm)
        for c in range(NC):
            ts_ = t3_slots.get("vector")
            tsl = t3_sb[:, ts_ * QT:(ts_ + 1) * QT]
            pp.op("vector", "tensor_mul", tmp_sb[:], x_sb[:, c * QT:(c + 1) * QT], brstd.t[:])
            pp.op("vector", "tensor_sub", tsl, tmp_sb[:], bmmr.t[:], inc=ev.inc())
            pp.wait("sync", ev)
            pp.op("sync", "dma_start", out=xxA[c * 128:(c + 1) * 128, t0:t0 + QT],
                  in_=tsl, inc=st_s.inc(16))
            t3_slots.used(ts_, st_s)
        banks.rel(brstd, ev); banks.rel(bmmr, ev)
        pp.block_end(mk)
    es0.close()

    for li in range(E_LAYERS):
        esA = ExitStack()
        sbA = lambda name, shape, dt: esA.enter_context(
            nc.sbuf_tensor(f"EL{li}a" + name, shape, dt))
        xsb = sbA("xsb", [128, 8 * QT], BF16)
        xres = sbA("xres", [128, 8 * QT], F32)
        wqkv_sb = sbA("wqkv", [128, 8 * 3072], BF16)
        wo_sb = sbA("wo", [128, 8 * D], BF16)
        q_fm = sbA("qfm", [128, 8 * QT], BF16)
        k_ring = sbA("kring", [128, NC * 8 * 128], BF16)
        v_ring = sbA("vring", [128, 8 * 1040], BF16)
        qa_sb = sbA("qa", [128, 2 * QT], BF16)
        sqE = sbA("sqE", [128, QT], F32)
        sqEb = sbA("sqEb", [128, QT], BF16)
        rowsE = sbA("rowsE", [2, 2 * QT], F32)
        ropt1 = sbA("ropt1", [128, QT], F32)
        ropt2 = sbA("ropt2", [128, QT], F32)
        exp_sb = sbA("exp", [128, 2 * QT], F32)
        attnT_sb = sbA("attnT", [128, 3 * QT], BF16)
        ctxb_sb = sbA("ctxb", [128, 8 * QT], BF16)
        dnr_sb = sbA("dnr", [1, 2 * QT], F32)
        y_sb = sbA("y", [128, QT], F32)
        qa_slots = Slots(pp, 2)
        exp_slots = Slots(pp, 2)
        at_slots = Slots(pp, 3)
        y_slots = Slots(pp, 1)
        RE = lambda r: rowsE[0:2, r * QT:(r + 1) * QT]

        mk = pp.block()
        pp.op("vector", "memset",
              v_ring.ap().rearrange("p (s h j) -> p s h j", s=8, h=NH, j=65)[:, :, :, 64:65],
              1.0)
        for kc in range(NC):
            pp.op("gpsimd", "dma_start", out=wqkv_sb[:, kc * 3072:(kc + 1) * 3072],
                  in_=inp["E_Wqkv"][li * D + kc * 128: li * D + (kc + 1) * 128, :],
                  inc=ld_g.inc(16))
            pp.op("gpsimd", "dma_start", out=wo_sb[:, kc * D:(kc + 1) * D],
                  in_=inp["E_Wo"][li * D + kc * 128: li * D + (kc + 1) * 128, :],
                  inc=ld_g.inc(16))
        pp.block_end(mk)

        def tr_(c, tile):
            s = tile % 8
            return k_ring[:, c * 1024 + s * 128: c * 1024 + (s + 1) * 128]

        def kr_qt_(c, qt):
            s = (qt * 4) % 8
            return k_ring[:, c * 1024 + s * 128: c * 1024 + s * 128 + 512]

        def vr_(tile, h):
            s = tile % 8
            return v_ring[:, s * 1040 + h * 65: s * 1040 + h * 65 + 65]

        def vr_w_(tile, g):
            s = tile % 8
            base = v_ring[:, s * 1040 + 8 * g * 65: s * 1040 + (8 * g + 8) * 65]
            return base.rearrange("p (h j) -> p h j", h=8, j=65)[:, :, 0:64]

        def norm_rope(kind, oc, qt, bank, dst):
            t0 = qt * QT
            scol = (qs_col if kind == "q" else ks_col)[:, li:li + 1]
            Ct = (Cq if kind == "q" else Ck)
            St = (Sq if kind == "q" else Sk)
            qs_ = qa_slots.get("scalar")
            qasl = qa_sb[:, qs_ * QT:(qs_ + 1) * QT]
            pp.op("scalar", "copy", qasl, bank.t[:], inc=ev.inc())
            pp.wait("vector", ev)
            pp.op("vector", "tensor_mul", sqEb[:], qasl, qasl, inc=ev.inc())
            banks.rel(bank, ev)
            bsq = banks.get()
            pp.wait("tensor", ev)
            pp.op("tensor", "matmul", bsq.t[0:2, 0:QT], bd2[:], sqEb[:],
                  start=True, stop=True, inc=mm.inc())
            pp.wait("scalar", mm)
            pp.op("scalar", "activation", RE(0), bsq.t[0:2, 0:QT], AF.Ln, inc=ev.inc())
            banks.rel(bsq, ev)
            pp.op("scalar", "activation", RE(0), RE(0), AF.Exp, scale=0.5, inc=ev.inc())
            pp.wait("vector", ev)
            pp.op("vector", "tensor_scalar_max", RE(1), RE(0), 1e-12)
            pp.op("vector", "reciprocal", RE(1), RE(1), inc=ev.inc())
            bsc = banks.get()
            pp.wait("tensor", ev)
            pp.op("tensor", "matmul", bsc.t[:], ex2f[:], RE(1), start=True, stop=True,
                  inc=mm.inc())
            pp.wait("vector", mm)
            qa2_ = qa_slots.get("vector")
            zsl = qa_sb[:, qa2_ * QT:(qa2_ + 1) * QT]
            pp.op("vector", "scalar_tensor_tensor", zsl, qasl, scol, bsc.t[:],
                  ALU.mult, ALU.mult, inc=ev.inc())
            qa_slots.used(qs_, ev)
            banks.rel(bsc, ev)
            bp = banks.get()
            pp.wait("tensor", ev)
            pp.op("tensor", "matmul", bp.t[:], permE[:], zsl, start=True, stop=True,
                  inc=mm.inc())
            pp.wait("vector", mm)
            pp.op("vector", "tensor_mul", ropt1[:], bp.t[:], St[:, t0:t0 + QT])
            pp.op("vector", "tensor_mul", ropt2[:], zsl, Ct[:, t0:t0 + QT])
            pp.op("vector", "tensor_add", dst, ropt2[:], ropt1[:], inc=ev.inc())
            qa_slots.used(qa2_, ev)
            banks.rel(bp, ev)

        for qt in range(NQT):
            mk = pp.block()
            t0 = qt * QT
            load_shifted_bf(lambda c: xsb[:, c * QT:(c + 1) * QT], xxA, qt)
            for c in range(NC):
                pp.op("sync", "dma_start", out=xres[:, c * QT:(c + 1) * QT],
                      in_=xxA[c * 128:(c + 1) * 128, t0:t0 + QT], inc=ld_s.inc(16))
            pp.wait("tensor", ld_s); pp.wait("tensor", ld_g)
            pp.wait("tensor", ev)
            pipe2 = Pipe(depth=1)
            for kind in ("q", "k"):
                off = 0 if kind == "q" else D
                for oc in range(NC):
                    bo = banks.get()
                    pairs = [(wqkv_sb[:, kc * 3072 + off + oc * 128:
                                      kc * 3072 + off + (oc + 1) * 128],
                              xsb[:, kc * QT:(kc + 1) * QT]) for kc in range(NC)]
                    mm_group(pp, bo.t[:], pairs, inc=mm.inc())
                    mmv = mm.n
                    dst = (q_fm[:, oc * QT:(oc + 1) * QT] if kind == "q"
                           else kr_qt_(oc, qt))

                    def consume2(kind=kind, oc=oc, bo=bo, dst=dst, mmv=mmv):
                        pp.wait("scalar", mm, mmv)
                        norm_rope(kind, oc, qt, bo, dst)
                    pipe2.push(consume2)
            pipe2.flush()
            for i4 in range(4):
                tile = qt * 4 + i4
                for g in range(2):
                    bo = banks.get()
                    for kc in range(NC):
                        pp.op("tensor", "matmul", bo.t[:],
                              xsb[:, kc * QT + i4 * 128: kc * QT + (i4 + 1) * 128],
                              wqkv_sb[:, kc * 3072 + 2 * D + g * 512:
                                      kc * 3072 + 2 * D + (g + 1) * 512],
                              start=(kc == 0), stop=(kc == NC - 1),
                              inc=(mm.inc() if kc == NC - 1 else None))
                    pp.wait("vector", mm)
                    pp.op("vector", "tensor_copy", vr_w_(tile, g),
                          bo.t.ap().rearrange("p (h j) -> p h j", h=8, j=64), inc=ev.inc())
                    banks.rel(bo, ev)
            kbs_all = [(i, qt * 4 - 2 + i) for i in range(6)]
            kbs = [(i, kb) for i, kb in kbs_all if kb >= 0]
            for c in range(NC):
                bctxs = {}
                pipe = Pipe(depth=3)
                for hh in range(2):
                    h = 2 * c + hh
                    rowoff = hh * 64
                    bctxs[hh] = banks.get()
                    first = True
                    for i, kb in kbs:
                        bs = banks.get()
                        pp.op("tensor", "matmul", bs.t[:],
                              tr_(c, kb)[rowoff:rowoff + 64, :],
                              q_fm[:, c * QT:(c + 1) * QT][rowoff:rowoff + 64, :],
                              start=True, stop=True, inc=mm.inc())
                        mmv = mm.n

                        def consume(bs=bs, h=h, hh=hh, i=i, kb=kb, mmv=mmv, first=first,
                                    last=(i == kbs[-1][0])):
                            pp.wait("scalar", mm, mmv)
                            es_ = exp_slots.get("scalar")
                            eslot = exp_sb[:, es_ * QT:(es_ + 1) * QT]
                            pp.op("scalar", "activation", eslot, bs.t[:], AF.Exp,
                                  scale=8.0, bias=cshE[:, 0:1], inc=ev.inc())
                            banks.rel(bs, ev)
                            pp.wait("vector", ev)
                            as_ = at_slots.get("vector")
                            aslot = attnT_sb[:, as_ * QT:(as_ + 1) * QT]
                            pp.op("vector", "scalar_tensor_tensor", aslot, eslot, 1e30,
                                  maskE_sb[:, i * QT:(i + 1) * QT], ALU.min, ALU.mult,
                                  inc=ev.inc())
                            exp_slots.used(es_, ev)
                            pp.wait("tensor", ev)
                            pp.op("tensor", "matmul", bctxs[hh].t[0:65, :], vr_(kb, h),
                                  aslot, start=first, stop=last, inc=mm.inc())
                            at_slots.used(as_, mm)
                        pipe.push(consume)
                        first = False
                pipe.flush()
                for hh in range(2):
                    rowoff = hh * 64
                    bctx = bctxs[hh]
                    pp.wait("scalar", mm)
                    pp.op("scalar", "copy",
                          ctxb_sb[rowoff:rowoff + 64, c * QT:(c + 1) * QT],
                          bctx.t[0:64, :], inc=ev.inc())
                    pp.wait("vector", mm)
                    pp.op("vector", "tensor_scalar_add", dnr_sb[0:1, hh * QT:(hh + 1) * QT],
                          bctx.t[64:65, :], 1e-30)
                    pp.op("vector", "reciprocal", dnr_sb[0:1, hh * QT:(hh + 1) * QT],
                          dnr_sb[0:1, hh * QT:(hh + 1) * QT], inc=ev.inc())
                    banks.rel(bctx, ev)
                br = banks.get()
                pp.wait("tensor", ev)
                pp.op("tensor", "matmul", br.t[0:64, :], ones_row[0:1, 0:64],
                      dnr_sb[0:1, 0:QT], start=True, stop=True)
                pp.op("tensor", "matmul", br.t[64:128, :], ones_row[0:1, 0:64],
                      dnr_sb[0:1, QT:2 * QT], start=True, stop=True, inc=mm.inc())
                pp.wait("vector", mm)
                pp.op("vector", "tensor_mul", ctxb_sb[:, c * QT:(c + 1) * QT],
                      ctxb_sb[:, c * QT:(c + 1) * QT], br.t[:], inc=ev.inc())
                banks.rel(br, ev)
            pp.wait("tensor", ev)
            for oc in range(NC):
                bo = banks.get()
                pairs = [(wo_sb[:, kc * D + oc * 128: kc * D + (oc + 1) * 128],
                          ctxb_sb[:, kc * QT:(kc + 1) * QT]) for kc in range(NC)]
                mm_group(pp, bo.t[:], pairs, inc=mm.inc())
                pp.wait("vector", mm)
                pp.wait("vector", ld_s); pp.wait("vector", ld_g)
                ys = y_slots.get("vector")
                ysl = y_sb[:, ys * QT:(ys + 1) * QT]
                pp.op("vector", "tensor_add", ysl, bo.t[:],
                      xres[:, oc * QT:(oc + 1) * QT], inc=ev.inc())
                banks.rel(bo, ev)
                pp.wait("sync", ev)
                pp.op("sync", "dma_start", out=xxB[oc * 128:(oc + 1) * 128, t0:t0 + QT],
                      in_=ysl, inc=st_s.inc(16))
                y_slots.used(ys, st_s)
            pp.block_end(mk)
        esA.close()

        esF = ExitStack()
        sbF = lambda name, shape, dt: esF.enter_context(
            nc.sbuf_tensor(f"EL{li}f" + name, shape, dt))
        xs_f = sbF("xs", [128, 8 * QT], F32)
        sqF = sbF("sq", [128, 2 * QT], F32)
        tmpF = sbF("tmp", [128, QT], F32)
        rowsF = sbF("rows", [1, 5 * QT], F32)
        xn_sb = sbF("xn", [128, 8 * QT], BF16)
        w1_sb = sbF("w1", [128, 8 * 5632], BF16)
        ga_sb = sbF("ga", [128, QT], BF16)
        gg_sb = sbF("gg", [128, QT], BF16)
        h_sb = sbF("h", [128, 2 * QT], BF16)
        sqF_slots = Slots(pp, 2)
        h_slots = Slots(pp, 2)
        RF = lambda r: rowsF[0:1, r * QT:(r + 1) * QT]

        mk = pp.block()
        for kc in range(NC):
            pp.op("gpsimd", "dma_start", out=w1_sb[:, kc * 5632:(kc + 1) * 5632],
                  in_=inp["E_W1p"][li * D + kc * 128: li * D + (kc + 1) * 128, :],
                  inc=ld_g.inc(16))
        pp.block_end(mk)

        def load_shifted_f32(dst_chunks, srcT, qt):
            t0 = qt * QT
            for c in range(NC):
                dsl = dst_chunks(c)
                if c < 4:
                    pp.op("sync", "dma_start", out=dsl,
                          in_=srcT[c * 128:(c + 1) * 128, t0:t0 + QT], inc=ld_s.inc(16))
                elif qt == 0:
                    pp.op("vector", "memset", dsl[:, 0:1], 0.0, inc=ev.inc())
                    pp.op("sync", "dma_start", out=dsl[:, 1:QT],
                          in_=srcT[c * 128:(c + 1) * 128, 0:QT - 1], inc=ld_s.inc(16))
                else:
                    pp.op("sync", "dma_start", out=dsl,
                          in_=srcT[c * 128:(c + 1) * 128, t0 - 1:t0 + QT - 1], inc=ld_s.inc(16))

        for qt in range(NQT):
            mk = pp.block()
            t0 = qt * QT
            load_shifted_f32(lambda c: xs_f[:, c * QT:(c + 1) * QT], xxB, qt)
            pp.wait("scalar", ld_s); pp.wait("scalar", ld_g)
            pp.wait("tensor", ld_s); pp.wait("tensor", ld_g)
            pp.wait("vector", ld_s); pp.wait("vector", ld_g)
            src = [xs_f[:, c * QT:(c + 1) * QT] for c in range(NC)]
            ln_stats_rows(pp, banks, sqF_slots, sqF, ones_col, src, RF, mm, ev, sqsem,
                          eps5[0:1, 0:1])
            brstd = banks.get(); bmmr = banks.get()
            pp.wait("tensor", ev)
            pp.op("tensor", "matmul", brstd.t[:], ones_row[:], RF(3), start=True, stop=True)
            pp.op("tensor", "matmul", bmmr.t[:], ones_row[:], RF(4), start=True, stop=True,
                  inc=mm.inc())
            pp.wait("vector", mm)
            for c in range(NC):
                pp.op("vector", "tensor_mul", tmpF[:], xs_f[:, c * QT:(c + 1) * QT], brstd.t[:])
                pp.op("vector", "tensor_sub", tmpF[:], tmpF[:], bmmr.t[:])
                pp.op("vector", "tensor_scalar", xn_sb[:, c * QT:(c + 1) * QT], tmpF[:],
                      lng_col[:, li * 8 + c: li * 8 + c + 1],
                      lnb_col[:, li * 8 + c: li * 8 + c + 1],
                      ALU.mult, ALU.add, inc=ev.inc())
            banks.rel(brstd, ev); banks.rel(bmmr, ev)
            pp.wait("tensor", ev)
            for j in range(22):
                ba = banks.get()
                pairs = [(w1_sb[:, kc * 5632 + j * 128: kc * 5632 + (j + 1) * 128],
                          xn_sb[:, kc * QT:(kc + 1) * QT]) for kc in range(NC)]
                mm_group(pp, ba.t[:], pairs, inc=mm.inc())
                pp.wait("vector", mm)
                pp.op("vector", "tensor_scalar_add", ga_sb[:], ba.t[:],
                      b1_col[:, li * 44 + j: li * 44 + j + 1], inc=ev.inc())
                banks.rel(ba, ev)
                bg = banks.get()
                pairs = [(w1_sb[:, kc * 5632 + 2816 + j * 128: kc * 5632 + 2816 + (j + 1) * 128],
                          xn_sb[:, kc * QT:(kc + 1) * QT]) for kc in range(NC)]
                mm_group(pp, bg.t[:], pairs, inc=mm.inc())
                pp.wait("scalar", mm)
                pp.op("scalar", "activation", gg_sb[:], bg.t[:], AF.Gelu,
                      bias=b1_col[:, li * 44 + 22 + j: li * 44 + 22 + j + 1], inc=ev.inc())
                banks.rel(bg, ev)
                hs = h_slots.get("vector")
                hsl = h_sb[:, hs * QT:(hs + 1) * QT]
                pp.wait("vector", ev)
                pp.op("vector", "tensor_mul", hsl, ga_sb[:], gg_sb[:], inc=ev.inc())
                pp.wait("sync", ev)
                pp.op("sync", "dma_start", out=hT[j * 128:(j + 1) * 128, t0:t0 + QT],
                      in_=hsl, inc=st_s.inc(16))
                h_slots.used(hs, st_s)
            pp.block_end(mk)
        esF.close()

        esG = ExitStack()
        sbG = lambda name, shape, dt: esG.enter_context(
            nc.sbuf_tensor(f"EL{li}g" + name, shape, dt))
        h_sl = sbG("h", [128, 22 * QT], BF16)
        xx_sl = sbG("xx", [128, 8 * QT], F32)
        z_sb = sbG("z", [128, 2 * QT], F32)
        w2_sb = sbG("w2", [128, 22 * D], BF16)
        z_slots = Slots(pp, 2)

        mk = pp.block()
        for kc in range(22):
            pp.op("gpsimd", "dma_start", out=w2_sb[:, kc * D:(kc + 1) * D],
                  in_=inp["E_W2p"][li * 2816 + kc * 128: li * 2816 + (kc + 1) * 128, :],
                  inc=ld_g.inc(16))
        pp.block_end(mk)

        for qt in range(NQT):
            mk = pp.block()
            t0 = qt * QT
            for c in range(22):
                pp.op("sync", "dma_start", out=h_sl[:, c * QT:(c + 1) * QT],
                      in_=hT[c * 128:(c + 1) * 128, t0:t0 + QT], inc=ld_s.inc(16))
            for c in range(NC):
                pp.op("sync", "dma_start", out=xx_sl[:, c * QT:(c + 1) * QT],
                      in_=xxB[c * 128:(c + 1) * 128, t0:t0 + QT], inc=ld_s.inc(16))
            pp.wait("tensor", ld_s); pp.wait("tensor", ld_g)
            for oc in range(NC):
                bo = banks.get()
                pairs = [(w2_sb[:, kc * D + oc * 128: kc * D + (oc + 1) * 128],
                          h_sl[:, kc * QT:(kc + 1) * QT]) for kc in range(22)]
                mm_group(pp, bo.t[:], pairs, inc=mm.inc())
                pp.wait("vector", mm)
                pp.wait("vector", ld_s); pp.wait("vector", ld_g)
                zs = z_slots.get("vector")
                zsl = z_sb[:, zs * QT:(zs + 1) * QT]
                pp.op("vector", "scalar_tensor_tensor", zsl, bo.t[:],
                      b2_col[:, li * 8 + oc: li * 8 + oc + 1],
                      xx_sl[:, oc * QT:(oc + 1) * QT], ALU.add, ALU.add, inc=ev.inc())
                banks.rel(bo, ev)
                pp.wait("sync", ev)
                pp.op("sync", "dma_start", out=xxA[oc * 128:(oc + 1) * 128, t0:t0 + QT],
                      in_=zsl, inc=st_s.inc(16))
                z_slots.used(zs, st_s)
            pp.block_end(mk)
        esG.close()

    esZ = ExitStack()
    sbZ = lambda name, shape, dt: esZ.enter_context(nc.sbuf_tensor("EZ" + name, shape, dt))
    z_sb2 = sbZ("z", [128, 8 * QT], F32)
    x3r = sbZ("x3r", [128, 8 * QT], F32)
    sqZ = sbZ("sq", [128, 2 * QT], F32)
    tmpZ = sbZ("tmp", [128, QT], F32)
    rowsZ = sbZ("rows", [1, 5 * QT], F32)
    x4_sb = sbZ("x4", [128, 2 * QT], F32)
    sqZ_slots = Slots(pp, 2)
    x4_slots = Slots(pp, 2)
    RZ = lambda r: rowsZ[0:1, r * QT:(r + 1) * QT]
    for qt in range(NQT):
        mk = pp.block()
        t0 = qt * QT
        for c in range(NC):
            pp.op("sync", "dma_start", out=z_sb2[:, c * QT:(c + 1) * QT],
                  in_=xxA[c * 128:(c + 1) * 128, t0:t0 + QT], inc=ld_s.inc(16))
        pp.wait("scalar", ld_s); pp.wait("scalar", ld_g)
        pp.wait("tensor", ld_s); pp.wait("tensor", ld_g)
        src = [z_sb2[:, c * QT:(c + 1) * QT] for c in range(NC)]
        ln_stats_rows(pp, banks, sqZ_slots, sqZ, ones_col, src, RZ, mm, ev, sqsem,
                      eps5[0:1, 0:1])
        brstd = banks.get(); bmmr = banks.get()
        pp.wait("tensor", ev)
        pp.op("tensor", "matmul", brstd.t[:], ones_row[:], RZ(3), start=True, stop=True)
        pp.op("tensor", "matmul", bmmr.t[:], ones_row[:], RZ(4), start=True, stop=True,
              inc=mm.inc())
        for c in range(NC):
            pp.op("sync", "dma_start", out=x3r[:, c * QT:(c + 1) * QT],
                  in_=x3T[c * 128:(c + 1) * 128, t0:t0 + QT], inc=ld_s.inc(16))
        pp.wait("vector", mm)
        pp.wait("vector", ld_s); pp.wait("vector", ld_g)
        for c in range(NC):
            s4_ = x4_slots.get("vector")
            x4sl = x4_sb[:, s4_ * QT:(s4_ + 1) * QT]
            pp.op("vector", "tensor_mul", tmpZ[:], z_sb2[:, c * QT:(c + 1) * QT], brstd.t[:])
            pp.op("vector", "tensor_sub", tmpZ[:], tmpZ[:], bmmr.t[:])
            pp.op("vector", "tensor_add", x4sl, tmpZ[:], x3r[:, c * QT:(c + 1) * QT],
                  inc=ev.inc())
            pp.wait("sync", ev)
            pp.op("sync", "dma_start", out=x4T[c * 128:(c + 1) * 128, t0:t0 + QT],
                  in_=x4sl, inc=st_s.inc(16))
            x4_slots.used(s4_, st_s)
        banks.rel(brstd, ev); banks.rel(bmmr, ev)
        pp.block_end(mk)
    esZ.close()
    pp.wait("sync", st_s); pp.wait("gpsimd", st_g)


def prep_shared_inputs(inputs):
    consts = host_constants()
    base = {
        "A_w_inT": np.ascontiguousarray(np.asarray(inputs["L0_attn_w_in"]).T),
        "A_b_in_cols": _cols(inputs["L0_attn_b_in"], 24),
        "A_b_vrow": np.ascontiguousarray(np.asarray(inputs["L0_attn_b_in"])[2048:3072].reshape(1, D)),
        "A_w_outT": np.ascontiguousarray(np.asarray(inputs["L0_attn_w_out"]).T),
        "A_b_out_cols": _cols(inputs["L0_attn_b_out"], 8),
        "maskA": consts["maskA"],
        "B_WqT": np.ascontiguousarray(np.asarray(inputs["L0_ffn_Wq"]).T),
        "B_WdkvT": np.ascontiguousarray(np.asarray(inputs["L0_ffn_Wdkv"]).T),
        "B_WukT": np.ascontiguousarray(np.asarray(inputs["L0_ffn_Wuk"]).T),
        "B_WuvT": np.ascontiguousarray(np.asarray(inputs["L0_ffn_Wuv"]).T),
        "B_WoT": np.ascontiguousarray(np.asarray(inputs["L0_ffn_Wo"]).T),
        "B_n1w_cols": _cols(inputs["L0_ffn_norm1_w"], 8),
        "B_lng_cols": _cols(inputs["L0_ffn_ln_g"], 4),
        "B_lnb_cols": _cols(inputs["L0_ffn_ln_b"], 4),
        "maskB": consts["maskB"],
        "maskE": consts["maskE"],
        "ropeB_C": consts["ropeB_C"], "ropeB_S": consts["ropeB_S"], "permB": consts["permB"],
        "C_n2w_cols": _cols(inputs["L0_ffn_norm2_w"], 8),
        "C_gwT": np.ascontiguousarray(np.asarray(inputs["L0_ffn_gate_w"]).T),
        "C_gb_col": np.ascontiguousarray(np.asarray(inputs["L0_ffn_gate_b"]).reshape(4, 1)),
        "C_W1T": np.ascontiguousarray(np.asarray(inputs["L0_ffn_W1"]).transpose(0, 2, 1).reshape(4 * D, 4096)),
        "C_W2T": np.ascontiguousarray(np.asarray(inputs["L0_ffn_W2"]).transpose(0, 2, 1).reshape(4 * 4096, D)),
        "C_b1_cols": np.concatenate([_cols(np.asarray(inputs["L0_ffn_b1"])[e], 32) for e in range(4)], axis=1),
        "C_b2_cols": np.concatenate([_cols(np.asarray(inputs["L0_ffn_b2"])[e], 8) for e in range(4)], axis=1),
        "D_wqT": np.ascontiguousarray(np.asarray(inputs["L1_attn_wq"]).T),
        "D_wkT": np.ascontiguousarray(np.asarray(inputs["L1_attn_wk"]).T),
        "D_wvT": np.ascontiguousarray(np.asarray(inputs["L1_attn_wv"]).T),
        "D_wg1T": np.ascontiguousarray(np.asarray(inputs["L1_attn_wg1"]).T),
        "D_bg1_col": np.ascontiguousarray(np.asarray(inputs["L1_attn_bg1"]).reshape(16, 1)),
        "D_wg2T": np.ascontiguousarray(np.asarray(inputs["L1_attn_wg2"]).T),
        "D_bg2_cols": _cols(inputs["L1_attn_bg2"], 8),
        "D_wogT": np.ascontiguousarray(np.asarray(inputs["L1_attn_wog"]).T),
        "D_bog_cols": _cols(inputs["L1_attn_bog"], 8),
        "D_woT": np.ascontiguousarray(np.asarray(inputs["L1_attn_wo"]).T),
        "D_wgateT": np.ascontiguousarray(np.asarray(inputs["L1_attn_w_gate"]).T),
        "D_wupT": np.ascontiguousarray(np.asarray(inputs["L1_attn_w_up"]).T),
        "D_wdownT": np.ascontiguousarray(np.asarray(inputs["L1_attn_w_down"]).T),
        "D_s1_cols": _cols(inputs["L1_attn_scale1"], 8),
        "D_s2_cols": _cols(inputs["L1_attn_scale2"], 8),
        "D_bd2": consts["bd2"], "D_ex2": consts["ex2"], "D_ex2f": consts["ex2f"],
        "E_Wqkv": np.ascontiguousarray(np.asarray(inputs["L1_ffn_Wqkv"]).reshape(6 * D, 3 * D)),
        "E_Wo": np.ascontiguousarray(np.asarray(inputs["L1_ffn_Wo"]).reshape(6 * D, D)),
        "E_qs_cols": np.ascontiguousarray(np.tile(np.asarray(inputs["L1_ffn_q_scale"]), (1, 2)).T),
        "E_ks_cols": np.ascontiguousarray(np.tile(np.asarray(inputs["L1_ffn_k_scale"]), (1, 2)).T),
        "E_lng_cols": np.concatenate([_cols(np.asarray(inputs["L1_ffn_ln_g"])[i], 8) for i in range(6)], axis=1),
        "E_lnb_cols": np.concatenate([_cols(np.asarray(inputs["L1_ffn_ln_b"])[i], 8) for i in range(6)], axis=1),
        "E_Cq": consts["E_Cq"], "E_Sq": consts["E_Sq"],
        "E_Ck": consts["E_Ck"], "E_Sk": consts["E_Sk"],
        "E_permE": consts["E_permE"],
    }
    DI = 2730
    W1 = np.asarray(inputs["L1_ffn_W1"])
    W1p = np.zeros((6, D, 5632), np.float32)
    W1p[:, :, 0:DI] = W1[:, :, 0:DI]
    W1p[:, :, 2816:2816 + DI] = W1[:, :, DI:2 * DI]
    base["E_W1p"] = np.ascontiguousarray(W1p.reshape(6 * D, 5632))
    W2 = np.asarray(inputs["L1_ffn_W2"])
    W2p = np.zeros((6, 2816, D), np.float32)
    W2p[:, 0:DI, :] = W2
    base["E_W2p"] = np.ascontiguousarray(W2p.reshape(6 * 2816, D))
    b1 = np.asarray(inputs["L1_ffn_b1"])
    b1p = np.zeros((6, 5632), np.float32)
    b1p[:, 0:DI] = b1[:, 0:DI]
    b1p[:, 2816:2816 + DI] = b1[:, DI:2 * DI]
    base["E_b1p_cols"] = np.concatenate([_cols(b1p[i], 44) for i in range(6)], axis=1)
    base["E_b2_cols"] = np.concatenate([_cols(np.asarray(inputs["L1_ffn_b2"])[i], 8) for i in range(6)], axis=1)
    return base


PHASES = ("A", "B", "C", "D", "E")
OUT_NAME = {"A": "x1T", "B": "aT", "C": "x2T", "D": "x3T", "E": "x4T"}

def kernel(**inputs):
    nc = build_program(phases=PHASES)
    base = prep_shared_inputs(inputs)
    x = np.asarray(inputs["L0_attn_x"])
    import ml_dtypes
    in_maps = []
    for b in range(8):
        m = dict(base)
        m["xT"] = np.ascontiguousarray(x[b].T)
        sm = (np.arange(8) <= b).astype(np.float32)
        m["D_smask"] = np.ascontiguousarray(np.tile(sm, (128, 1)))
        m["D_imask"] = np.ascontiguousarray(np.tile(1.0 - sm, (128, 1)))
        m["D_wm2"] = np.ascontiguousarray(np.tile(sm, (2, 1)))
        in_maps.append(m)
    res = run_bass_kernel_spmd(nc, in_maps, core_ids=list(range(8)))
    oname = OUT_NAME[PHASES[-1]]
    out = np.stack([np.ascontiguousarray(np.asarray(r[oname]).astype(np.float32).T)
                    for r in res.results], axis=0)
    return out
